# version 04
# baseline (speedup 1.0000x reference)
import numpy as np
import concourse.bass as bass
import concourse.mybir as mybir
from concourse.bass_utils import run_bass_kernel_spmd

F32 = mybir.dt.float32
ACT = mybir.ActivationFunctionType
ALU = mybir.AluOpType
AXX = mybir.AxisListType.X

D = 2048
H = 16
BATCH = 4
S = 2048
INTER = 8192
NCORES = 8
TOK = 1024
NCTRL = 130156
NS = 16777216
EPS = float(np.float32(1e-5))
FMIN = np.float32(np.finfo(np.float32).min)
SQRT_HD = float(np.float32(np.sqrt(np.float32(128.0))))

TPC = 128
TILE_EL = 128 * 128
WOFF = TPC * TILE_EL
SHARD = 3 * WOFF

KCS0 = [0, 1, 2, 3, 8, 9, 10, 11]
KCS1 = list(range(16))
MASK_PAIRS = [(0, kc) for kc in KCS0] + [(1, kc) for kc in (4, 5, 6, 7, 12, 13, 14, 15)]
MASK_SLOT = {p: i for i, p in enumerate(MASK_PAIRS)}

_static = {}
DEBUG = False



def _alibi_bias_f32():
    import jax
    import jax.numpy as jnp
    with jax.default_device(jax.devices("cpu")[0]):
        ratio = 2.0 ** (-8.0 / H)
        slopes = ratio ** jnp.arange(1, H + 1, dtype=jnp.float32)
        distances = jnp.arange(1 - S, 1, dtype=jnp.float32)
        bias = slopes[:, None] * distances[None, :]
        return np.asarray(bias), np.asarray(slopes)


def _recon_tiles_for(rank, w):
    tiles = []
    if w < 2:
        for t in range(TPC):
            oc, hcw = t // 2, t % 2
            tiles.append((oc * 128, (2 * rank + hcw) * 128, D))
    else:
        for t in range(TPC):
            icw, hpc = t // 16, t % 16
            tiles.append((hpc * 128, (8 * rank + icw) * 128, INTER))
    return tiles


def _build_static():
    if _static:
        return _static
    t64 = np.linspace(0.0, NCTRL - 1.0, NS, dtype=np.float64)
    i_arr = np.clip(np.floor(t64).astype(np.int64), 0, NCTRL - 2)
    f_arr = (t64 - i_arr).astype(np.float32)
    del t64
    bias_f32, slopes = _alibi_bias_f32()

    perms, btabs, rpts, masks = [], [], [], []
    for c in range(NCORES):
        par = c % 2
        blocks = [2 * k + par for k in range(8)] + [2 * k + 1 - par for k in range(8)]
        perms.append(blocks)
        gl = np.concatenate([np.arange(b * 128, b * 128 + 128) for b in blocks])
        kglob, qglob = gl, gl[:TOK]
        btab = np.empty((128, H * 16), np.float32)
        for h in range(H):
            for kc in range(16):
                btab[:, h * 16 + kc] = bias_f32[h, kglob[kc * 128:(kc + 1) * 128]]
        btabs.append(btab)
        rpt = np.empty((H, 2, 128, 512), np.float32)
        for h in range(H):
            row = (slopes[h] * (np.float32(S - 1) - qglob.astype(np.float32))
                   ).astype(np.float32)
            rpt[h, 0] = np.broadcast_to(row[:512], (128, 512))
            rpt[h, 1] = np.broadcast_to(row[512:], (128, 512))
        rpts.append(rpt)
        mk = np.zeros((len(MASK_PAIRS), 128, 512), np.float32)
        for mi, (tile, kc) in enumerate(MASK_PAIRS):
            kpos = kglob[kc * 128:(kc + 1) * 128]
            qpos = qglob[tile * 512:(tile + 1) * 512]
            mk[mi][kpos[:, None] > qpos[None, :]] = FMIN
        masks.append(mk)

    cps_idx, cpps, fplans = [], [], []
    u128 = np.arange(128)
    for c in range(NCORES):
        ci = np.empty((3, 128, TPC, 5), np.int64)
        cp = np.empty((3, 128, TPC), np.float32)
        fp = np.empty((3, 128, TPC, 128), np.float32)
        for w in range(3):
            for t, (row0, col0, IN) in enumerate(_recon_tiles_for(c, w)):
                j0 = (row0 + u128) * IN + col0
                ib = i_arr[j0]
                iv = i_arr[j0[:, None] + u128[None, :]]
                dlt = iv - ib[:, None]
                assert dlt.min() >= 0 and dlt.max() <= 1
                adv = dlt.argmax(axis=1).astype(np.float32)
                adv[dlt.max(axis=1) == 0] = 128.0
                cp[w, :, t] = adv
                for m in range(5):
                    ci[w, :, t, m] = np.clip(ib + m - 1, 0, NCTRL - 1)
                fp[w, :, t, :] = f_arr[j0[:, None] + u128[None, :]]
        cps_idx.append(ci)
        cpps.append(cp)
        fplans.append(fp.reshape(3, 128, TPC * 128))

    _static.update(dict(
        perms=perms, btabs=btabs, rpts=rpts, masks=masks,
        cps_idx=cps_idx, cpps=cpps, fplans=fplans,
        iota=np.ascontiguousarray(
            np.broadcast_to(np.arange(128, dtype=np.float32), (128, 128))),
        ident=np.eye(128, dtype=np.float32),
        misc=np.concatenate([
            np.ones((128, 129), np.float32),
            np.full((128, 1), np.float32(1e-5), np.float32),
            np.full((128, 1), np.float32(SQRT_HD), np.float32)], axis=1),
    ))
    return _static



class Ctr:
    def __init__(self, nc, name):
        self.h = nc.alloc_semaphore(name)
        self.n = 0

    def bump(self, k):
        self.n += k
        return self.n


def build_nc():
    nc = bass.Bass("TRN2", target_bir_lowering=False, debug=False,
                   detect_race_conditions=False)

    din = lambda n, sh: nc.dram_tensor(n, sh, F32, kind="ExternalInput")
    x_in = din("x", [S, D])
    wqb_in = din("wqb", [16, 16, 128, 128])
    wkb_in = din("wkb", [16, 16, 128, 128])
    wvb_in = din("wvb", [16, 16, 128, 128])
    wob_in = din("wob", [16, 16, 128, 128])
    btab_in = din("btab", [128, H * 16])
    rpt_in = din("rpt", [H, 2, 128, 512])
    mask_in = din("maskt", [len(MASK_PAIRS), 128, 512])
    bq_in = din("bqt", [128, 16])
    bk_in = din("bkt", [128, 16])
    bv_in = din("bvt", [128, 16])
    bo_in = din("bot", [128, 16])
    cp_in = din("cpst", [3, 128, TPC * 5])
    cpp_in = din("cppt", [3, 128, TPC])
    fpl_in = din("fplt", [3, 128, TPC * 128])
    iota_in = din("iota", [128, 128])
    ident_in = din("ident", [128, 128])
    misc_in = din("misc", [128, 131])

    y_out = nc.dram_tensor("y", [TOK, D], F32, kind="ExternalOutput")
    if DEBUG:
        dbg_xno = nc.dram_tensor("dbg_xno", [128, 16 * TOK], F32, kind="ExternalOutput")
        dbg_kt = nc.dram_tensor("dbg_kt", [D, S], F32, kind="ExternalOutput")
        dbg_vt = nc.dram_tensor("dbg_vt", [D, S], F32, kind="ExternalOutput")
        dbg_qt = nc.dram_tensor("dbg_qt", [D, TOK], F32, kind="ExternalOutput")
        dbg_at = nc.dram_tensor("dbg_at", [D, TOK], F32, kind="ExternalOutput")
        dbg_ag = nc.dram_tensor("dbg_ag", [SHARD], F32, kind="ExternalOutput")
        dbg_ago = nc.dram_tensor("dbg_ago", [2, SHARD], F32, kind="ExternalOutput")

    ag_in = nc.dram_tensor("ag_in", [SHARD], F32)
    ag_out = nc.dram_tensor("ag_out", [NCORES, SHARD], F32, addr_space="Shared")
    kt_sc = nc.dram_tensor("kt_sc", [D, S], F32)
    vt_sc = nc.dram_tensor("vt_sc", [D, S], F32)
    qt_sc = nc.dram_tensor("qt_sc", [D, TOK], F32)
    xno_sc = nc.dram_tensor("xno_sc", [128, 16 * TOK], F32)
    at_sc = nc.dram_tensor("at_sc", [D, TOK], F32)

    BIG = nc.alloc_sbuf_tensor("BIG", [128, 34816], F32)
    SMALL = nc.alloc_sbuf_tensor("SMALL", [128, 15360], F32)
    CON = nc.alloc_sbuf_tensor("CON", [128, 768], F32)
    big = lambda o, n: BIG[:, o:o + n]
    small = lambda o, n: SMALL[:, o:o + n]

    c_ident = CON[:, 0:128]
    c_iota = CON[:, 128:256]
    c_btab = CON[:, 256:512]
    c_bq = CON[:, 512:528]
    c_bk = CON[:, 528:544]
    c_bv = CON[:, 544:560]
    c_bo = CON[:, 560:576]
    c_misc = CON[:, 576:707]
    c_onesr = c_misc[:, 0:128]
    c_ones = c_misc[:, 128:129]
    c_eps = c_misc[:, 129:130]
    c_sqrt = c_misc[:, 130:131]
    c_rsq = CON[:, 707:708]

    p0 = nc.alloc_psum_tensor("p0", [128, 512], F32)
    p1 = nc.alloc_psum_tensor("p1", [128, 512], F32)
    p2 = nc.alloc_psum_tensor("p2", [128, 512], F32)
    p3 = nc.alloc_psum_tensor("p3", [128, 512], F32)
    p4 = nc.alloc_psum_tensor("p4", [128, 512], F32)
    p5 = nc.alloc_psum_tensor("p5", [128, 512], F32)
    p6 = nc.alloc_psum_tensor("p6", [128, 512], F32)
    p7 = nc.alloc_psum_tensor("p7", [128, 512], F32)

    Cm = lambda n: Ctr(nc, n)
    d_con = Cm("d_con")
    d_f = [Cm("d_f0"), Cm("d_f1")]
    d_x = [Cm("d_x0"), Cm("d_x1")]
    d_w = [Cm("d_w0"), Cm("d_w1")]
    d_h = [Cm("d_h0"), Cm("d_h1")]
    d_g = [Cm("d_g0"), Cm("d_g1")]
    d_u = [Cm("d_u0"), Cm("d_u1")]
    d_d = [Cm("d_d0"), Cm("d_d1")]
    d_a = [Cm("d_a0"), Cm("d_a1")]
    d_xr = Cm("d_xr")
    d_xh = Cm("d_xh")
    d_sp = Cm("d_sp")
    d_y = Cm("d_y")
    s_cc = Cm("s_cc")
    s_t = Cm("s_t")
    s_v = Cm("s_v")
    s_s = Cm("s_s")

    with nc.Block() as blk:
        @blk.sync
        def _(e):
            for dst, src in ((c_ident, ident_in), (c_iota, iota_in),
                             (c_btab, btab_in), (c_bq, bq_in), (c_bk, bk_in),
                             (c_bv, bv_in), (c_bo, bo_in), (c_misc, misc_in)):
                e.dma_start(out=dst, in_=src.ap()[:, :]).then_inc(d_con.h, 16)
                d_con.bump(16)

        @blk.vector
        def _(e):
            e.wait_ge(d_con.h, d_con.n)
            e.reciprocal(out=c_rsq, in_=c_sqrt)
    nc.const_aps.register(F32, EPS, c_eps[:, 0:1])

    TC = 32
    NCH = TPC // TC
    r_f = [big(0, 4096), big(4096, 4096)]
    r_f2 = big(8192, 4096)
    r_f3 = big(12288, 4096)
    r_t1 = big(16384, 4096)
    r_acc = big(20480, 4096)
    r_bac = big(24576, 4096)
    r_stage = big(28672, 4096)
    t_cps = small(0, TPC * 5)
    t_cpp = small(640, TPC)
    t_q = small(768, 8 * TPC)
    t_tmp = small(1792, 2 * TPC)

    with nc.Block() as blk:
        @blk.sync
        def _(e):
            e.dma_start(out=r_f[0], in_=fpl_in.ap()[0, :, 0:TC * 128]
                        ).then_inc(d_f[0].h, 16)
            d_f[0].bump(16)

    for w in range(3):
        with nc.Block() as blk:
            @blk.sync
            def _(e, w=w):
                e.dma_start(out=t_cps, in_=cp_in.ap()[w, :, :]).then_inc(d_con.h, 16)
                d_con.bump(16)
                e.dma_start(out=t_cpp, in_=cpp_in.ap()[w, :, :]).then_inc(d_con.h, 16)
                d_con.bump(16)

            @blk.vector
            def _(e, w=w):
                e.wait_ge(d_con.h, d_con.n)
                cv = t_cps.rearrange("p (t m) -> p t m", m=5)
                sv = [cv[:, :, m] for m in range(5)]
                q = [t_q[:, k * TPC:(k + 1) * TPC] for k in range(8)]
                tp0 = t_tmp[:, 0:TPC]
                tp1 = t_tmp[:, TPC:2 * TPC]

                def cubic(d0, d1, d2, d3, a, b, cc, d):
                    e.tensor_copy(out=d0, in_=b)
                    e.tensor_tensor(out=tp0, in0=cc, in1=a, op=ALU.subtract)
                    e.tensor_scalar_mul(d1, tp0, 0.5)
                    e.tensor_scalar_mul(tp0, b, -2.5)
                    e.tensor_tensor(out=tp0, in0=tp0, in1=a, op=ALU.add)
                    e.tensor_scalar_mul(tp1, cc, 2.0)
                    e.tensor_tensor(out=tp0, in0=tp0, in1=tp1, op=ALU.add)
                    e.tensor_scalar_mul(tp1, d, -0.5)
                    e.tensor_tensor(out=d2, in0=tp0, in1=tp1, op=ALU.add)
                    e.tensor_scalar_mul(tp0, b, 1.5)
                    e.tensor_scalar_mul(tp1, a, -0.5)
                    e.tensor_tensor(out=tp0, in0=tp0, in1=tp1, op=ALU.add)
                    e.tensor_scalar_mul(tp1, cc, -1.5)
                    e.tensor_tensor(out=tp0, in0=tp0, in1=tp1, op=ALU.add)
                    e.tensor_scalar_mul(tp1, d, 0.5)
                    e.tensor_tensor(out=d3, in0=tp0, in1=tp1, op=ALU.add)

                cubic(q[0], q[1], q[2], q[3], sv[0], sv[1], sv[2], sv[3])
                cubic(q[4], q[5], q[6], q[7], sv[1], sv[2], sv[3], sv[4])
                for k in range(4):
                    e.tensor_tensor(out=q[4 + k], in0=q[4 + k], in1=q[k],
                                    op=ALU.subtract)

        for ch in range(NCH):
            sl = ch % 2
            glob_last = (w == 2 and ch == NCH - 1)
            E_SQ = s_s.n + 1
            E_EVAL = s_v.n + 1
            E_TG = [s_t.n + g + 1 for g in range(TC // 4)]
            E_EV = [s_v.n + 1 + g + 1 for g in range(TC // 4)]
            D_SP_PREV = d_sp.n
            with nc.Block() as blk:
                @blk.sync
                def _(e, w=w, ch=ch, sl=sl, glob_last=glob_last):
                    if not glob_last:
                        nw, nch = (w, ch + 1) if ch + 1 < NCH else (w + 1, 0)
                        e.dma_start(
                            out=r_f[1 - sl],
                            in_=fpl_in.ap()[nw, :, nch * TC * 128:(nch + 1) * TC * 128]
                        ).then_inc(d_f[1 - sl].h, 16)
                        d_f[1 - sl].bump(16)

                @blk.scalar
                def _(e, sl=sl):
                    e.wait_ge(d_f[sl].h, d_f[sl].n)
                    e.activation(out=r_f2, in_=r_f[sl], func=ACT.Square
                                 ).then_inc(s_s.h, 1)
                    s_s.bump(1)

                @blk.vector
                def _(e, ch=ch, sl=sl):
                    e.wait_ge(s_s.h, E_SQ)
                    t0 = ch * TC
                    fb = r_f[sl].rearrange("p (t u) -> p t u", t=TC)
                    f2 = r_f2.rearrange("p (t u) -> p t u", t=TC)
                    f3 = r_f3.rearrange("p (t u) -> p t u", t=TC)
                    t1 = r_t1.rearrange("p (t u) -> p t u", t=TC)
                    acc = r_acc.rearrange("p (t u) -> p t u", t=TC)
                    bac = r_bac.rearrange("p (t u) -> p t u", t=TC)
                    qv = lambda k: t_q[:, k * TPC + t0:k * TPC + t0 + TC] \
                        .unsqueeze(2).broadcast_to([128, TC, 128])
                    cpv = t_cpp[:, t0:t0 + TC].unsqueeze(2).broadcast_to([128, TC, 128])
                    iov = c_iota.unsqueeze(1).broadcast_to([128, TC, 128])
                    e.tensor_tensor(out=f3, in0=f2, in1=fb, op=ALU.mult)
                    e.tensor_tensor(out=acc, in0=fb, in1=qv(1), op=ALU.mult)
                    e.tensor_tensor(out=acc, in0=acc, in1=qv(0), op=ALU.add)
                    e.tensor_tensor(out=t1, in0=f2, in1=qv(2), op=ALU.mult)
                    e.tensor_tensor(out=acc, in0=acc, in1=t1, op=ALU.add)
                    e.tensor_tensor(out=t1, in0=f3, in1=qv(3), op=ALU.mult)
                    e.tensor_tensor(out=acc, in0=acc, in1=t1, op=ALU.add)
                    e.tensor_tensor(out=bac, in0=fb, in1=qv(5), op=ALU.mult)
                    e.tensor_tensor(out=bac, in0=bac, in1=qv(4), op=ALU.add)
                    e.tensor_tensor(out=t1, in0=f2, in1=qv(6), op=ALU.mult)
                    e.tensor_tensor(out=bac, in0=bac, in1=t1, op=ALU.add)
                    e.tensor_tensor(out=t1, in0=f3, in1=qv(7), op=ALU.mult)
                    e.tensor_tensor(out=bac, in0=bac, in1=t1, op=ALU.add)
                    e.tensor_tensor(out=f2, in0=iov, in1=cpv, op=ALU.is_ge)
                    e.tensor_tensor(out=t1, in0=f2, in1=bac, op=ALU.mult)
                    e.tensor_tensor(out=acc, in0=acc, in1=t1, op=ALU.add
                                    ).then_inc(s_v.h, 1)
                    s_v.bump(1)
                    for g in range(TC // 4):
                        e.wait_ge(s_t.h, E_TG[g])
                        if g == 0:
                            e.wait_ge(d_sp.h, D_SP_PREV)
                        e.tensor_copy(out=r_stage[:, g * 512:(g + 1) * 512],
                                      in_=p6[:, :]).then_inc(s_v.h, 1)
                        s_v.bump(1)

                @blk.tensor
                def _(e):
                    e.wait_ge(s_v.h, E_EVAL)
                    for g in range(TC // 4):
                        if g >= 1:
                            e.wait_ge(s_v.h, E_EV[g - 1])
                        for j in range(4):
                            ti = g * 4 + j
                            ins = e.transpose(p6[:, j * 128:(j + 1) * 128],
                                              r_acc[:, ti * 128:(ti + 1) * 128],
                                              c_ident)
                        ins.then_inc(s_t.h, 1)
                        s_t.bump(1)

                @blk.gpsimd
                def _(e, w=w, ch=ch):
                    e.wait_ge(s_v.h, E_EV[-1])
                    t0 = ch * TC
                    e.dma_start(
                        out=ag_in.ap()[w * WOFF + t0 * TILE_EL:
                                       w * WOFF + (t0 + TC) * TILE_EL]
                        .rearrange("(t p f) -> p t f", p=128, f=128),
                        in_=r_stage.rearrange("p (t f) -> p t f", t=TC),
                    ).then_inc(d_sp.h, 16)
                    d_sp.bump(16)

    with nc.Block() as blk:
        @blk.gpsimd
        def _(e):
            e.wait_ge(d_sp.h, d_sp.n)
            e.collective_compute(
                "AllGather", ALU.bypass,
                replica_groups=[list(range(NCORES))],
                ins=[ag_in.ap().opt()],
                outs=[ag_out.ap().opt()],
            ).then_inc(s_cc.h, 1)
            s_cc.bump(1)

    xnt = big(0, 32768)
    xnt3 = xnt.rearrange("p (fc t) -> p fc t", fc=16)
    ln_x = [small(0, 2048), small(2048, 2048)]
    ln_t = small(4096, 2048)
    ln_s = small(6144, 2048)
    ln_c = small(8192, 8)

    T0, V0, S0 = s_t.n, s_v.n, s_s.n
    LNV = lambda tc, i: V0 + tc * 10 + i
    LNS = lambda tc, i: S0 + tc * 2 + i
    LNT = lambda tc, g: T0 + tc * 4 + g + 1

    with nc.Block() as blk:
        @blk.sync
        def _(e):
            for tc in range(16):
                if tc >= 2:
                    e.wait_ge(s_t.h, LNT(tc - 2, 3))
                e.dma_start(out=ln_x[tc % 2],
                            in_=x_in.ap()[tc * 128:(tc + 1) * 128, :]
                            ).then_inc(d_x[tc % 2].h, 16)
                d_x[tc % 2].bump(16)

        @blk.vector
        def _(e):
            for tc in range(16):
                e.wait_ge(d_x[tc % 2].h, 16 * (tc // 2 + 1))
                xs = ln_x[tc % 2]
                e.reduce_sum(out=ln_c[:, 0:1], in_=xs, axis=AXX).then_inc(s_v.h, 1)
                e.wait_ge(s_v.h, s_v.bump(1))
                e.tensor_scalar_mul(ln_c[:, 1:2], ln_c[:, 0:1], 1.0 / 2048.0
                                    ).then_inc(s_v.h, 1)
                e.wait_ge(s_v.h, s_v.bump(1))
                e.tensor_scalar(out=ln_t, in0=xs, scalar1=ln_c[:, 1:2],
                                scalar2=None, op0=ALU.subtract).then_inc(s_v.h, 1)
                s_v.bump(1)
                e.wait_ge(s_s.h, LNS(tc, 1))
                e.reduce_sum(out=ln_c[:, 0:1], in_=ln_s, axis=AXX).then_inc(s_v.h, 1)
                s_v.bump(1)
                e.wait_ge(s_s.h, LNS(tc, 2))
                e.reciprocal(out=ln_c[:, 3:4], in_=ln_c[:, 2:3]).then_inc(s_v.h, 1)
                e.wait_ge(s_v.h, s_v.bump(1))
                e.tensor_scalar(out=xs, in0=ln_t, scalar1=ln_c[:, 3:4],
                                scalar2=None, op0=ALU.mult).then_inc(s_v.h, 1)
                s_v.bump(1)
                for g in range(4):
                    e.wait_ge(s_t.h, LNT(tc, g))
                    dst = xnt3[:, g * 4:(g + 1) * 4, tc * 128:(tc + 1) * 128]
                    e.tensor_copy(out=dst,
                                  in_=p6.rearrange("p (j t) -> p j t", j=4)
                                  ).then_inc(s_v.h, 1)
                    s_v.bump(1)

        @blk.scalar
        def _(e):
            for tc in range(16):
                e.wait_ge(s_v.h, LNV(tc, 3))
                e.activation(out=ln_s, in_=ln_t, func=ACT.Square).then_inc(s_s.h, 1)
                s_s.bump(1)
                e.wait_ge(s_v.h, LNV(tc, 4))
                e.activation(out=ln_c[:, 2:3], in_=ln_c[:, 0:1], func=ACT.Sqrt,
                             scale=1.0 / 2048.0, bias=EPS).then_inc(s_s.h, 1)
                s_s.bump(1)

        @blk.tensor
        def _(e):
            for tc in range(16):
                e.wait_ge(s_v.h, LNV(tc, 6))
                for g in range(4):
                    prev = LNV(tc, 6 + g) if g >= 1 else (
                        LNV(tc - 1, 10) if tc >= 1 else None)
                    if prev is not None:
                        e.wait_ge(s_v.h, prev)
                    for j in range(4):
                        fc = g * 4 + j
                        ins = e.transpose(p6[:, j * 128:(j + 1) * 128],
                                          ln_x[tc % 2][:, fc * 128:(fc + 1) * 128],
                                          c_ident)
                    ins.then_inc(s_t.h, 1)
                    s_t.bump(1)

    with nc.Block() as blk:
        @blk.gpsimd
        def _(e):
            e.dma_start(out=xno_sc.ap().rearrange("p (fc t) -> p fc t", fc=16),
                        in_=xnt3[:, :, 0:TOK]).then_inc(d_sp.h, 16)
            d_sp.bump(16)

    w_slab = [small(0, 2048), small(2048, 2048)]
    pr_st = [small(4096, 512), small(4608, 512)]

    def proj(wten, bias_col, n_tt, dst_sc, div, rhs_fn):
        TT0, VV0, SP0 = s_t.n, s_v.n, d_sp.n
        W0 = (d_w[0].n, d_w[1].n)
        PT = lambda idx: TT0 + idx + 1
        PV = lambda idx: VV0 + idx + 1
        PS = lambda idx: SP0 + 16 * (idx + 1)
        with nc.Block() as blk:
            @blk.sync
            def _(e):
                for dc in range(16):
                    if dc >= 2:
                        e.wait_ge(s_t.h, PT((dc - 1) * n_tt - 1))
                    e.dma_start(out=w_slab[dc % 2]
                                .rearrange("p (k f) -> p k f", k=16),
                                in_=wten.ap()[dc].rearrange("k p f -> p k f")
                                ).then_inc(d_w[dc % 2].h, 16)
                    d_w[dc % 2].bump(16)

            @blk.tensor
            def _(e):
                for dc in range(16):
                    e.wait_ge(d_w[dc % 2].h, W0[dc % 2] + 16 * (dc // 2 + 1))
                    for tt in range(n_tt):
                        idx = dc * n_tt + tt
                        ps = (p0, p1)[idx % 2]
                        if idx >= 2:
                            e.wait_ge(s_v.h, PV(idx - 2))
                        for hc in range(16):
                            ins = e.matmul(ps[:, :],
                                           w_slab[dc % 2][:, hc * 128:(hc + 1) * 128],
                                           rhs_fn(hc, tt),
                                           start=(hc == 0), stop=(hc == 15))
                        ins.then_inc(s_t.h, 1)
                        s_t.bump(1)

            @blk.vector
            def _(e):
                for dc in range(16):
                    for tt in range(n_tt):
                        idx = dc * n_tt + tt
                        ps = (p0, p1)[idx % 2]
                        e.wait_ge(s_t.h, PT(idx))
                        if idx >= 2:
                            e.wait_ge(d_sp.h, PS(idx - 2))
                        if div:
                            e.tensor_scalar(out=pr_st[idx % 2], in0=ps[:, :],
                                            scalar1=bias_col[:, dc:dc + 1],
                                            scalar2=c_rsq[:, 0:1],
                                            op0=ALU.add, op1=ALU.mult
                                            ).then_inc(s_v.h, 1)
                        else:
                            e.tensor_scalar(out=pr_st[idx % 2], in0=ps[:, :],
                                            scalar1=bias_col[:, dc:dc + 1],
                                            scalar2=None, op0=ALU.add
                                            ).then_inc(s_v.h, 1)
                        s_v.bump(1)

            @blk.gpsimd
            def _(e):
                for dc in range(16):
                    for tt in range(n_tt):
                        idx = dc * n_tt + tt
                        e.wait_ge(s_v.h, PV(idx))
                        e.dma_start(out=dst_sc.ap()[dc * 128:(dc + 1) * 128,
                                                    tt * 512:(tt + 1) * 512],
                                    in_=pr_st[idx % 2]).then_inc(d_sp.h, 16)
                        d_sp.bump(16)

    xnt_rhs = lambda hc, tt: xnt3[:, hc, tt * 512:(tt + 1) * 512]
    proj(wkb_in, c_bk, 4, kt_sc, False, xnt_rhs)
    proj(wvb_in, c_bv, 4, vt_sc, False, xnt_rhs)
    proj(wqb_in, c_bq, 2, qt_sc, True, xnt_rhs)

    ctxt = big(0, 16384)
    mks = big(16384, 8192)
    x_exp = [big(24576, 512), big(25088, 512)]
    x_run = big(25600, 512)
    a_kt = [small(0, 2048), small(2048, 2048)]
    a_vt = [small(4096, 2048), small(6144, 2048)]
    a_qt = [small(8192, 1024), small(9216, 1024)]
    a_pl = [small(10240, 1024), small(11264, 1024)]
    a_vtr = small(12288, 2048)

    with nc.Block() as blk:
        @blk.sync
        def _(e):
            e.dma_start(out=mks.rearrange("p (m f) -> p m f", m=16),
                        in_=mask_in.ap().rearrange("m p f -> p m f")
                        ).then_inc(d_con.h, 16)
            d_con.bump(16)

    AT, AV, AS = {}, {}, {}
    tn, vn, sn = s_t.n, s_v.n, s_s.n
    exp_slot_seq = []
    sc_bank_seq = []
    for h in range(H):
        for g in range(4):
            tn += 1
            AT[("vtr", h, g)] = tn
        for tile in range(2):
            kcl = (KCS0, KCS1)[tile]
            n = len(kcl)
            order = [("sc", 0), ("sc", 1)]
            for k in range(n - 2):
                order += [("ctx", k), ("sc", k + 2)]
            order += [("ctx", n - 2), ("ctx", n - 1), ("den",), ("bc",)]
            for op in order:
                tn += 1
                AT[(op[0], h, tile) + tuple(op[1:])] = tn
                if op[0] == "sc":
                    sc_bank_seq.append((h, tile, op[1]))
        for g in range(4):
            vn += 1
            AV[("vtrev", h, g)] = vn
        for tile in range(2):
            kcl = (KCS0, KCS1)[tile]
            n = len(kcl)
            order = [("arg", 0), ("arg", 1)]
            for k in range(n - 2):
                order += [("run", k), ("arg", k + 2)]
            order += [("run", n - 2), ("run", n - 1), ("recip",), ("bcev",),
                      ("norm",)]
            for op in order:
                vn += 1
                AV[(op[0], h, tile) + tuple(op[1:])] = vn
        for tile in range(2):
            n = len((KCS0, KCS1)[tile])
            for ki in range(n):
                sn += 1
                AS[("exp", h, tile, ki)] = sn
                exp_slot_seq.append((h, tile, ki))
    sc_prev2 = {key: (sc_bank_seq[i - 2] if i >= 2 else None)
                for i, key in enumerate(sc_bank_seq)}
    exp_prev2 = {key: (exp_slot_seq[i - 2] if i >= 2 else None)
                 for i, key in enumerate(exp_slot_seq)}
    bcev_order = [(h, t) for h in range(H) for t in range(2)]

    with nc.Block() as blk:
        @blk.tensor
        def _(e):
            for h in range(H):
                sl = h % 2
                e.wait_ge(d_h[sl].h, 64 * (h // 2 + 1))
                if h == 0:
                    e.wait_ge(d_con.h, d_con.n)
                for g in range(4):
                    prev = ("vtrev", h, g - 1) if g >= 1 else (
                        ("vtrev", h - 1, 3) if h >= 1 else None)
                    if prev is not None:
                        e.wait_ge(s_v.h, AV[prev])
                    for j in range(4):
                        kc = g * 4 + j
                        ins = e.transpose(p6[:, j * 128:(j + 1) * 128],
                                          a_vt[sl][:, kc * 128:(kc + 1) * 128],
                                          c_ident)
                    ins.then_inc(s_t.h, 1)
                    s_t.bump(1)
                for tile in range(2):
                    kcl = (KCS0, KCS1)[tile]
                    n = len(kcl)
                    pctx = (p2, p3)[tile]

                    def esc(ki, h=h, tile=tile, sl=sl, kcl=kcl):
                        key = (h, tile, ki)
                        prev = sc_prev2[key]
                        if prev is not None:
                            e.wait_ge(s_s.h, AS[("exp",) + prev])
                        kc = kcl[ki]
                        ps = (p0, p1)[ki % 2]
                        e.matmul(ps[:, :],
                                 a_kt[sl][:, kc * 128:(kc + 1) * 128],
                                 a_qt[sl][:, tile * 512:(tile + 1) * 512],
                                 start=True, stop=True).then_inc(s_t.h, 1)
                        s_t.bump(1)

                    def ectx(ki, h=h, tile=tile, kcl=kcl, n=n, pctx=pctx):
                        if ki == 0:
                            e.wait_ge(s_v.h, AV[("vtrev", h, 3)])
                            if h >= 1:
                                e.wait_ge(s_v.h, AV[("norm", h - 1, tile)])
                        e.wait_ge(s_s.h, AS[("exp", h, tile, ki)])
                        kc = kcl[ki]
                        e.matmul(pctx[:, :], a_vtr[:, kc * 128:(kc + 1) * 128],
                                 x_exp[ki % 2],
                                 start=(ki == 0), stop=(ki == n - 1)
                                 ).then_inc(s_t.h, 1)
                        s_t.bump(1)

                    esc(0)
                    esc(1)
                    for k in range(n - 2):
                        ectx(k)
                        esc(k + 2)
                    ectx(n - 2)
                    ectx(n - 1)
                    e.wait_ge(s_v.h, AV[("run", h, tile, n - 1)])
                    bi = bcev_order.index((h, tile))
                    if bi >= 1:
                        e.wait_ge(s_v.h, AV[("bcev",) + bcev_order[bi - 1]])
                    e.matmul(p7[0:1, :], c_ones, x_run, start=True, stop=True
                             ).then_inc(s_t.h, 1)
                    s_t.bump(1)
                    e.wait_ge(s_v.h, AV[("recip", h, tile)])
                    e.matmul(p7[:, :], c_onesr[0:1, :], x_run[0:1, :],
                             start=True, stop=True).then_inc(s_t.h, 1)
                    s_t.bump(1)

        @blk.vector
        def _(e):
            for h in range(H):
                sl = h % 2
                for g in range(4):
                    e.wait_ge(s_t.h, AT[("vtr", h, g)])
                    if h >= 1 and g == 0:
                        e.wait_ge(s_t.h, AT[("ctx", h - 1, 1, len(KCS1) - 1)])
                    e.tensor_copy(out=a_vtr[:, g * 512:(g + 1) * 512], in_=p6[:, :]
                                  ).then_inc(s_v.h, 1)
                    s_v.bump(1)
                for tile in range(2):
                    kcl = (KCS0, KCS1)[tile]
                    n = len(kcl)
                    pctx = (p2, p3)[tile]

                    def earg(ki, h=h, tile=tile, sl=sl, kcl=kcl):
                        e.wait_ge(s_t.h, AT[("sc", h, tile, ki)])
                        kc = kcl[ki]
                        ps = (p0, p1)[ki % 2]
                        ins = e.tensor_tensor(
                            out=ps[:, :], in0=ps[:, :],
                            in1=a_pl[sl][:, tile * 512:(tile + 1) * 512],
                            op=ALU.add)
                        mi = MASK_SLOT.get((tile, kc))
                        if mi is not None:
                            ins = e.tensor_tensor(
                                out=ps[:, :], in0=ps[:, :],
                                in1=mks[:, mi * 512:(mi + 1) * 512], op=ALU.add)
                        ins.then_inc(s_v.h, 1)
                        s_v.bump(1)

                    def erun(ki, h=h, tile=tile, n=n):
                        e.wait_ge(s_s.h, AS[("exp", h, tile, ki)])
                        if ki == 0:
                            bi = bcev_order.index((h, tile))
                            if bi >= 1:
                                e.wait_ge(s_t.h, AT[("den",) + bcev_order[bi - 1]])
                            e.tensor_copy(out=x_run, in_=x_exp[0]
                                          ).then_inc(s_v.h, 1)
                        else:
                            e.tensor_tensor(out=x_run, in0=x_run,
                                            in1=x_exp[ki % 2], op=ALU.add
                                            ).then_inc(s_v.h, 1)
                        s_v.bump(1)

                    earg(0)
                    earg(1)
                    for k in range(n - 2):
                        erun(k)
                        earg(k + 2)
                    erun(n - 2)
                    erun(n - 1)
                    e.wait_ge(s_t.h, AT[("den", h, tile)])
                    e.reciprocal(out=x_run[0:1, :], in_=p7[0:1, :]
                                 ).then_inc(s_v.h, 1)
                    s_v.bump(1)
                    e.wait_ge(s_t.h, AT[("bc", h, tile)])
                    e.tensor_copy(out=x_run, in_=p7[:, :]).then_inc(s_v.h, 1)
                    s_v.bump(1)
                    e.wait_ge(s_t.h, AT[("ctx", h, tile, n - 1)])
                    e.scalar_tensor_tensor(
                        out=ctxt[:, h * 1024 + tile * 512:h * 1024 + (tile + 1) * 512],
                        in0=pctx[:, :], scalar=1.0, in1=x_run,
                        op0=ALU.mult, op1=ALU.mult).then_inc(s_v.h, 1)
                    s_v.bump(1)

        @blk.scalar
        def _(e):
            for h in range(H):
                for tile in range(2):
                    n = len((KCS0, KCS1)[tile])
                    for ki in range(n):
                        key = (h, tile, ki)
                        e.wait_ge(s_v.h, AV[("arg", h, tile, ki)])
                        prev = exp_prev2[key]
                        if prev is not None:
                            ph, pt, pk = prev
                            e.wait_ge(s_t.h, AT[("ctx", ph, pt, pk)])
                            e.wait_ge(s_v.h, AV[("run", ph, pt, pk)])
                        ps = (p0, p1)[ki % 2]
                        e.activation(out=x_exp[ki % 2], in_=ps[:, :], func=ACT.Exp,
                                     bias=c_btab[:, h * 16 + (KCS0, KCS1)[tile][ki]:
                                                 h * 16 + (KCS0, KCS1)[tile][ki] + 1],
                                     scale=1.0).then_inc(s_s.h, 1)
                        s_s.bump(1)

        @blk.sync
        def _(e):
            for h in range(H):
                sl = h % 2
                if h >= 2:
                    e.wait_ge(s_t.h, AT[("ctx", h - 2, 1, len(KCS1) - 1)])
                if h == 0:
                    e.wait_ge(d_sp.h, d_sp.n)
                e.dma_start(out=a_kt[sl], in_=kt_sc.ap()[h * 128:(h + 1) * 128, :]
                            ).then_inc(d_h[sl].h, 16)
                d_h[sl].bump(16)
                e.dma_start(out=a_vt[sl], in_=vt_sc.ap()[h * 128:(h + 1) * 128, :]
                            ).then_inc(d_h[sl].h, 16)
                d_h[sl].bump(16)
                e.dma_start(out=a_qt[sl], in_=qt_sc.ap()[h * 128:(h + 1) * 128, :]
                            ).then_inc(d_h[sl].h, 16)
                d_h[sl].bump(16)
                e.dma_start(out=a_pl[sl].rearrange("p (t f) -> p t f", t=2),
                            in_=rpt_in.ap()[h].rearrange("t p f -> p t f")
                            ).then_inc(d_h[sl].h, 16)
                d_h[sl].bump(16)

    ctx_rhs = lambda hc, tt: ctxt[:, hc * 1024 + tt * 512:hc * 1024 + (tt + 1) * 512]
    proj(wob_in, c_bo, 2, at_sc, False, ctx_rhs)

    intT = big(0, 32768)
    intT3 = intT.rearrange("p (ic t) -> p ic t", ic=64)
    f_xnh = small(0, 8192)
    xnh3 = f_xnh.rearrange("p (hc t) -> p hc t", hc=16)
    f_sl = [small(8192, 2048), small(10240, 2048)]
    f_si = [small(12288, 512), small(12800, 512)]
    f_dsl = [small(0, 2048), small(2048, 2048)]
    f_at = [small(4096, 512), small(4608, 512)]
    f_sum = small(5120, 8192)
    f_os = [small(0, 2048), small(2048, 2048)]
    f_xt = small(13312, 2048)

    def gate_slab_ap(w, ic, k):
        base = w * WOFF
        return ag_out.ap()[:, base + ic * 32768:base + (ic + 1) * 32768] \
            .rearrange("r (k p f) -> k r p f", k=2, p=128, f=128)[k] \
            .rearrange("r p f -> p r f")

    def down_slab_ap(hp, pc, r):
        return ag_out.ap()[2 * pc + r, 2 * WOFF:3 * WOFF] \
            .rearrange("(icw hq p f) -> icw hq p f",
                       icw=8, hq=16, p=128, f=128)[:, hp, :, :] \
            .rearrange("icw p f -> p icw f")

    def slg(t, k):
        return t.rearrange("p (r kk f) -> p kk r f", kk=2, f=128)[:, k, :, :]
    def sld(t, r):
        return t.rearrange("p (r icw f) -> p r icw f", r=2, icw=8, f=128)[:, r, :, :]

    for th in range(2):
        TT0, VV0, SS0 = s_t.n, s_v.n, s_s.n
        G0, U0 = d_g[0].n, d_g[1].n
        FTg = lambda ic: TT0 + 2 * ic + 1
        FTu = lambda ic: TT0 + 2 * ic + 2
        FV = lambda ic: VV0 + ic + 1
        FS = lambda ic: SS0 + ic + 1
        with nc.Block() as blk:
            @blk.sync
            def _(e, th=th):
                e.dma_start(out=xnh3,
                            in_=xno_sc.ap().rearrange("p (hc t) -> p hc t", hc=16)
                            [:, :, th * 512:(th + 1) * 512]).then_inc(d_xh.h, 16)
                d_xh.bump(16)

            @blk.gpsimd
            def _(e, th=th):
                if th == 0:
                    e.wait_ge(s_cc.h, 1)
                for ic in range(64):
                    if ic >= 1:
                        e.wait_ge(s_t.h, FTg(ic - 1))
                    for kkk in range(2):
                        e.dma_start(out=slg(f_sl[0], kkk),
                                    in_=gate_slab_ap(0, ic, kkk)
                                    ).then_inc(d_g[0].h, 16)
                        d_g[0].bump(16)
                    if ic >= 1:
                        e.wait_ge(s_t.h, FTu(ic - 1))
                    for kkk in range(2):
                        e.dma_start(out=slg(f_sl[1], kkk),
                                    in_=gate_slab_ap(1, ic, kkk)
                                    ).then_inc(d_g[1].h, 16)
                        d_g[1].bump(16)

            @blk.tensor
            def _(e, th=th):
                e.wait_ge(d_xh.h, d_xh.n)
                for ic in range(64):
                    pg = (p0, p1)[ic % 2]
                    pu = (p2, p3)[ic % 2]
                    if ic >= 2:
                        e.wait_ge(s_s.h, FS(ic - 2))
                    e.wait_ge(d_g[0].h, G0 + 32 * (ic + 1))
                    for hc in range(16):
                        ins = e.matmul(pg[:, :], f_sl[0][:, hc * 128:(hc + 1) * 128],
                                       xnh3[:, hc, :], start=(hc == 0),
                                       stop=(hc == 15))
                    ins.then_inc(s_t.h, 1)
                    s_t.bump(1)
                    if ic >= 2:
                        e.wait_ge(s_v.h, FV(ic - 2))
                    e.wait_ge(d_g[1].h, U0 + 32 * (ic + 1))
                    for hc in range(16):
                        ins = e.matmul(pu[:, :], f_sl[1][:, hc * 128:(hc + 1) * 128],
                                       xnh3[:, hc, :], start=(hc == 0),
                                       stop=(hc == 15))
                    ins.then_inc(s_t.h, 1)
                    s_t.bump(1)

            @blk.scalar
            def _(e):
                for ic in range(64):
                    e.wait_ge(s_t.h, FTg(ic))
                    if ic >= 2:
                        e.wait_ge(s_v.h, FV(ic - 2))
                    pg = (p0, p1)[ic % 2]
                    e.activation(out=f_si[ic % 2], in_=pg[:, :], func=ACT.Silu
                                 ).then_inc(s_s.h, 1)
                    s_s.bump(1)

            @blk.vector
            def _(e):
                for ic in range(64):
                    e.wait_ge(s_s.h, FS(ic))
                    e.wait_ge(s_t.h, FTu(ic))
                    pu = (p2, p3)[ic % 2]
                    e.tensor_tensor(out=intT3[:, ic, :], in0=f_si[ic % 2],
                                    in1=pu[:, :], op=ALU.mult).then_inc(s_v.h, 1)
                    s_v.bump(1)

        TT1, VV1 = s_t.n, s_v.n
        A0 = (d_a[0].n, d_a[1].n)
        D0 = (d_d[0].n, d_d[1].n)
        DT = lambda hp, pc: TT1 + hp * 4 + pc + 1
        DV = lambda hp: VV1 + hp + 1
        with nc.Block() as blk:
            @blk.sync
            def _(e, th=th):
                for hp in range(16):
                    if hp >= 2:
                        e.wait_ge(s_v.h, DV(hp - 2))
                    e.dma_start(out=f_at[hp % 2],
                                in_=at_sc.ap()[hp * 128:(hp + 1) * 128,
                                               th * 512:(th + 1) * 512]
                                ).then_inc(d_a[hp % 2].h, 16)
                    d_a[hp % 2].bump(16)

            @blk.gpsimd
            def _(e, th=th):
                for hp in range(16):
                    for pc in range(4):
                        gi = hp * 4 + pc
                        if gi >= 2:
                            e.wait_ge(s_t.h, DT((gi - 2) // 4, (gi - 2) % 4))
                        for rrr in range(2):
                            e.dma_start(out=sld(f_dsl[gi % 2], rrr),
                                        in_=down_slab_ap(hp, pc, rrr)
                                        ).then_inc(d_d[gi % 2].h, 16)
                            d_d[gi % 2].bump(16)

            @blk.tensor
            def _(e, th=th):
                for hp in range(16):
                    pd = (p4, p5)[hp % 2]
                    if hp >= 2:
                        e.wait_ge(s_v.h, DV(hp - 2))
                    for pc in range(4):
                        gi = hp * 4 + pc
                        e.wait_ge(d_d[gi % 2].h, D0[gi % 2] + 32 * (gi // 2 + 1))
                        for kk in range(16):
                            ic = pc * 16 + kk
                            ins = e.matmul(pd[:, :],
                                           f_dsl[gi % 2][:, kk * 128:(kk + 1) * 128],
                                           intT3[:, ic, :],
                                           start=(pc == 0 and kk == 0),
                                           stop=(pc == 3 and kk == 15))
                        ins.then_inc(s_t.h, 1)
                        s_t.bump(1)

            @blk.vector
            def _(e, th=th):
                for hp in range(16):
                    e.wait_ge(s_t.h, DT(hp, 3))
                    e.wait_ge(d_a[hp % 2].h, A0[hp % 2] + 16 * (hp // 2 + 1))
                    pd = (p4, p5)[hp % 2]
                    e.scalar_tensor_tensor(
                        out=f_sum[:, hp * 512:(hp + 1) * 512],
                        in0=pd[:, :], scalar=1.0, in1=f_at[hp % 2],
                        op0=ALU.mult, op1=ALU.add).then_inc(s_v.h, 1)
                    s_v.bump(1)

        TT2, VV2, XR0, DY0 = s_t.n, s_v.n, d_xr.n, d_y.n
        GT = lambda tc, g: TT2 + tc * 4 + g + 1
        GV = lambda tc, i: VV2 + tc * 5 + i + 1
        with nc.Block() as blk:
            @blk.sync
            def _(e, th=th):
                for tc in range(4):
                    if tc >= 1:
                        e.wait_ge(s_v.h, GV(tc - 1, 4))
                    e.dma_start(out=f_xt,
                                in_=x_in.ap()[(th * 4 + tc) * 128:
                                              (th * 4 + tc + 1) * 128, :]
                                ).then_inc(d_xr.h, 16)
                    d_xr.bump(16)

            @blk.tensor
            def _(e, th=th):
                for tc in range(4):
                    for g in range(4):
                        prev = (tc, g - 1) if g >= 1 else (
                            (tc - 1, 3) if tc >= 1 else None)
                        if prev is not None:
                            e.wait_ge(s_v.h, GV(prev[0], prev[1]))
                        for j in range(4):
                            hp = g * 4 + j
                            ins = e.transpose(p6[:, j * 128:(j + 1) * 128],
                                              f_sum[:, hp * 512 + tc * 128:
                                                    hp * 512 + (tc + 1) * 128],
                                              c_ident)
                        ins.then_inc(s_t.h, 1)
                        s_t.bump(1)

            @blk.vector
            def _(e, th=th):
                for tc in range(4):
                    for g in range(4):
                        e.wait_ge(s_t.h, GT(tc, g))
                        if g == 0:
                            if tc >= 2:
                                e.wait_ge(d_y.h, DY0 + 16 * (tc - 1))
                            elif th > 0:
                                e.wait_ge(d_y.h, DY0)
                        e.tensor_copy(out=f_os[tc % 2][:, g * 512:(g + 1) * 512],
                                      in_=p6[:, :]).then_inc(s_v.h, 1)
                        s_v.bump(1)
                    e.wait_ge(d_xr.h, XR0 + 16 * (tc + 1))
                    e.tensor_tensor(out=f_os[tc % 2], in0=f_os[tc % 2],
                                    in1=f_xt, op=ALU.add).then_inc(s_v.h, 1)
                    s_v.bump(1)

            @blk.gpsimd
            def _(e, th=th):
                for tc in range(4):
                    e.wait_ge(s_v.h, GV(tc, 4))
                    e.dma_start(out=y_out.ap()[(th * 4 + tc) * 128:
                                               (th * 4 + tc + 1) * 128, :],
                                in_=f_os[tc % 2]).then_inc(d_y.h, 16)
                    d_y.bump(16)

    with nc.Block() as blk:
        @blk.gpsimd
        def _(e):
            e.wait_ge(d_y.h, d_y.n)
            if DEBUG:
                for dst, srcp in ((dbg_xno, xno_sc), (dbg_kt, kt_sc),
                                  (dbg_vt, vt_sc), (dbg_qt, qt_sc),
                                  (dbg_at, at_sc), (dbg_ag, ag_in)):
                    e.dma_start(out=dst.ap()[:], in_=srcp.ap()[:]
                                ).then_inc(d_y.h, 16)
                    d_y.bump(16)
                e.dma_start(out=dbg_ago.ap()[:], in_=ag_out.ap()[0:2]
                            ).then_inc(d_y.h, 16)
                d_y.bump(16)
                e.wait_ge(d_y.h, d_y.n)

    nc.finalize()
    return nc



def _block4(w):
    hh, dd = w.shape
    return np.ascontiguousarray(
        w.reshape(hh // 128, 128, dd // 128, 128).transpose(2, 0, 1, 3))


def kernel(L0_attn_x, L0_attn_Wq, L0_attn_bq, L0_attn_Wk, L0_attn_bk,
           L0_attn_Wv, L0_attn_bv, L0_attn_Wo, L0_attn_bo,
           L0_ffn_gate_cp, L0_ffn_up_cp, L0_ffn_down_cp):
    st = _build_static()
    x = np.asarray(L0_attn_x, dtype=np.float32)
    wq = _block4(np.asarray(L0_attn_Wq, np.float32))
    wk = _block4(np.asarray(L0_attn_Wk, np.float32))
    wv = _block4(np.asarray(L0_attn_Wv, np.float32))
    wo = _block4(np.asarray(L0_attn_Wo, np.float32))
    bq = np.ascontiguousarray(np.asarray(L0_attn_bq, np.float32).reshape(16, 128).T)
    bk = np.ascontiguousarray(np.asarray(L0_attn_bk, np.float32).reshape(16, 128).T)
    bv = np.ascontiguousarray(np.asarray(L0_attn_bv, np.float32).reshape(16, 128).T)
    bo = np.ascontiguousarray(np.asarray(L0_attn_bo, np.float32).reshape(16, 128).T)
    cps = [np.asarray(L0_ffn_gate_cp, np.float32),
           np.asarray(L0_ffn_up_cp, np.float32),
           np.asarray(L0_ffn_down_cp, np.float32)]

    nc = build_nc()
    in_maps = []
    for c in range(NCORES):
        b = c // 2
        rows = np.concatenate([np.arange(bb * 128, bb * 128 + 128)
                               for bb in st["perms"][c]])
        xp = np.ascontiguousarray(x[b][rows])
        cgat = np.empty((3, 128, TPC * 5), np.float32)
        for w in range(3):
            cgat[w] = cps[w][st["cps_idx"][c][w]].reshape(128, TPC * 5)
        in_maps.append(dict(
            x=xp, wqb=wq, wkb=wk, wvb=wv, wob=wo,
            btab=st["btabs"][c], rpt=st["rpts"][c], maskt=st["masks"][c],
            bqt=bq, bkt=bk, bvt=bv, bot=bo,
            cpst=cgat, cppt=st["cpps"][c], fplt=st["fplans"][c],
            iota=st["iota"], ident=st["ident"], misc=st["misc"],
        ))

    res = run_bass_kernel_spmd(nc, in_maps, core_ids=list(range(NCORES)))
    out = np.empty((BATCH, S, D), np.float32)
    for c in range(NCORES):
        b = c // 2
        rows = np.concatenate([np.arange(bb * 128, bb * 128 + 128)
                               for bb in st["perms"][c]])[:TOK]
        out[b][rows] = res.results[c]["y"]
    return out


# version 10
# speedup vs baseline: 1.0139x; 1.0139x over previous
import numpy as np
import concourse.bass as bass
import concourse.mybir as mybir
from concourse.bass_utils import run_bass_kernel_spmd

F32 = mybir.dt.float32
ACT = mybir.ActivationFunctionType
ALU = mybir.AluOpType
AXX = mybir.AxisListType.X

D = 2048
H = 16
BATCH = 4
S = 2048
INTER = 8192
NCORES = 8
TOK = 1024
NCTRL = 130156
NS = 16777216
EPS = float(np.float32(1e-5))
FMIN = np.float32(np.finfo(np.float32).min)
SQRT_HD = float(np.float32(np.sqrt(np.float32(128.0))))

TPC = 128
TILE_EL = 128 * 128
WOFF = TPC * TILE_EL
SHARD = 3 * WOFF

KCS0 = [0, 1, 2, 3, 8, 9, 10, 11]
KCS1 = list(range(16))
MASK_PAIRS = [(0, kc) for kc in KCS0] + [(1, kc) for kc in (4, 5, 6, 7, 12, 13, 14, 15)]
MASK_SLOT = {p: i for i, p in enumerate(MASK_PAIRS)}

_static = {}
DEBUG = False



def _alibi_bias_f32():
    import jax
    import jax.numpy as jnp
    with jax.default_device(jax.devices("cpu")[0]):
        ratio = 2.0 ** (-8.0 / H)
        slopes = ratio ** jnp.arange(1, H + 1, dtype=jnp.float32)
        distances = jnp.arange(1 - S, 1, dtype=jnp.float32)
        bias = slopes[:, None] * distances[None, :]
        return np.asarray(bias), np.asarray(slopes)


def _recon_tiles_for(rank, w):
    tiles = []
    if w < 2:
        for t in range(TPC):
            oc, hcw = t // 2, t % 2
            tiles.append((oc * 128, (2 * rank + hcw) * 128, D))
    else:
        for t in range(TPC):
            icw, hpc = t // 16, t % 16
            tiles.append((hpc * 128, (8 * rank + icw) * 128, INTER))
    return tiles


def _build_static():
    if _static:
        return _static
    t64 = np.linspace(0.0, NCTRL - 1.0, NS, dtype=np.float64)
    i_arr = np.clip(np.floor(t64).astype(np.int64), 0, NCTRL - 2)
    f_arr = (t64 - i_arr).astype(np.float32)
    del t64
    bias_f32, slopes = _alibi_bias_f32()

    perms, btabs, rpts, masks = [], [], [], []
    for c in range(NCORES):
        par = c % 2
        blocks = [2 * k + par for k in range(8)] + [2 * k + 1 - par for k in range(8)]
        perms.append(blocks)
        gl = np.concatenate([np.arange(b * 128, b * 128 + 128) for b in blocks])
        kglob, qglob = gl, gl[:TOK]
        btab = np.empty((128, H * 16), np.float32)
        for h in range(H):
            for kc in range(16):
                btab[:, h * 16 + kc] = bias_f32[h, kglob[kc * 128:(kc + 1) * 128]]
        btabs.append(btab)
        rpt = np.empty((H, 2, 128, 512), np.float32)
        for h in range(H):
            row = (slopes[h] * (np.float32(S - 1) - qglob.astype(np.float32))
                   ).astype(np.float32)
            rpt[h, 0] = np.broadcast_to(row[:512], (128, 512))
            rpt[h, 1] = np.broadcast_to(row[512:], (128, 512))
        rpts.append(rpt)
        mk = np.zeros((len(MASK_PAIRS), 128, 512), np.float32)
        for mi, (tile, kc) in enumerate(MASK_PAIRS):
            kpos = kglob[kc * 128:(kc + 1) * 128]
            qpos = qglob[tile * 512:(tile + 1) * 512]
            mk[mi][kpos[:, None] > qpos[None, :]] = FMIN
        masks.append(mk)

    cps_idx, cpps, fplans = [], [], []
    u128 = np.arange(128)
    for c in range(NCORES):
        ci = np.empty((3, 128, TPC, 5), np.int64)
        cp = np.empty((3, 128, TPC), np.float32)
        fp = np.empty((3, 128, TPC, 128), np.float32)
        for w in range(3):
            for t, (row0, col0, IN) in enumerate(_recon_tiles_for(c, w)):
                j0 = (row0 + u128) * IN + col0
                ib = i_arr[j0]
                iv = i_arr[j0[:, None] + u128[None, :]]
                dlt = iv - ib[:, None]
                assert dlt.min() >= 0 and dlt.max() <= 1
                adv = dlt.argmax(axis=1).astype(np.float32)
                adv[dlt.max(axis=1) == 0] = 128.0
                cp[w, :, t] = adv
                for m in range(5):
                    ci[w, :, t, m] = np.clip(ib + m - 1, 0, NCTRL - 1)
                fp[w, :, t, :] = f_arr[j0[:, None] + u128[None, :]]
        cps_idx.append(ci)
        cpps.append(cp)
        fplans.append(fp.reshape(3, 128, TPC * 128))

    _static.update(dict(
        perms=perms, btabs=btabs, rpts=rpts, masks=masks,
        cps_idx=cps_idx, cpps=cpps, fplans=fplans,
        iota=np.ascontiguousarray(
            np.broadcast_to(np.arange(128, dtype=np.float32), (128, 128))),
        ident=np.eye(128, dtype=np.float32),
        misc=np.concatenate([
            np.ones((128, 129), np.float32),
            np.full((128, 1), np.float32(1e-5), np.float32),
            np.full((128, 1), np.float32(SQRT_HD), np.float32)], axis=1),
    ))
    return _static



class Ctr:
    def __init__(self, nc, name):
        self.h = nc.alloc_semaphore(name)
        self.n = 0

    def bump(self, k):
        self.n += k
        return self.n


def build_nc():
    nc = bass.Bass("TRN2", target_bir_lowering=False, debug=False,
                   detect_race_conditions=False)

    din = lambda n, sh: nc.dram_tensor(n, sh, F32, kind="ExternalInput")
    x_in = din("x", [S, D])
    wqb_in = din("wqb", [16, 16, 128, 128])
    wkb_in = din("wkb", [16, 16, 128, 128])
    wvb_in = din("wvb", [16, 16, 128, 128])
    wob_in = din("wob", [16, 16, 128, 128])
    btab_in = din("btab", [128, H * 16])
    rpt_in = din("rpt", [H, 2, 128, 512])
    mask_in = din("maskt", [len(MASK_PAIRS), 128, 512])
    bq_in = din("bqt", [128, 16])
    bk_in = din("bkt", [128, 16])
    bv_in = din("bvt", [128, 16])
    bo_in = din("bot", [128, 16])
    cp_in = din("cpst", [3, 128, TPC * 5])
    cpp_in = din("cppt", [3, 128, TPC])
    fpl_in = din("fplt", [3, 128, TPC * 128])
    iota_in = din("iota", [128, 128])
    ident_in = din("ident", [128, 128])
    misc_in = din("misc", [128, 131])

    y_out = nc.dram_tensor("y", [TOK, D], F32, kind="ExternalOutput")
    if DEBUG:
        dbg_xno = nc.dram_tensor("dbg_xno", [128, 16 * TOK], F32, kind="ExternalOutput")
        dbg_kt = nc.dram_tensor("dbg_kt", [D, S], F32, kind="ExternalOutput")
        dbg_vt = nc.dram_tensor("dbg_vt", [D, S], F32, kind="ExternalOutput")
        dbg_qt = nc.dram_tensor("dbg_qt", [D, TOK], F32, kind="ExternalOutput")
        dbg_at = nc.dram_tensor("dbg_at", [D, TOK], F32, kind="ExternalOutput")
        dbg_ag = nc.dram_tensor("dbg_ag", [SHARD], F32, kind="ExternalOutput")
        dbg_ago = nc.dram_tensor("dbg_ago", [NCORES, SHARD], F32, kind="ExternalOutput")

    ag_in_gu = nc.dram_tensor("ag_in_gu", [2 * WOFF], F32)
    ag_out_gu = nc.dram_tensor("ag_out_gu", [NCORES, 2 * WOFF], F32, addr_space="Shared")
    ag_in_d = nc.dram_tensor("ag_in_d", [WOFF], F32)
    ag_out_d = nc.dram_tensor("ag_out_d", [NCORES, WOFF], F32, addr_space="Shared")
    kt_sc = nc.dram_tensor("kt_sc", [D, S], F32)
    vt_sc = nc.dram_tensor("vt_sc", [D, S], F32)
    qt_sc = nc.dram_tensor("qt_sc", [D, TOK], F32)
    xno_sc = nc.dram_tensor("xno_sc", [128, 16 * TOK], F32)
    at_sc = nc.dram_tensor("at_sc", [D, TOK], F32)

    BIG = nc.alloc_sbuf_tensor("BIG", [128, 34816], F32)
    SMALL = nc.alloc_sbuf_tensor("SMALL", [128, 15360], F32)
    CON = nc.alloc_sbuf_tensor("CON", [128, 768], F32)
    big = lambda o, n: BIG[:, o:o + n]
    small = lambda o, n: SMALL[:, o:o + n]

    c_ident = CON[:, 0:128]
    c_iota = CON[:, 128:256]
    c_btab = CON[:, 256:512]
    c_bq = CON[:, 512:528]
    c_bk = CON[:, 528:544]
    c_bv = CON[:, 544:560]
    c_bo = CON[:, 560:576]
    c_misc = CON[:, 576:707]
    c_onesr = c_misc[:, 0:128]
    c_ones = c_misc[:, 128:129]
    c_eps = c_misc[:, 129:130]
    c_sqrt = c_misc[:, 130:131]
    c_rsq = CON[:, 707:708]

    p0 = nc.alloc_psum_tensor("p0", [128, 512], F32)
    p1 = nc.alloc_psum_tensor("p1", [128, 512], F32)
    p2 = nc.alloc_psum_tensor("p2", [128, 512], F32)
    p3 = nc.alloc_psum_tensor("p3", [128, 512], F32)
    p4 = nc.alloc_psum_tensor("p4", [128, 512], F32)
    p5 = nc.alloc_psum_tensor("p5", [128, 512], F32)
    p6 = nc.alloc_psum_tensor("p6", [128, 512], F32)
    p7 = nc.alloc_psum_tensor("p7", [128, 512], F32)

    Cm = lambda n: Ctr(nc, n)
    d_con = Cm("d_con")
    d_f = [Cm("d_f0"), Cm("d_f1")]
    d_x = [Cm("d_x0"), Cm("d_x1")]
    d_w = [Cm("d_w0"), Cm("d_w1")]
    d_h = [Cm("d_h0"), Cm("d_h1")]
    d_g = [Cm("d_g0"), Cm("d_g1")]
    d_u = [Cm("d_u0"), Cm("d_u1")]
    d_d = [Cm("d_d0"), Cm("d_d1")]
    d_a = [Cm("d_a0"), Cm("d_a1")]
    d_xr = Cm("d_xr")
    d_xh = Cm("d_xh")
    d_sp = Cm("d_sp")
    d_y = Cm("d_y")
    s_cc = Cm("s_cc")
    d_rc = Cm("d_rc")
    s_x = Cm("s_x")
    s_t = Cm("s_t")
    s_v = Cm("s_v")
    s_s = Cm("s_s")

    with nc.Block() as blk:
        @blk.sync
        def _(e):
            for dst, src in ((c_ident, ident_in), (c_iota, iota_in),
                             (c_btab, btab_in), (c_bq, bq_in), (c_bk, bk_in),
                             (c_bv, bv_in), (c_bo, bo_in), (c_misc, misc_in)):
                e.dma_start(out=dst, in_=src.ap()[:, :]).then_inc(d_con.h, 16)
                d_con.bump(16)

        @blk.vector
        def _(e):
            e.wait_ge(d_con.h, d_con.n)
            e.reciprocal(out=c_rsq, in_=c_sqrt)
    nc.const_aps.register(F32, EPS, c_eps[:, 0:1])

    RTC = 8
    RNCH = TPC // RTC
    NRCH = 3 * RNCH

    xnt = big(0, 32768)
    xnt3 = xnt.rearrange("p (fc t) -> p fc t", fc=16)
    ln_x = [small(0, 2048), small(2048, 2048)]
    ln_t = small(4096, 2048)
    ln_s = small(6144, 2048)
    ln_c = small(8192, 8)

    T0, V0, S0 = s_t.n, s_v.n, s_s.n
    LNV = lambda tc, i: V0 + tc * 10 + i
    LNS = lambda tc, i: S0 + tc * 2 + i
    LNT = lambda tc, g: T0 + tc * 4 + g + 1

    with nc.Block() as blk:
        @blk.sync
        def _(e):
            for tc in range(16):
                if tc >= 2:
                    e.wait_ge(s_t.h, LNT(tc - 2, 3))
                e.dma_start(out=ln_x[tc % 2],
                            in_=x_in.ap()[tc * 128:(tc + 1) * 128, :]
                            ).then_inc(d_x[tc % 2].h, 16)
                d_x[tc % 2].bump(16)

        @blk.vector
        def _(e):
            for tc in range(16):
                e.wait_ge(d_x[tc % 2].h, 16 * (tc // 2 + 1))
                xs = ln_x[tc % 2]
                e.reduce_sum(out=ln_c[:, 0:1], in_=xs, axis=AXX).then_inc(s_v.h, 1)
                e.wait_ge(s_v.h, s_v.bump(1))
                e.tensor_scalar_mul(ln_c[:, 1:2], ln_c[:, 0:1], 1.0 / 2048.0
                                    ).then_inc(s_v.h, 1)
                e.wait_ge(s_v.h, s_v.bump(1))
                e.tensor_scalar(out=ln_t, in0=xs, scalar1=ln_c[:, 1:2],
                                scalar2=None, op0=ALU.subtract).then_inc(s_v.h, 1)
                s_v.bump(1)
                e.wait_ge(s_s.h, LNS(tc, 1))
                e.reduce_sum(out=ln_c[:, 0:1], in_=ln_s, axis=AXX).then_inc(s_v.h, 1)
                s_v.bump(1)
                e.wait_ge(s_s.h, LNS(tc, 2))
                e.reciprocal(out=ln_c[:, 3:4], in_=ln_c[:, 2:3]).then_inc(s_v.h, 1)
                e.wait_ge(s_v.h, s_v.bump(1))
                e.tensor_scalar(out=xs, in0=ln_t, scalar1=ln_c[:, 3:4],
                                scalar2=None, op0=ALU.mult).then_inc(s_v.h, 1)
                s_v.bump(1)
                for g in range(4):
                    e.wait_ge(s_t.h, LNT(tc, g))
                    dst = xnt3[:, g * 4:(g + 1) * 4, tc * 128:(tc + 1) * 128]
                    e.tensor_copy(out=dst,
                                  in_=p6.rearrange("p (j t) -> p j t", j=4)
                                  ).then_inc(s_v.h, 1)
                    s_v.bump(1)

        @blk.scalar
        def _(e):
            for tc in range(16):
                e.wait_ge(s_v.h, LNV(tc, 3))
                e.activation(out=ln_s, in_=ln_t, func=ACT.Square).then_inc(s_s.h, 1)
                s_s.bump(1)
                e.wait_ge(s_v.h, LNV(tc, 4))
                e.activation(out=ln_c[:, 2:3], in_=ln_c[:, 0:1], func=ACT.Sqrt,
                             scale=1.0 / 2048.0, bias=EPS).then_inc(s_s.h, 1)
                s_s.bump(1)

        @blk.tensor
        def _(e):
            for tc in range(16):
                e.wait_ge(s_v.h, LNV(tc, 6))
                for g in range(4):
                    prev = LNV(tc, 6 + g) if g >= 1 else (
                        LNV(tc - 1, 10) if tc >= 1 else None)
                    if prev is not None:
                        e.wait_ge(s_v.h, prev)
                    for j in range(4):
                        fc = g * 4 + j
                        ins = e.transpose(p6[:, j * 128:(j + 1) * 128],
                                          ln_x[tc % 2][:, fc * 128:(fc + 1) * 128],
                                          c_ident)
                    ins.then_inc(s_t.h, 1)
                    s_t.bump(1)

    with nc.Block() as blk:
        @blk.gpsimd
        def _(e):
            e.dma_start(out=xno_sc.ap().rearrange("p (fc t) -> p fc t", fc=16),
                        in_=xnt3[:, :, 0:TOK]).then_inc(d_sp.h, 16)
            d_sp.bump(16)

    w_slab = [small(0, 2048), small(2048, 2048)]
    pr_st = [small(4096, 512), small(4608, 512)]

    def proj(wten, bias_col, n_tt, dst_sc, div, rhs_fn):
        TT0, VV0, SP0 = s_t.n, s_v.n, d_sp.n
        W0 = (d_w[0].n, d_w[1].n)
        PT = lambda idx: TT0 + idx + 1
        PV = lambda idx: VV0 + idx + 1
        PS = lambda idx: SP0 + 16 * (idx + 1)
        with nc.Block() as blk:
            @blk.sync
            def _(e):
                for dc in range(16):
                    if dc >= 2:
                        e.wait_ge(s_t.h, PT((dc - 1) * n_tt - 1))
                    e.dma_start(out=w_slab[dc % 2]
                                .rearrange("p (k f) -> p k f", k=16),
                                in_=wten.ap()[dc].rearrange("k p f -> p k f")
                                ).then_inc(d_w[dc % 2].h, 16)
                    d_w[dc % 2].bump(16)

            @blk.tensor
            def _(e):
                for dc in range(16):
                    e.wait_ge(d_w[dc % 2].h, W0[dc % 2] + 16 * (dc // 2 + 1))
                    for tt in range(n_tt):
                        idx = dc * n_tt + tt
                        ps = (p0, p1)[idx % 2]
                        if idx >= 2:
                            e.wait_ge(s_v.h, PV(idx - 2))
                        for hc in range(16):
                            ins = e.matmul(ps[:, :],
                                           w_slab[dc % 2][:, hc * 128:(hc + 1) * 128],
                                           rhs_fn(hc, tt),
                                           start=(hc == 0), stop=(hc == 15))
                        ins.then_inc(s_t.h, 1)
                        s_t.bump(1)

            @blk.vector
            def _(e):
                for dc in range(16):
                    for tt in range(n_tt):
                        idx = dc * n_tt + tt
                        ps = (p0, p1)[idx % 2]
                        e.wait_ge(s_t.h, PT(idx))
                        if idx >= 2:
                            e.wait_ge(d_sp.h, PS(idx - 2))
                        if div:
                            e.tensor_scalar(out=pr_st[idx % 2], in0=ps[:, :],
                                            scalar1=bias_col[:, dc:dc + 1],
                                            scalar2=c_rsq[:, 0:1],
                                            op0=ALU.add, op1=ALU.mult
                                            ).then_inc(s_v.h, 1)
                        else:
                            e.tensor_scalar(out=pr_st[idx % 2], in0=ps[:, :],
                                            scalar1=bias_col[:, dc:dc + 1],
                                            scalar2=None, op0=ALU.add
                                            ).then_inc(s_v.h, 1)
                        s_v.bump(1)

            @blk.gpsimd
            def _(e):
                for dc in range(16):
                    for tt in range(n_tt):
                        idx = dc * n_tt + tt
                        e.wait_ge(s_v.h, PV(idx))
                        e.dma_start(out=dst_sc.ap()[dc * 128:(dc + 1) * 128,
                                                    tt * 512:(tt + 1) * 512],
                                    in_=pr_st[idx % 2]).then_inc(d_sp.h, 16)
                        d_sp.bump(16)

    xnt_rhs = lambda hc, tt: xnt3[:, hc, tt * 512:(tt + 1) * 512]
    proj(wkb_in, c_bk, 4, kt_sc, False, xnt_rhs)
    proj(wvb_in, c_bv, 4, vt_sc, False, xnt_rhs)
    proj(wqb_in, c_bq, 2, qt_sc, True, xnt_rhs)

    ctxt = big(0, 16384)
    mks = big(16384, 8192)
    x_exp = [big(24576, 512), big(25088, 512)]
    x_run = big(25600, 512)
    x_run2 = big(34304, 512)
    r_f = [big(26112, 1024), big(27136, 1024)]
    r_acc = big(28160, 1024)
    r_bac = big(29184, 1024)
    r_dlt = big(30208, 1024)
    r_stage = [big(31232, 1024), big(32256, 1024)]
    t_q = big(33280, 1024)
    t_cps = small(14336, 640)
    t_cpp = small(14976, 128)
    t_tmp = small(15104, 256)
    a_kt = [small(0, 2048), small(2048, 2048)]
    a_vt = [small(4096, 2048), small(6144, 2048)]
    a_qt = [small(8192, 1024), small(9216, 1024)]
    a_pl = [small(10240, 1024), small(11264, 1024)]
    a_vtr = small(12288, 2048)

    with nc.Block() as blk:
        @blk.sync
        def _(e):
            e.dma_start(out=mks.rearrange("p (m f) -> p m f", m=16),
                        in_=mask_in.ap().rearrange("m p f -> p m f")
                        ).then_inc(d_con.h, 16)
            d_con.bump(16)
            e.dma_start(out=t_cps, in_=cp_in.ap()[0, :, :]).then_inc(d_rc.h, 16)
            d_rc.bump(16)
            e.dma_start(out=t_cpp, in_=cpp_in.ap()[0, :, :]).then_inc(d_rc.h, 16)
            d_rc.bump(16)
            e.dma_start(out=r_f[0], in_=fpl_in.ap()[0, :, 0:RTC * 128]
                        ).then_inc(d_f[0].h, 16)
            d_f[0].bump(16)

    s_rv = Cm("s_rv")
    s_rt = Cm("s_rt")
    d_ag = [Cm("d_ag0"), Cm("d_ag1")]
    RPREP, REVAL, REVA, RT = {}, {}, {}, {}
    _rv = 0
    _rt = 0
    for c in range(NRCH):
        w, t = c // RNCH, c % RNCH
        if t == 0:
            _rv += 1
            RPREP[w] = _rv
        _rv += 1
        REVAL[c] = _rv
        REVA[c] = []
        for g in range(2):
            _rv += 1
            REVA[c].append(_rv)
        RT[c] = []
        for g in range(2):
            _rt += 1
            RT[c].append(_rt)
    _sched = [[0, 1, 2, 3], [4, 5, 6], [7, 8, 9], [10, 11, 12], [13, 14, 15],
              [],
              [16, 17, 18, 19], [20, 21, 22], [23, 24, 25], [26, 27, 28],
              [29, 30, 31],
              [],
              [32, 33, 34, 35], [36, 37, 38, 39], [40, 41, 42, 43],
              [44, 45, 46, 47]]
    CHUNKS_OF = {h: _sched[h] for h in range(H)}

    def emit_recon_sync(e, h):
        for wv in (1, 2):
            if h == (5 if wv == 1 else 11):
                e.wait_ge(s_rv.h, REVAL[16 * wv - 1])
                e.dma_start(out=t_cps, in_=cp_in.ap()[wv, :, :]
                            ).then_inc(d_rc.h, 16)
                d_rc.bump(16)
                e.dma_start(out=t_cpp, in_=cpp_in.ap()[wv, :, :]
                            ).then_inc(d_rc.h, 16)
                d_rc.bump(16)
        for c in CHUNKS_OF[h]:
            nx = c + 1
            if nx >= NRCH:
                continue
            w, t = nx // RNCH, nx % RNCH
            if nx >= 2:
                e.wait_ge(s_rv.h, REVAL[nx - 2])
            e.dma_start(out=r_f[nx % 2],
                        in_=fpl_in.ap()[w, :, t * RTC * 128:(t + 1) * RTC * 128]
                        ).then_inc(d_f[nx % 2].h, 16)
            d_f[nx % 2].bump(16)

    def emit_recon_vector(e, h):
        for c in CHUNKS_OF[h]:
            w, t = c // RNCH, c % RNCH
            if t == 0:
                e.wait_ge(d_rc.h, 32 * (w + 1))
                cv = t_cps.rearrange("p (tt m) -> p tt m", m=5)
                sv = [cv[:, :, m] for m in range(5)]
                q = [t_q[:, k * TPC:(k + 1) * TPC] for k in range(8)]
                tp0 = t_tmp[:, 0:TPC]
                tp1 = t_tmp[:, TPC:2 * TPC]

                tmp = r_acc
                tcol = lambda i: tmp[:, i * TPC:(i + 1) * TPC]
                ops = []

                def plan(dst, a, b, cc, d, base):
                    u = [tcol(base + i) for i in range(4)]
                    ops.append(("c", dst[0], b))
                    ops.append(("s", u[0], cc, a, ALU.subtract))
                    ops.append(("m", dst[1], u[0], 0.5))
                    ops.append(("m", u[1], b, -2.5))
                    ops.append(("s", u[1], u[1], a, ALU.add))
                    ops.append(("m", u[2], cc, 2.0))
                    ops.append(("s", u[1], u[1], u[2], ALU.add))
                    ops.append(("m", u[2], d, -0.5))
                    ops.append(("s", dst[2], u[1], u[2], ALU.add))
                    ops.append(("m", u[3], b, 1.5))
                    ops.append(("m", u[0], a, -0.5))
                    ops.append(("s", u[3], u[3], u[0], ALU.add))
                    ops.append(("m", u[0], cc, -1.5))
                    ops.append(("s", u[3], u[3], u[0], ALU.add))
                    ops.append(("m", u[0], d, 0.5))
                    ops.append(("s", dst[3], u[3], u[0], ALU.add))

                plan([q[0], q[1], q[2], q[3]], sv[0], sv[1], sv[2], sv[3], 0)
                opsA = ops
                ops = []
                plan([q[4], q[5], q[6], q[7]], sv[1], sv[2], sv[3], sv[4], 4)
                opsB = ops
                seq = []
                for i in range(max(len(opsA), len(opsB))):
                    if i < len(opsA):
                        seq.append(opsA[i])
                    if i < len(opsB):
                        seq.append(opsB[i])
                for k in range(4):
                    seq.append(("s", q[4 + k], q[4 + k], q[k], ALU.subtract))
                last = None
                for op in seq:
                    if op[0] == "c":
                        last = e.tensor_copy(out=op[1], in_=op[2])
                    elif op[0] == "m":
                        last = e.tensor_scalar_mul(op[1], op[2], op[3])
                    else:
                        last = e.tensor_tensor(out=op[1], in0=op[2], in1=op[3],
                                               op=op[4])
                last.then_inc(s_rv.h, 1)
                s_rv.bump(1)
            sl = c % 2
            t0 = t * RTC
            e.wait_ge(d_f[sl].h, rf_fill_at[c])
            if c >= 1:
                e.wait_ge(s_rt.h, RT[c - 1][1])
            fb = r_f[sl].rearrange("p (tt u) -> p tt u", tt=RTC)
            acc = r_acc.rearrange("p (tt u) -> p tt u", tt=RTC)
            bac = r_bac.rearrange("p (tt u) -> p tt u", tt=RTC)
            dlt = r_dlt.rearrange("p (tt u) -> p tt u", tt=RTC)
            qv = lambda k: t_q[:, k * TPC + t0:k * TPC + t0 + RTC] \
                .unsqueeze(2).broadcast_to([128, RTC, 128])
            cpv = t_cpp[:, t0:t0 + RTC].unsqueeze(2).broadcast_to([128, RTC, 128])
            iov = c_iota.unsqueeze(1).broadcast_to([128, RTC, 128])
            e.tensor_tensor(out=dlt, in0=iov, in1=cpv, op=ALU.is_ge)
            e.tensor_tensor(out=acc, in0=fb, in1=qv(3), op=ALU.mult)
            e.tensor_tensor(out=bac, in0=fb, in1=qv(7), op=ALU.mult)
            e.tensor_tensor(out=acc, in0=acc, in1=qv(2), op=ALU.add)
            e.tensor_tensor(out=bac, in0=bac, in1=qv(6), op=ALU.add)
            e.tensor_tensor(out=acc, in0=acc, in1=fb, op=ALU.mult)
            e.tensor_tensor(out=bac, in0=bac, in1=fb, op=ALU.mult)
            e.tensor_tensor(out=acc, in0=acc, in1=qv(1), op=ALU.add)
            e.tensor_tensor(out=bac, in0=bac, in1=qv(5), op=ALU.add)
            e.tensor_tensor(out=acc, in0=acc, in1=fb, op=ALU.mult)
            e.tensor_tensor(out=bac, in0=bac, in1=fb, op=ALU.mult)
            e.tensor_tensor(out=acc, in0=acc, in1=qv(0), op=ALU.add)
            e.tensor_tensor(out=bac, in0=bac, in1=qv(4), op=ALU.add
                            ).then_inc(s_x.h, 1)
            e.wait_ge(s_x.h, s_x.bump(1))
            e.tensor_tensor(out=bac, in0=bac, in1=dlt, op=ALU.mult
                            ).then_inc(s_x.h, 1)
            e.wait_ge(s_x.h, s_x.bump(1))
            e.tensor_tensor(out=acc, in0=acc, in1=bac, op=ALU.add
                            ).then_inc(s_rv.h, 1)
            s_rv.bump(1)
            for g in range(2):
                e.wait_ge(s_rt.h, RT[c][g])
                if g == 0 and c >= 2:
                    e.wait_ge(d_ag[c % 2].h, ag_fill_at[c])
                e.tensor_copy(out=r_stage[c % 2][:, g * 512:(g + 1) * 512],
                              in_=p4[:, :]).then_inc(s_rv.h, 1)
                s_rv.bump(1)

    def emit_recon_tensor(e, h):
        for c in CHUNKS_OF[h]:
            e.wait_ge(s_rv.h, REVAL[c])
            for g in range(2):
                if g == 1:
                    e.wait_ge(s_rv.h, REVA[c][0])
                elif c >= 1:
                    e.wait_ge(s_rv.h, REVA[c - 1][1])
                for j in range(4):
                    ti = g * 4 + j
                    ins = e.transpose(p4[:, j * 128:(j + 1) * 128],
                                      r_acc[:, ti * 128:(ti + 1) * 128], c_ident)
                ins.then_inc(s_rt.h, 1)
                s_rt.bump(1)

    def emit_recon_gpsimd(e, h):
        for c in CHUNKS_OF[h]:
            w, t = c // RNCH, c % RNCH
            e.wait_ge(s_rv.h, REVA[c][1])
            dst = (ag_in_gu.ap()[w * WOFF + t * RTC * TILE_EL:
                                 w * WOFF + (t + 1) * RTC * TILE_EL]
                   if w < 2 else
                   ag_in_d.ap()[t * RTC * TILE_EL:(t + 1) * RTC * TILE_EL])
            e.dma_start(out=dst.rearrange("(tt p f) -> p tt f", p=128, f=128),
                        in_=r_stage[c % 2].rearrange("p (tt f) -> p tt f", tt=RTC),
                        ).then_inc(d_ag[c % 2].h, 16)
            d_ag[c % 2].bump(16)
            if c == NRCH - 1:
                e.wait_ge(d_ag[0].h, d_ag[0].n)
                e.wait_ge(d_ag[1].h, d_ag[1].n)
                e.collective_compute(
                    "AllGather", ALU.bypass,
                    replica_groups=[list(range(NCORES))],
                    ins=[ag_in_gu.ap().opt()],
                    outs=[ag_out_gu.ap().opt()],
                ).then_inc(s_cc.h, 1)
                s_cc.bump(1)
            if c == NRCH - 1:
                e.wait_ge(d_ag[0].h, d_ag[0].n)
                e.wait_ge(d_ag[1].h, d_ag[1].n)
                e.collective_compute(
                    "AllGather", ALU.bypass,
                    replica_groups=[list(range(NCORES))],
                    ins=[ag_in_d.ap().opt()],
                    outs=[ag_out_d.ap().opt()],
                ).then_inc(s_cc.h, 1)
                s_cc.bump(1)

    rf_fill_at = {}
    _f = [0, 0]
    _f[0] = 16
    rf_fill_at[0] = 16
    for nx in range(1, NRCH):
        _f[nx % 2] += 16
        rf_fill_at[nx] = _f[nx % 2]
    ag_fill_at = {}
    _a = [0, 0]
    for c in range(NRCH):
        ag_fill_at[c] = _a[c % 2] if c >= 2 else 0
        _a[c % 2] += 16

    mask_slot = {p: i for i, p in enumerate(MASK_PAIRS)}
    ev_ctx_last = {}
    ev_norm = {}
    ev_vtrev = {}
    psc_hist = []

    AT, AV, AS = {}, {}, {}
    tn, vn, sn = s_t.n, s_v.n, s_s.n
    exp_slot_seq = []
    sc_bank_seq = []
    for h in range(H):
        for g in range(4):
            tn += 1
            AT[("vtr", h, g)] = tn
        for tile in range(2):
            kcl = (KCS0, KCS1)[tile]
            n = len(kcl)
            order = [("sc", 0), ("sc", 1)]
            for k in range(n - 2):
                order += [("ctx", k), ("sc", k + 2)]
            order += [("ctx", n - 2), ("ctx", n - 1), ("den",), ("bc",)]
            for op in order:
                tn += 1
                AT[(op[0], h, tile) + tuple(op[1:])] = tn
                if op[0] == "sc":
                    sc_bank_seq.append((h, tile, op[1]))
        for g in range(4):
            vn += 1
            AV[("vtrev", h, g)] = vn
        for tile in range(2):
            kcl = (KCS0, KCS1)[tile]
            n = len(kcl)
            order = [("arg", 0), ("arg", 1)]
            for k in range(n - 2):
                order += [("run", k), ("arg", k + 2)]
            order += [("run", n - 2), ("run", n - 1), ("recip",), ("bcev",),
                      ("norm",)]
            for op in order:
                vn += 1
                AV[(op[0], h, tile) + tuple(op[1:])] = vn
        for tile in range(2):
            n = len((KCS0, KCS1)[tile])
            for ki in range(n):
                sn += 1
                AS[("exp", h, tile, ki)] = sn
                exp_slot_seq.append((h, tile, ki))
    sc_prev2 = {key: (sc_bank_seq[i - 2] if i >= 2 else None)
                for i, key in enumerate(sc_bank_seq)}
    exp_prev2 = {key: (exp_slot_seq[i - 2] if i >= 2 else None)
                 for i, key in enumerate(exp_slot_seq)}
    bcev_order = [(h, t) for h in range(H) for t in range(2)]

    with nc.Block() as blk:
        @blk.tensor
        def _(e):
            for h in range(H):
                sl = h % 2
                e.wait_ge(d_h[sl].h, 64 * (h // 2 + 1))
                if h == 0:
                    e.wait_ge(d_con.h, d_con.n)
                for g in range(4):
                    prev = ("vtrev", h, g - 1) if g >= 1 else (
                        ("vtrev", h - 1, 3) if h >= 1 else None)
                    if prev is not None:
                        e.wait_ge(s_v.h, AV[prev])
                    for j in range(4):
                        kc = g * 4 + j
                        ins = e.transpose(p6[:, j * 128:(j + 1) * 128],
                                          a_vt[sl][:, kc * 128:(kc + 1) * 128],
                                          c_ident)
                    ins.then_inc(s_t.h, 1)
                    s_t.bump(1)
                for tile in range(2):
                    kcl = (KCS0, KCS1)[tile]
                    n = len(kcl)
                    pctx = (p2, p3)[tile]

                    def esc(ki, h=h, tile=tile, sl=sl, kcl=kcl):
                        key = (h, tile, ki)
                        prev = sc_prev2[key]
                        if prev is not None:
                            e.wait_ge(s_s.h, AS[("exp",) + prev])
                        kc = kcl[ki]
                        ps = (p0, p1)[ki % 2]
                        e.matmul(ps[:, :],
                                 a_kt[sl][:, kc * 128:(kc + 1) * 128],
                                 a_qt[sl][:, tile * 512:(tile + 1) * 512],
                                 start=True, stop=True).then_inc(s_t.h, 1)
                        s_t.bump(1)

                    def ectx(ki, h=h, tile=tile, kcl=kcl, n=n, pctx=pctx):
                        if ki == 0:
                            e.wait_ge(s_v.h, AV[("vtrev", h, 3)])
                            if h >= 1:
                                e.wait_ge(s_v.h, AV[("norm", h - 1, tile)])
                        e.wait_ge(s_s.h, AS[("exp", h, tile, ki)])
                        kc = kcl[ki]
                        e.matmul(pctx[:, :], a_vtr[:, kc * 128:(kc + 1) * 128],
                                 x_exp[ki % 2],
                                 start=(ki == 0), stop=(ki == n - 1)
                                 ).then_inc(s_t.h, 1)
                        s_t.bump(1)

                    esc(0)
                    esc(1)
                    for k in range(n - 2):
                        ectx(k)
                        esc(k + 2)
                    ectx(n - 2)
                    ectx(n - 1)
                    e.wait_ge(s_v.h, AV[("run", h, tile, n - 1)])
                    bi = bcev_order.index((h, tile))
                    if bi >= 1:
                        e.wait_ge(s_v.h, AV[("bcev",) + bcev_order[bi - 1]])
                    e.matmul(p7[0:1, :], c_ones, x_run, start=True, stop=False)
                    e.matmul(p7[0:1, :], c_ones, x_run2, start=False, stop=True
                             ).then_inc(s_t.h, 1)
                    s_t.bump(1)
                    e.wait_ge(s_v.h, AV[("recip", h, tile)])
                    e.matmul(p7[:, :], c_onesr[0:1, :], x_run[0:1, :],
                             start=True, stop=True).then_inc(s_t.h, 1)
                    s_t.bump(1)
                emit_recon_tensor(e, h)

        @blk.vector
        def _(e):
            for h in range(H):
                sl = h % 2
                for g in range(4):
                    e.wait_ge(s_t.h, AT[("vtr", h, g)])
                    if h >= 1 and g == 0:
                        e.wait_ge(s_t.h, AT[("ctx", h - 1, 1, len(KCS1) - 1)])
                    e.tensor_copy(out=a_vtr[:, g * 512:(g + 1) * 512], in_=p6[:, :]
                                  ).then_inc(s_v.h, 1)
                    s_v.bump(1)
                for tile in range(2):
                    kcl = (KCS0, KCS1)[tile]
                    n = len(kcl)
                    pctx = (p2, p3)[tile]

                    def earg(ki, h=h, tile=tile, sl=sl, kcl=kcl):
                        e.wait_ge(s_t.h, AT[("sc", h, tile, ki)])
                        kc = kcl[ki]
                        ps = (p0, p1)[ki % 2]
                        ins = e.tensor_tensor(
                            out=ps[:, :], in0=ps[:, :],
                            in1=a_pl[sl][:, tile * 512:(tile + 1) * 512],
                            op=ALU.add)
                        mi = MASK_SLOT.get((tile, kc))
                        if mi is not None:
                            ins.then_inc(s_x.h, 1)
                            e.wait_ge(s_x.h, s_x.bump(1))
                            ins = e.tensor_tensor(
                                out=ps[:, :], in0=ps[:, :],
                                in1=mks[:, mi * 512:(mi + 1) * 512], op=ALU.add)
                        ins.then_inc(s_v.h, 1)
                        s_v.bump(1)

                    def erun(ki, h=h, tile=tile, n=n):
                        e.wait_ge(s_s.h, AS[("exp", h, tile, ki)])
                        rbuf = (x_run, x_run2)[ki % 2]
                        if ki < 2:
                            bi = bcev_order.index((h, tile))
                            if bi >= 1:
                                e.wait_ge(s_t.h, AT[("den",) + bcev_order[bi - 1]])
                            e.tensor_copy(out=rbuf, in_=x_exp[ki % 2]
                                          ).then_inc(s_v.h, 1)
                        else:
                            e.tensor_tensor(out=rbuf, in0=rbuf,
                                            in1=x_exp[ki % 2], op=ALU.add
                                            ).then_inc(s_v.h, 1)
                        s_v.bump(1)

                    earg(0)
                    earg(1)
                    for k in range(n - 2):
                        erun(k)
                        earg(k + 2)
                    erun(n - 2)
                    erun(n - 1)
                    e.wait_ge(s_t.h, AT[("den", h, tile)])
                    e.reciprocal(out=x_run[0:1, :], in_=p7[0:1, :]
                                 ).then_inc(s_v.h, 1)
                    s_v.bump(1)
                    e.wait_ge(s_t.h, AT[("bc", h, tile)])
                    e.tensor_copy(out=x_run, in_=p7[:, :]).then_inc(s_x.h, 1)
                    e.wait_ge(s_x.h, s_x.bump(1))
                    e.engine_nop().then_inc(s_v.h, 1)
                    s_v.bump(1)
                    e.wait_ge(s_t.h, AT[("ctx", h, tile, n - 1)])
                    e.scalar_tensor_tensor(
                        out=ctxt[:, h * 1024 + tile * 512:h * 1024 + (tile + 1) * 512],
                        in0=pctx[:, :], scalar=1.0, in1=x_run,
                        op0=ALU.mult, op1=ALU.mult).then_inc(s_v.h, 1)
                    s_v.bump(1)
                emit_recon_vector(e, h)

        @blk.scalar
        def _(e):
            for h in range(H):
                for tile in range(2):
                    n = len((KCS0, KCS1)[tile])
                    for ki in range(n):
                        key = (h, tile, ki)
                        e.wait_ge(s_v.h, AV[("arg", h, tile, ki)])
                        prev = exp_prev2[key]
                        if prev is not None:
                            ph, pt, pk = prev
                            e.wait_ge(s_t.h, AT[("ctx", ph, pt, pk)])
                            e.wait_ge(s_v.h, AV[("run", ph, pt, pk)])
                        ps = (p0, p1)[ki % 2]
                        e.activation(out=x_exp[ki % 2], in_=ps[:, :], func=ACT.Exp,
                                     bias=c_btab[:, h * 16 + (KCS0, KCS1)[tile][ki]:
                                                 h * 16 + (KCS0, KCS1)[tile][ki] + 1],
                                     scale=1.0).then_inc(s_s.h, 1)
                        s_s.bump(1)

        @blk.sync
        def _(e):
            for h in range(H):
                sl = h % 2
                if h >= 2:
                    e.wait_ge(s_t.h, AT[("ctx", h - 2, 1, len(KCS1) - 1)])
                if h == 0:
                    e.wait_ge(d_sp.h, d_sp.n)
                e.dma_start(out=a_kt[sl], in_=kt_sc.ap()[h * 128:(h + 1) * 128, :]
                            ).then_inc(d_h[sl].h, 16)
                d_h[sl].bump(16)
                e.dma_start(out=a_vt[sl], in_=vt_sc.ap()[h * 128:(h + 1) * 128, :]
                            ).then_inc(d_h[sl].h, 16)
                d_h[sl].bump(16)
                e.dma_start(out=a_qt[sl], in_=qt_sc.ap()[h * 128:(h + 1) * 128, :]
                            ).then_inc(d_h[sl].h, 16)
                d_h[sl].bump(16)
                e.dma_start(out=a_pl[sl].rearrange("p (t f) -> p t f", t=2),
                            in_=rpt_in.ap()[h].rearrange("t p f -> p t f")
                            ).then_inc(d_h[sl].h, 16)
                d_h[sl].bump(16)
                emit_recon_sync(e, h)

        @blk.gpsimd
        def _(e):
            for h in range(H):
                emit_recon_gpsimd(e, h)

    ctx_rhs = lambda hc, tt: ctxt[:, hc * 1024 + tt * 512:hc * 1024 + (tt + 1) * 512]
    proj(wob_in, c_bo, 2, at_sc, False, ctx_rhs)

    intT = big(0, 32768)
    intT3 = intT.rearrange("p (ic t) -> p ic t", ic=64)
    f_xnh = small(0, 8192)
    xnh3 = f_xnh.rearrange("p (hc t) -> p hc t", hc=16)
    f_sl = [small(8192, 2048), small(10240, 2048)]
    f_si = [small(12288, 512), small(12800, 512)]
    f_dsl = [small(0, 2048), small(2048, 2048)]
    f_at = [small(4096, 512), small(4608, 512)]
    f_sum = small(5120, 8192)
    f_os = [small(0, 2048), small(2048, 2048)]
    f_xt = small(13312, 2048)

    def gate_slab_ap(w, ic, k):
        base = w * WOFF
        return ag_out_gu.ap()[:, base + ic * 32768:base + (ic + 1) * 32768] \
            .rearrange("r (k p f) -> k r p f", k=2, p=128, f=128)[k] \
            .rearrange("r p f -> p r f")

    def down_slab_ap(hp, pc, r):
        return ag_out_d.ap()[2 * pc + r, :] \
            .rearrange("(icw hq p f) -> icw hq p f",
                       icw=8, hq=16, p=128, f=128)[:, hp, :, :] \
            .rearrange("icw p f -> p icw f")

    def slg(t, k):
        return t.rearrange("p (r kk f) -> p kk r f", kk=2, f=128)[:, k, :, :]
    def sld(t, r):
        return t.rearrange("p (r icw f) -> p r icw f", r=2, icw=8, f=128)[:, r, :, :]

    for th in range(2):
        TT0, VV0, SS0 = s_t.n, s_v.n, s_s.n
        G0, U0 = d_g[0].n, d_g[1].n
        FTg = lambda ic: TT0 + 2 * ic + 1
        FTu = lambda ic: TT0 + 2 * ic + 2
        FV = lambda ic: VV0 + ic + 1
        FS = lambda ic: SS0 + ic + 1
        with nc.Block() as blk:
            @blk.sync
            def _(e, th=th):
                e.dma_start(out=xnh3,
                            in_=xno_sc.ap().rearrange("p (hc t) -> p hc t", hc=16)
                            [:, :, th * 512:(th + 1) * 512]).then_inc(d_xh.h, 16)
                d_xh.bump(16)

            @blk.gpsimd
            def _(e, th=th):
                if th == 0:
                    e.wait_ge(s_cc.h, 1)
                for ic in range(64):
                    if ic >= 1:
                        e.wait_ge(s_t.h, FTg(ic - 1))
                    for kkk in range(2):
                        e.dma_start(out=slg(f_sl[0], kkk),
                                    in_=gate_slab_ap(0, ic, kkk)
                                    ).then_inc(d_g[0].h, 16)
                        d_g[0].bump(16)
                    if ic >= 1:
                        e.wait_ge(s_t.h, FTu(ic - 1))
                    for kkk in range(2):
                        e.dma_start(out=slg(f_sl[1], kkk),
                                    in_=gate_slab_ap(1, ic, kkk)
                                    ).then_inc(d_g[1].h, 16)
                        d_g[1].bump(16)

            @blk.tensor
            def _(e, th=th):
                e.wait_ge(d_xh.h, d_xh.n)
                for ic in range(64):
                    pg = (p0, p1)[ic % 2]
                    pu = (p2, p3)[ic % 2]
                    if ic >= 2:
                        e.wait_ge(s_s.h, FS(ic - 2))
                    e.wait_ge(d_g[0].h, G0 + 32 * (ic + 1))
                    for hc in range(16):
                        ins = e.matmul(pg[:, :], f_sl[0][:, hc * 128:(hc + 1) * 128],
                                       xnh3[:, hc, :], start=(hc == 0),
                                       stop=(hc == 15))
                    ins.then_inc(s_t.h, 1)
                    s_t.bump(1)
                    if ic >= 2:
                        e.wait_ge(s_v.h, FV(ic - 2))
                    e.wait_ge(d_g[1].h, U0 + 32 * (ic + 1))
                    for hc in range(16):
                        ins = e.matmul(pu[:, :], f_sl[1][:, hc * 128:(hc + 1) * 128],
                                       xnh3[:, hc, :], start=(hc == 0),
                                       stop=(hc == 15))
                    ins.then_inc(s_t.h, 1)
                    s_t.bump(1)

            @blk.scalar
            def _(e):
                for ic in range(64):
                    e.wait_ge(s_t.h, FTg(ic))
                    if ic >= 2:
                        e.wait_ge(s_v.h, FV(ic - 2))
                    pg = (p0, p1)[ic % 2]
                    e.activation(out=f_si[ic % 2], in_=pg[:, :], func=ACT.Silu
                                 ).then_inc(s_s.h, 1)
                    s_s.bump(1)

            @blk.vector
            def _(e):
                for ic in range(64):
                    e.wait_ge(s_s.h, FS(ic))
                    e.wait_ge(s_t.h, FTu(ic))
                    pu = (p2, p3)[ic % 2]
                    e.tensor_tensor(out=intT3[:, ic, :], in0=f_si[ic % 2],
                                    in1=pu[:, :], op=ALU.mult).then_inc(s_v.h, 1)
                    s_v.bump(1)

        TT1, VV1 = s_t.n, s_v.n
        A0 = (d_a[0].n, d_a[1].n)
        D0 = (d_d[0].n, d_d[1].n)
        DT = lambda hp, pc: TT1 + hp * 4 + pc + 1
        DV = lambda hp: VV1 + hp + 1
        with nc.Block() as blk:
            @blk.sync
            def _(e, th=th):
                for hp in range(16):
                    if hp >= 2:
                        e.wait_ge(s_v.h, DV(hp - 2))
                    e.dma_start(out=f_at[hp % 2],
                                in_=at_sc.ap()[hp * 128:(hp + 1) * 128,
                                               th * 512:(th + 1) * 512]
                                ).then_inc(d_a[hp % 2].h, 16)
                    d_a[hp % 2].bump(16)

            @blk.gpsimd
            def _(e, th=th):
                if th == 0:
                    e.wait_ge(s_cc.h, 2)
                for hp in range(16):
                    for pc in range(4):
                        gi = hp * 4 + pc
                        if gi >= 2:
                            e.wait_ge(s_t.h, DT((gi - 2) // 4, (gi - 2) % 4))
                        for rrr in range(2):
                            e.dma_start(out=sld(f_dsl[gi % 2], rrr),
                                        in_=down_slab_ap(hp, pc, rrr)
                                        ).then_inc(d_d[gi % 2].h, 16)
                            d_d[gi % 2].bump(16)

            @blk.tensor
            def _(e, th=th):
                for hp in range(16):
                    pd = (p4, p5)[hp % 2]
                    if hp >= 2:
                        e.wait_ge(s_v.h, DV(hp - 2))
                    for pc in range(4):
                        gi = hp * 4 + pc
                        e.wait_ge(d_d[gi % 2].h, D0[gi % 2] + 32 * (gi // 2 + 1))
                        for kk in range(16):
                            ic = pc * 16 + kk
                            ins = e.matmul(pd[:, :],
                                           f_dsl[gi % 2][:, kk * 128:(kk + 1) * 128],
                                           intT3[:, ic, :],
                                           start=(pc == 0 and kk == 0),
                                           stop=(pc == 3 and kk == 15))
                        ins.then_inc(s_t.h, 1)
                        s_t.bump(1)

            @blk.vector
            def _(e, th=th):
                for hp in range(16):
                    e.wait_ge(s_t.h, DT(hp, 3))
                    e.wait_ge(d_a[hp % 2].h, A0[hp % 2] + 16 * (hp // 2 + 1))
                    pd = (p4, p5)[hp % 2]
                    e.scalar_tensor_tensor(
                        out=f_sum[:, hp * 512:(hp + 1) * 512],
                        in0=pd[:, :], scalar=1.0, in1=f_at[hp % 2],
                        op0=ALU.mult, op1=ALU.add).then_inc(s_v.h, 1)
                    s_v.bump(1)

        TT2, VV2, XR0, DY0 = s_t.n, s_v.n, d_xr.n, d_y.n
        GT = lambda tc, g: TT2 + tc * 4 + g + 1
        GV = lambda tc, i: VV2 + tc * 5 + i + 1
        with nc.Block() as blk:
            @blk.sync
            def _(e, th=th):
                for tc in range(4):
                    if tc >= 1:
                        e.wait_ge(s_v.h, GV(tc - 1, 4))
                    e.dma_start(out=f_xt,
                                in_=x_in.ap()[(th * 4 + tc) * 128:
                                              (th * 4 + tc + 1) * 128, :]
                                ).then_inc(d_xr.h, 16)
                    d_xr.bump(16)

            @blk.tensor
            def _(e, th=th):
                for tc in range(4):
                    for g in range(4):
                        prev = (tc, g - 1) if g >= 1 else (
                            (tc - 1, 3) if tc >= 1 else None)
                        if prev is not None:
                            e.wait_ge(s_v.h, GV(prev[0], prev[1]))
                        for j in range(4):
                            hp = g * 4 + j
                            ins = e.transpose(p6[:, j * 128:(j + 1) * 128],
                                              f_sum[:, hp * 512 + tc * 128:
                                                    hp * 512 + (tc + 1) * 128],
                                              c_ident)
                        ins.then_inc(s_t.h, 1)
                        s_t.bump(1)

            @blk.vector
            def _(e, th=th):
                for tc in range(4):
                    for g in range(4):
                        e.wait_ge(s_t.h, GT(tc, g))
                        if g == 0:
                            if tc >= 2:
                                e.wait_ge(d_y.h, DY0 + 16 * (tc - 1))
                            elif th > 0:
                                e.wait_ge(d_y.h, DY0)
                        e.tensor_copy(out=f_os[tc % 2][:, g * 512:(g + 1) * 512],
                                      in_=p6[:, :]).then_inc(s_v.h, 1)
                        s_v.bump(1)
                    e.wait_ge(d_xr.h, XR0 + 16 * (tc + 1))
                    e.wait_ge(s_v.h, s_v.n)
                    e.tensor_tensor(out=f_os[tc % 2], in0=f_os[tc % 2],
                                    in1=f_xt, op=ALU.add).then_inc(s_v.h, 1)
                    s_v.bump(1)

            @blk.gpsimd
            def _(e, th=th):
                for tc in range(4):
                    e.wait_ge(s_v.h, GV(tc, 4))
                    e.dma_start(out=y_out.ap()[(th * 4 + tc) * 128:
                                               (th * 4 + tc + 1) * 128, :],
                                in_=f_os[tc % 2]).then_inc(d_y.h, 16)
                    d_y.bump(16)

    with nc.Block() as blk:
        @blk.gpsimd
        def _(e):
            e.wait_ge(d_y.h, d_y.n)
            if DEBUG:
                for dst, srcp in ((dbg_xno, xno_sc), (dbg_kt, kt_sc),
                                  (dbg_vt, vt_sc), (dbg_qt, qt_sc),
                                  (dbg_at, at_sc)):
                    e.dma_start(out=dst.ap()[:], in_=srcp.ap()[:]
                                ).then_inc(d_y.h, 16)
                    d_y.bump(16)
                e.dma_start(out=dbg_ag.ap()[0:2 * WOFF], in_=ag_in_gu.ap()[:]
                            ).then_inc(d_y.h, 16)
                d_y.bump(16)
                e.dma_start(out=dbg_ag.ap()[2 * WOFF:], in_=ag_in_d.ap()[:]
                            ).then_inc(d_y.h, 16)
                d_y.bump(16)
                e.dma_start(out=dbg_ago.ap()[:, 0:2 * WOFF], in_=ag_out_gu.ap()[:]
                            ).then_inc(d_y.h, 16)
                d_y.bump(16)
                e.dma_start(out=dbg_ago.ap()[:, 2 * WOFF:], in_=ag_out_d.ap()[:]
                            ).then_inc(d_y.h, 16)
                d_y.bump(16)
                e.wait_ge(d_y.h, d_y.n)

    nc.finalize()
    return nc



def _block4(w):
    hh, dd = w.shape
    return np.ascontiguousarray(
        w.reshape(hh // 128, 128, dd // 128, 128).transpose(2, 0, 1, 3))


def kernel(L0_attn_x, L0_attn_Wq, L0_attn_bq, L0_attn_Wk, L0_attn_bk,
           L0_attn_Wv, L0_attn_bv, L0_attn_Wo, L0_attn_bo,
           L0_ffn_gate_cp, L0_ffn_up_cp, L0_ffn_down_cp):
    st = _build_static()
    x = np.asarray(L0_attn_x, dtype=np.float32)
    wq = _block4(np.asarray(L0_attn_Wq, np.float32))
    wk = _block4(np.asarray(L0_attn_Wk, np.float32))
    wv = _block4(np.asarray(L0_attn_Wv, np.float32))
    wo = _block4(np.asarray(L0_attn_Wo, np.float32))
    bq = np.ascontiguousarray(np.asarray(L0_attn_bq, np.float32).reshape(16, 128).T)
    bk = np.ascontiguousarray(np.asarray(L0_attn_bk, np.float32).reshape(16, 128).T)
    bv = np.ascontiguousarray(np.asarray(L0_attn_bv, np.float32).reshape(16, 128).T)
    bo = np.ascontiguousarray(np.asarray(L0_attn_bo, np.float32).reshape(16, 128).T)
    cps = [np.asarray(L0_ffn_gate_cp, np.float32),
           np.asarray(L0_ffn_up_cp, np.float32),
           np.asarray(L0_ffn_down_cp, np.float32)]

    nc = build_nc()
    in_maps = []
    for c in range(NCORES):
        b = c // 2
        rows = np.concatenate([np.arange(bb * 128, bb * 128 + 128)
                               for bb in st["perms"][c]])
        xp = np.ascontiguousarray(x[b][rows])
        cgat = np.empty((3, 128, TPC * 5), np.float32)
        for w in range(3):
            cgat[w] = cps[w][st["cps_idx"][c][w]].reshape(128, TPC * 5)
        in_maps.append(dict(
            x=xp, wqb=wq, wkb=wk, wvb=wv, wob=wo,
            btab=st["btabs"][c], rpt=st["rpts"][c], maskt=st["masks"][c],
            bqt=bq, bkt=bk, bvt=bv, bot=bo,
            cpst=cgat, cppt=st["cpps"][c], fplt=st["fplans"][c],
            iota=st["iota"], ident=st["ident"], misc=st["misc"],
        ))

    res = run_bass_kernel_spmd(nc, in_maps, core_ids=list(range(NCORES)))
    out = np.empty((BATCH, S, D), np.float32)
    for c in range(NCORES):
        b = c // 2
        rows = np.concatenate([np.arange(bb * 128, bb * 128 + 128)
                               for bb in st["perms"][c]])[:TOK]
        out[b][rows] = res.results[c]["y"]
    return out


# version 11
# speedup vs baseline: 1.0279x; 1.0138x over previous
import numpy as np
import concourse.bass as bass
import concourse.mybir as mybir
from concourse.bass_utils import run_bass_kernel_spmd

F32 = mybir.dt.float32
ACT = mybir.ActivationFunctionType
ALU = mybir.AluOpType
AXX = mybir.AxisListType.X

D = 2048
H = 16
BATCH = 4
S = 2048
INTER = 8192
NCORES = 8
TOK = 1024
NCTRL = 130156
NS = 16777216
EPS = float(np.float32(1e-5))
FMIN = np.float32(np.finfo(np.float32).min)
SQRT_HD = float(np.float32(np.sqrt(np.float32(128.0))))

TPC = 128
TILE_EL = 128 * 128
WOFF = TPC * TILE_EL
SHARD = 3 * WOFF

KCS0 = [0, 1, 2, 3, 8, 9, 10, 11]
KCS1 = list(range(16))
MASK_PAIRS = [(0, kc) for kc in KCS0] + [(1, kc) for kc in (4, 5, 6, 7, 12, 13, 14, 15)]
MASK_SLOT = {p: i for i, p in enumerate(MASK_PAIRS)}

_static = {}
DEBUG = False



def _alibi_bias_f32():
    import jax
    import jax.numpy as jnp
    with jax.default_device(jax.devices("cpu")[0]):
        ratio = 2.0 ** (-8.0 / H)
        slopes = ratio ** jnp.arange(1, H + 1, dtype=jnp.float32)
        distances = jnp.arange(1 - S, 1, dtype=jnp.float32)
        bias = slopes[:, None] * distances[None, :]
        return np.asarray(bias), np.asarray(slopes)


def _recon_tiles_for(rank, w):
    tiles = []
    if w < 2:
        for t in range(TPC):
            oc, hcw = t // 2, t % 2
            tiles.append((oc * 128, (2 * rank + hcw) * 128, D))
    else:
        for t in range(TPC):
            icw, hpc = t // 16, t % 16
            tiles.append((hpc * 128, (8 * rank + icw) * 128, INTER))
    return tiles


def _build_static():
    if _static:
        return _static
    t64 = np.linspace(0.0, NCTRL - 1.0, NS, dtype=np.float64)
    i_arr = np.clip(np.floor(t64).astype(np.int64), 0, NCTRL - 2)
    f_arr = (t64 - i_arr).astype(np.float32)
    del t64
    bias_f32, slopes = _alibi_bias_f32()

    perms, btabs, rpts, masks = [], [], [], []
    for c in range(NCORES):
        par = c % 2
        blocks = [2 * k + par for k in range(8)] + [2 * k + 1 - par for k in range(8)]
        perms.append(blocks)
        gl = np.concatenate([np.arange(b * 128, b * 128 + 128) for b in blocks])
        kglob, qglob = gl, gl[:TOK]
        btab = np.empty((128, H * 16), np.float32)
        for h in range(H):
            for kc in range(16):
                btab[:, h * 16 + kc] = bias_f32[h, kglob[kc * 128:(kc + 1) * 128]]
        btabs.append(btab)
        rpt = np.empty((H, 2, 128, 512), np.float32)
        for h in range(H):
            row = (slopes[h] * (np.float32(S - 1) - qglob.astype(np.float32))
                   ).astype(np.float32)
            rpt[h, 0] = np.broadcast_to(row[:512], (128, 512))
            rpt[h, 1] = np.broadcast_to(row[512:], (128, 512))
        rpts.append(rpt)
        mk = np.zeros((len(MASK_PAIRS), 128, 512), np.float32)
        for mi, (tile, kc) in enumerate(MASK_PAIRS):
            kpos = kglob[kc * 128:(kc + 1) * 128]
            qpos = qglob[tile * 512:(tile + 1) * 512]
            mk[mi][kpos[:, None] > qpos[None, :]] = FMIN
        masks.append(mk)

    cps_idx, cpps, fplans = [], [], []
    u128 = np.arange(128)
    for c in range(NCORES):
        ci = np.empty((3, 128, TPC, 5), np.int64)
        cp = np.empty((3, 128, TPC), np.float32)
        fp = np.empty((3, 128, TPC, 128), np.float32)
        for w in range(3):
            for t, (row0, col0, IN) in enumerate(_recon_tiles_for(c, w)):
                j0 = (row0 + u128) * IN + col0
                ib = i_arr[j0]
                iv = i_arr[j0[:, None] + u128[None, :]]
                dlt = iv - ib[:, None]
                assert dlt.min() >= 0 and dlt.max() <= 1
                adv = dlt.argmax(axis=1).astype(np.float32)
                adv[dlt.max(axis=1) == 0] = 128.0
                cp[w, :, t] = adv
                for m in range(5):
                    ci[w, :, t, m] = np.clip(ib + m - 1, 0, NCTRL - 1)
                fp[w, :, t, :] = f_arr[j0[:, None] + u128[None, :]]
        cps_idx.append(ci)
        cpps.append(cp)
        fplans.append(fp.reshape(3, 128, TPC * 128))

    _static.update(dict(
        perms=perms, btabs=btabs, rpts=rpts, masks=masks,
        cps_idx=cps_idx, cpps=cpps, fplans=fplans,
        iota=np.ascontiguousarray(
            np.broadcast_to(np.arange(128, dtype=np.float32), (128, 128))),
        ident=np.eye(128, dtype=np.float32),
        misc=np.concatenate([
            np.ones((128, 129), np.float32),
            np.full((128, 1), np.float32(1e-5), np.float32),
            np.full((128, 1), np.float32(SQRT_HD), np.float32)], axis=1),
    ))
    return _static



class Ctr:
    def __init__(self, nc, name):
        self.h = nc.alloc_semaphore(name)
        self.n = 0

    def bump(self, k):
        self.n += k
        return self.n


def build_nc():
    nc = bass.Bass("TRN2", target_bir_lowering=False, debug=False,
                   detect_race_conditions=False)

    din = lambda n, sh: nc.dram_tensor(n, sh, F32, kind="ExternalInput")
    x_in = din("x", [S, D])
    wqb_in = din("wqb", [16, 16, 128, 128])
    wkb_in = din("wkb", [16, 16, 128, 128])
    wvb_in = din("wvb", [16, 16, 128, 128])
    wob_in = din("wob", [16, 16, 128, 128])
    btab_in = din("btab", [128, H * 16])
    rpt_in = din("rpt", [H, 2, 128, 512])
    mask_in = din("maskt", [len(MASK_PAIRS), 128, 512])
    bq_in = din("bqt", [128, 16])
    bk_in = din("bkt", [128, 16])
    bv_in = din("bvt", [128, 16])
    bo_in = din("bot", [128, 16])
    cp_in = din("cpst", [3, 128, TPC * 5])
    cpp_in = din("cppt", [3, 128, TPC])
    fpl_in = din("fplt", [3, 128, TPC * 128])
    iota_in = din("iota", [128, 128])
    ident_in = din("ident", [128, 128])
    misc_in = din("misc", [128, 131])

    y_out = nc.dram_tensor("y", [TOK, D], F32, kind="ExternalOutput")
    if DEBUG:
        dbg_xno = nc.dram_tensor("dbg_xno", [128, 16 * TOK], F32, kind="ExternalOutput")
        dbg_kt = nc.dram_tensor("dbg_kt", [D, S], F32, kind="ExternalOutput")
        dbg_vt = nc.dram_tensor("dbg_vt", [D, S], F32, kind="ExternalOutput")
        dbg_qt = nc.dram_tensor("dbg_qt", [D, TOK], F32, kind="ExternalOutput")
        dbg_at = nc.dram_tensor("dbg_at", [D, TOK], F32, kind="ExternalOutput")
        dbg_ag = nc.dram_tensor("dbg_ag", [SHARD], F32, kind="ExternalOutput")
        dbg_ago = nc.dram_tensor("dbg_ago", [NCORES, SHARD], F32, kind="ExternalOutput")

    ag_in_gu = nc.dram_tensor("ag_in_gu", [2 * WOFF], F32)
    ag_out_gu = nc.dram_tensor("ag_out_gu", [NCORES, 2 * WOFF], F32, addr_space="Shared")
    ag_in_d = nc.dram_tensor("ag_in_d", [WOFF], F32)
    ag_out_d = nc.dram_tensor("ag_out_d", [NCORES, WOFF], F32, addr_space="Shared")
    kt_sc = nc.dram_tensor("kt_sc", [D, S], F32)
    vt_sc = nc.dram_tensor("vt_sc", [D, S], F32)
    qt_sc = nc.dram_tensor("qt_sc", [D, TOK], F32)
    xno_sc = nc.dram_tensor("xno_sc", [128, 16 * TOK], F32)
    at_sc = nc.dram_tensor("at_sc", [D, TOK], F32)

    BIG = nc.alloc_sbuf_tensor("BIG", [128, 34816], F32)
    SMALL = nc.alloc_sbuf_tensor("SMALL", [128, 15360], F32)
    CON = nc.alloc_sbuf_tensor("CON", [128, 768], F32)
    big = lambda o, n: BIG[:, o:o + n]
    small = lambda o, n: SMALL[:, o:o + n]

    c_ident = CON[:, 0:128]
    c_iota = CON[:, 128:256]
    c_btab = CON[:, 256:512]
    c_bq = CON[:, 512:528]
    c_bk = CON[:, 528:544]
    c_bv = CON[:, 544:560]
    c_bo = CON[:, 560:576]
    c_misc = CON[:, 576:707]
    c_onesr = c_misc[:, 0:128]
    c_ones = c_misc[:, 128:129]
    c_eps = c_misc[:, 129:130]
    c_sqrt = c_misc[:, 130:131]
    c_rsq = CON[:, 707:708]

    p0 = nc.alloc_psum_tensor("p0", [128, 512], F32)
    p1 = nc.alloc_psum_tensor("p1", [128, 512], F32)
    p2 = nc.alloc_psum_tensor("p2", [128, 512], F32)
    p3 = nc.alloc_psum_tensor("p3", [128, 512], F32)
    p4 = nc.alloc_psum_tensor("p4", [128, 512], F32)
    p5 = nc.alloc_psum_tensor("p5", [128, 512], F32)
    p6 = nc.alloc_psum_tensor("p6", [128, 512], F32)
    p7 = nc.alloc_psum_tensor("p7", [128, 512], F32)

    Cm = lambda n: Ctr(nc, n)
    d_con = Cm("d_con")
    d_f = [Cm("d_f0"), Cm("d_f1")]
    d_x = [Cm("d_x0"), Cm("d_x1")]
    d_w = [Cm("d_w0"), Cm("d_w1")]
    d_h = [Cm("d_h0"), Cm("d_h1")]
    d_g = [Cm("d_g0"), Cm("d_g1")]
    d_u = [Cm("d_u0"), Cm("d_u1")]
    d_d = [Cm("d_d0"), Cm("d_d1")]
    d_a = [Cm("d_a0"), Cm("d_a1")]
    d_xr = Cm("d_xr")
    d_xh = Cm("d_xh")
    d_sp = Cm("d_sp")
    d_y = Cm("d_y")
    s_cc = Cm("s_cc")
    d_rc = Cm("d_rc")
    s_x = Cm("s_x")
    s_t = Cm("s_t")
    s_v = Cm("s_v")
    s_s = Cm("s_s")

    with nc.Block() as blk:
        @blk.sync
        def _(e):
            for dst, src in ((c_ident, ident_in), (c_iota, iota_in),
                             (c_btab, btab_in), (c_bq, bq_in), (c_bk, bk_in),
                             (c_bv, bv_in), (c_bo, bo_in), (c_misc, misc_in)):
                e.dma_start(out=dst, in_=src.ap()[:, :]).then_inc(d_con.h, 16)
                d_con.bump(16)

        @blk.vector
        def _(e):
            e.wait_ge(d_con.h, d_con.n)
            e.reciprocal(out=c_rsq, in_=c_sqrt)
    nc.const_aps.register(F32, EPS, c_eps[:, 0:1])

    RTC = 8
    RNCH = TPC // RTC
    NRCH = 3 * RNCH

    xnt = big(0, 32768)
    xnt3 = xnt.rearrange("p (fc t) -> p fc t", fc=16)
    ln_x = [small(0, 2048), small(2048, 2048)]
    ln_t = small(4096, 2048)
    ln_s = small(6144, 2048)
    ln_c = small(8192, 8)

    T0, V0, S0 = s_t.n, s_v.n, s_s.n
    LNV = lambda tc, i: V0 + tc * 10 + i
    LNS = lambda tc, i: S0 + tc * 2 + i
    LNT = lambda tc, g: T0 + tc * 4 + g + 1

    with nc.Block() as blk:
        @blk.sync
        def _(e):
            for tc in range(16):
                if tc >= 2:
                    e.wait_ge(s_t.h, LNT(tc - 2, 3))
                e.dma_start(out=ln_x[tc % 2],
                            in_=x_in.ap()[tc * 128:(tc + 1) * 128, :]
                            ).then_inc(d_x[tc % 2].h, 16)
                d_x[tc % 2].bump(16)

        @blk.vector
        def _(e):
            for tc in range(16):
                e.wait_ge(d_x[tc % 2].h, 16 * (tc // 2 + 1))
                xs = ln_x[tc % 2]
                e.reduce_sum(out=ln_c[:, 0:1], in_=xs, axis=AXX).then_inc(s_v.h, 1)
                e.wait_ge(s_v.h, s_v.bump(1))
                e.tensor_scalar_mul(ln_c[:, 1:2], ln_c[:, 0:1], 1.0 / 2048.0
                                    ).then_inc(s_v.h, 1)
                e.wait_ge(s_v.h, s_v.bump(1))
                e.tensor_scalar(out=ln_t, in0=xs, scalar1=ln_c[:, 1:2],
                                scalar2=None, op0=ALU.subtract).then_inc(s_v.h, 1)
                s_v.bump(1)
                e.wait_ge(s_s.h, LNS(tc, 1))
                e.reduce_sum(out=ln_c[:, 0:1], in_=ln_s, axis=AXX).then_inc(s_v.h, 1)
                s_v.bump(1)
                e.wait_ge(s_s.h, LNS(tc, 2))
                e.reciprocal(out=ln_c[:, 3:4], in_=ln_c[:, 2:3]).then_inc(s_v.h, 1)
                e.wait_ge(s_v.h, s_v.bump(1))
                e.tensor_scalar(out=xs, in0=ln_t, scalar1=ln_c[:, 3:4],
                                scalar2=None, op0=ALU.mult).then_inc(s_v.h, 1)
                s_v.bump(1)
                for g in range(4):
                    e.wait_ge(s_t.h, LNT(tc, g))
                    dst = xnt3[:, g * 4:(g + 1) * 4, tc * 128:(tc + 1) * 128]
                    e.tensor_copy(out=dst,
                                  in_=p6.rearrange("p (j t) -> p j t", j=4)
                                  ).then_inc(s_v.h, 1)
                    s_v.bump(1)

        @blk.scalar
        def _(e):
            for tc in range(16):
                e.wait_ge(s_v.h, LNV(tc, 3))
                e.activation(out=ln_s, in_=ln_t, func=ACT.Square).then_inc(s_s.h, 1)
                s_s.bump(1)
                e.wait_ge(s_v.h, LNV(tc, 4))
                e.activation(out=ln_c[:, 2:3], in_=ln_c[:, 0:1], func=ACT.Sqrt,
                             scale=1.0 / 2048.0, bias=EPS).then_inc(s_s.h, 1)
                s_s.bump(1)

        @blk.tensor
        def _(e):
            for tc in range(16):
                e.wait_ge(s_v.h, LNV(tc, 6))
                for g in range(4):
                    prev = LNV(tc, 6 + g) if g >= 1 else (
                        LNV(tc - 1, 10) if tc >= 1 else None)
                    if prev is not None:
                        e.wait_ge(s_v.h, prev)
                    for j in range(4):
                        fc = g * 4 + j
                        ins = e.transpose(p6[:, j * 128:(j + 1) * 128],
                                          ln_x[tc % 2][:, fc * 128:(fc + 1) * 128],
                                          c_ident)
                    ins.then_inc(s_t.h, 1)
                    s_t.bump(1)

    with nc.Block() as blk:
        @blk.gpsimd
        def _(e):
            e.dma_start(out=xno_sc.ap().rearrange("p (fc t) -> p fc t", fc=16),
                        in_=xnt3[:, :, 0:TOK]).then_inc(d_sp.h, 16)
            d_sp.bump(16)

    w_slab = [small(0, 2048), small(2048, 2048)]
    pr_st = [small(4096, 512), small(4608, 512)]
    r_f = [small(5120, 1024), small(6144, 1024)]
    r_acc = small(7168, 1024)
    r_bac = small(8192, 1024)
    r_dlt = small(9216, 1024)
    r_stage = [small(10240, 1024), small(11264, 1024)]
    t_q = small(12288, 1024)
    t_cps = small(13312, 640)
    t_cpp = small(13952, 128)

    s_rv = Cm("s_rv")
    s_rt = Cm("s_rt")
    d_ag = [Cm("d_ag0"), Cm("d_ag1")]
    RPREP, REVAL, REVA, RT = {}, {}, {}, {}
    _rv = 0
    _rt = 0
    for c in range(NRCH):
        w, t = c // RNCH, c % RNCH
        if t == 0:
            _rv += 1
            RPREP[w] = _rv
        _rv += 1
        REVAL[c] = _rv
        REVA[c] = []
        for g in range(2):
            _rv += 1
            REVA[c].append(_rv)
        RT[c] = []
        for g in range(2):
            _rt += 1
            RT[c].append(_rt)

    def emit_recon_tables(e, wv):
        e.dma_start(out=t_cps, in_=cp_in.ap()[wv, :, :]).then_inc(d_rc.h, 16)
        d_rc.bump(16)
        e.dma_start(out=t_cpp, in_=cpp_in.ap()[wv, :, :]).then_inc(d_rc.h, 16)
        d_rc.bump(16)
        if wv == 0:
            e.dma_start(out=r_f[0], in_=fpl_in.ap()[0, :, 0:RTC * 128]
                        ).then_inc(d_f[0].h, 16)
            d_f[0].bump(16)

    def emit_recon_sync(e, cl):
        for c in cl:
            nx = c + 1
            if nx >= NRCH:
                continue
            w, t = nx // RNCH, nx % RNCH
            if nx >= 2:
                e.wait_ge(s_rv.h, REVAL[nx - 2])
            e.dma_start(out=r_f[nx % 2],
                        in_=fpl_in.ap()[w, :, t * RTC * 128:(t + 1) * RTC * 128]
                        ).then_inc(d_f[nx % 2].h, 16)
            d_f[nx % 2].bump(16)

    def emit_recon_prep(e, w):
        if True:
            if True:
                e.wait_ge(d_rc.h, 32 * (w + 1))
                cv = t_cps.rearrange("p (tt m) -> p tt m", m=5)
                sv = [cv[:, :, m] for m in range(5)]
                q = [t_q[:, k * TPC:(k + 1) * TPC] for k in range(8)]



                tmp = r_acc
                tcol = lambda i: tmp[:, i * TPC:(i + 1) * TPC]
                ops = []

                def plan(dst, a, b, cc, d, base):
                    u = [tcol(base + i) for i in range(4)]
                    ops.append(("c", dst[0], b))
                    ops.append(("s", u[0], cc, a, ALU.subtract))
                    ops.append(("m", dst[1], u[0], 0.5))
                    ops.append(("m", u[1], b, -2.5))
                    ops.append(("s", u[1], u[1], a, ALU.add))
                    ops.append(("m", u[2], cc, 2.0))
                    ops.append(("s", u[1], u[1], u[2], ALU.add))
                    ops.append(("m", u[2], d, -0.5))
                    ops.append(("s", dst[2], u[1], u[2], ALU.add))
                    ops.append(("m", u[3], b, 1.5))
                    ops.append(("m", u[0], a, -0.5))
                    ops.append(("s", u[3], u[3], u[0], ALU.add))
                    ops.append(("m", u[0], cc, -1.5))
                    ops.append(("s", u[3], u[3], u[0], ALU.add))
                    ops.append(("m", u[0], d, 0.5))
                    ops.append(("s", dst[3], u[3], u[0], ALU.add))

                plan([q[0], q[1], q[2], q[3]], sv[0], sv[1], sv[2], sv[3], 0)
                opsA = ops
                ops = []
                plan([q[4], q[5], q[6], q[7]], sv[1], sv[2], sv[3], sv[4], 4)
                opsB = ops
                seq = []
                for i in range(max(len(opsA), len(opsB))):
                    if i < len(opsA):
                        seq.append(opsA[i])
                    if i < len(opsB):
                        seq.append(opsB[i])
                for k in range(4):
                    seq.append(("s", q[4 + k], q[4 + k], q[k], ALU.subtract))
                last = None
                for op in seq:
                    if op[0] == "c":
                        last = e.tensor_copy(out=op[1], in_=op[2])
                    elif op[0] == "m":
                        last = e.tensor_scalar_mul(op[1], op[2], op[3])
                    else:
                        last = e.tensor_tensor(out=op[1], in0=op[2], in1=op[3],
                                               op=op[4])
                last.then_inc(s_rv.h, 1)
                s_rv.bump(1)
    def emit_recon_vector(e, cl):
        for c in cl:
            w, t = c // RNCH, c % RNCH
            sl = c % 2
            t0 = t * RTC
            e.wait_ge(d_f[sl].h, rf_fill_at[c])
            if c >= 1:
                e.wait_ge(s_rt.h, RT[c - 1][1])
            fb = r_f[sl].rearrange("p (tt u) -> p tt u", tt=RTC)
            acc = r_acc.rearrange("p (tt u) -> p tt u", tt=RTC)
            bac = r_bac.rearrange("p (tt u) -> p tt u", tt=RTC)
            dlt = r_dlt.rearrange("p (tt u) -> p tt u", tt=RTC)
            qv = lambda k: t_q[:, k * TPC + t0:k * TPC + t0 + RTC] \
                .unsqueeze(2).broadcast_to([128, RTC, 128])
            cpv = t_cpp[:, t0:t0 + RTC].unsqueeze(2).broadcast_to([128, RTC, 128])
            iov = c_iota.unsqueeze(1).broadcast_to([128, RTC, 128])
            e.tensor_tensor(out=dlt, in0=iov, in1=cpv, op=ALU.is_ge)
            e.tensor_tensor(out=acc, in0=fb, in1=qv(3), op=ALU.mult)
            e.tensor_tensor(out=bac, in0=fb, in1=qv(7), op=ALU.mult)
            e.tensor_tensor(out=acc, in0=acc, in1=qv(2), op=ALU.add)
            e.tensor_tensor(out=bac, in0=bac, in1=qv(6), op=ALU.add)
            e.tensor_tensor(out=acc, in0=acc, in1=fb, op=ALU.mult)
            e.tensor_tensor(out=bac, in0=bac, in1=fb, op=ALU.mult)
            e.tensor_tensor(out=acc, in0=acc, in1=qv(1), op=ALU.add)
            e.tensor_tensor(out=bac, in0=bac, in1=qv(5), op=ALU.add)
            e.tensor_tensor(out=acc, in0=acc, in1=fb, op=ALU.mult)
            e.tensor_tensor(out=bac, in0=bac, in1=fb, op=ALU.mult)
            e.tensor_tensor(out=acc, in0=acc, in1=qv(0), op=ALU.add)
            e.tensor_tensor(out=bac, in0=bac, in1=qv(4), op=ALU.add
                            ).then_inc(s_x.h, 1)
            e.wait_ge(s_x.h, s_x.bump(1))
            e.tensor_tensor(out=bac, in0=bac, in1=dlt, op=ALU.mult
                            ).then_inc(s_x.h, 1)
            e.wait_ge(s_x.h, s_x.bump(1))
            e.tensor_tensor(out=acc, in0=acc, in1=bac, op=ALU.add
                            ).then_inc(s_rv.h, 1)
            s_rv.bump(1)
            for g in range(2):
                e.wait_ge(s_rt.h, RT[c][g])
                if g == 0 and c >= 2:
                    e.wait_ge(d_ag[c % 2].h, ag_fill_at[c])
                e.tensor_copy(out=r_stage[c % 2][:, g * 512:(g + 1) * 512],
                              in_=p4[:, :]).then_inc(s_rv.h, 1)
                s_rv.bump(1)


    def emit_recon_tensor(e, cl):
        for c in cl:
            e.wait_ge(s_rv.h, REVAL[c])
            for g in range(2):
                if g == 1:
                    e.wait_ge(s_rv.h, REVA[c][0])
                elif c >= 1:
                    e.wait_ge(s_rv.h, REVA[c - 1][1])
                for j in range(4):
                    ti = g * 4 + j
                    ins = e.transpose(p4[:, j * 128:(j + 1) * 128],
                                      r_acc[:, ti * 128:(ti + 1) * 128], c_ident)
                ins.then_inc(s_rt.h, 1)
                s_rt.bump(1)

    def emit_recon_gpsimd(e, cl):
        for c in cl:
            w, t = c // RNCH, c % RNCH
            e.wait_ge(s_rv.h, REVA[c][1])
            dst = (ag_in_gu.ap()[w * WOFF + t * RTC * TILE_EL:
                                 w * WOFF + (t + 1) * RTC * TILE_EL]
                   if w < 2 else
                   ag_in_d.ap()[t * RTC * TILE_EL:(t + 1) * RTC * TILE_EL])
            e.dma_start(out=dst.rearrange("(tt p f) -> p tt f", p=128, f=128),
                        in_=r_stage[c % 2].rearrange("p (tt f) -> p tt f", tt=RTC),
                        ).then_inc(d_ag[c % 2].h, 16)
            d_ag[c % 2].bump(16)
            if c == NRCH - 1:
                e.wait_ge(d_ag[0].h, d_ag[0].n)
                e.wait_ge(d_ag[1].h, d_ag[1].n)
                e.collective_compute(
                    "AllGather", ALU.bypass,
                    replica_groups=[list(range(NCORES))],
                    ins=[ag_in_gu.ap().opt()],
                    outs=[ag_out_gu.ap().opt()],
                ).then_inc(s_cc.h, 1)
                s_cc.bump(1)
            if c == NRCH - 1:
                e.wait_ge(d_ag[0].h, d_ag[0].n)
                e.wait_ge(d_ag[1].h, d_ag[1].n)
                e.collective_compute(
                    "AllGather", ALU.bypass,
                    replica_groups=[list(range(NCORES))],
                    ins=[ag_in_d.ap().opt()],
                    outs=[ag_out_d.ap().opt()],
                ).then_inc(s_cc.h, 1)
                s_cc.bump(1)

    rf_fill_at = {}
    _f = [0, 0]
    _f[0] = 16
    rf_fill_at[0] = 16
    for nx in range(1, NRCH):
        _f[nx % 2] += 16
        rf_fill_at[nx] = _f[nx % 2]
    ag_fill_at = {}
    _a = [0, 0]
    for c in range(NRCH):
        ag_fill_at[c] = _a[c % 2] if c >= 2 else 0
        _a[c % 2] += 16


    def proj(wten, bias_col, n_tt, dst_sc, div, rhs_fn, rw=None):
        TT0, VV0, SP0 = s_t.n, s_v.n, d_sp.n
        W0 = (d_w[0].n, d_w[1].n)
        PT = lambda idx: TT0 + idx + 1
        PV = lambda idx: VV0 + idx + 1
        PS = lambda idx: SP0 + 16 * (idx + 1)
        rch = lambda dc: ([rw * RNCH + dc] if rw is not None else [])
        with nc.Block() as blk:
            @blk.sync
            def _(e):
                if rw is not None:
                    emit_recon_tables(e, rw)
                for dc in range(16):
                    if dc >= 2:
                        e.wait_ge(s_t.h, PT((dc - 1) * n_tt - 1))
                    e.dma_start(out=w_slab[dc % 2]
                                .rearrange("p (k f) -> p k f", k=16),
                                in_=wten.ap()[dc].rearrange("k p f -> p k f")
                                ).then_inc(d_w[dc % 2].h, 16)
                    d_w[dc % 2].bump(16)
                    emit_recon_sync(e, rch(dc))

            @blk.tensor
            def _(e):
                for dc in range(16):
                    e.wait_ge(d_w[dc % 2].h, W0[dc % 2] + 16 * (dc // 2 + 1))
                    for tt in range(n_tt):
                        idx = dc * n_tt + tt
                        ps = (p0, p1)[idx % 2]
                        if idx >= 2:
                            e.wait_ge(s_v.h, PV(idx - 2))
                        for hc in range(16):
                            ins = e.matmul(ps[:, :],
                                           w_slab[dc % 2][:, hc * 128:(hc + 1) * 128],
                                           rhs_fn(hc, tt),
                                           start=(hc == 0), stop=(hc == 15))
                        ins.then_inc(s_t.h, 1)
                        s_t.bump(1)
                    emit_recon_tensor(e, rch(dc))

            @blk.vector
            def _(e):
                if rw is not None:
                    emit_recon_prep(e, rw)
                for dc in range(16):
                    for tt in range(n_tt):
                        idx = dc * n_tt + tt
                        ps = (p0, p1)[idx % 2]
                        e.wait_ge(s_t.h, PT(idx))
                        if idx >= 2:
                            e.wait_ge(d_sp.h, PS(idx - 2))
                        if div:
                            e.tensor_scalar(out=pr_st[idx % 2], in0=ps[:, :],
                                            scalar1=bias_col[:, dc:dc + 1],
                                            scalar2=c_rsq[:, 0:1],
                                            op0=ALU.add, op1=ALU.mult
                                            ).then_inc(s_v.h, 1)
                        else:
                            e.tensor_scalar(out=pr_st[idx % 2], in0=ps[:, :],
                                            scalar1=bias_col[:, dc:dc + 1],
                                            scalar2=None, op0=ALU.add
                                            ).then_inc(s_v.h, 1)
                        s_v.bump(1)
                    emit_recon_vector(e, rch(dc))

            @blk.gpsimd
            def _(e):
                for dc in range(16):
                    for tt in range(n_tt):
                        idx = dc * n_tt + tt
                        e.wait_ge(s_v.h, PV(idx))
                        e.dma_start(out=dst_sc.ap()[dc * 128:(dc + 1) * 128,
                                                    tt * 512:(tt + 1) * 512],
                                    in_=pr_st[idx % 2]).then_inc(d_sp.h, 16)
                        d_sp.bump(16)
                    emit_recon_gpsimd(e, rch(dc))

    xnt_rhs = lambda hc, tt: xnt3[:, hc, tt * 512:(tt + 1) * 512]
    proj(wkb_in, c_bk, 4, kt_sc, False, xnt_rhs, rw=0)
    proj(wvb_in, c_bv, 4, vt_sc, False, xnt_rhs, rw=1)
    proj(wqb_in, c_bq, 2, qt_sc, True, xnt_rhs, rw=2)

    ctxt = big(0, 16384)
    mks = big(16384, 8192)
    x_exp = [big(24576, 512), big(25088, 512)]
    x_run = big(25600, 512)
    x_run2 = big(26112, 512)
    a_kt = [small(0, 2048), small(2048, 2048)]
    a_vt = [small(4096, 2048), small(6144, 2048)]
    a_qt = [small(8192, 1024), small(9216, 1024)]
    a_pl = [small(10240, 1024), small(11264, 1024)]
    a_vtr = small(12288, 2048)

    with nc.Block() as blk:
        @blk.sync
        def _(e):
            e.dma_start(out=mks.rearrange("p (m f) -> p m f", m=16),
                        in_=mask_in.ap().rearrange("m p f -> p m f")
                        ).then_inc(d_con.h, 16)
            d_con.bump(16)

    mask_slot = {p: i for i, p in enumerate(MASK_PAIRS)}
    ev_ctx_last = {}
    ev_norm = {}
    ev_vtrev = {}
    psc_hist = []

    AT, AV, AS = {}, {}, {}
    tn, vn, sn = s_t.n, s_v.n, s_s.n
    exp_slot_seq = []
    sc_bank_seq = []
    for h in range(H):
        for g in range(4):
            tn += 1
            AT[("vtr", h, g)] = tn
        for tile in range(2):
            kcl = (KCS0, KCS1)[tile]
            n = len(kcl)
            order = [("sc", 0), ("sc", 1)]
            for k in range(n - 2):
                order += [("ctx", k), ("sc", k + 2)]
            order += [("ctx", n - 2), ("ctx", n - 1), ("den",), ("bc",)]
            for op in order:
                tn += 1
                AT[(op[0], h, tile) + tuple(op[1:])] = tn
                if op[0] == "sc":
                    sc_bank_seq.append((h, tile, op[1]))
        for g in range(4):
            vn += 1
            AV[("vtrev", h, g)] = vn
        for tile in range(2):
            kcl = (KCS0, KCS1)[tile]
            n = len(kcl)
            order = [("arg", 0), ("arg", 1)]
            for k in range(n - 2):
                order += [("run", k), ("arg", k + 2)]
            order += [("run", n - 2), ("run", n - 1), ("recip",), ("bcev",),
                      ("norm",)]
            for op in order:
                vn += 1
                AV[(op[0], h, tile) + tuple(op[1:])] = vn
        for tile in range(2):
            n = len((KCS0, KCS1)[tile])
            for ki in range(n):
                sn += 1
                AS[("exp", h, tile, ki)] = sn
                exp_slot_seq.append((h, tile, ki))
    sc_prev2 = {key: (sc_bank_seq[i - 2] if i >= 2 else None)
                for i, key in enumerate(sc_bank_seq)}
    exp_prev2 = {key: (exp_slot_seq[i - 2] if i >= 2 else None)
                 for i, key in enumerate(exp_slot_seq)}
    bcev_order = [(h, t) for h in range(H) for t in range(2)]

    with nc.Block() as blk:
        @blk.tensor
        def _(e):
            for h in range(H):
                sl = h % 2
                e.wait_ge(d_h[sl].h, 64 * (h // 2 + 1))
                if h == 0:
                    e.wait_ge(d_con.h, d_con.n)
                for g in range(4):
                    prev = ("vtrev", h, g - 1) if g >= 1 else (
                        ("vtrev", h - 1, 3) if h >= 1 else None)
                    if prev is not None:
                        e.wait_ge(s_v.h, AV[prev])
                    for j in range(4):
                        kc = g * 4 + j
                        ins = e.transpose(p6[:, j * 128:(j + 1) * 128],
                                          a_vt[sl][:, kc * 128:(kc + 1) * 128],
                                          c_ident)
                    ins.then_inc(s_t.h, 1)
                    s_t.bump(1)
                for tile in range(2):
                    kcl = (KCS0, KCS1)[tile]
                    n = len(kcl)
                    pctx = (p2, p3)[tile]

                    def esc(ki, h=h, tile=tile, sl=sl, kcl=kcl):
                        key = (h, tile, ki)
                        prev = sc_prev2[key]
                        if prev is not None:
                            e.wait_ge(s_s.h, AS[("exp",) + prev])
                        kc = kcl[ki]
                        ps = (p0, p1)[ki % 2]
                        e.matmul(ps[:, :],
                                 a_kt[sl][:, kc * 128:(kc + 1) * 128],
                                 a_qt[sl][:, tile * 512:(tile + 1) * 512],
                                 start=True, stop=True).then_inc(s_t.h, 1)
                        s_t.bump(1)

                    def ectx(ki, h=h, tile=tile, kcl=kcl, n=n, pctx=pctx):
                        if ki == 0:
                            e.wait_ge(s_v.h, AV[("vtrev", h, 3)])
                            if h >= 1:
                                e.wait_ge(s_v.h, AV[("norm", h - 1, tile)])
                        e.wait_ge(s_s.h, AS[("exp", h, tile, ki)])
                        kc = kcl[ki]
                        e.matmul(pctx[:, :], a_vtr[:, kc * 128:(kc + 1) * 128],
                                 x_exp[ki % 2],
                                 start=(ki == 0), stop=(ki == n - 1)
                                 ).then_inc(s_t.h, 1)
                        s_t.bump(1)

                    esc(0)
                    esc(1)
                    for k in range(n - 2):
                        ectx(k)
                        esc(k + 2)
                    ectx(n - 2)
                    ectx(n - 1)
                    e.wait_ge(s_v.h, AV[("run", h, tile, n - 1)])
                    bi = bcev_order.index((h, tile))
                    if bi >= 1:
                        e.wait_ge(s_v.h, AV[("bcev",) + bcev_order[bi - 1]])
                    e.matmul(p7[0:1, :], c_ones, x_run, start=True, stop=False)
                    e.matmul(p7[0:1, :], c_ones, x_run2, start=False, stop=True
                             ).then_inc(s_t.h, 1)
                    s_t.bump(1)
                    e.wait_ge(s_v.h, AV[("recip", h, tile)])
                    e.matmul(p7[:, :], c_onesr[0:1, :], x_run[0:1, :],
                             start=True, stop=True).then_inc(s_t.h, 1)
                    s_t.bump(1)

        @blk.vector
        def _(e):
            for h in range(H):
                sl = h % 2
                for g in range(4):
                    e.wait_ge(s_t.h, AT[("vtr", h, g)])
                    if h >= 1 and g == 0:
                        e.wait_ge(s_t.h, AT[("ctx", h - 1, 1, len(KCS1) - 1)])
                    e.tensor_copy(out=a_vtr[:, g * 512:(g + 1) * 512], in_=p6[:, :]
                                  ).then_inc(s_v.h, 1)
                    s_v.bump(1)
                for tile in range(2):
                    kcl = (KCS0, KCS1)[tile]
                    n = len(kcl)
                    pctx = (p2, p3)[tile]

                    def earg(ki, h=h, tile=tile, sl=sl, kcl=kcl):
                        e.wait_ge(s_t.h, AT[("sc", h, tile, ki)])
                        kc = kcl[ki]
                        ps = (p0, p1)[ki % 2]
                        ins = e.tensor_tensor(
                            out=ps[:, :], in0=ps[:, :],
                            in1=a_pl[sl][:, tile * 512:(tile + 1) * 512],
                            op=ALU.add)
                        mi = MASK_SLOT.get((tile, kc))
                        if mi is not None:
                            ins.then_inc(s_x.h, 1)
                            e.wait_ge(s_x.h, s_x.bump(1))
                            ins = e.tensor_tensor(
                                out=ps[:, :], in0=ps[:, :],
                                in1=mks[:, mi * 512:(mi + 1) * 512], op=ALU.add)
                        ins.then_inc(s_v.h, 1)
                        s_v.bump(1)

                    def erun(ki, h=h, tile=tile, n=n):
                        e.wait_ge(s_s.h, AS[("exp", h, tile, ki)])
                        rbuf = (x_run, x_run2)[ki % 2]
                        if ki < 2:
                            bi = bcev_order.index((h, tile))
                            if bi >= 1:
                                e.wait_ge(s_t.h, AT[("den",) + bcev_order[bi - 1]])
                            e.tensor_copy(out=rbuf, in_=x_exp[ki % 2]
                                          ).then_inc(s_v.h, 1)
                        else:
                            e.tensor_tensor(out=rbuf, in0=rbuf,
                                            in1=x_exp[ki % 2], op=ALU.add
                                            ).then_inc(s_v.h, 1)
                        s_v.bump(1)

                    earg(0)
                    earg(1)
                    for k in range(n - 2):
                        erun(k)
                        earg(k + 2)
                    erun(n - 2)
                    erun(n - 1)
                    e.wait_ge(s_t.h, AT[("den", h, tile)])
                    e.reciprocal(out=x_run[0:1, :], in_=p7[0:1, :]
                                 ).then_inc(s_v.h, 1)
                    s_v.bump(1)
                    e.wait_ge(s_t.h, AT[("bc", h, tile)])
                    e.tensor_copy(out=x_run, in_=p7[:, :]).then_inc(s_x.h, 1)
                    e.wait_ge(s_x.h, s_x.bump(1))
                    e.engine_nop().then_inc(s_v.h, 1)
                    s_v.bump(1)
                    e.wait_ge(s_t.h, AT[("ctx", h, tile, n - 1)])
                    e.scalar_tensor_tensor(
                        out=ctxt[:, h * 1024 + tile * 512:h * 1024 + (tile + 1) * 512],
                        in0=pctx[:, :], scalar=1.0, in1=x_run,
                        op0=ALU.mult, op1=ALU.mult).then_inc(s_v.h, 1)
                    s_v.bump(1)

        @blk.scalar
        def _(e):
            for h in range(H):
                for tile in range(2):
                    n = len((KCS0, KCS1)[tile])
                    for ki in range(n):
                        key = (h, tile, ki)
                        e.wait_ge(s_v.h, AV[("arg", h, tile, ki)])
                        prev = exp_prev2[key]
                        if prev is not None:
                            ph, pt, pk = prev
                            e.wait_ge(s_t.h, AT[("ctx", ph, pt, pk)])
                            e.wait_ge(s_v.h, AV[("run", ph, pt, pk)])
                        ps = (p0, p1)[ki % 2]
                        e.activation(out=x_exp[ki % 2], in_=ps[:, :], func=ACT.Exp,
                                     bias=c_btab[:, h * 16 + (KCS0, KCS1)[tile][ki]:
                                                 h * 16 + (KCS0, KCS1)[tile][ki] + 1],
                                     scale=1.0).then_inc(s_s.h, 1)
                        s_s.bump(1)

        @blk.sync
        def _(e):
            for h in range(H):
                sl = h % 2
                if h >= 2:
                    e.wait_ge(s_t.h, AT[("ctx", h - 2, 1, len(KCS1) - 1)])
                if h == 0:
                    e.wait_ge(d_sp.h, d_sp.n)
                e.dma_start(out=a_kt[sl], in_=kt_sc.ap()[h * 128:(h + 1) * 128, :]
                            ).then_inc(d_h[sl].h, 16)
                d_h[sl].bump(16)
                e.dma_start(out=a_vt[sl], in_=vt_sc.ap()[h * 128:(h + 1) * 128, :]
                            ).then_inc(d_h[sl].h, 16)
                d_h[sl].bump(16)
                e.dma_start(out=a_qt[sl], in_=qt_sc.ap()[h * 128:(h + 1) * 128, :]
                            ).then_inc(d_h[sl].h, 16)
                d_h[sl].bump(16)
                e.dma_start(out=a_pl[sl].rearrange("p (t f) -> p t f", t=2),
                            in_=rpt_in.ap()[h].rearrange("t p f -> p t f")
                            ).then_inc(d_h[sl].h, 16)
                d_h[sl].bump(16)

    ctx_rhs = lambda hc, tt: ctxt[:, hc * 1024 + tt * 512:hc * 1024 + (tt + 1) * 512]
    proj(wob_in, c_bo, 2, at_sc, False, ctx_rhs)

    intT = big(0, 32768)
    intT3 = intT.rearrange("p (ic t) -> p ic t", ic=64)
    f_xnh = small(0, 8192)
    xnh3 = f_xnh.rearrange("p (hc t) -> p hc t", hc=16)
    f_sl = [small(8192, 2048), small(10240, 2048)]
    f_si = [small(12288, 512), small(12800, 512)]
    f_dsl = [small(0, 2048), small(2048, 2048)]
    f_at = [small(4096, 512), small(4608, 512)]
    f_sum = small(5120, 8192)
    f_os = [small(0, 2048), small(2048, 2048)]
    f_xt = small(13312, 2048)

    def gate_slab_ap(w, ic, k):
        base = w * WOFF
        return ag_out_gu.ap()[:, base + ic * 32768:base + (ic + 1) * 32768] \
            .rearrange("r (k p f) -> k r p f", k=2, p=128, f=128)[k] \
            .rearrange("r p f -> p r f")

    def down_slab_ap(hp, pc, r):
        return ag_out_d.ap()[2 * pc + r, :] \
            .rearrange("(icw hq p f) -> icw hq p f",
                       icw=8, hq=16, p=128, f=128)[:, hp, :, :] \
            .rearrange("icw p f -> p icw f")

    def slg(t, k):
        return t.rearrange("p (r kk f) -> p kk r f", kk=2, f=128)[:, k, :, :]
    def sld(t, r):
        return t.rearrange("p (r icw f) -> p r icw f", r=2, icw=8, f=128)[:, r, :, :]

    for th in range(2):
        TT0, VV0, SS0 = s_t.n, s_v.n, s_s.n
        G0, U0 = d_g[0].n, d_g[1].n
        FTg = lambda ic: TT0 + 2 * ic + 1
        FTu = lambda ic: TT0 + 2 * ic + 2
        FV = lambda ic: VV0 + ic + 1
        FS = lambda ic: SS0 + ic + 1
        with nc.Block() as blk:
            @blk.sync
            def _(e, th=th):
                e.dma_start(out=xnh3,
                            in_=xno_sc.ap().rearrange("p (hc t) -> p hc t", hc=16)
                            [:, :, th * 512:(th + 1) * 512]).then_inc(d_xh.h, 16)
                d_xh.bump(16)

            @blk.gpsimd
            def _(e, th=th):
                if th == 0:
                    e.wait_ge(s_cc.h, 1)
                for ic in range(64):
                    if ic >= 1:
                        e.wait_ge(s_t.h, FTg(ic - 1))
                    for kkk in range(2):
                        e.dma_start(out=slg(f_sl[0], kkk),
                                    in_=gate_slab_ap(0, ic, kkk)
                                    ).then_inc(d_g[0].h, 16)
                        d_g[0].bump(16)
                    if ic >= 1:
                        e.wait_ge(s_t.h, FTu(ic - 1))
                    for kkk in range(2):
                        e.dma_start(out=slg(f_sl[1], kkk),
                                    in_=gate_slab_ap(1, ic, kkk)
                                    ).then_inc(d_g[1].h, 16)
                        d_g[1].bump(16)

            @blk.tensor
            def _(e, th=th):
                e.wait_ge(d_xh.h, d_xh.n)
                for ic in range(64):
                    pg = (p0, p1)[ic % 2]
                    pu = (p2, p3)[ic % 2]
                    if ic >= 2:
                        e.wait_ge(s_s.h, FS(ic - 2))
                    e.wait_ge(d_g[0].h, G0 + 32 * (ic + 1))
                    for hc in range(16):
                        ins = e.matmul(pg[:, :], f_sl[0][:, hc * 128:(hc + 1) * 128],
                                       xnh3[:, hc, :], start=(hc == 0),
                                       stop=(hc == 15))
                    ins.then_inc(s_t.h, 1)
                    s_t.bump(1)
                    if ic >= 2:
                        e.wait_ge(s_v.h, FV(ic - 2))
                    e.wait_ge(d_g[1].h, U0 + 32 * (ic + 1))
                    for hc in range(16):
                        ins = e.matmul(pu[:, :], f_sl[1][:, hc * 128:(hc + 1) * 128],
                                       xnh3[:, hc, :], start=(hc == 0),
                                       stop=(hc == 15))
                    ins.then_inc(s_t.h, 1)
                    s_t.bump(1)

            @blk.scalar
            def _(e):
                for ic in range(64):
                    e.wait_ge(s_t.h, FTg(ic))
                    if ic >= 2:
                        e.wait_ge(s_v.h, FV(ic - 2))
                    pg = (p0, p1)[ic % 2]
                    e.activation(out=f_si[ic % 2], in_=pg[:, :], func=ACT.Silu
                                 ).then_inc(s_s.h, 1)
                    s_s.bump(1)

            @blk.vector
            def _(e):
                for ic in range(64):
                    e.wait_ge(s_s.h, FS(ic))
                    e.wait_ge(s_t.h, FTu(ic))
                    pu = (p2, p3)[ic % 2]
                    e.tensor_tensor(out=intT3[:, ic, :], in0=f_si[ic % 2],
                                    in1=pu[:, :], op=ALU.mult).then_inc(s_v.h, 1)
                    s_v.bump(1)

        TT1, VV1 = s_t.n, s_v.n
        A0 = (d_a[0].n, d_a[1].n)
        D0 = (d_d[0].n, d_d[1].n)
        DT = lambda hp, pc: TT1 + hp * 4 + pc + 1
        DV = lambda hp: VV1 + hp + 1
        with nc.Block() as blk:
            @blk.sync
            def _(e, th=th):
                for hp in range(16):
                    if hp >= 2:
                        e.wait_ge(s_v.h, DV(hp - 2))
                    e.dma_start(out=f_at[hp % 2],
                                in_=at_sc.ap()[hp * 128:(hp + 1) * 128,
                                               th * 512:(th + 1) * 512]
                                ).then_inc(d_a[hp % 2].h, 16)
                    d_a[hp % 2].bump(16)

            @blk.gpsimd
            def _(e, th=th):
                if th == 0:
                    e.wait_ge(s_cc.h, 2)
                for hp in range(16):
                    for pc in range(4):
                        gi = hp * 4 + pc
                        if gi >= 2:
                            e.wait_ge(s_t.h, DT((gi - 2) // 4, (gi - 2) % 4))
                        for rrr in range(2):
                            e.dma_start(out=sld(f_dsl[gi % 2], rrr),
                                        in_=down_slab_ap(hp, pc, rrr)
                                        ).then_inc(d_d[gi % 2].h, 16)
                            d_d[gi % 2].bump(16)

            @blk.tensor
            def _(e, th=th):
                for hp in range(16):
                    pd = (p4, p5)[hp % 2]
                    if hp >= 2:
                        e.wait_ge(s_v.h, DV(hp - 2))
                    for pc in range(4):
                        gi = hp * 4 + pc
                        e.wait_ge(d_d[gi % 2].h, D0[gi % 2] + 32 * (gi // 2 + 1))
                        for kk in range(16):
                            ic = pc * 16 + kk
                            ins = e.matmul(pd[:, :],
                                           f_dsl[gi % 2][:, kk * 128:(kk + 1) * 128],
                                           intT3[:, ic, :],
                                           start=(pc == 0 and kk == 0),
                                           stop=(pc == 3 and kk == 15))
                        ins.then_inc(s_t.h, 1)
                        s_t.bump(1)

            @blk.vector
            def _(e, th=th):
                for hp in range(16):
                    e.wait_ge(s_t.h, DT(hp, 3))
                    e.wait_ge(d_a[hp % 2].h, A0[hp % 2] + 16 * (hp // 2 + 1))
                    pd = (p4, p5)[hp % 2]
                    e.scalar_tensor_tensor(
                        out=f_sum[:, hp * 512:(hp + 1) * 512],
                        in0=pd[:, :], scalar=1.0, in1=f_at[hp % 2],
                        op0=ALU.mult, op1=ALU.add).then_inc(s_v.h, 1)
                    s_v.bump(1)

        TT2, VV2, XR0, DY0 = s_t.n, s_v.n, d_xr.n, d_y.n
        GT = lambda tc, g: TT2 + tc * 4 + g + 1
        GV = lambda tc, i: VV2 + tc * 5 + i + 1
        with nc.Block() as blk:
            @blk.sync
            def _(e, th=th):
                for tc in range(4):
                    if tc >= 1:
                        e.wait_ge(s_v.h, GV(tc - 1, 4))
                    e.dma_start(out=f_xt,
                                in_=x_in.ap()[(th * 4 + tc) * 128:
                                              (th * 4 + tc + 1) * 128, :]
                                ).then_inc(d_xr.h, 16)
                    d_xr.bump(16)

            @blk.tensor
            def _(e, th=th):
                for tc in range(4):
                    for g in range(4):
                        prev = (tc, g - 1) if g >= 1 else (
                            (tc - 1, 3) if tc >= 1 else None)
                        if prev is not None:
                            e.wait_ge(s_v.h, GV(prev[0], prev[1]))
                        for j in range(4):
                            hp = g * 4 + j
                            ins = e.transpose(p6[:, j * 128:(j + 1) * 128],
                                              f_sum[:, hp * 512 + tc * 128:
                                                    hp * 512 + (tc + 1) * 128],
                                              c_ident)
                        ins.then_inc(s_t.h, 1)
                        s_t.bump(1)

            @blk.vector
            def _(e, th=th):
                for tc in range(4):
                    for g in range(4):
                        e.wait_ge(s_t.h, GT(tc, g))
                        if g == 0:
                            if tc >= 2:
                                e.wait_ge(d_y.h, DY0 + 16 * (tc - 1))
                            elif th > 0:
                                e.wait_ge(d_y.h, DY0)
                        e.tensor_copy(out=f_os[tc % 2][:, g * 512:(g + 1) * 512],
                                      in_=p6[:, :]).then_inc(s_v.h, 1)
                        s_v.bump(1)
                    e.wait_ge(d_xr.h, XR0 + 16 * (tc + 1))
                    e.wait_ge(s_v.h, s_v.n)
                    e.tensor_tensor(out=f_os[tc % 2], in0=f_os[tc % 2],
                                    in1=f_xt, op=ALU.add).then_inc(s_v.h, 1)
                    s_v.bump(1)

            @blk.gpsimd
            def _(e, th=th):
                for tc in range(4):
                    e.wait_ge(s_v.h, GV(tc, 4))
                    e.dma_start(out=y_out.ap()[(th * 4 + tc) * 128:
                                               (th * 4 + tc + 1) * 128, :],
                                in_=f_os[tc % 2]).then_inc(d_y.h, 16)
                    d_y.bump(16)

    with nc.Block() as blk:
        @blk.gpsimd
        def _(e):
            e.wait_ge(d_y.h, d_y.n)
            if DEBUG:
                for dst, srcp in ((dbg_xno, xno_sc), (dbg_kt, kt_sc),
                                  (dbg_vt, vt_sc), (dbg_qt, qt_sc),
                                  (dbg_at, at_sc)):
                    e.dma_start(out=dst.ap()[:], in_=srcp.ap()[:]
                                ).then_inc(d_y.h, 16)
                    d_y.bump(16)
                e.dma_start(out=dbg_ag.ap()[0:2 * WOFF], in_=ag_in_gu.ap()[:]
                            ).then_inc(d_y.h, 16)
                d_y.bump(16)
                e.dma_start(out=dbg_ag.ap()[2 * WOFF:], in_=ag_in_d.ap()[:]
                            ).then_inc(d_y.h, 16)
                d_y.bump(16)
                e.dma_start(out=dbg_ago.ap()[:, 0:2 * WOFF], in_=ag_out_gu.ap()[:]
                            ).then_inc(d_y.h, 16)
                d_y.bump(16)
                e.dma_start(out=dbg_ago.ap()[:, 2 * WOFF:], in_=ag_out_d.ap()[:]
                            ).then_inc(d_y.h, 16)
                d_y.bump(16)
                e.wait_ge(d_y.h, d_y.n)

    nc.finalize()
    return nc



def _block4(w):
    hh, dd = w.shape
    return np.ascontiguousarray(
        w.reshape(hh // 128, 128, dd // 128, 128).transpose(2, 0, 1, 3))


def kernel(L0_attn_x, L0_attn_Wq, L0_attn_bq, L0_attn_Wk, L0_attn_bk,
           L0_attn_Wv, L0_attn_bv, L0_attn_Wo, L0_attn_bo,
           L0_ffn_gate_cp, L0_ffn_up_cp, L0_ffn_down_cp):
    st = _build_static()
    x = np.asarray(L0_attn_x, dtype=np.float32)
    wq = _block4(np.asarray(L0_attn_Wq, np.float32))
    wk = _block4(np.asarray(L0_attn_Wk, np.float32))
    wv = _block4(np.asarray(L0_attn_Wv, np.float32))
    wo = _block4(np.asarray(L0_attn_Wo, np.float32))
    bq = np.ascontiguousarray(np.asarray(L0_attn_bq, np.float32).reshape(16, 128).T)
    bk = np.ascontiguousarray(np.asarray(L0_attn_bk, np.float32).reshape(16, 128).T)
    bv = np.ascontiguousarray(np.asarray(L0_attn_bv, np.float32).reshape(16, 128).T)
    bo = np.ascontiguousarray(np.asarray(L0_attn_bo, np.float32).reshape(16, 128).T)
    cps = [np.asarray(L0_ffn_gate_cp, np.float32),
           np.asarray(L0_ffn_up_cp, np.float32),
           np.asarray(L0_ffn_down_cp, np.float32)]

    nc = build_nc()
    in_maps = []
    for c in range(NCORES):
        b = c // 2
        rows = np.concatenate([np.arange(bb * 128, bb * 128 + 128)
                               for bb in st["perms"][c]])
        xp = np.ascontiguousarray(x[b][rows])
        cgat = np.empty((3, 128, TPC * 5), np.float32)
        for w in range(3):
            cgat[w] = cps[w][st["cps_idx"][c][w]].reshape(128, TPC * 5)
        in_maps.append(dict(
            x=xp, wqb=wq, wkb=wk, wvb=wv, wob=wo,
            btab=st["btabs"][c], rpt=st["rpts"][c], maskt=st["masks"][c],
            bqt=bq, bkt=bk, bvt=bv, bot=bo,
            cpst=cgat, cppt=st["cpps"][c], fplt=st["fplans"][c],
            iota=st["iota"], ident=st["ident"], misc=st["misc"],
        ))

    res = run_bass_kernel_spmd(nc, in_maps, core_ids=list(range(NCORES)))
    out = np.empty((BATCH, S, D), np.float32)
    for c in range(NCORES):
        b = c // 2
        rows = np.concatenate([np.arange(bb * 128, bb * 128 + 128)
                               for bb in st["perms"][c]])[:TOK]
        out[b][rows] = res.results[c]["y"]
    return out


# version 12
# speedup vs baseline: 1.1160x; 1.0857x over previous
import numpy as np
import concourse.bass as bass
import concourse.mybir as mybir
from concourse.bass_utils import run_bass_kernel_spmd

F32 = mybir.dt.float32
ACT = mybir.ActivationFunctionType
ALU = mybir.AluOpType
AXX = mybir.AxisListType.X

D = 2048
H = 16
BATCH = 4
S = 2048
INTER = 8192
NCORES = 8
TOK = 1024
NCTRL = 130156
NS = 16777216
EPS = float(np.float32(1e-5))
FMIN = np.float32(np.finfo(np.float32).min)
SQRT_HD = float(np.float32(np.sqrt(np.float32(128.0))))

TPC = 128
TILE_EL = 128 * 128
WOFF = TPC * TILE_EL
SHARD = 3 * WOFF

KCS0 = [0, 1, 2, 3, 8, 9, 10, 11]
KCS1 = list(range(16))
MASK_PAIRS = [(0, kc) for kc in KCS0] + [(1, kc) for kc in (4, 5, 6, 7, 12, 13, 14, 15)]
MASK_SLOT = {p: i for i, p in enumerate(MASK_PAIRS)}

_static = {}
DEBUG = False



def _alibi_bias_f32():
    import jax
    import jax.numpy as jnp
    with jax.default_device(jax.devices("cpu")[0]):
        ratio = 2.0 ** (-8.0 / H)
        slopes = ratio ** jnp.arange(1, H + 1, dtype=jnp.float32)
        distances = jnp.arange(1 - S, 1, dtype=jnp.float32)
        bias = slopes[:, None] * distances[None, :]
        return np.asarray(bias), np.asarray(slopes)


def _recon_tiles_for(rank, w):
    tiles = []
    if w < 2:
        for t in range(TPC):
            oc, hcw = t // 2, t % 2
            tiles.append((oc * 128, (2 * rank + hcw) * 128, D))
    else:
        for t in range(TPC):
            icw, hpc = t // 16, t % 16
            tiles.append((hpc * 128, (8 * rank + icw) * 128, INTER))
    return tiles


def _build_static():
    if _static:
        return _static
    t64 = np.linspace(0.0, NCTRL - 1.0, NS, dtype=np.float64)
    i_arr = np.clip(np.floor(t64).astype(np.int64), 0, NCTRL - 2)
    f_arr = (t64 - i_arr).astype(np.float32)
    del t64
    bias_f32, slopes = _alibi_bias_f32()

    perms, btabs, rpts, masks = [], [], [], []
    for c in range(NCORES):
        par = c % 2
        blocks = [2 * k + par for k in range(8)] + [2 * k + 1 - par for k in range(8)]
        perms.append(blocks)
        gl = np.concatenate([np.arange(b * 128, b * 128 + 128) for b in blocks])
        kglob, qglob = gl, gl[:TOK]
        btab = np.empty((128, H * 16), np.float32)
        for h in range(H):
            for kc in range(16):
                btab[:, h * 16 + kc] = bias_f32[h, kglob[kc * 128:(kc + 1) * 128]]
        btabs.append(btab)
        rpt = np.empty((H, 2, 128, 512), np.float32)
        for h in range(H):
            row = (slopes[h] * (np.float32(S - 1) - qglob.astype(np.float32))
                   ).astype(np.float32)
            rpt[h, 0] = np.broadcast_to(row[:512], (128, 512))
            rpt[h, 1] = np.broadcast_to(row[512:], (128, 512))
        rpts.append(rpt)
        mk = np.zeros((len(MASK_PAIRS), 128, 512), np.float32)
        for mi, (tile, kc) in enumerate(MASK_PAIRS):
            kpos = kglob[kc * 128:(kc + 1) * 128]
            qpos = qglob[tile * 512:(tile + 1) * 512]
            mk[mi][kpos[:, None] > qpos[None, :]] = FMIN
        masks.append(mk)

    cps_idx, cpps, fplans = [], [], []
    u128 = np.arange(128)
    for c in range(NCORES):
        ci = np.empty((3, 128, TPC, 5), np.int64)
        cp = np.empty((3, 128, TPC), np.float32)
        fp = np.empty((3, 128, TPC, 128), np.float32)
        for w in range(3):
            for t, (row0, col0, IN) in enumerate(_recon_tiles_for(c, w)):
                j0 = (row0 + u128) * IN + col0
                ib = i_arr[j0]
                iv = i_arr[j0[:, None] + u128[None, :]]
                dlt = iv - ib[:, None]
                assert dlt.min() >= 0 and dlt.max() <= 1
                adv = dlt.argmax(axis=1).astype(np.float32)
                adv[dlt.max(axis=1) == 0] = 128.0
                cp[w, :, t] = adv
                for m in range(5):
                    ci[w, :, t, m] = np.clip(ib + m - 1, 0, NCTRL - 1)
                fp[w, :, t, :] = f_arr[j0[:, None] + u128[None, :]]
        cps_idx.append(ci)
        cpps.append(cp)
        fplans.append(fp.reshape(3, 128, TPC * 128))

    _static.update(dict(
        perms=perms, btabs=btabs, rpts=rpts, masks=masks,
        cps_idx=cps_idx, cpps=cpps, fplans=fplans,
        iota=np.ascontiguousarray(
            np.broadcast_to(np.arange(128, dtype=np.float32), (128, 128))),
        ident=np.eye(128, dtype=np.float32),
        misc=np.concatenate([
            np.ones((128, 129), np.float32),
            np.full((128, 1), np.float32(1e-5), np.float32),
            np.full((128, 1), np.float32(SQRT_HD), np.float32)], axis=1),
    ))
    return _static



class Ctr:
    def __init__(self, nc, name):
        self.h = nc.alloc_semaphore(name)
        self.n = 0

    def bump(self, k):
        self.n += k
        return self.n


def build_nc():
    nc = bass.Bass("TRN2", target_bir_lowering=False, debug=False,
                   detect_race_conditions=False)

    din = lambda n, sh: nc.dram_tensor(n, sh, F32, kind="ExternalInput")
    x_in = din("x", [S, D])
    wqb_in = din("wqb", [16, 16, 128, 128])
    wkb_in = din("wkb", [16, 16, 128, 128])
    wvb_in = din("wvb", [16, 16, 128, 128])
    wob_in = din("wob", [16, 16, 128, 128])
    btab_in = din("btab", [128, H * 16])
    rpt_in = din("rpt", [H, 2, 128, 512])
    mask_in = din("maskt", [len(MASK_PAIRS), 128, 512])
    bq_in = din("bqt", [128, 16])
    bk_in = din("bkt", [128, 16])
    bv_in = din("bvt", [128, 16])
    bo_in = din("bot", [128, 16])
    cp_in = din("cpst", [3, 128, TPC * 5])
    cpp_in = din("cppt", [3, 128, TPC])
    fpl_in = din("fplt", [3, 128, TPC * 128])
    iota_in = din("iota", [128, 128])
    ident_in = din("ident", [128, 128])
    misc_in = din("misc", [128, 131])

    y_out = nc.dram_tensor("y", [TOK, D], F32, kind="ExternalOutput")
    if DEBUG:
        dbg_xno = nc.dram_tensor("dbg_xno", [128, 16 * TOK], F32, kind="ExternalOutput")
        dbg_kt = nc.dram_tensor("dbg_kt", [D, S], F32, kind="ExternalOutput")
        dbg_vt = nc.dram_tensor("dbg_vt", [D, S], F32, kind="ExternalOutput")
        dbg_qt = nc.dram_tensor("dbg_qt", [D, TOK], F32, kind="ExternalOutput")
        dbg_at = nc.dram_tensor("dbg_at", [D, TOK], F32, kind="ExternalOutput")
        dbg_ag = nc.dram_tensor("dbg_ag", [SHARD], F32, kind="ExternalOutput")
        dbg_ago = nc.dram_tensor("dbg_ago", [NCORES, SHARD], F32, kind="ExternalOutput")

    ag_in_gu = nc.dram_tensor("ag_in_gu", [2 * WOFF], F32)
    ag_out_gu = nc.dram_tensor("ag_out_gu", [NCORES, 2 * WOFF], F32, addr_space="Shared")
    ag_in_d = nc.dram_tensor("ag_in_d", [WOFF], F32)
    ag_out_d = nc.dram_tensor("ag_out_d", [NCORES, WOFF], F32, addr_space="Shared")
    kt_sc = nc.dram_tensor("kt_sc", [D, S], F32)
    vt_sc = nc.dram_tensor("vt_sc", [D, S], F32)
    qt_sc = nc.dram_tensor("qt_sc", [D, TOK], F32)
    xno_sc = nc.dram_tensor("xno_sc", [128, 16 * TOK], F32)
    at_sc = nc.dram_tensor("at_sc", [D, TOK], F32)

    BIG = nc.alloc_sbuf_tensor("BIG", [128, 34816], F32)
    SMALL = nc.alloc_sbuf_tensor("SMALL", [128, 15360], F32)
    CON = nc.alloc_sbuf_tensor("CON", [128, 768], F32)
    big = lambda o, n: BIG[:, o:o + n]
    small = lambda o, n: SMALL[:, o:o + n]

    c_ident = CON[:, 0:128]
    c_iota = CON[:, 128:256]
    c_btab = CON[:, 256:512]
    c_bq = CON[:, 512:528]
    c_bk = CON[:, 528:544]
    c_bv = CON[:, 544:560]
    c_bo = CON[:, 560:576]
    c_misc = CON[:, 576:707]
    c_onesr = c_misc[:, 0:128]
    c_ones = c_misc[:, 128:129]
    c_eps = c_misc[:, 129:130]
    c_sqrt = c_misc[:, 130:131]
    c_rsq = CON[:, 707:708]

    p0 = nc.alloc_psum_tensor("p0", [128, 512], F32)
    p1 = nc.alloc_psum_tensor("p1", [128, 512], F32)
    p2 = nc.alloc_psum_tensor("p2", [128, 512], F32)
    p3 = nc.alloc_psum_tensor("p3", [128, 512], F32)
    p4 = nc.alloc_psum_tensor("p4", [128, 512], F32)
    p5 = nc.alloc_psum_tensor("p5", [128, 512], F32)
    p6 = nc.alloc_psum_tensor("p6", [128, 512], F32)
    p7 = nc.alloc_psum_tensor("p7", [128, 512], F32)

    Cm = lambda n: Ctr(nc, n)
    d_con = Cm("d_con")
    d_f = [Cm("d_f0"), Cm("d_f1")]
    d_x = [Cm("d_x0"), Cm("d_x1")]
    d_w = [Cm("d_w0"), Cm("d_w1")]
    d_h = [Cm("d_h0"), Cm("d_h1")]
    d_g = [Cm("d_g0"), Cm("d_g1")]
    d_u = [Cm("d_u0"), Cm("d_u1")]
    d_d = [Cm("d_d0"), Cm("d_d1")]
    d_a = [Cm("d_a0"), Cm("d_a1")]
    d_xr = Cm("d_xr")
    d_xh = Cm("d_xh")
    d_sp = Cm("d_sp")
    d_y = Cm("d_y")
    s_cc = Cm("s_cc")
    d_rc = Cm("d_rc")
    s_x = Cm("s_x")
    s_t = Cm("s_t")
    s_v = Cm("s_v")
    s_s = Cm("s_s")

    with nc.Block() as blk:
        @blk.sync
        def _(e):
            for dst, src in ((c_ident, ident_in), (c_iota, iota_in),
                             (c_btab, btab_in), (c_bq, bq_in), (c_bk, bk_in),
                             (c_bv, bv_in), (c_bo, bo_in), (c_misc, misc_in)):
                e.dma_start(out=dst, in_=src.ap()[:, :]).then_inc(d_con.h, 16)
                d_con.bump(16)

        @blk.vector
        def _(e):
            e.wait_ge(d_con.h, d_con.n)
            e.reciprocal(out=c_rsq, in_=c_sqrt)
    nc.const_aps.register(F32, EPS, c_eps[:, 0:1])

    RTC = 8
    RNCH = TPC // RTC
    NRCH = 3 * RNCH

    xnt = big(0, 32768)
    xnt3 = xnt.rearrange("p (fc t) -> p fc t", fc=16)
    ln_x = [small(0, 2048), small(2048, 2048)]
    ln_t = small(4096, 2048)
    ln_s = small(6144, 2048)
    ln_c = small(8192, 8)

    T0, V0, S0 = s_t.n, s_v.n, s_s.n
    LNV = lambda tc, i: V0 + tc * 10 + i
    LNS = lambda tc, i: S0 + tc * 2 + i
    LNT = lambda tc, g: T0 + tc * 4 + g + 1

    with nc.Block() as blk:
        @blk.sync
        def _(e):
            for tc in range(16):
                if tc >= 2:
                    e.wait_ge(s_t.h, LNT(tc - 2, 3))
                e.dma_start(out=ln_x[tc % 2],
                            in_=x_in.ap()[tc * 128:(tc + 1) * 128, :]
                            ).then_inc(d_x[tc % 2].h, 16)
                d_x[tc % 2].bump(16)

        @blk.vector
        def _(e):
            for tc in range(16):
                e.wait_ge(d_x[tc % 2].h, 16 * (tc // 2 + 1))
                xs = ln_x[tc % 2]
                e.reduce_sum(out=ln_c[:, 0:1], in_=xs, axis=AXX).then_inc(s_v.h, 1)
                e.wait_ge(s_v.h, s_v.bump(1))
                e.tensor_scalar_mul(ln_c[:, 1:2], ln_c[:, 0:1], 1.0 / 2048.0
                                    ).then_inc(s_v.h, 1)
                e.wait_ge(s_v.h, s_v.bump(1))
                e.tensor_scalar(out=ln_t, in0=xs, scalar1=ln_c[:, 1:2],
                                scalar2=None, op0=ALU.subtract).then_inc(s_v.h, 1)
                s_v.bump(1)
                e.wait_ge(s_s.h, LNS(tc, 1))
                e.reduce_sum(out=ln_c[:, 0:1], in_=ln_s, axis=AXX).then_inc(s_v.h, 1)
                s_v.bump(1)
                e.wait_ge(s_s.h, LNS(tc, 2))
                e.reciprocal(out=ln_c[:, 3:4], in_=ln_c[:, 2:3]).then_inc(s_v.h, 1)
                e.wait_ge(s_v.h, s_v.bump(1))
                e.tensor_scalar(out=xs, in0=ln_t, scalar1=ln_c[:, 3:4],
                                scalar2=None, op0=ALU.mult).then_inc(s_v.h, 1)
                s_v.bump(1)
                for g in range(4):
                    e.wait_ge(s_t.h, LNT(tc, g))
                    dst = xnt3[:, g * 4:(g + 1) * 4, tc * 128:(tc + 1) * 128]
                    e.tensor_copy(out=dst,
                                  in_=p6.rearrange("p (j t) -> p j t", j=4)
                                  ).then_inc(s_v.h, 1)
                    s_v.bump(1)

        @blk.scalar
        def _(e):
            for tc in range(16):
                e.wait_ge(s_v.h, LNV(tc, 3))
                e.activation(out=ln_s, in_=ln_t, func=ACT.Square).then_inc(s_s.h, 1)
                s_s.bump(1)
                e.wait_ge(s_v.h, LNV(tc, 4))
                e.activation(out=ln_c[:, 2:3], in_=ln_c[:, 0:1], func=ACT.Sqrt,
                             scale=1.0 / 2048.0, bias=EPS).then_inc(s_s.h, 1)
                s_s.bump(1)

        @blk.tensor
        def _(e):
            for tc in range(16):
                e.wait_ge(s_v.h, LNV(tc, 6))
                for g in range(4):
                    prev = LNV(tc, 6 + g) if g >= 1 else (
                        LNV(tc - 1, 10) if tc >= 1 else None)
                    if prev is not None:
                        e.wait_ge(s_v.h, prev)
                    for j in range(4):
                        fc = g * 4 + j
                        ins = e.transpose(p6[:, j * 128:(j + 1) * 128],
                                          ln_x[tc % 2][:, fc * 128:(fc + 1) * 128],
                                          c_ident)
                    ins.then_inc(s_t.h, 1)
                    s_t.bump(1)

    with nc.Block() as blk:
        @blk.gpsimd
        def _(e):
            e.dma_start(out=xno_sc.ap().rearrange("p (fc t) -> p fc t", fc=16),
                        in_=xnt3[:, :, 0:TOK]).then_inc(d_sp.h, 16)
            d_sp.bump(16)

    w_slab = [small(0, 2048), small(2048, 2048)]
    pr_st = [small(4096, 512), small(4608, 512)]
    r_f = [small(5120, 1024), small(6144, 1024)]
    r_acc = small(7168, 1024)
    r_bac = small(8192, 1024)
    r_dlt = small(9216, 1024)
    r_stage = [small(10240, 1024), small(11264, 1024)]
    t_q = small(12288, 1024)
    t_cps = small(13312, 640)
    t_cpp = small(13952, 128)

    s_rv = Cm("s_rv")
    s_rt = Cm("s_rt")
    d_ag = [Cm("d_ag0"), Cm("d_ag1")]
    RPREP, REVAL, REVA, RT = {}, {}, {}, {}
    _rv = 0
    _rt = 0
    for c in range(NRCH):
        w, t = c // RNCH, c % RNCH
        if t == 0:
            _rv += 1
            RPREP[w] = _rv
        _rv += 1
        REVAL[c] = _rv
        REVA[c] = []
        for g in range(2):
            _rv += 1
            REVA[c].append(_rv)
        RT[c] = []
        for g in range(2):
            _rt += 1
            RT[c].append(_rt)

    def emit_recon_tables(e, wv):
        e.dma_start(out=t_cps, in_=cp_in.ap()[wv, :, :]).then_inc(d_rc.h, 16)
        d_rc.bump(16)
        e.dma_start(out=t_cpp, in_=cpp_in.ap()[wv, :, :]).then_inc(d_rc.h, 16)
        d_rc.bump(16)
        if wv == 0:
            e.dma_start(out=r_f[0], in_=fpl_in.ap()[0, :, 0:RTC * 128]
                        ).then_inc(d_f[0].h, 16)
            d_f[0].bump(16)

    def emit_recon_sync(e, cl):
        for c in cl:
            nx = c + 1
            if nx >= NRCH:
                continue
            w, t = nx // RNCH, nx % RNCH
            if nx >= 2:
                e.wait_ge(s_rv.h, REVAL[nx - 2])
            e.dma_start(out=r_f[nx % 2],
                        in_=fpl_in.ap()[w, :, t * RTC * 128:(t + 1) * RTC * 128]
                        ).then_inc(d_f[nx % 2].h, 16)
            d_f[nx % 2].bump(16)

    def emit_recon_prep(e, w):
        if True:
            if True:
                e.wait_ge(d_rc.h, 32 * (w + 1))
                cv = t_cps.rearrange("p (tt m) -> p tt m", m=5)
                sv = [cv[:, :, m] for m in range(5)]
                q = [t_q[:, k * TPC:(k + 1) * TPC] for k in range(8)]



                tmp = r_acc
                tcol = lambda i: tmp[:, i * TPC:(i + 1) * TPC]
                ops = []

                def plan(dst, a, b, cc, d, base):
                    u = [tcol(base + i) for i in range(4)]
                    ops.append(("c", dst[0], b))
                    ops.append(("s", u[0], cc, a, ALU.subtract))
                    ops.append(("m", dst[1], u[0], 0.5))
                    ops.append(("m", u[1], b, -2.5))
                    ops.append(("s", u[1], u[1], a, ALU.add))
                    ops.append(("m", u[2], cc, 2.0))
                    ops.append(("s", u[1], u[1], u[2], ALU.add))
                    ops.append(("m", u[2], d, -0.5))
                    ops.append(("s", dst[2], u[1], u[2], ALU.add))
                    ops.append(("m", u[3], b, 1.5))
                    ops.append(("m", u[0], a, -0.5))
                    ops.append(("s", u[3], u[3], u[0], ALU.add))
                    ops.append(("m", u[0], cc, -1.5))
                    ops.append(("s", u[3], u[3], u[0], ALU.add))
                    ops.append(("m", u[0], d, 0.5))
                    ops.append(("s", dst[3], u[3], u[0], ALU.add))

                plan([q[0], q[1], q[2], q[3]], sv[0], sv[1], sv[2], sv[3], 0)
                opsA = ops
                ops = []
                plan([q[4], q[5], q[6], q[7]], sv[1], sv[2], sv[3], sv[4], 4)
                opsB = ops
                seq = []
                for i in range(max(len(opsA), len(opsB))):
                    if i < len(opsA):
                        seq.append(opsA[i])
                    if i < len(opsB):
                        seq.append(opsB[i])
                for k in range(4):
                    seq.append(("s", q[4 + k], q[4 + k], q[k], ALU.subtract))
                last = None
                for op in seq:
                    if op[0] == "c":
                        last = e.tensor_copy(out=op[1], in_=op[2])
                    elif op[0] == "m":
                        last = e.tensor_scalar_mul(op[1], op[2], op[3])
                    else:
                        last = e.tensor_tensor(out=op[1], in0=op[2], in1=op[3],
                                               op=op[4])
                last.then_inc(s_rv.h, 1)
                s_rv.bump(1)
    def emit_recon_eval(e, cl):
        for c in cl:
            w, t = c // RNCH, c % RNCH
            sl = c % 2
            t0 = t * RTC
            e.wait_ge(d_f[sl].h, rf_fill_at[c])
            if c >= 1:
                e.wait_ge(s_rt.h, RT[c - 1][1])
            fb = r_f[sl].rearrange("p (tt u) -> p tt u", tt=RTC)
            acc = r_acc.rearrange("p (tt u) -> p tt u", tt=RTC)
            bac = r_bac.rearrange("p (tt u) -> p tt u", tt=RTC)
            dlt = r_dlt.rearrange("p (tt u) -> p tt u", tt=RTC)
            qv = lambda k: t_q[:, k * TPC + t0:k * TPC + t0 + RTC] \
                .unsqueeze(2).broadcast_to([128, RTC, 128])
            cpv = t_cpp[:, t0:t0 + RTC].unsqueeze(2).broadcast_to([128, RTC, 128])
            iov = c_iota.unsqueeze(1).broadcast_to([128, RTC, 128])
            e.tensor_tensor(out=dlt, in0=iov, in1=cpv, op=ALU.is_ge)
            e.tensor_tensor(out=acc, in0=fb, in1=qv(3), op=ALU.mult)
            e.tensor_tensor(out=bac, in0=fb, in1=qv(7), op=ALU.mult)
            e.tensor_tensor(out=acc, in0=acc, in1=qv(2), op=ALU.add)
            e.tensor_tensor(out=bac, in0=bac, in1=qv(6), op=ALU.add)
            e.tensor_tensor(out=acc, in0=acc, in1=fb, op=ALU.mult)
            e.tensor_tensor(out=bac, in0=bac, in1=fb, op=ALU.mult)
            e.tensor_tensor(out=acc, in0=acc, in1=qv(1), op=ALU.add)
            e.tensor_tensor(out=bac, in0=bac, in1=qv(5), op=ALU.add)
            e.tensor_tensor(out=acc, in0=acc, in1=fb, op=ALU.mult)
            e.tensor_tensor(out=bac, in0=bac, in1=fb, op=ALU.mult)
            e.tensor_tensor(out=acc, in0=acc, in1=qv(0), op=ALU.add)
            e.tensor_tensor(out=bac, in0=bac, in1=qv(4), op=ALU.add
                            ).then_inc(s_x.h, 1)
            e.wait_ge(s_x.h, s_x.bump(1))
            e.tensor_tensor(out=bac, in0=bac, in1=dlt, op=ALU.mult
                            ).then_inc(s_x.h, 1)
            e.wait_ge(s_x.h, s_x.bump(1))
            e.tensor_tensor(out=acc, in0=acc, in1=bac, op=ALU.add
                            ).then_inc(s_rv.h, 1)
            s_rv.bump(1)

    def emit_recon_stevac(e, cl):
        for c in cl:
            for g in range(2):
                e.wait_ge(s_rt.h, RT[c][g])
                if g == 0 and c >= 2:
                    e.wait_ge(d_ag[c % 2].h, ag_fill_at[c])
                e.tensor_copy(out=r_stage[c % 2][:, g * 512:(g + 1) * 512],
                              in_=p4[:, :]).then_inc(s_rv.h, 1)
                s_rv.bump(1)


    def emit_recon_tensor(e, cl):
        for c in cl:
            e.wait_ge(s_rv.h, REVAL[c])
            for g in range(2):
                if g == 1:
                    e.wait_ge(s_rv.h, REVA[c][0])
                elif c >= 1:
                    e.wait_ge(s_rv.h, REVA[c - 1][1])
                for j in range(4):
                    ti = g * 4 + j
                    ins = e.transpose(p4[:, j * 128:(j + 1) * 128],
                                      r_acc[:, ti * 128:(ti + 1) * 128], c_ident)
                ins.then_inc(s_rt.h, 1)
                s_rt.bump(1)

    def emit_recon_gpsimd(e, cl):
        for c in cl:
            w, t = c // RNCH, c % RNCH
            e.wait_ge(s_rv.h, REVA[c][1])
            dst = (ag_in_gu.ap()[w * WOFF + t * RTC * TILE_EL:
                                 w * WOFF + (t + 1) * RTC * TILE_EL]
                   if w < 2 else
                   ag_in_d.ap()[t * RTC * TILE_EL:(t + 1) * RTC * TILE_EL])
            e.dma_start(out=dst.rearrange("(tt p f) -> p tt f", p=128, f=128),
                        in_=r_stage[c % 2].rearrange("p (tt f) -> p tt f", tt=RTC),
                        ).then_inc(d_ag[c % 2].h, 16)
            d_ag[c % 2].bump(16)
            if c == NRCH - 1:
                e.wait_ge(d_ag[0].h, d_ag[0].n)
                e.wait_ge(d_ag[1].h, d_ag[1].n)
                e.collective_compute(
                    "AllGather", ALU.bypass,
                    replica_groups=[list(range(NCORES))],
                    ins=[ag_in_gu.ap().opt()],
                    outs=[ag_out_gu.ap().opt()],
                ).then_inc(s_cc.h, 1)
                s_cc.bump(1)
            if c == NRCH - 1:
                e.wait_ge(d_ag[0].h, d_ag[0].n)
                e.wait_ge(d_ag[1].h, d_ag[1].n)
                e.collective_compute(
                    "AllGather", ALU.bypass,
                    replica_groups=[list(range(NCORES))],
                    ins=[ag_in_d.ap().opt()],
                    outs=[ag_out_d.ap().opt()],
                ).then_inc(s_cc.h, 1)
                s_cc.bump(1)

    rf_fill_at = {}
    _f = [0, 0]
    _f[0] = 16
    rf_fill_at[0] = 16
    for nx in range(1, NRCH):
        _f[nx % 2] += 16
        rf_fill_at[nx] = _f[nx % 2]
    ag_fill_at = {}
    _a = [0, 0]
    for c in range(NRCH):
        ag_fill_at[c] = _a[c % 2] if c >= 2 else 0
        _a[c % 2] += 16


    def proj(wten, bias_col, n_tt, dst_sc, div, rhs_fn, rw=None):
        TT0, VV0, SP0 = s_t.n, s_v.n, d_sp.n
        W0 = (d_w[0].n, d_w[1].n)
        PT = lambda idx: TT0 + idx + 1
        PV = lambda idx: VV0 + idx + 1
        PS = lambda idx: SP0 + 16 * (idx + 1)
        rch = lambda dc: ([rw * RNCH + dc] if rw is not None else [])
        with nc.Block() as blk:
            @blk.sync
            def _(e):
                if rw is not None:
                    emit_recon_tables(e, rw)
                for dc in range(16):
                    if dc >= 2:
                        e.wait_ge(s_t.h, PT((dc - 1) * n_tt - 1))
                    e.dma_start(out=w_slab[dc % 2]
                                .rearrange("p (k f) -> p k f", k=16),
                                in_=wten.ap()[dc].rearrange("k p f -> p k f")
                                ).then_inc(d_w[dc % 2].h, 16)
                    d_w[dc % 2].bump(16)
                    emit_recon_sync(e, rch(dc))

            @blk.tensor
            def _(e):
                for dc in range(16):
                    e.wait_ge(d_w[dc % 2].h, W0[dc % 2] + 16 * (dc // 2 + 1))
                    for tt in range(n_tt):
                        idx = dc * n_tt + tt
                        ps = (p0, p1)[idx % 2]
                        if idx >= 2:
                            e.wait_ge(s_v.h, PV(idx - 2))
                        for hc in range(16):
                            ins = e.matmul(ps[:, :],
                                           w_slab[dc % 2][:, hc * 128:(hc + 1) * 128],
                                           rhs_fn(hc, tt),
                                           start=(hc == 0), stop=(hc == 15))
                        ins.then_inc(s_t.h, 1)
                        s_t.bump(1)
                    emit_recon_tensor(e, rch(dc))

            @blk.vector
            def _(e):
                if rw is not None:
                    emit_recon_prep(e, rw)
                for dc in range(16):
                    emit_recon_eval(e, rch(dc))
                    for tt in range(n_tt):
                        idx = dc * n_tt + tt
                        ps = (p0, p1)[idx % 2]
                        e.wait_ge(s_t.h, PT(idx))
                        if idx >= 2:
                            e.wait_ge(d_sp.h, PS(idx - 2))
                        if div:
                            e.tensor_scalar(out=pr_st[idx % 2], in0=ps[:, :],
                                            scalar1=bias_col[:, dc:dc + 1],
                                            scalar2=c_rsq[:, 0:1],
                                            op0=ALU.add, op1=ALU.mult
                                            ).then_inc(s_v.h, 1)
                        else:
                            e.tensor_scalar(out=pr_st[idx % 2], in0=ps[:, :],
                                            scalar1=bias_col[:, dc:dc + 1],
                                            scalar2=None, op0=ALU.add
                                            ).then_inc(s_v.h, 1)
                        s_v.bump(1)
                    emit_recon_stevac(e, rch(dc))

            @blk.gpsimd
            def _(e):
                for dc in range(16):
                    for tt in range(n_tt):
                        idx = dc * n_tt + tt
                        e.wait_ge(s_v.h, PV(idx))
                        e.dma_start(out=dst_sc.ap()[dc * 128:(dc + 1) * 128,
                                                    tt * 512:(tt + 1) * 512],
                                    in_=pr_st[idx % 2]).then_inc(d_sp.h, 16)
                        d_sp.bump(16)
                    emit_recon_gpsimd(e, rch(dc))

    xnt_rhs = lambda hc, tt: xnt3[:, hc, tt * 512:(tt + 1) * 512]
    proj(wkb_in, c_bk, 4, kt_sc, False, xnt_rhs, rw=0)
    proj(wvb_in, c_bv, 4, vt_sc, False, xnt_rhs, rw=1)
    proj(wqb_in, c_bq, 2, qt_sc, True, xnt_rhs, rw=2)

    ctxt = big(0, 16384)
    mks = big(16384, 8192)
    x_exp = [big(24576, 512), big(25088, 512)]
    x_run = big(25600, 512)
    x_run2 = big(26112, 512)
    a_kt = [small(0, 2048), small(2048, 2048)]
    a_vt = [small(4096, 2048), small(6144, 2048)]
    a_qt = [small(8192, 1024), small(9216, 1024)]
    a_pl = [small(10240, 1024), small(11264, 1024)]
    a_vtr = small(12288, 2048)

    with nc.Block() as blk:
        @blk.sync
        def _(e):
            e.dma_start(out=mks.rearrange("p (m f) -> p m f", m=16),
                        in_=mask_in.ap().rearrange("m p f -> p m f")
                        ).then_inc(d_con.h, 16)
            d_con.bump(16)

    mask_slot = {p: i for i, p in enumerate(MASK_PAIRS)}
    ev_ctx_last = {}
    ev_norm = {}
    ev_vtrev = {}
    psc_hist = []

    AT, AV, AS = {}, {}, {}
    tn, vn, sn = s_t.n, s_v.n, s_s.n
    exp_slot_seq = []
    sc_bank_seq = []
    for h in range(H):
        for g in range(4):
            tn += 1
            AT[("vtr", h, g)] = tn
        for tile in range(2):
            kcl = (KCS0, KCS1)[tile]
            n = len(kcl)
            order = [("sc", 0), ("sc", 1)]
            for k in range(n - 2):
                order += [("ctx", k), ("sc", k + 2)]
            order += [("ctx", n - 2), ("ctx", n - 1), ("den",), ("bc",)]
            for op in order:
                tn += 1
                AT[(op[0], h, tile) + tuple(op[1:])] = tn
                if op[0] == "sc":
                    sc_bank_seq.append((h, tile, op[1]))
        for g in range(4):
            vn += 1
            AV[("vtrev", h, g)] = vn
        for tile in range(2):
            kcl = (KCS0, KCS1)[tile]
            n = len(kcl)
            order = [("arg", 0), ("arg", 1)]
            for k in range(n - 2):
                order += [("run", k), ("arg", k + 2)]
            order += [("run", n - 2), ("run", n - 1), ("recip",), ("bcev",),
                      ("norm",)]
            for op in order:
                vn += 1
                AV[(op[0], h, tile) + tuple(op[1:])] = vn
        for tile in range(2):
            n = len((KCS0, KCS1)[tile])
            for ki in range(n):
                sn += 1
                AS[("exp", h, tile, ki)] = sn
                exp_slot_seq.append((h, tile, ki))
    sc_prev2 = {key: (sc_bank_seq[i - 2] if i >= 2 else None)
                for i, key in enumerate(sc_bank_seq)}
    exp_prev2 = {key: (exp_slot_seq[i - 2] if i >= 2 else None)
                 for i, key in enumerate(exp_slot_seq)}
    bcev_order = [(h, t) for h in range(H) for t in range(2)]

    with nc.Block() as blk:
        @blk.tensor
        def _(e):
            for h in range(H):
                sl = h % 2
                e.wait_ge(d_h[sl].h, 64 * (h // 2 + 1))
                if h == 0:
                    e.wait_ge(d_con.h, d_con.n)
                for g in range(4):
                    prev = ("vtrev", h, g - 1) if g >= 1 else (
                        ("vtrev", h - 1, 3) if h >= 1 else None)
                    if prev is not None:
                        e.wait_ge(s_v.h, AV[prev])
                    for j in range(4):
                        kc = g * 4 + j
                        ins = e.transpose(p6[:, j * 128:(j + 1) * 128],
                                          a_vt[sl][:, kc * 128:(kc + 1) * 128],
                                          c_ident)
                    ins.then_inc(s_t.h, 1)
                    s_t.bump(1)
                for tile in range(2):
                    kcl = (KCS0, KCS1)[tile]
                    n = len(kcl)
                    pctx = (p2, p3)[tile]

                    def esc(ki, h=h, tile=tile, sl=sl, kcl=kcl):
                        key = (h, tile, ki)
                        prev = sc_prev2[key]
                        if prev is not None:
                            e.wait_ge(s_s.h, AS[("exp",) + prev])
                        kc = kcl[ki]
                        ps = (p0, p1)[ki % 2]
                        e.matmul(ps[:, :],
                                 a_kt[sl][:, kc * 128:(kc + 1) * 128],
                                 a_qt[sl][:, tile * 512:(tile + 1) * 512],
                                 start=True, stop=True).then_inc(s_t.h, 1)
                        s_t.bump(1)

                    def ectx(ki, h=h, tile=tile, kcl=kcl, n=n, pctx=pctx):
                        if ki == 0:
                            e.wait_ge(s_v.h, AV[("vtrev", h, 3)])
                            if h >= 1:
                                e.wait_ge(s_v.h, AV[("norm", h - 1, tile)])
                        e.wait_ge(s_s.h, AS[("exp", h, tile, ki)])
                        kc = kcl[ki]
                        e.matmul(pctx[:, :], a_vtr[:, kc * 128:(kc + 1) * 128],
                                 x_exp[ki % 2],
                                 start=(ki == 0), stop=(ki == n - 1)
                                 ).then_inc(s_t.h, 1)
                        s_t.bump(1)

                    esc(0)
                    esc(1)
                    for k in range(n - 2):
                        ectx(k)
                        esc(k + 2)
                    ectx(n - 2)
                    ectx(n - 1)
                    e.wait_ge(s_v.h, AV[("run", h, tile, n - 1)])
                    bi = bcev_order.index((h, tile))
                    if bi >= 1:
                        e.wait_ge(s_v.h, AV[("bcev",) + bcev_order[bi - 1]])
                    e.matmul(p7[0:1, :], c_ones, x_run, start=True, stop=False)
                    e.matmul(p7[0:1, :], c_ones, x_run2, start=False, stop=True
                             ).then_inc(s_t.h, 1)
                    s_t.bump(1)
                    e.wait_ge(s_v.h, AV[("recip", h, tile)])
                    e.matmul(p7[:, :], c_onesr[0:1, :], x_run[0:1, :],
                             start=True, stop=True).then_inc(s_t.h, 1)
                    s_t.bump(1)

        @blk.vector
        def _(e):
            for h in range(H):
                sl = h % 2
                for g in range(4):
                    e.wait_ge(s_t.h, AT[("vtr", h, g)])
                    if h >= 1 and g == 0:
                        e.wait_ge(s_t.h, AT[("ctx", h - 1, 1, len(KCS1) - 1)])
                    e.tensor_copy(out=a_vtr[:, g * 512:(g + 1) * 512], in_=p6[:, :]
                                  ).then_inc(s_v.h, 1)
                    s_v.bump(1)
                for tile in range(2):
                    kcl = (KCS0, KCS1)[tile]
                    n = len(kcl)
                    pctx = (p2, p3)[tile]

                    def earg(ki, h=h, tile=tile, sl=sl, kcl=kcl):
                        e.wait_ge(s_t.h, AT[("sc", h, tile, ki)])
                        kc = kcl[ki]
                        ps = (p0, p1)[ki % 2]
                        ins = e.tensor_tensor(
                            out=ps[:, :], in0=ps[:, :],
                            in1=a_pl[sl][:, tile * 512:(tile + 1) * 512],
                            op=ALU.add)
                        mi = MASK_SLOT.get((tile, kc))
                        if mi is not None:
                            ins.then_inc(s_x.h, 1)
                            e.wait_ge(s_x.h, s_x.bump(1))
                            ins = e.tensor_tensor(
                                out=ps[:, :], in0=ps[:, :],
                                in1=mks[:, mi * 512:(mi + 1) * 512], op=ALU.add)
                        ins.then_inc(s_v.h, 1)
                        s_v.bump(1)

                    def erun(ki, h=h, tile=tile, n=n):
                        e.wait_ge(s_s.h, AS[("exp", h, tile, ki)])
                        rbuf = (x_run, x_run2)[ki % 2]
                        if ki < 2:
                            bi = bcev_order.index((h, tile))
                            if bi >= 1:
                                e.wait_ge(s_t.h, AT[("den",) + bcev_order[bi - 1]])
                            e.tensor_copy(out=rbuf, in_=x_exp[ki % 2]
                                          ).then_inc(s_v.h, 1)
                        else:
                            e.tensor_tensor(out=rbuf, in0=rbuf,
                                            in1=x_exp[ki % 2], op=ALU.add
                                            ).then_inc(s_v.h, 1)
                        s_v.bump(1)

                    earg(0)
                    earg(1)
                    for k in range(n - 2):
                        erun(k)
                        earg(k + 2)
                    erun(n - 2)
                    erun(n - 1)
                    e.wait_ge(s_t.h, AT[("den", h, tile)])
                    e.reciprocal(out=x_run[0:1, :], in_=p7[0:1, :]
                                 ).then_inc(s_v.h, 1)
                    s_v.bump(1)
                    e.wait_ge(s_t.h, AT[("bc", h, tile)])
                    e.tensor_copy(out=x_run, in_=p7[:, :]).then_inc(s_x.h, 1)
                    e.wait_ge(s_x.h, s_x.bump(1))
                    e.engine_nop().then_inc(s_v.h, 1)
                    s_v.bump(1)
                    e.wait_ge(s_t.h, AT[("ctx", h, tile, n - 1)])
                    e.scalar_tensor_tensor(
                        out=ctxt[:, h * 1024 + tile * 512:h * 1024 + (tile + 1) * 512],
                        in0=pctx[:, :], scalar=1.0, in1=x_run,
                        op0=ALU.mult, op1=ALU.mult).then_inc(s_v.h, 1)
                    s_v.bump(1)

        @blk.scalar
        def _(e):
            for h in range(H):
                for tile in range(2):
                    n = len((KCS0, KCS1)[tile])
                    for ki in range(n):
                        key = (h, tile, ki)
                        e.wait_ge(s_v.h, AV[("arg", h, tile, ki)])
                        prev = exp_prev2[key]
                        if prev is not None:
                            ph, pt, pk = prev
                            e.wait_ge(s_t.h, AT[("ctx", ph, pt, pk)])
                            e.wait_ge(s_v.h, AV[("run", ph, pt, pk)])
                        ps = (p0, p1)[ki % 2]
                        e.activation(out=x_exp[ki % 2], in_=ps[:, :], func=ACT.Exp,
                                     bias=c_btab[:, h * 16 + (KCS0, KCS1)[tile][ki]:
                                                 h * 16 + (KCS0, KCS1)[tile][ki] + 1],
                                     scale=1.0).then_inc(s_s.h, 1)
                        s_s.bump(1)

        @blk.sync
        def _(e):
            for h in range(H):
                sl = h % 2
                if h >= 2:
                    e.wait_ge(s_t.h, AT[("ctx", h - 2, 1, len(KCS1) - 1)])
                if h == 0:
                    e.wait_ge(d_sp.h, d_sp.n)
                e.dma_start(out=a_kt[sl], in_=kt_sc.ap()[h * 128:(h + 1) * 128, :]
                            ).then_inc(d_h[sl].h, 16)
                d_h[sl].bump(16)
                e.dma_start(out=a_vt[sl], in_=vt_sc.ap()[h * 128:(h + 1) * 128, :]
                            ).then_inc(d_h[sl].h, 16)
                d_h[sl].bump(16)
                e.dma_start(out=a_qt[sl], in_=qt_sc.ap()[h * 128:(h + 1) * 128, :]
                            ).then_inc(d_h[sl].h, 16)
                d_h[sl].bump(16)
                e.dma_start(out=a_pl[sl].rearrange("p (t f) -> p t f", t=2),
                            in_=rpt_in.ap()[h].rearrange("t p f -> p t f")
                            ).then_inc(d_h[sl].h, 16)
                d_h[sl].bump(16)

    ctx_rhs = lambda hc, tt: ctxt[:, hc * 1024 + tt * 512:hc * 1024 + (tt + 1) * 512]
    proj(wob_in, c_bo, 2, at_sc, False, ctx_rhs)

    intT = big(0, 32768)
    intT3 = intT.rearrange("p (ic t) -> p ic t", ic=64)
    f_xnh = small(0, 8192)
    xnh3 = f_xnh.rearrange("p (hc t) -> p hc t", hc=16)
    f_sl = [small(8192, 2048), small(10240, 2048)]
    f_si = [small(12288, 512), small(12800, 512)]
    f_dsl = [small(0, 2048), small(2048, 2048)]
    f_at = [small(4096, 512), small(4608, 512)]
    f_sum = small(5120, 8192)
    f_os = [small(0, 2048), small(2048, 2048)]
    f_xt = small(13312, 2048)

    def gate_slab_ap(w, ic, k):
        base = w * WOFF
        return ag_out_gu.ap()[:, base + ic * 32768:base + (ic + 1) * 32768] \
            .rearrange("r (k p f) -> k r p f", k=2, p=128, f=128)[k] \
            .rearrange("r p f -> p r f")

    def down_slab_ap(hp, pc, r):
        return ag_out_d.ap()[2 * pc + r, :] \
            .rearrange("(icw hq p f) -> icw hq p f",
                       icw=8, hq=16, p=128, f=128)[:, hp, :, :] \
            .rearrange("icw p f -> p icw f")

    def slg(t, k):
        return t.rearrange("p (r kk f) -> p kk r f", kk=2, f=128)[:, k, :, :]
    def sld(t, r):
        return t.rearrange("p (r icw f) -> p r icw f", r=2, icw=8, f=128)[:, r, :, :]

    for th in range(2):
        TT0, VV0, SS0 = s_t.n, s_v.n, s_s.n
        G0, U0 = d_g[0].n, d_g[1].n
        FTg = lambda ic: TT0 + 2 * ic + 1
        FTu = lambda ic: TT0 + 2 * ic + 2
        FV = lambda ic: VV0 + ic + 1
        FS = lambda ic: SS0 + ic + 1
        with nc.Block() as blk:
            @blk.sync
            def _(e, th=th):
                e.dma_start(out=xnh3,
                            in_=xno_sc.ap().rearrange("p (hc t) -> p hc t", hc=16)
                            [:, :, th * 512:(th + 1) * 512]).then_inc(d_xh.h, 16)
                d_xh.bump(16)

            @blk.gpsimd
            def _(e, th=th):
                if th == 0:
                    e.wait_ge(s_cc.h, 1)
                for ic in range(64):
                    if ic >= 1:
                        e.wait_ge(s_t.h, FTg(ic - 1))
                    for kkk in range(2):
                        e.dma_start(out=slg(f_sl[0], kkk),
                                    in_=gate_slab_ap(0, ic, kkk)
                                    ).then_inc(d_g[0].h, 16)
                        d_g[0].bump(16)
                    if ic >= 1:
                        e.wait_ge(s_t.h, FTu(ic - 1))
                    for kkk in range(2):
                        e.dma_start(out=slg(f_sl[1], kkk),
                                    in_=gate_slab_ap(1, ic, kkk)
                                    ).then_inc(d_g[1].h, 16)
                        d_g[1].bump(16)

            @blk.tensor
            def _(e, th=th):
                e.wait_ge(d_xh.h, d_xh.n)
                for ic in range(64):
                    pg = (p0, p1)[ic % 2]
                    pu = (p2, p3)[ic % 2]
                    if ic >= 2:
                        e.wait_ge(s_s.h, FS(ic - 2))
                    e.wait_ge(d_g[0].h, G0 + 32 * (ic + 1))
                    for hc in range(16):
                        ins = e.matmul(pg[:, :], f_sl[0][:, hc * 128:(hc + 1) * 128],
                                       xnh3[:, hc, :], start=(hc == 0),
                                       stop=(hc == 15))
                    ins.then_inc(s_t.h, 1)
                    s_t.bump(1)
                    if ic >= 2:
                        e.wait_ge(s_v.h, FV(ic - 2))
                    e.wait_ge(d_g[1].h, U0 + 32 * (ic + 1))
                    for hc in range(16):
                        ins = e.matmul(pu[:, :], f_sl[1][:, hc * 128:(hc + 1) * 128],
                                       xnh3[:, hc, :], start=(hc == 0),
                                       stop=(hc == 15))
                    ins.then_inc(s_t.h, 1)
                    s_t.bump(1)

            @blk.scalar
            def _(e):
                for ic in range(64):
                    e.wait_ge(s_t.h, FTg(ic))
                    if ic >= 2:
                        e.wait_ge(s_v.h, FV(ic - 2))
                    pg = (p0, p1)[ic % 2]
                    e.activation(out=f_si[ic % 2], in_=pg[:, :], func=ACT.Silu
                                 ).then_inc(s_s.h, 1)
                    s_s.bump(1)

            @blk.vector
            def _(e):
                for ic in range(64):
                    e.wait_ge(s_s.h, FS(ic))
                    e.wait_ge(s_t.h, FTu(ic))
                    pu = (p2, p3)[ic % 2]
                    e.tensor_tensor(out=intT3[:, ic, :], in0=f_si[ic % 2],
                                    in1=pu[:, :], op=ALU.mult).then_inc(s_v.h, 1)
                    s_v.bump(1)

        TT1, VV1 = s_t.n, s_v.n
        A0 = (d_a[0].n, d_a[1].n)
        D0 = (d_d[0].n, d_d[1].n)
        DT = lambda hp, pc: TT1 + hp * 4 + pc + 1
        DV = lambda hp: VV1 + hp + 1
        with nc.Block() as blk:
            @blk.sync
            def _(e, th=th):
                for hp in range(16):
                    if hp >= 2:
                        e.wait_ge(s_v.h, DV(hp - 2))
                    e.dma_start(out=f_at[hp % 2],
                                in_=at_sc.ap()[hp * 128:(hp + 1) * 128,
                                               th * 512:(th + 1) * 512]
                                ).then_inc(d_a[hp % 2].h, 16)
                    d_a[hp % 2].bump(16)

            @blk.gpsimd
            def _(e, th=th):
                if th == 0:
                    e.wait_ge(s_cc.h, 2)
                for hp in range(16):
                    for pc in range(4):
                        gi = hp * 4 + pc
                        if gi >= 2:
                            e.wait_ge(s_t.h, DT((gi - 2) // 4, (gi - 2) % 4))
                        for rrr in range(2):
                            e.dma_start(out=sld(f_dsl[gi % 2], rrr),
                                        in_=down_slab_ap(hp, pc, rrr)
                                        ).then_inc(d_d[gi % 2].h, 16)
                            d_d[gi % 2].bump(16)

            @blk.tensor
            def _(e, th=th):
                for hp in range(16):
                    pd = (p4, p5)[hp % 2]
                    if hp >= 2:
                        e.wait_ge(s_v.h, DV(hp - 2))
                    for pc in range(4):
                        gi = hp * 4 + pc
                        e.wait_ge(d_d[gi % 2].h, D0[gi % 2] + 32 * (gi // 2 + 1))
                        for kk in range(16):
                            ic = pc * 16 + kk
                            ins = e.matmul(pd[:, :],
                                           f_dsl[gi % 2][:, kk * 128:(kk + 1) * 128],
                                           intT3[:, ic, :],
                                           start=(pc == 0 and kk == 0),
                                           stop=(pc == 3 and kk == 15))
                        ins.then_inc(s_t.h, 1)
                        s_t.bump(1)

            @blk.vector
            def _(e, th=th):
                for hp in range(16):
                    e.wait_ge(s_t.h, DT(hp, 3))
                    e.wait_ge(d_a[hp % 2].h, A0[hp % 2] + 16 * (hp // 2 + 1))
                    pd = (p4, p5)[hp % 2]
                    e.scalar_tensor_tensor(
                        out=f_sum[:, hp * 512:(hp + 1) * 512],
                        in0=pd[:, :], scalar=1.0, in1=f_at[hp % 2],
                        op0=ALU.mult, op1=ALU.add).then_inc(s_v.h, 1)
                    s_v.bump(1)

        TT2, VV2, XR0, DY0 = s_t.n, s_v.n, d_xr.n, d_y.n
        GT = lambda tc, g: TT2 + tc * 4 + g + 1
        GV = lambda tc, i: VV2 + tc * 5 + i + 1
        with nc.Block() as blk:
            @blk.sync
            def _(e, th=th):
                for tc in range(4):
                    if tc >= 1:
                        e.wait_ge(s_v.h, GV(tc - 1, 4))
                    e.dma_start(out=f_xt,
                                in_=x_in.ap()[(th * 4 + tc) * 128:
                                              (th * 4 + tc + 1) * 128, :]
                                ).then_inc(d_xr.h, 16)
                    d_xr.bump(16)

            @blk.tensor
            def _(e, th=th):
                for tc in range(4):
                    for g in range(4):
                        prev = (tc, g - 1) if g >= 1 else (
                            (tc - 1, 3) if tc >= 1 else None)
                        if prev is not None:
                            e.wait_ge(s_v.h, GV(prev[0], prev[1]))
                        for j in range(4):
                            hp = g * 4 + j
                            ins = e.transpose(p6[:, j * 128:(j + 1) * 128],
                                              f_sum[:, hp * 512 + tc * 128:
                                                    hp * 512 + (tc + 1) * 128],
                                              c_ident)
                        ins.then_inc(s_t.h, 1)
                        s_t.bump(1)

            @blk.vector
            def _(e, th=th):
                for tc in range(4):
                    for g in range(4):
                        e.wait_ge(s_t.h, GT(tc, g))
                        if g == 0:
                            if tc >= 2:
                                e.wait_ge(d_y.h, DY0 + 16 * (tc - 1))
                            elif th > 0:
                                e.wait_ge(d_y.h, DY0)
                        e.tensor_copy(out=f_os[tc % 2][:, g * 512:(g + 1) * 512],
                                      in_=p6[:, :]).then_inc(s_v.h, 1)
                        s_v.bump(1)
                    e.wait_ge(d_xr.h, XR0 + 16 * (tc + 1))
                    e.wait_ge(s_v.h, s_v.n)
                    e.tensor_tensor(out=f_os[tc % 2], in0=f_os[tc % 2],
                                    in1=f_xt, op=ALU.add).then_inc(s_v.h, 1)
                    s_v.bump(1)

            @blk.gpsimd
            def _(e, th=th):
                for tc in range(4):
                    e.wait_ge(s_v.h, GV(tc, 4))
                    e.dma_start(out=y_out.ap()[(th * 4 + tc) * 128:
                                               (th * 4 + tc + 1) * 128, :],
                                in_=f_os[tc % 2]).then_inc(d_y.h, 16)
                    d_y.bump(16)

    with nc.Block() as blk:
        @blk.gpsimd
        def _(e):
            e.wait_ge(d_y.h, d_y.n)
            if DEBUG:
                for dst, srcp in ((dbg_xno, xno_sc), (dbg_kt, kt_sc),
                                  (dbg_vt, vt_sc), (dbg_qt, qt_sc),
                                  (dbg_at, at_sc)):
                    e.dma_start(out=dst.ap()[:], in_=srcp.ap()[:]
                                ).then_inc(d_y.h, 16)
                    d_y.bump(16)
                e.dma_start(out=dbg_ag.ap()[0:2 * WOFF], in_=ag_in_gu.ap()[:]
                            ).then_inc(d_y.h, 16)
                d_y.bump(16)
                e.dma_start(out=dbg_ag.ap()[2 * WOFF:], in_=ag_in_d.ap()[:]
                            ).then_inc(d_y.h, 16)
                d_y.bump(16)
                e.dma_start(out=dbg_ago.ap()[:, 0:2 * WOFF], in_=ag_out_gu.ap()[:]
                            ).then_inc(d_y.h, 16)
                d_y.bump(16)
                e.dma_start(out=dbg_ago.ap()[:, 2 * WOFF:], in_=ag_out_d.ap()[:]
                            ).then_inc(d_y.h, 16)
                d_y.bump(16)
                e.wait_ge(d_y.h, d_y.n)

    nc.finalize()
    return nc



def _block4(w):
    hh, dd = w.shape
    return np.ascontiguousarray(
        w.reshape(hh // 128, 128, dd // 128, 128).transpose(2, 0, 1, 3))


def kernel(L0_attn_x, L0_attn_Wq, L0_attn_bq, L0_attn_Wk, L0_attn_bk,
           L0_attn_Wv, L0_attn_bv, L0_attn_Wo, L0_attn_bo,
           L0_ffn_gate_cp, L0_ffn_up_cp, L0_ffn_down_cp):
    st = _build_static()
    x = np.asarray(L0_attn_x, dtype=np.float32)
    wq = _block4(np.asarray(L0_attn_Wq, np.float32))
    wk = _block4(np.asarray(L0_attn_Wk, np.float32))
    wv = _block4(np.asarray(L0_attn_Wv, np.float32))
    wo = _block4(np.asarray(L0_attn_Wo, np.float32))
    bq = np.ascontiguousarray(np.asarray(L0_attn_bq, np.float32).reshape(16, 128).T)
    bk = np.ascontiguousarray(np.asarray(L0_attn_bk, np.float32).reshape(16, 128).T)
    bv = np.ascontiguousarray(np.asarray(L0_attn_bv, np.float32).reshape(16, 128).T)
    bo = np.ascontiguousarray(np.asarray(L0_attn_bo, np.float32).reshape(16, 128).T)
    cps = [np.asarray(L0_ffn_gate_cp, np.float32),
           np.asarray(L0_ffn_up_cp, np.float32),
           np.asarray(L0_ffn_down_cp, np.float32)]

    nc = build_nc()
    in_maps = []
    for c in range(NCORES):
        b = c // 2
        rows = np.concatenate([np.arange(bb * 128, bb * 128 + 128)
                               for bb in st["perms"][c]])
        xp = np.ascontiguousarray(x[b][rows])
        cgat = np.empty((3, 128, TPC * 5), np.float32)
        for w in range(3):
            cgat[w] = cps[w][st["cps_idx"][c][w]].reshape(128, TPC * 5)
        in_maps.append(dict(
            x=xp, wqb=wq, wkb=wk, wvb=wv, wob=wo,
            btab=st["btabs"][c], rpt=st["rpts"][c], maskt=st["masks"][c],
            bqt=bq, bkt=bk, bvt=bv, bot=bo,
            cpst=cgat, cppt=st["cpps"][c], fplt=st["fplans"][c],
            iota=st["iota"], ident=st["ident"], misc=st["misc"],
        ))

    res = run_bass_kernel_spmd(nc, in_maps, core_ids=list(range(NCORES)))
    out = np.empty((BATCH, S, D), np.float32)
    for c in range(NCORES):
        b = c // 2
        rows = np.concatenate([np.arange(bb * 128, bb * 128 + 128)
                               for bb in st["perms"][c]])[:TOK]
        out[b][rows] = res.results[c]["y"]
    return out


# version 14
# speedup vs baseline: 1.2705x; 1.1384x over previous
import numpy as np
import concourse.bass as bass
import concourse.mybir as mybir
from concourse.bass_utils import run_bass_kernel_spmd

F32 = mybir.dt.float32
ACT = mybir.ActivationFunctionType
ALU = mybir.AluOpType
AXX = mybir.AxisListType.X

D = 2048
H = 16
BATCH = 4
S = 2048
INTER = 8192
NCORES = 8
TOK = 1024
NCTRL = 130156
NS = 16777216
EPS = float(np.float32(1e-5))
FMIN = np.float32(np.finfo(np.float32).min)
SQRT_HD = float(np.float32(np.sqrt(np.float32(128.0))))

TPC = 128
TILE_EL = 128 * 128
WOFF = TPC * TILE_EL
SHARD = 3 * WOFF

KCS0 = [0, 1, 2, 3, 8, 9, 10, 11]
KCS1 = list(range(16))
MASK_PAIRS = [(0, kc) for kc in KCS0] + [(1, kc) for kc in (4, 5, 6, 7, 12, 13, 14, 15)]
MASK_SLOT = {p: i for i, p in enumerate(MASK_PAIRS)}

_static = {}
DEBUG = False



def _alibi_bias_f32():
    import jax
    import jax.numpy as jnp
    with jax.default_device(jax.devices("cpu")[0]):
        ratio = 2.0 ** (-8.0 / H)
        slopes = ratio ** jnp.arange(1, H + 1, dtype=jnp.float32)
        distances = jnp.arange(1 - S, 1, dtype=jnp.float32)
        bias = slopes[:, None] * distances[None, :]
        return np.asarray(bias), np.asarray(slopes)


def _recon_tiles_for(rank, w):
    tiles = []
    if w < 2:
        for t in range(TPC):
            oc, hcw = t // 2, t % 2
            tiles.append((oc * 128, (2 * rank + hcw) * 128, D))
    else:
        for t in range(TPC):
            icw, hpc = t // 16, t % 16
            tiles.append((hpc * 128, (8 * rank + icw) * 128, INTER))
    return tiles


def _build_static():
    if _static:
        return _static
    t64 = np.linspace(0.0, NCTRL - 1.0, NS, dtype=np.float64)
    i_arr = np.clip(np.floor(t64).astype(np.int64), 0, NCTRL - 2)
    f_arr = (t64 - i_arr).astype(np.float32)
    del t64
    bias_f32, slopes = _alibi_bias_f32()

    perms, btabs, rpts, masks = [], [], [], []
    for c in range(NCORES):
        par = c % 2
        blocks = [2 * k + par for k in range(8)] + [2 * k + 1 - par for k in range(8)]
        perms.append(blocks)
        gl = np.concatenate([np.arange(b * 128, b * 128 + 128) for b in blocks])
        kglob, qglob = gl, gl[:TOK]
        btab = np.empty((128, H * 16), np.float32)
        for h in range(H):
            for kc in range(16):
                btab[:, h * 16 + kc] = bias_f32[h, kglob[kc * 128:(kc + 1) * 128]]
        btabs.append(btab)
        rpt = np.empty((H, 2, 128, 512), np.float32)
        for h in range(H):
            row = (slopes[h] * (np.float32(S - 1) - qglob.astype(np.float32))
                   ).astype(np.float32)
            rpt[h, 0] = np.broadcast_to(row[:512], (128, 512))
            rpt[h, 1] = np.broadcast_to(row[512:], (128, 512))
        rpts.append(rpt)
        mk = np.zeros((len(MASK_PAIRS), 128, 512), np.float32)
        for mi, (tile, kc) in enumerate(MASK_PAIRS):
            kpos = kglob[kc * 128:(kc + 1) * 128]
            qpos = qglob[tile * 512:(tile + 1) * 512]
            mk[mi][kpos[:, None] > qpos[None, :]] = FMIN
        masks.append(mk)

    cps_idx, cpps, fplans = [], [], []
    u128 = np.arange(128)
    for c in range(NCORES):
        ci = np.empty((3, 128, TPC, 5), np.int64)
        cp = np.empty((3, 128, TPC), np.float32)
        fp = np.empty((3, 128, TPC, 128), np.float32)
        for w in range(3):
            for t, (row0, col0, IN) in enumerate(_recon_tiles_for(c, w)):
                j0 = (row0 + u128) * IN + col0
                ib = i_arr[j0]
                iv = i_arr[j0[:, None] + u128[None, :]]
                dlt = iv - ib[:, None]
                assert dlt.min() >= 0 and dlt.max() <= 1
                adv = dlt.argmax(axis=1).astype(np.float32)
                adv[dlt.max(axis=1) == 0] = 128.0
                cp[w, :, t] = adv
                for m in range(5):
                    ci[w, :, t, m] = np.clip(ib + m - 1, 0, NCTRL - 1)
                fp[w, :, t, :] = f_arr[j0[:, None] + u128[None, :]]
        cps_idx.append(ci)
        cpps.append(cp)
        fplans.append(fp.reshape(3, 128, TPC * 128))

    _static.update(dict(
        perms=perms, btabs=btabs, rpts=rpts, masks=masks,
        cps_idx=cps_idx, cpps=cpps, fplans=fplans,
        iota=np.ascontiguousarray(
            np.broadcast_to(np.arange(128, dtype=np.float32), (128, 128))),
        ident=np.eye(128, dtype=np.float32),
        misc=np.concatenate([
            np.ones((128, 129), np.float32),
            np.full((128, 1), np.float32(1e-5), np.float32),
            np.full((128, 1), np.float32(SQRT_HD), np.float32)], axis=1),
    ))
    return _static



class Ctr:
    def __init__(self, nc, name):
        self.h = nc.alloc_semaphore(name)
        self.n = 0

    def bump(self, k):
        self.n += k
        return self.n


def build_nc():
    nc = bass.Bass("TRN2", target_bir_lowering=False, debug=False,
                   detect_race_conditions=False)

    din = lambda n, sh: nc.dram_tensor(n, sh, F32, kind="ExternalInput")
    x_in = din("x", [S, D])
    wqb_in = din("wqb", [16, 16, 128, 128])
    wkb_in = din("wkb", [16, 16, 128, 128])
    wvb_in = din("wvb", [16, 16, 128, 128])
    wob_in = din("wob", [16, 16, 128, 128])
    btab_in = din("btab", [128, H * 16])
    rpt_in = din("rpt", [H, 2, 128, 512])
    mask_in = din("maskt", [len(MASK_PAIRS), 128, 512])
    bq_in = din("bqt", [128, 16])
    bk_in = din("bkt", [128, 16])
    bv_in = din("bvt", [128, 16])
    bo_in = din("bot", [128, 16])
    cp_in = din("cpst", [3, 128, TPC * 5])
    cpp_in = din("cppt", [3, 128, TPC])
    fpl_in = din("fplt", [3, 128, TPC * 128])
    iota_in = din("iota", [128, 128])
    ident_in = din("ident", [128, 128])
    misc_in = din("misc", [128, 131])

    y_out = nc.dram_tensor("y", [TOK, D], F32, kind="ExternalOutput")
    if DEBUG:
        dbg_xno = nc.dram_tensor("dbg_xno", [128, 16 * TOK], F32, kind="ExternalOutput")
        dbg_kt = nc.dram_tensor("dbg_kt", [D, S], F32, kind="ExternalOutput")
        dbg_vt = nc.dram_tensor("dbg_vt", [D, S], F32, kind="ExternalOutput")
        dbg_qt = nc.dram_tensor("dbg_qt", [D, TOK], F32, kind="ExternalOutput")
        dbg_at = nc.dram_tensor("dbg_at", [D, TOK], F32, kind="ExternalOutput")
        dbg_ag = nc.dram_tensor("dbg_ag", [SHARD], F32, kind="ExternalOutput")
        dbg_ago = nc.dram_tensor("dbg_ago", [NCORES, SHARD], F32, kind="ExternalOutput")

    BF16 = mybir.dt.bfloat16
    ag_in_gu = nc.dram_tensor("ag_in_gu", [2 * TPC * 2 * TILE_EL], BF16)
    ag_out_gu = nc.dram_tensor("ag_out_gu", [NCORES, 2 * TPC * 2 * TILE_EL], BF16,
                               addr_space="Shared")
    ag_in_d = nc.dram_tensor("ag_in_d", [TPC * 2 * TILE_EL], BF16)
    ag_out_d = nc.dram_tensor("ag_out_d", [NCORES, TPC * 2 * TILE_EL], BF16,
                              addr_space="Shared")
    kt_sc = nc.dram_tensor("kt_sc", [D, S], F32)
    vt_sc = nc.dram_tensor("vt_sc", [D, S], F32)
    qt_sc = nc.dram_tensor("qt_sc", [D, TOK], F32)
    xno_sc = nc.dram_tensor("xno_sc", [128, 16 * TOK], F32)
    at_sc = nc.dram_tensor("at_sc", [D, TOK], F32)

    BIG = nc.alloc_sbuf_tensor("BIG", [128, 34816], F32)
    SMALL = nc.alloc_sbuf_tensor("SMALL", [128, 15360], F32)
    CON = nc.alloc_sbuf_tensor("CON", [128, 768], F32)
    big = lambda o, n: BIG[:, o:o + n]
    small = lambda o, n: SMALL[:, o:o + n]

    c_ident = CON[:, 0:128]
    c_iota = CON[:, 128:256]
    c_btab = CON[:, 256:512]
    c_bq = CON[:, 512:528]
    c_bk = CON[:, 528:544]
    c_bv = CON[:, 544:560]
    c_bo = CON[:, 560:576]
    c_misc = CON[:, 576:707]
    c_onesr = c_misc[:, 0:128]
    c_ones = c_misc[:, 128:129]
    c_eps = c_misc[:, 129:130]
    c_sqrt = c_misc[:, 130:131]
    c_rsq = CON[:, 707:708]

    p0 = nc.alloc_psum_tensor("p0", [128, 512], F32)
    p1 = nc.alloc_psum_tensor("p1", [128, 512], F32)
    p2 = nc.alloc_psum_tensor("p2", [128, 512], F32)
    p3 = nc.alloc_psum_tensor("p3", [128, 512], F32)
    p4 = nc.alloc_psum_tensor("p4", [128, 512], F32)
    p5 = nc.alloc_psum_tensor("p5", [128, 512], F32)
    p6 = nc.alloc_psum_tensor("p6", [128, 512], F32)
    p7 = nc.alloc_psum_tensor("p7", [128, 512], F32)

    Cm = lambda n: Ctr(nc, n)
    d_con = Cm("d_con")
    d_f = [Cm("d_f0"), Cm("d_f1")]
    d_x = [Cm("d_x0"), Cm("d_x1")]
    d_w = [Cm("d_w0"), Cm("d_w1")]
    d_h = [Cm("d_h0"), Cm("d_h1")]
    d_g = [Cm("d_g0"), Cm("d_g1")]
    d_u = [Cm("d_u0"), Cm("d_u1")]
    d_d = [Cm("d_d0"), Cm("d_d1")]
    d_a = [Cm("d_a0"), Cm("d_a1")]
    d_xr = Cm("d_xr")
    d_xh = Cm("d_xh")
    d_sp = Cm("d_sp")
    d_y = Cm("d_y")
    s_cc = Cm("s_cc")
    d_rc = Cm("d_rc")
    s_x = Cm("s_x")
    s_t = Cm("s_t")
    s_v = Cm("s_v")
    s_s = Cm("s_s")

    with nc.Block() as blk:
        @blk.sync
        def _(e):
            for dst, src in ((c_ident, ident_in), (c_iota, iota_in),
                             (c_btab, btab_in), (c_bq, bq_in), (c_bk, bk_in),
                             (c_bv, bv_in), (c_bo, bo_in), (c_misc, misc_in)):
                e.dma_start(out=dst, in_=src.ap()[:, :]).then_inc(d_con.h, 16)
                d_con.bump(16)

        @blk.vector
        def _(e):
            e.wait_ge(d_con.h, d_con.n)
            e.reciprocal(out=c_rsq, in_=c_sqrt)
            e.tensor_copy(out=SMALL[:, 14080:14144].bitcast(mybir.dt.bfloat16),
                          in_=c_ident)
    nc.const_aps.register(F32, EPS, c_eps[:, 0:1])

    RTC = 8
    RNCH = TPC // RTC
    NRCH = 3 * RNCH

    xnt = big(0, 32768)
    xnt3 = xnt.rearrange("p (fc t) -> p fc t", fc=16)
    ln_x = [small(0, 2048), small(2048, 2048)]
    ln_t = small(4096, 2048)
    ln_s = small(6144, 2048)
    ln_c = small(8192, 8)

    T0, V0, S0 = s_t.n, s_v.n, s_s.n
    LNV = lambda tc, i: V0 + tc * 10 + i
    LNS = lambda tc, i: S0 + tc * 2 + i
    LNT = lambda tc, g: T0 + tc * 4 + g + 1

    with nc.Block() as blk:
        @blk.sync
        def _(e):
            for tc in range(16):
                if tc >= 2:
                    e.wait_ge(s_t.h, LNT(tc - 2, 3))
                e.dma_start(out=ln_x[tc % 2],
                            in_=x_in.ap()[tc * 128:(tc + 1) * 128, :]
                            ).then_inc(d_x[tc % 2].h, 16)
                d_x[tc % 2].bump(16)

        @blk.vector
        def _(e):
            for tc in range(16):
                e.wait_ge(d_x[tc % 2].h, 16 * (tc // 2 + 1))
                xs = ln_x[tc % 2]
                e.reduce_sum(out=ln_c[:, 0:1], in_=xs, axis=AXX).then_inc(s_v.h, 1)
                e.wait_ge(s_v.h, s_v.bump(1))
                e.tensor_scalar_mul(ln_c[:, 1:2], ln_c[:, 0:1], 1.0 / 2048.0
                                    ).then_inc(s_v.h, 1)
                e.wait_ge(s_v.h, s_v.bump(1))
                e.tensor_scalar(out=ln_t, in0=xs, scalar1=ln_c[:, 1:2],
                                scalar2=None, op0=ALU.subtract).then_inc(s_v.h, 1)
                s_v.bump(1)
                e.wait_ge(s_s.h, LNS(tc, 1))
                e.reduce_sum(out=ln_c[:, 0:1], in_=ln_s, axis=AXX).then_inc(s_v.h, 1)
                s_v.bump(1)
                e.wait_ge(s_s.h, LNS(tc, 2))
                e.reciprocal(out=ln_c[:, 3:4], in_=ln_c[:, 2:3]).then_inc(s_v.h, 1)
                e.wait_ge(s_v.h, s_v.bump(1))
                e.tensor_scalar(out=xs, in0=ln_t, scalar1=ln_c[:, 3:4],
                                scalar2=None, op0=ALU.mult).then_inc(s_v.h, 1)
                s_v.bump(1)
                for g in range(4):
                    e.wait_ge(s_t.h, LNT(tc, g))
                    dst = xnt3[:, g * 4:(g + 1) * 4, tc * 128:(tc + 1) * 128]
                    e.tensor_copy(out=dst,
                                  in_=p6.rearrange("p (j t) -> p j t", j=4)
                                  ).then_inc(s_v.h, 1)
                    s_v.bump(1)

        @blk.scalar
        def _(e):
            for tc in range(16):
                e.wait_ge(s_v.h, LNV(tc, 3))
                e.activation(out=ln_s, in_=ln_t, func=ACT.Square).then_inc(s_s.h, 1)
                s_s.bump(1)
                e.wait_ge(s_v.h, LNV(tc, 4))
                e.activation(out=ln_c[:, 2:3], in_=ln_c[:, 0:1], func=ACT.Sqrt,
                             scale=1.0 / 2048.0, bias=EPS).then_inc(s_s.h, 1)
                s_s.bump(1)

        @blk.tensor
        def _(e):
            for tc in range(16):
                e.wait_ge(s_v.h, LNV(tc, 6))
                for g in range(4):
                    prev = LNV(tc, 6 + g) if g >= 1 else (
                        LNV(tc - 1, 10) if tc >= 1 else None)
                    if prev is not None:
                        e.wait_ge(s_v.h, prev)
                    for j in range(4):
                        fc = g * 4 + j
                        ins = e.transpose(p6[:, j * 128:(j + 1) * 128],
                                          ln_x[tc % 2][:, fc * 128:(fc + 1) * 128],
                                          c_ident)
                    ins.then_inc(s_t.h, 1)
                    s_t.bump(1)

    with nc.Block() as blk:
        @blk.gpsimd
        def _(e):
            e.dma_start(out=xno_sc.ap().rearrange("p (fc t) -> p fc t", fc=16),
                        in_=xnt3[:, :, 0:TOK]).then_inc(d_sp.h, 16)
            d_sp.bump(16)

    w_slab = [small(0, 2048), small(2048, 2048)]
    pr_st = [small(4096, 512), small(4608, 512)]
    r_f = [small(5120, 1024), small(6144, 1024)]
    r_acc = small(7168, 1024)
    r_bac = small(8192, 1024)
    r_dlt = small(9216, 1024)
    r_stage = [small(10240, 1024), small(11264, 1024)]
    t_q = small(12288, 1024)
    t_cps = small(13312, 640)
    t_cpp = small(13952, 128)

    s_rv = Cm("s_rv")
    s_rt = Cm("s_rt")
    d_ag = [Cm("d_ag0"), Cm("d_ag1")]
    RPREP, REVAL, RCAST, REVA, RT = {}, {}, {}, {}, {}
    _rv = 0
    _rt = 0
    for c in range(NRCH):
        w, t = c // RNCH, c % RNCH
        if t == 0:
            _rv += 1
            RPREP[w] = _rv
        _rv += 1
        REVAL[c] = _rv
        _rv += 2
        RCAST[c] = _rv
        REVA[c] = []
        for g in range(4):
            _rv += 1
            REVA[c].append(_rv)
        RT[c] = []
        for g in range(4):
            _rt += 1
            RT[c].append(_rt)

    def emit_recon_tables(e, wv):
        e.dma_start(out=t_cps, in_=cp_in.ap()[wv, :, :]).then_inc(d_rc.h, 16)
        d_rc.bump(16)
        e.dma_start(out=t_cpp, in_=cpp_in.ap()[wv, :, :]).then_inc(d_rc.h, 16)
        d_rc.bump(16)
        if wv == 0:
            e.dma_start(out=r_f[0], in_=fpl_in.ap()[0, :, 0:RTC * 128]
                        ).then_inc(d_f[0].h, 16)
            d_f[0].bump(16)

    def emit_recon_sync(e, cl):
        for c in cl:
            nx = c + 1
            if nx >= NRCH:
                continue
            w, t = nx // RNCH, nx % RNCH
            if nx >= 2:
                e.wait_ge(s_rv.h, REVAL[nx - 2])
            e.dma_start(out=r_f[nx % 2],
                        in_=fpl_in.ap()[w, :, t * RTC * 128:(t + 1) * RTC * 128]
                        ).then_inc(d_f[nx % 2].h, 16)
            d_f[nx % 2].bump(16)

    def emit_recon_prep(e, w):
        if True:
            if True:
                e.wait_ge(d_rc.h, 32 * (w + 1))
                cv = t_cps.rearrange("p (tt m) -> p tt m", m=5)
                sv = [cv[:, :, m] for m in range(5)]
                q = [t_q[:, k * TPC:(k + 1) * TPC] for k in range(8)]



                tmp = r_acc
                tcol = lambda i: tmp[:, i * TPC:(i + 1) * TPC]
                ops = []

                def plan(dst, a, b, cc, d, base):
                    u = [tcol(base + i) for i in range(4)]
                    ops.append(("c", dst[0], b))
                    ops.append(("s", u[0], cc, a, ALU.subtract))
                    ops.append(("m", dst[1], u[0], 0.5))
                    ops.append(("m", u[1], b, -2.5))
                    ops.append(("s", u[1], u[1], a, ALU.add))
                    ops.append(("m", u[2], cc, 2.0))
                    ops.append(("s", u[1], u[1], u[2], ALU.add))
                    ops.append(("m", u[2], d, -0.5))
                    ops.append(("s", dst[2], u[1], u[2], ALU.add))
                    ops.append(("m", u[3], b, 1.5))
                    ops.append(("m", u[0], a, -0.5))
                    ops.append(("s", u[3], u[3], u[0], ALU.add))
                    ops.append(("m", u[0], cc, -1.5))
                    ops.append(("s", u[3], u[3], u[0], ALU.add))
                    ops.append(("m", u[0], d, 0.5))
                    ops.append(("s", dst[3], u[3], u[0], ALU.add))

                plan([q[0], q[1], q[2], q[3]], sv[0], sv[1], sv[2], sv[3], 0)
                opsA = ops
                ops = []
                plan([q[4], q[5], q[6], q[7]], sv[1], sv[2], sv[3], sv[4], 4)
                opsB = ops
                seq = []
                for i in range(max(len(opsA), len(opsB))):
                    if i < len(opsA):
                        seq.append(opsA[i])
                    if i < len(opsB):
                        seq.append(opsB[i])
                for k in range(4):
                    seq.append(("s", q[4 + k], q[4 + k], q[k], ALU.subtract))
                last = None
                for op in seq:
                    if op[0] == "c":
                        last = e.tensor_copy(out=op[1], in_=op[2])
                    elif op[0] == "m":
                        last = e.tensor_scalar_mul(op[1], op[2], op[3])
                    else:
                        last = e.tensor_tensor(out=op[1], in0=op[2], in1=op[3],
                                               op=op[4])
                last.then_inc(s_rv.h, 1)
                s_rv.bump(1)
    def emit_recon_eval(e, cl):
        for c in cl:
            w, t = c // RNCH, c % RNCH
            sl = c % 2
            t0 = t * RTC
            e.wait_ge(d_f[sl].h, rf_fill_at[c])
            if c >= 1:
                e.wait_ge(s_rt.h, RT[c - 1][1])
            fb = r_f[sl].rearrange("p (tt u) -> p tt u", tt=RTC)
            acc = r_acc.rearrange("p (tt u) -> p tt u", tt=RTC)
            bac = r_bac.rearrange("p (tt u) -> p tt u", tt=RTC)
            dlt = r_dlt.rearrange("p (tt u) -> p tt u", tt=RTC)
            qv = lambda k: t_q[:, k * TPC + t0:k * TPC + t0 + RTC] \
                .unsqueeze(2).broadcast_to([128, RTC, 128])
            cpv = t_cpp[:, t0:t0 + RTC].unsqueeze(2).broadcast_to([128, RTC, 128])
            iov = c_iota.unsqueeze(1).broadcast_to([128, RTC, 128])
            e.tensor_tensor(out=dlt, in0=iov, in1=cpv, op=ALU.is_ge)
            e.tensor_tensor(out=acc, in0=fb, in1=qv(3), op=ALU.mult)
            e.tensor_tensor(out=bac, in0=fb, in1=qv(7), op=ALU.mult)
            e.tensor_tensor(out=acc, in0=acc, in1=qv(2), op=ALU.add)
            e.tensor_tensor(out=bac, in0=bac, in1=qv(6), op=ALU.add)
            e.tensor_tensor(out=acc, in0=acc, in1=fb, op=ALU.mult)
            e.tensor_tensor(out=bac, in0=bac, in1=fb, op=ALU.mult)
            e.tensor_tensor(out=acc, in0=acc, in1=qv(1), op=ALU.add)
            e.tensor_tensor(out=bac, in0=bac, in1=qv(5), op=ALU.add)
            e.tensor_tensor(out=acc, in0=acc, in1=fb, op=ALU.mult)
            e.tensor_tensor(out=bac, in0=bac, in1=fb, op=ALU.mult)
            e.tensor_tensor(out=acc, in0=acc, in1=qv(0), op=ALU.add)
            e.tensor_tensor(out=bac, in0=bac, in1=qv(4), op=ALU.add
                            ).then_inc(s_x.h, 1)
            e.wait_ge(s_x.h, s_x.bump(1))
            e.tensor_tensor(out=bac, in0=bac, in1=dlt, op=ALU.mult
                            ).then_inc(s_x.h, 1)
            e.wait_ge(s_x.h, s_x.bump(1))
            e.tensor_tensor(out=acc, in0=acc, in1=bac, op=ALU.add
                            ).then_inc(s_rv.h, 1)
            e.wait_ge(s_rv.h, s_rv.bump(1))
            rbhi = r_bac.bitcast(BF16)[:, 0:1024]
            rblo = r_dlt.bitcast(BF16)[:, 0:1024]
            e.tensor_copy(out=rbhi, in_=r_acc).then_inc(s_x.h, 1)
            e.wait_ge(s_x.h, s_x.bump(1))
            e.tensor_tensor(out=rblo, in0=r_acc, in1=rbhi, op=ALU.subtract
                            ).then_inc(s_rv.h, 1)
            s_rv.bump(1)
            e.engine_nop().then_inc(s_rv.h, 1)
            s_rv.bump(1)

    def emit_recon_stevac(e, cl):
        for c in cl:
            stg4 = r_stage[c % 2].bitcast(BF16).rearrange(
                "p (tt x f) -> p tt x f", tt=8, x=2, f=128)
            for g in range(4):
                e.wait_ge(s_rt.h, RT[c][g])
                if g == 0 and c >= 2:
                    e.wait_ge(d_ag[c % 2].h, ag_fill_at[c])
                xx, g2 = g // 2, g % 2
                e.tensor_copy(out=stg4[:, g2 * 4:(g2 + 1) * 4, xx, :],
                              in_=p4.bitcast(BF16)[:, 0:512]
                              .rearrange("p (j f) -> p j f", j=4)
                              ).then_inc(s_rv.h, 1)
                s_rv.bump(1)


    def emit_recon_tensor(e, cl):
        for c in cl:
            e.wait_ge(s_rv.h, RCAST[c])
            rbhi = r_bac.bitcast(BF16)[:, 0:1024]
            rblo = r_dlt.bitcast(BF16)[:, 0:1024]
            for g in range(4):
                srcb = (rbhi, rblo)[g // 2]
                if g >= 1:
                    e.wait_ge(s_rv.h, REVA[c][g - 1])
                elif c >= 1:
                    e.wait_ge(s_rv.h, REVA[c - 1][3])
                identb = SMALL[:, 14080:14144].bitcast(BF16)
                p4b = p4.bitcast(BF16)
                for j in range(4):
                    ti = (g % 2) * 4 + j
                    ins = e.transpose(p4b[:, j * 128:(j + 1) * 128],
                                      srcb[:, ti * 128:(ti + 1) * 128], identb)
                ins.then_inc(s_rt.h, 1)
                s_rt.bump(1)

    def emit_recon_gpsimd(e, cl):
        for c in cl:
            w, t = c // RNCH, c % RNCH
            e.wait_ge(s_rv.h, REVA[c][3])
            PTILE = 2 * TILE_EL
            dst = (ag_in_gu.ap()[(w * TPC + t * RTC) * PTILE:
                                 (w * TPC + (t + 1) * RTC) * PTILE]
                   if w < 2 else
                   ag_in_d.ap()[t * RTC * PTILE:(t + 1) * RTC * PTILE])
            stg = r_stage[c % 2].bitcast(BF16)
            e.dma_start(out=dst.rearrange("(tx p f) -> p tx f",
                                          tx=2 * RTC, p=128, f=128),
                        in_=stg.rearrange("p (tx f) -> p tx f", tx=2 * RTC),
                        ).then_inc(d_ag[c % 2].h, 16)
            d_ag[c % 2].bump(16)
            if c == NRCH - 1:
                e.wait_ge(d_ag[0].h, d_ag[0].n)
                e.wait_ge(d_ag[1].h, d_ag[1].n)
                e.collective_compute(
                    "AllGather", ALU.bypass,
                    replica_groups=[list(range(NCORES))],
                    ins=[ag_in_gu.ap().opt()],
                    outs=[ag_out_gu.ap().opt()],
                ).then_inc(s_cc.h, 1)
                s_cc.bump(1)
            if c == NRCH - 1:
                e.wait_ge(d_ag[0].h, d_ag[0].n)
                e.wait_ge(d_ag[1].h, d_ag[1].n)
                e.collective_compute(
                    "AllGather", ALU.bypass,
                    replica_groups=[list(range(NCORES))],
                    ins=[ag_in_d.ap().opt()],
                    outs=[ag_out_d.ap().opt()],
                ).then_inc(s_cc.h, 1)
                s_cc.bump(1)

    rf_fill_at = {}
    _f = [0, 0]
    _f[0] = 16
    rf_fill_at[0] = 16
    for nx in range(1, NRCH):
        _f[nx % 2] += 16
        rf_fill_at[nx] = _f[nx % 2]
    ag_fill_at = {}
    _a = [0, 0]
    for c in range(NRCH):
        ag_fill_at[c] = _a[c % 2] if c >= 2 else 0
        _a[c % 2] += 16


    def proj(wten, bias_col, n_tt, dst_sc, div, rhs_fn, rw=None):
        TT0, VV0, SP0 = s_t.n, s_v.n, d_sp.n
        W0 = (d_w[0].n, d_w[1].n)
        PT = lambda idx: TT0 + idx + 1
        PV = lambda idx: VV0 + idx + 1
        PS = lambda idx: SP0 + 16 * (idx + 1)
        rch = lambda dc: ([rw * RNCH + dc] if rw is not None else [])
        with nc.Block() as blk:
            @blk.sync
            def _(e):
                if rw is not None:
                    emit_recon_tables(e, rw)
                for dc in range(16):
                    if dc >= 2:
                        e.wait_ge(s_t.h, PT((dc - 1) * n_tt - 1))
                    e.dma_start(out=w_slab[dc % 2]
                                .rearrange("p (k f) -> p k f", k=16),
                                in_=wten.ap()[dc].rearrange("k p f -> p k f")
                                ).then_inc(d_w[dc % 2].h, 16)
                    d_w[dc % 2].bump(16)
                    emit_recon_sync(e, rch(dc))

            @blk.tensor
            def _(e):
                for dc in range(16):
                    e.wait_ge(d_w[dc % 2].h, W0[dc % 2] + 16 * (dc // 2 + 1))
                    for tt in range(n_tt):
                        idx = dc * n_tt + tt
                        ps = (p0, p1)[idx % 2]
                        if idx >= 2:
                            e.wait_ge(s_v.h, PV(idx - 2))
                        for hc in range(16):
                            ins = e.matmul(ps[:, :],
                                           w_slab[dc % 2][:, hc * 128:(hc + 1) * 128],
                                           rhs_fn(hc, tt),
                                           start=(hc == 0), stop=(hc == 15))
                        ins.then_inc(s_t.h, 1)
                        s_t.bump(1)
                    emit_recon_tensor(e, rch(dc))

            @blk.vector
            def _(e):
                if rw is not None:
                    emit_recon_prep(e, rw)
                for dc in range(16):
                    emit_recon_eval(e, rch(dc))
                    for tt in range(n_tt):
                        idx = dc * n_tt + tt
                        ps = (p0, p1)[idx % 2]
                        e.wait_ge(s_t.h, PT(idx))
                        if idx >= 2:
                            e.wait_ge(d_sp.h, PS(idx - 2))
                        if div:
                            e.tensor_scalar(out=pr_st[idx % 2], in0=ps[:, :],
                                            scalar1=bias_col[:, dc:dc + 1],
                                            scalar2=c_rsq[:, 0:1],
                                            op0=ALU.add, op1=ALU.mult
                                            ).then_inc(s_v.h, 1)
                        else:
                            e.tensor_scalar(out=pr_st[idx % 2], in0=ps[:, :],
                                            scalar1=bias_col[:, dc:dc + 1],
                                            scalar2=None, op0=ALU.add
                                            ).then_inc(s_v.h, 1)
                        s_v.bump(1)
                    emit_recon_stevac(e, rch(dc))

            @blk.gpsimd
            def _(e):
                for dc in range(16):
                    for tt in range(n_tt):
                        idx = dc * n_tt + tt
                        e.wait_ge(s_v.h, PV(idx))
                        e.dma_start(out=dst_sc.ap()[dc * 128:(dc + 1) * 128,
                                                    tt * 512:(tt + 1) * 512],
                                    in_=pr_st[idx % 2]).then_inc(d_sp.h, 16)
                        d_sp.bump(16)
                    emit_recon_gpsimd(e, rch(dc))

    xnt_rhs = lambda hc, tt: xnt3[:, hc, tt * 512:(tt + 1) * 512]
    proj(wkb_in, c_bk, 4, kt_sc, False, xnt_rhs, rw=0)
    proj(wvb_in, c_bv, 4, vt_sc, False, xnt_rhs, rw=1)
    proj(wqb_in, c_bq, 2, qt_sc, True, xnt_rhs, rw=2)

    ctxt = big(0, 16384)
    mks = big(16384, 8192)
    x_exp = [big(24576, 512), big(25088, 512)]
    x_run = big(25600, 512)
    x_run2 = big(26112, 512)
    a_kt = [small(0, 2048), small(2048, 2048)]
    a_vt = [small(4096, 2048), small(6144, 2048)]
    a_qt = [small(8192, 1024), small(9216, 1024)]
    a_pl = [small(10240, 1024), small(11264, 1024)]
    a_vtr = small(12288, 2048)

    with nc.Block() as blk:
        @blk.sync
        def _(e):
            e.dma_start(out=mks.rearrange("p (m f) -> p m f", m=16),
                        in_=mask_in.ap().rearrange("m p f -> p m f")
                        ).then_inc(d_con.h, 16)
            d_con.bump(16)

    mask_slot = {p: i for i, p in enumerate(MASK_PAIRS)}
    ev_ctx_last = {}
    ev_norm = {}
    ev_vtrev = {}
    psc_hist = []

    AT, AV, AS = {}, {}, {}
    tn, vn, sn = s_t.n, s_v.n, s_s.n
    exp_slot_seq = []
    sc_bank_seq = []
    for h in range(H):
        for g in range(4):
            tn += 1
            AT[("vtr", h, g)] = tn
        for tile in range(2):
            kcl = (KCS0, KCS1)[tile]
            n = len(kcl)
            order = [("sc", 0), ("sc", 1)]
            for k in range(n - 2):
                order += [("ctx", k), ("sc", k + 2)]
            order += [("ctx", n - 2), ("ctx", n - 1), ("den",), ("bc",)]
            for op in order:
                tn += 1
                AT[(op[0], h, tile) + tuple(op[1:])] = tn
                if op[0] == "sc":
                    sc_bank_seq.append((h, tile, op[1]))
        for g in range(4):
            vn += 1
            AV[("vtrev", h, g)] = vn
        for tile in range(2):
            kcl = (KCS0, KCS1)[tile]
            n = len(kcl)
            order = [("arg", 0), ("arg", 1)]
            for k in range(n - 2):
                order += [("run", k), ("arg", k + 2)]
            order += [("run", n - 2), ("run", n - 1), ("recip",), ("bcev",),
                      ("norm",)]
            for op in order:
                vn += 1
                AV[(op[0], h, tile) + tuple(op[1:])] = vn
        for tile in range(2):
            n = len((KCS0, KCS1)[tile])
            for ki in range(n):
                sn += 1
                AS[("exp", h, tile, ki)] = sn
                exp_slot_seq.append((h, tile, ki))
    sc_prev2 = {key: (sc_bank_seq[i - 2] if i >= 2 else None)
                for i, key in enumerate(sc_bank_seq)}
    exp_prev2 = {key: (exp_slot_seq[i - 2] if i >= 2 else None)
                 for i, key in enumerate(exp_slot_seq)}
    bcev_order = [(h, t) for h in range(H) for t in range(2)]

    with nc.Block() as blk:
        @blk.tensor
        def _(e):
            for h in range(H):
                sl = h % 2
                e.wait_ge(d_h[sl].h, 64 * (h // 2 + 1))
                if h == 0:
                    e.wait_ge(d_con.h, d_con.n)
                for g in range(4):
                    prev = ("vtrev", h, g - 1) if g >= 1 else (
                        ("vtrev", h - 1, 3) if h >= 1 else None)
                    if prev is not None:
                        e.wait_ge(s_v.h, AV[prev])
                    for j in range(4):
                        kc = g * 4 + j
                        ins = e.transpose(p6[:, j * 128:(j + 1) * 128],
                                          a_vt[sl][:, kc * 128:(kc + 1) * 128],
                                          c_ident)
                    ins.then_inc(s_t.h, 1)
                    s_t.bump(1)
                for tile in range(2):
                    kcl = (KCS0, KCS1)[tile]
                    n = len(kcl)
                    pctx = (p2, p3)[tile]

                    def esc(ki, h=h, tile=tile, sl=sl, kcl=kcl):
                        key = (h, tile, ki)
                        prev = sc_prev2[key]
                        if prev is not None:
                            e.wait_ge(s_s.h, AS[("exp",) + prev])
                        kc = kcl[ki]
                        ps = (p0, p1)[ki % 2]
                        e.matmul(ps[:, :],
                                 a_kt[sl][:, kc * 128:(kc + 1) * 128],
                                 a_qt[sl][:, tile * 512:(tile + 1) * 512],
                                 start=True, stop=True).then_inc(s_t.h, 1)
                        s_t.bump(1)

                    def ectx(ki, h=h, tile=tile, kcl=kcl, n=n, pctx=pctx):
                        if ki == 0:
                            e.wait_ge(s_v.h, AV[("vtrev", h, 3)])
                            if h >= 1:
                                e.wait_ge(s_v.h, AV[("norm", h - 1, tile)])
                        e.wait_ge(s_s.h, AS[("exp", h, tile, ki)])
                        kc = kcl[ki]
                        e.matmul(pctx[:, :], a_vtr[:, kc * 128:(kc + 1) * 128],
                                 x_exp[ki % 2],
                                 start=(ki == 0), stop=(ki == n - 1)
                                 ).then_inc(s_t.h, 1)
                        s_t.bump(1)

                    esc(0)
                    esc(1)
                    for k in range(n - 2):
                        ectx(k)
                        esc(k + 2)
                    ectx(n - 2)
                    ectx(n - 1)
                    e.wait_ge(s_v.h, AV[("run", h, tile, n - 1)])
                    bi = bcev_order.index((h, tile))
                    if bi >= 1:
                        e.wait_ge(s_v.h, AV[("bcev",) + bcev_order[bi - 1]])
                    e.matmul(p7[0:1, :], c_ones, x_run, start=True, stop=False)
                    e.matmul(p7[0:1, :], c_ones, x_run2, start=False, stop=True
                             ).then_inc(s_t.h, 1)
                    s_t.bump(1)
                    e.wait_ge(s_v.h, AV[("recip", h, tile)])
                    e.matmul(p7[:, :], c_onesr[0:1, :], x_run[0:1, :],
                             start=True, stop=True).then_inc(s_t.h, 1)
                    s_t.bump(1)

        @blk.vector
        def _(e):
            for h in range(H):
                sl = h % 2
                for g in range(4):
                    e.wait_ge(s_t.h, AT[("vtr", h, g)])
                    if h >= 1 and g == 0:
                        e.wait_ge(s_t.h, AT[("ctx", h - 1, 1, len(KCS1) - 1)])
                    e.tensor_copy(out=a_vtr[:, g * 512:(g + 1) * 512], in_=p6[:, :]
                                  ).then_inc(s_v.h, 1)
                    s_v.bump(1)
                for tile in range(2):
                    kcl = (KCS0, KCS1)[tile]
                    n = len(kcl)
                    pctx = (p2, p3)[tile]

                    def earg(ki, h=h, tile=tile, sl=sl, kcl=kcl):
                        e.wait_ge(s_t.h, AT[("sc", h, tile, ki)])
                        kc = kcl[ki]
                        ps = (p0, p1)[ki % 2]
                        ins = e.tensor_tensor(
                            out=ps[:, :], in0=ps[:, :],
                            in1=a_pl[sl][:, tile * 512:(tile + 1) * 512],
                            op=ALU.add)
                        mi = MASK_SLOT.get((tile, kc))
                        if mi is not None:
                            ins.then_inc(s_x.h, 1)
                            e.wait_ge(s_x.h, s_x.bump(1))
                            ins = e.tensor_tensor(
                                out=ps[:, :], in0=ps[:, :],
                                in1=mks[:, mi * 512:(mi + 1) * 512], op=ALU.add)
                        ins.then_inc(s_v.h, 1)
                        s_v.bump(1)

                    def erun(ki, h=h, tile=tile, n=n):
                        e.wait_ge(s_s.h, AS[("exp", h, tile, ki)])
                        rbuf = (x_run, x_run2)[ki % 2]
                        if ki < 2:
                            bi = bcev_order.index((h, tile))
                            if bi >= 1:
                                e.wait_ge(s_t.h, AT[("den",) + bcev_order[bi - 1]])
                            e.tensor_copy(out=rbuf, in_=x_exp[ki % 2]
                                          ).then_inc(s_v.h, 1)
                        else:
                            e.tensor_tensor(out=rbuf, in0=rbuf,
                                            in1=x_exp[ki % 2], op=ALU.add
                                            ).then_inc(s_v.h, 1)
                        s_v.bump(1)

                    earg(0)
                    earg(1)
                    for k in range(n - 2):
                        erun(k)
                        earg(k + 2)
                    erun(n - 2)
                    erun(n - 1)
                    e.wait_ge(s_t.h, AT[("den", h, tile)])
                    e.reciprocal(out=x_run[0:1, :], in_=p7[0:1, :]
                                 ).then_inc(s_v.h, 1)
                    s_v.bump(1)
                    e.wait_ge(s_t.h, AT[("bc", h, tile)])
                    e.tensor_copy(out=x_run, in_=p7[:, :]).then_inc(s_x.h, 1)
                    e.wait_ge(s_x.h, s_x.bump(1))
                    e.engine_nop().then_inc(s_v.h, 1)
                    s_v.bump(1)
                    e.wait_ge(s_t.h, AT[("ctx", h, tile, n - 1)])
                    e.scalar_tensor_tensor(
                        out=ctxt[:, h * 1024 + tile * 512:h * 1024 + (tile + 1) * 512],
                        in0=pctx[:, :], scalar=1.0, in1=x_run,
                        op0=ALU.mult, op1=ALU.mult).then_inc(s_v.h, 1)
                    s_v.bump(1)

        @blk.scalar
        def _(e):
            for h in range(H):
                for tile in range(2):
                    n = len((KCS0, KCS1)[tile])
                    for ki in range(n):
                        key = (h, tile, ki)
                        e.wait_ge(s_v.h, AV[("arg", h, tile, ki)])
                        prev = exp_prev2[key]
                        if prev is not None:
                            ph, pt, pk = prev
                            e.wait_ge(s_t.h, AT[("ctx", ph, pt, pk)])
                            e.wait_ge(s_v.h, AV[("run", ph, pt, pk)])
                        ps = (p0, p1)[ki % 2]
                        e.activation(out=x_exp[ki % 2], in_=ps[:, :], func=ACT.Exp,
                                     bias=c_btab[:, h * 16 + (KCS0, KCS1)[tile][ki]:
                                                 h * 16 + (KCS0, KCS1)[tile][ki] + 1],
                                     scale=1.0).then_inc(s_s.h, 1)
                        s_s.bump(1)

        @blk.sync
        def _(e):
            for h in range(H):
                sl = h % 2
                if h >= 2:
                    e.wait_ge(s_t.h, AT[("ctx", h - 2, 1, len(KCS1) - 1)])
                if h == 0:
                    e.wait_ge(d_sp.h, d_sp.n)
                e.dma_start(out=a_kt[sl], in_=kt_sc.ap()[h * 128:(h + 1) * 128, :]
                            ).then_inc(d_h[sl].h, 16)
                d_h[sl].bump(16)
                e.dma_start(out=a_vt[sl], in_=vt_sc.ap()[h * 128:(h + 1) * 128, :]
                            ).then_inc(d_h[sl].h, 16)
                d_h[sl].bump(16)
                e.dma_start(out=a_qt[sl], in_=qt_sc.ap()[h * 128:(h + 1) * 128, :]
                            ).then_inc(d_h[sl].h, 16)
                d_h[sl].bump(16)
                e.dma_start(out=a_pl[sl].rearrange("p (t f) -> p t f", t=2),
                            in_=rpt_in.ap()[h].rearrange("t p f -> p t f")
                            ).then_inc(d_h[sl].h, 16)
                d_h[sl].bump(16)

    ctx_rhs = lambda hc, tt: ctxt[:, hc * 1024 + tt * 512:hc * 1024 + (tt + 1) * 512]
    proj(wob_in, c_bo, 2, at_sc, False, ctx_rhs)

    intT_hi = BIG[:, 0:16384].bitcast(BF16)
    intT_lo = BIG[:, 16384:32768].bitcast(BF16)
    iT_hi3 = intT_hi.rearrange("p (ic t) -> p ic t", ic=64)
    iT_lo3 = intT_lo.rearrange("p (ic t) -> p ic t", ic=64)
    f_si = [big(32768, 512), big(33280, 512)]
    f_imt = [big(33792, 512), big(34304, 512)]
    f_xq = small(0, 2048)
    xnh_hi = SMALL[:, 2048:6144].bitcast(BF16)
    xnh_lo = SMALL[:, 6144:10240].bitcast(BF16)
    xh3 = xnh_hi.rearrange("p (hc t) -> p hc t", hc=16)
    xl3 = xnh_lo.rearrange("p (hc t) -> p hc t", hc=16)
    f_sl = [SMALL[:, 10240:12288].bitcast(BF16),
            SMALL[:, 12288:14336].bitcast(BF16)]
    f_dsl = [SMALL[:, 0:2048].bitcast(BF16),
              SMALL[:, 2048:4096].bitcast(BF16)]
    f_at = [small(4096, 512), small(4608, 512)]
    f_sum = small(5120, 8192)
    f_os = [small(0, 2048), small(2048, 2048)]
    f_xt = small(13312, 2048)
    PTILE = 2 * TILE_EL

    def gate_slab_ap(w, ic, k, x):
        base = (w * TPC + ic * 2 + k) * PTILE + x * TILE_EL
        return ag_out_gu.ap()[:, base:base + TILE_EL] \
            .rearrange("r (p f) -> p r f", p=128, f=128)

    def down_slab_ap(hp, pc, r, x):
        return ag_out_d.ap()[2 * pc + r, :] \
            .rearrange("(icw hq x p f) -> icw hq x p f",
                       icw=8, hq=16, x=2, p=128, f=128)[:, hp, x] \
            .rearrange("icw p f -> p icw f")

    def slg(t, k, x):
        return t.rearrange("p (hc x f) -> p hc x f", hc=16, x=2)[:, k::2, x, :]

    def sld(t, r, x):
        return t.rearrange("p (r icw x f) -> p r icw x f",
                           r=2, icw=8, x=2, f=128)[:, r, :, x, :]

    for th in range(2):
        TT0, VV0, SS0 = s_t.n, s_v.n, s_s.n
        G0, U0 = d_g[0].n, d_g[1].n
        FTg = lambda ic: TT0 + 2 * ic + 1
        FTu = lambda ic: TT0 + 2 * ic + 2
        FV1 = lambda ic: VV0 + 4 + 2 * ic + 1
        FV2 = lambda ic: VV0 + 4 + 2 * ic + 2
        FS1 = lambda ic: SS0 + 4 + 2 * ic + 1
        FS2 = lambda ic: SS0 + 4 + 2 * ic + 2
        XC_V = VV0 + 4
        XH0 = d_xh.n
        with nc.Block() as blk:
            @blk.sync
            def _(e, th=th):
                for qq in range(4):
                    if qq >= 1:
                        e.wait_ge(s_v.h, VV0 + qq)
                    e.dma_start(out=f_xq.rearrange("p (hc t) -> p hc t", hc=4),
                                in_=xno_sc.ap()
                                .rearrange("p (hc t) -> p hc t", hc=16)
                                [:, qq * 4:(qq + 1) * 4, th * 512:(th + 1) * 512]
                                ).then_inc(d_xh.h, 16)
                    d_xh.bump(16)

            @blk.scalar
            def _(e, th=th):
                for qq in range(4):
                    e.wait_ge(d_xh.h, XH0 + 16 * (qq + 1))
                    if qq >= 2:
                        e.wait_ge(s_v.h, VV0 + qq - 1)
                    e.activation(out=xh3[:, qq * 4:(qq + 1) * 4, :],
                                 in_=f_xq.rearrange("p (hc t) -> p hc t", hc=4),
                                 func=ACT.Copy).then_inc(s_s.h, 1)
                    s_s.bump(1)
                for ic in range(64):
                    e.wait_ge(s_t.h, FTg(ic))
                    if ic >= 2:
                        e.wait_ge(s_v.h, FV1(ic - 2))
                    pg = (p0, p1)[ic % 2]
                    e.activation(out=f_si[ic % 2], in_=pg[:, :], func=ACT.Silu
                                 ).then_inc(s_s.h, 1)
                    s_s.bump(1)
                    e.wait_ge(s_v.h, FV1(ic))
                    e.activation(out=iT_hi3[:, ic, :], in_=f_imt[ic % 2],
                                 func=ACT.Copy).then_inc(s_s.h, 1)
                    s_s.bump(1)

            @blk.vector
            def _(e, th=th):
                for qq in range(4):
                    e.wait_ge(s_s.h, SS0 + qq + 1)
                    e.tensor_tensor(out=xl3[:, qq * 4:(qq + 1) * 4, :],
                                    in0=f_xq.rearrange("p (hc t) -> p hc t", hc=4),
                                    in1=xh3[:, qq * 4:(qq + 1) * 4, :],
                                    op=ALU.subtract).then_inc(s_v.h, 1)
                    s_v.bump(1)
                for ic in range(64):
                    e.wait_ge(s_s.h, FS1(ic))
                    e.wait_ge(s_t.h, FTu(ic))
                    if ic >= 2:
                        e.wait_ge(s_s.h, FS2(ic - 2))
                    pu = (p2, p3)[ic % 2]
                    e.tensor_tensor(out=f_imt[ic % 2], in0=f_si[ic % 2],
                                    in1=pu[:, :], op=ALU.mult).then_inc(s_v.h, 1)
                    s_v.bump(1)
                    e.wait_ge(s_s.h, FS2(ic))
                    e.tensor_tensor(out=iT_lo3[:, ic, :], in0=f_imt[ic % 2],
                                    in1=iT_hi3[:, ic, :], op=ALU.subtract
                                    ).then_inc(s_v.h, 1)
                    s_v.bump(1)

            @blk.gpsimd
            def _(e, th=th):
                if th == 0:
                    e.wait_ge(s_cc.h, 1)
                for ic in range(64):
                    if ic >= 1:
                        e.wait_ge(s_t.h, FTg(ic - 1))
                    for kkk in range(2):
                        for xx in range(2):
                            e.dma_start(out=slg(f_sl[0], kkk, xx),
                                        in_=gate_slab_ap(0, ic, kkk, xx)
                                        ).then_inc(d_g[0].h, 16)
                            d_g[0].bump(16)
                    if ic >= 1:
                        e.wait_ge(s_t.h, FTu(ic - 1))
                    for kkk in range(2):
                        for xx in range(2):
                            e.dma_start(out=slg(f_sl[1], kkk, xx),
                                        in_=gate_slab_ap(1, ic, kkk, xx)
                                        ).then_inc(d_g[1].h, 16)
                            d_g[1].bump(16)

            @blk.tensor
            def _(e, th=th):
                e.wait_ge(s_v.h, XC_V)
                for ic in range(64):
                    pg = (p0, p1)[ic % 2]
                    pu = (p2, p3)[ic % 2]
                    if ic >= 2:
                        e.wait_ge(s_s.h, FS1(ic - 2))
                    e.wait_ge(d_g[0].h, G0 + 64 * (ic + 1))
                    for hc in range(16):
                        wh = f_sl[0][:, hc * 256:hc * 256 + 128]
                        wl = f_sl[0][:, hc * 256 + 128:hc * 256 + 256]
                        e.matmul(pg[:, :], wh, xh3[:, hc, :],
                                 start=(hc == 0), stop=False)
                        e.matmul(pg[:, :], wh, xl3[:, hc, :],
                                 start=False, stop=False)
                        ins = e.matmul(pg[:, :], wl, xh3[:, hc, :],
                                       start=False, stop=(hc == 15))
                    ins.then_inc(s_t.h, 1)
                    s_t.bump(1)
                    if ic >= 2:
                        e.wait_ge(s_v.h, FV1(ic - 2))
                    e.wait_ge(d_g[1].h, U0 + 64 * (ic + 1))
                    for hc in range(16):
                        wh = f_sl[1][:, hc * 256:hc * 256 + 128]
                        wl = f_sl[1][:, hc * 256 + 128:hc * 256 + 256]
                        e.matmul(pu[:, :], wh, xh3[:, hc, :],
                                 start=(hc == 0), stop=False)
                        e.matmul(pu[:, :], wh, xl3[:, hc, :],
                                 start=False, stop=False)
                        ins = e.matmul(pu[:, :], wl, xh3[:, hc, :],
                                       start=False, stop=(hc == 15))
                    ins.then_inc(s_t.h, 1)
                    s_t.bump(1)

        TT1, VV1 = s_t.n, s_v.n
        A0 = (d_a[0].n, d_a[1].n)
        D0 = (d_d[0].n, d_d[1].n)
        DT = lambda hp, pc: TT1 + hp * 4 + pc + 1
        DV = lambda hp: VV1 + hp + 1
        with nc.Block() as blk:
            @blk.sync
            def _(e, th=th):
                for hp in range(16):
                    if hp >= 2:
                        e.wait_ge(s_v.h, DV(hp - 2))
                    e.dma_start(out=f_at[hp % 2],
                                in_=at_sc.ap()[hp * 128:(hp + 1) * 128,
                                               th * 512:(th + 1) * 512]
                                ).then_inc(d_a[hp % 2].h, 16)
                    d_a[hp % 2].bump(16)

            @blk.gpsimd
            def _(e, th=th):
                if th == 0:
                    e.wait_ge(s_cc.h, 2)
                for hp in range(16):
                    for pc in range(4):
                        gi = hp * 4 + pc
                        if gi >= 2:
                            e.wait_ge(s_t.h, DT((gi - 2) // 4, (gi - 2) % 4))
                        for rrr in range(2):
                            for xx in range(2):
                                e.dma_start(out=sld(f_dsl[gi % 2], rrr, xx),
                                            in_=down_slab_ap(hp, pc, rrr, xx)
                                            ).then_inc(d_d[gi % 2].h, 16)
                                d_d[gi % 2].bump(16)

            @blk.tensor
            def _(e, th=th):
                for hp in range(16):
                    pd = (p4, p5)[hp % 2]
                    if hp >= 2:
                        e.wait_ge(s_v.h, DV(hp - 2))
                    for pc in range(4):
                        gi = hp * 4 + pc
                        e.wait_ge(d_d[gi % 2].h, D0[gi % 2] + 64 * (gi // 2 + 1))
                        for kk in range(16):
                            ic = pc * 16 + kk
                            wh = f_dsl[gi % 2][:, kk * 256:kk * 256 + 128]
                            wl = f_dsl[gi % 2][:, kk * 256 + 128:kk * 256 + 256]
                            e.matmul(pd[:, :], wh, iT_hi3[:, ic, :],
                                     start=(pc == 0 and kk == 0), stop=False)
                            e.matmul(pd[:, :], wh, iT_lo3[:, ic, :],
                                     start=False, stop=False)
                            ins = e.matmul(pd[:, :], wl, iT_hi3[:, ic, :],
                                           start=False,
                                           stop=(pc == 3 and kk == 15))
                        ins.then_inc(s_t.h, 1)
                        s_t.bump(1)

            @blk.vector
            def _(e, th=th):
                for hp in range(16):
                    e.wait_ge(s_t.h, DT(hp, 3))
                    e.wait_ge(d_a[hp % 2].h, A0[hp % 2] + 16 * (hp // 2 + 1))
                    pd = (p4, p5)[hp % 2]
                    e.scalar_tensor_tensor(
                        out=f_sum[:, hp * 512:(hp + 1) * 512],
                        in0=pd[:, :], scalar=1.0, in1=f_at[hp % 2],
                        op0=ALU.mult, op1=ALU.add).then_inc(s_v.h, 1)
                    s_v.bump(1)

        TT2, VV2, XR0, DY0 = s_t.n, s_v.n, d_xr.n, d_y.n
        GT = lambda tc, g: TT2 + tc * 4 + g + 1
        GV = lambda tc, i: VV2 + tc * 5 + i + 1
        with nc.Block() as blk:
            @blk.sync
            def _(e, th=th):
                for tc in range(4):
                    if tc >= 1:
                        e.wait_ge(s_v.h, GV(tc - 1, 4))
                    e.dma_start(out=f_xt,
                                in_=x_in.ap()[(th * 4 + tc) * 128:
                                              (th * 4 + tc + 1) * 128, :]
                                ).then_inc(d_xr.h, 16)
                    d_xr.bump(16)

            @blk.tensor
            def _(e, th=th):
                for tc in range(4):
                    for g in range(4):
                        prev = (tc, g - 1) if g >= 1 else (
                            (tc - 1, 3) if tc >= 1 else None)
                        if prev is not None:
                            e.wait_ge(s_v.h, GV(prev[0], prev[1]))
                        for j in range(4):
                            hp = g * 4 + j
                            ins = e.transpose(p6[:, j * 128:(j + 1) * 128],
                                              f_sum[:, hp * 512 + tc * 128:
                                                    hp * 512 + (tc + 1) * 128],
                                              c_ident)
                        ins.then_inc(s_t.h, 1)
                        s_t.bump(1)

            @blk.vector
            def _(e, th=th):
                for tc in range(4):
                    for g in range(4):
                        e.wait_ge(s_t.h, GT(tc, g))
                        if g == 0:
                            if tc >= 2:
                                e.wait_ge(d_y.h, DY0 + 16 * (tc - 1))
                            elif th > 0:
                                e.wait_ge(d_y.h, DY0)
                        e.tensor_copy(out=f_os[tc % 2][:, g * 512:(g + 1) * 512],
                                      in_=p6[:, :]).then_inc(s_v.h, 1)
                        s_v.bump(1)
                    e.wait_ge(d_xr.h, XR0 + 16 * (tc + 1))
                    e.wait_ge(s_v.h, s_v.n)
                    e.tensor_tensor(out=f_os[tc % 2], in0=f_os[tc % 2],
                                    in1=f_xt, op=ALU.add).then_inc(s_v.h, 1)
                    s_v.bump(1)

            @blk.gpsimd
            def _(e, th=th):
                for tc in range(4):
                    e.wait_ge(s_v.h, GV(tc, 4))
                    e.dma_start(out=y_out.ap()[(th * 4 + tc) * 128:
                                               (th * 4 + tc + 1) * 128, :],
                                in_=f_os[tc % 2]).then_inc(d_y.h, 16)
                    d_y.bump(16)

    with nc.Block() as blk:
        @blk.gpsimd
        def _(e):
            e.wait_ge(d_y.h, d_y.n)
            if DEBUG:
                for dst, srcp in ((dbg_xno, xno_sc), (dbg_kt, kt_sc),
                                  (dbg_vt, vt_sc), (dbg_qt, qt_sc),
                                  (dbg_at, at_sc)):
                    e.dma_start(out=dst.ap()[:], in_=srcp.ap()[:]
                                ).then_inc(d_y.h, 16)
                    d_y.bump(16)
                e.dma_start(out=dbg_ag.ap()[0:2 * WOFF], in_=ag_in_gu.ap()[:]
                            ).then_inc(d_y.h, 16)
                d_y.bump(16)
                e.dma_start(out=dbg_ag.ap()[2 * WOFF:], in_=ag_in_d.ap()[:]
                            ).then_inc(d_y.h, 16)
                d_y.bump(16)
                e.dma_start(out=dbg_ago.ap()[:, 0:2 * WOFF], in_=ag_out_gu.ap()[:]
                            ).then_inc(d_y.h, 16)
                d_y.bump(16)
                e.dma_start(out=dbg_ago.ap()[:, 2 * WOFF:], in_=ag_out_d.ap()[:]
                            ).then_inc(d_y.h, 16)
                d_y.bump(16)
                e.wait_ge(d_y.h, d_y.n)

    nc.finalize()
    return nc



def _block4(w):
    hh, dd = w.shape
    return np.ascontiguousarray(
        w.reshape(hh // 128, 128, dd // 128, 128).transpose(2, 0, 1, 3))


def kernel(L0_attn_x, L0_attn_Wq, L0_attn_bq, L0_attn_Wk, L0_attn_bk,
           L0_attn_Wv, L0_attn_bv, L0_attn_Wo, L0_attn_bo,
           L0_ffn_gate_cp, L0_ffn_up_cp, L0_ffn_down_cp):
    st = _build_static()
    x = np.asarray(L0_attn_x, dtype=np.float32)
    wq = _block4(np.asarray(L0_attn_Wq, np.float32))
    wk = _block4(np.asarray(L0_attn_Wk, np.float32))
    wv = _block4(np.asarray(L0_attn_Wv, np.float32))
    wo = _block4(np.asarray(L0_attn_Wo, np.float32))
    bq = np.ascontiguousarray(np.asarray(L0_attn_bq, np.float32).reshape(16, 128).T)
    bk = np.ascontiguousarray(np.asarray(L0_attn_bk, np.float32).reshape(16, 128).T)
    bv = np.ascontiguousarray(np.asarray(L0_attn_bv, np.float32).reshape(16, 128).T)
    bo = np.ascontiguousarray(np.asarray(L0_attn_bo, np.float32).reshape(16, 128).T)
    cps = [np.asarray(L0_ffn_gate_cp, np.float32),
           np.asarray(L0_ffn_up_cp, np.float32),
           np.asarray(L0_ffn_down_cp, np.float32)]

    nc = build_nc()
    in_maps = []
    for c in range(NCORES):
        b = c // 2
        rows = np.concatenate([np.arange(bb * 128, bb * 128 + 128)
                               for bb in st["perms"][c]])
        xp = np.ascontiguousarray(x[b][rows])
        cgat = np.empty((3, 128, TPC * 5), np.float32)
        for w in range(3):
            cgat[w] = cps[w][st["cps_idx"][c][w]].reshape(128, TPC * 5)
        in_maps.append(dict(
            x=xp, wqb=wq, wkb=wk, wvb=wv, wob=wo,
            btab=st["btabs"][c], rpt=st["rpts"][c], maskt=st["masks"][c],
            bqt=bq, bkt=bk, bvt=bv, bot=bo,
            cpst=cgat, cppt=st["cpps"][c], fplt=st["fplans"][c],
            iota=st["iota"], ident=st["ident"], misc=st["misc"],
        ))

    res = run_bass_kernel_spmd(nc, in_maps, core_ids=list(range(NCORES)))
    out = np.empty((BATCH, S, D), np.float32)
    for c in range(NCORES):
        b = c // 2
        rows = np.concatenate([np.arange(bb * 128, bb * 128 + 128)
                               for bb in st["perms"][c]])[:TOK]
        out[b][rows] = res.results[c]["y"]
    return out


# version 15
# speedup vs baseline: 1.3297x; 1.0466x over previous
import numpy as np
import concourse.bass as bass
import concourse.mybir as mybir
from concourse.bass_utils import run_bass_kernel_spmd

F32 = mybir.dt.float32
ACT = mybir.ActivationFunctionType
ALU = mybir.AluOpType
AXX = mybir.AxisListType.X

D = 2048
H = 16
BATCH = 4
S = 2048
INTER = 8192
NCORES = 8
TOK = 1024
NCTRL = 130156
NS = 16777216
EPS = float(np.float32(1e-5))
FMIN = np.float32(np.finfo(np.float32).min)
SQRT_HD = float(np.float32(np.sqrt(np.float32(128.0))))

TPC = 128
TILE_EL = 128 * 128
WOFF = TPC * TILE_EL
SHARD = 3 * WOFF

KCS0 = [0, 1, 2, 3, 8, 9, 10, 11]
KCS1 = list(range(16))
MASK_PAIRS = [(0, kc) for kc in KCS0] + [(1, kc) for kc in (4, 5, 6, 7, 12, 13, 14, 15)]
MASK_SLOT = {p: i for i, p in enumerate(MASK_PAIRS)}

_static = {}
DEBUG = False



def _alibi_bias_f32():
    import jax
    import jax.numpy as jnp
    with jax.default_device(jax.devices("cpu")[0]):
        ratio = 2.0 ** (-8.0 / H)
        slopes = ratio ** jnp.arange(1, H + 1, dtype=jnp.float32)
        distances = jnp.arange(1 - S, 1, dtype=jnp.float32)
        bias = slopes[:, None] * distances[None, :]
        return np.asarray(bias), np.asarray(slopes)


def _recon_tiles_for(rank, w):
    tiles = []
    if w < 2:
        for t in range(TPC):
            oc, hcw = t // 2, t % 2
            tiles.append((oc * 128, (2 * rank + hcw) * 128, D))
    else:
        for t in range(TPC):
            icw, hpc = t // 16, t % 16
            tiles.append((hpc * 128, (8 * rank + icw) * 128, INTER))
    return tiles


def _build_static():
    if _static:
        return _static
    t64 = np.linspace(0.0, NCTRL - 1.0, NS, dtype=np.float64)
    i_arr = np.clip(np.floor(t64).astype(np.int64), 0, NCTRL - 2)
    f_arr = (t64 - i_arr).astype(np.float32)
    del t64
    bias_f32, slopes = _alibi_bias_f32()

    perms, btabs, rpts, masks = [], [], [], []
    for c in range(NCORES):
        par = c % 2
        blocks = [2 * k + par for k in range(8)] + [2 * k + 1 - par for k in range(8)]
        perms.append(blocks)
        gl = np.concatenate([np.arange(b * 128, b * 128 + 128) for b in blocks])
        kglob, qglob = gl, gl[:TOK]
        btab = np.empty((128, H * 16), np.float32)
        for h in range(H):
            for kc in range(16):
                btab[:, h * 16 + kc] = bias_f32[h, kglob[kc * 128:(kc + 1) * 128]]
        btabs.append(btab)
        rpt = np.empty((H, 2, 128, 512), np.float32)
        for h in range(H):
            row = (slopes[h] * (np.float32(S - 1) - qglob.astype(np.float32))
                   ).astype(np.float32)
            rpt[h, 0] = np.broadcast_to(row[:512], (128, 512))
            rpt[h, 1] = np.broadcast_to(row[512:], (128, 512))
        rpts.append(rpt)
        mk = np.zeros((len(MASK_PAIRS), 128, 512), np.float32)
        for mi, (tile, kc) in enumerate(MASK_PAIRS):
            kpos = kglob[kc * 128:(kc + 1) * 128]
            qpos = qglob[tile * 512:(tile + 1) * 512]
            mk[mi][kpos[:, None] > qpos[None, :]] = FMIN
        masks.append(mk)

    cps_idx, cpps, fplans = [], [], []
    u128 = np.arange(128)
    for c in range(NCORES):
        ci = np.empty((3, 128, TPC, 5), np.int64)
        cp = np.empty((3, 128, TPC), np.float32)
        fp = np.empty((3, 128, TPC, 128), np.float32)
        for w in range(3):
            for t, (row0, col0, IN) in enumerate(_recon_tiles_for(c, w)):
                j0 = (row0 + u128) * IN + col0
                ib = i_arr[j0]
                iv = i_arr[j0[:, None] + u128[None, :]]
                dlt = iv - ib[:, None]
                assert dlt.min() >= 0 and dlt.max() <= 1
                adv = dlt.argmax(axis=1).astype(np.float32)
                adv[dlt.max(axis=1) == 0] = 128.0
                cp[w, :, t] = adv
                for m in range(5):
                    ci[w, :, t, m] = np.clip(ib + m - 1, 0, NCTRL - 1)
                fp[w, :, t, :] = f_arr[j0[:, None] + u128[None, :]]
        cps_idx.append(ci)
        cpps.append(cp)
        fplans.append(fp.reshape(3, 128, TPC * 128))

    _static.update(dict(
        perms=perms, btabs=btabs, rpts=rpts, masks=masks,
        cps_idx=cps_idx, cpps=cpps, fplans=fplans,
        iota=np.ascontiguousarray(
            np.broadcast_to(np.arange(128, dtype=np.float32), (128, 128))),
        ident=np.eye(128, dtype=np.float32),
        misc=np.concatenate([
            np.ones((128, 129), np.float32),
            np.full((128, 1), np.float32(1e-5), np.float32),
            np.full((128, 1), np.float32(SQRT_HD), np.float32)], axis=1),
    ))
    return _static



class Ctr:
    def __init__(self, nc, name):
        self.h = nc.alloc_semaphore(name)
        self.n = 0

    def bump(self, k):
        self.n += k
        return self.n


def build_nc():
    nc = bass.Bass("TRN2", target_bir_lowering=False, debug=False,
                   detect_race_conditions=False)

    din = lambda n, sh: nc.dram_tensor(n, sh, F32, kind="ExternalInput")
    x_in = din("x", [S, D])
    wqb_in = din("wqb", [16, 16, 128, 128])
    wkb_in = din("wkb", [16, 16, 128, 128])
    wvb_in = din("wvb", [16, 16, 128, 128])
    wob_in = din("wob", [16, 16, 128, 128])
    btab_in = din("btab", [128, H * 16])
    rpt_in = din("rpt", [H, 2, 128, 512])
    mask_in = din("maskt", [len(MASK_PAIRS), 128, 512])
    bq_in = din("bqt", [128, 16])
    bk_in = din("bkt", [128, 16])
    bv_in = din("bvt", [128, 16])
    bo_in = din("bot", [128, 16])
    cp_in = din("cpst", [3, 128, TPC * 5])
    cpp_in = din("cppt", [3, 128, TPC])
    fpl_in = din("fplt", [3, 128, TPC * 128])
    iota_in = din("iota", [128, 128])
    ident_in = din("ident", [128, 128])
    misc_in = din("misc", [128, 131])

    y_out = nc.dram_tensor("y", [TOK, D], F32, kind="ExternalOutput")
    if DEBUG:
        dbg_xno = nc.dram_tensor("dbg_xno", [128, 16 * TOK], F32, kind="ExternalOutput")
        dbg_kt = nc.dram_tensor("dbg_kt", [D, S], F32, kind="ExternalOutput")
        dbg_vt = nc.dram_tensor("dbg_vt", [D, S], F32, kind="ExternalOutput")
        dbg_qt = nc.dram_tensor("dbg_qt", [D, TOK], F32, kind="ExternalOutput")
        dbg_at = nc.dram_tensor("dbg_at", [D, TOK], F32, kind="ExternalOutput")
        dbg_ag = nc.dram_tensor("dbg_ag", [SHARD], F32, kind="ExternalOutput")
        dbg_ago = nc.dram_tensor("dbg_ago", [NCORES, SHARD], F32, kind="ExternalOutput")

    BF16 = mybir.dt.bfloat16
    ag_in_gu = nc.dram_tensor("ag_in_gu", [2 * TPC * 2 * TILE_EL], BF16)
    ag_out_gu = nc.dram_tensor("ag_out_gu", [NCORES, 2 * TPC * 2 * TILE_EL], BF16,
                               addr_space="Shared")
    ag_in_d = nc.dram_tensor("ag_in_d", [TPC * 2 * TILE_EL], BF16)
    ag_out_d = nc.dram_tensor("ag_out_d", [NCORES, TPC * 2 * TILE_EL], BF16,
                              addr_space="Shared")
    kt_sc = nc.dram_tensor("kt_sc", [D, S], F32)
    vt_sc = nc.dram_tensor("vt_sc", [D, S], F32)
    qt_sc = nc.dram_tensor("qt_sc", [D, TOK], F32)
    xno_sc = nc.dram_tensor("xno_sc", [128, 2 * 16 * TOK], mybir.dt.bfloat16)
    at_sc = nc.dram_tensor("at_sc", [D, TOK], F32)

    BIG = nc.alloc_sbuf_tensor("BIG", [128, 34816], F32)
    SMALL = nc.alloc_sbuf_tensor("SMALL", [128, 15360], F32)
    CON = nc.alloc_sbuf_tensor("CON", [128, 768], F32)
    big = lambda o, n: BIG[:, o:o + n]
    small = lambda o, n: SMALL[:, o:o + n]

    c_ident = CON[:, 0:128]
    c_iota = CON[:, 128:256]
    c_btab = CON[:, 256:512]
    c_bq = CON[:, 512:528]
    c_bk = CON[:, 528:544]
    c_bv = CON[:, 544:560]
    c_bo = CON[:, 560:576]
    c_misc = CON[:, 576:707]
    c_onesr = c_misc[:, 0:128]
    c_ones = c_misc[:, 128:129]
    c_eps = c_misc[:, 129:130]
    c_sqrt = c_misc[:, 130:131]
    c_rsq = CON[:, 707:708]

    p0 = nc.alloc_psum_tensor("p0", [128, 512], F32)
    p1 = nc.alloc_psum_tensor("p1", [128, 512], F32)
    p2 = nc.alloc_psum_tensor("p2", [128, 512], F32)
    p3 = nc.alloc_psum_tensor("p3", [128, 512], F32)
    p4 = nc.alloc_psum_tensor("p4", [128, 512], F32)
    p5 = nc.alloc_psum_tensor("p5", [128, 512], F32)
    p6 = nc.alloc_psum_tensor("p6", [128, 512], F32)
    p7 = nc.alloc_psum_tensor("p7", [128, 512], F32)

    Cm = lambda n: Ctr(nc, n)
    d_con = Cm("d_con")
    d_f = [Cm("d_f0"), Cm("d_f1")]
    d_x = [Cm("d_x0"), Cm("d_x1")]
    d_w = [Cm("d_w0"), Cm("d_w1")]
    d_h = [Cm("d_h0"), Cm("d_h1")]
    d_g = [Cm("d_g0"), Cm("d_g1")]
    d_u = [Cm("d_u0"), Cm("d_u1")]
    d_d = [Cm("d_d0"), Cm("d_d1")]
    d_a = [Cm("d_a0"), Cm("d_a1")]
    d_xr = Cm("d_xr")
    d_xh = Cm("d_xh")
    d_sp = Cm("d_sp")
    d_y = Cm("d_y")
    s_cc = Cm("s_cc")
    d_rc = Cm("d_rc")
    s_x = Cm("s_x")
    s_t = Cm("s_t")
    s_v = Cm("s_v")
    s_s = Cm("s_s")

    with nc.Block() as blk:
        @blk.sync
        def _(e):
            for dst, src in ((c_ident, ident_in), (c_iota, iota_in),
                             (c_btab, btab_in), (c_bq, bq_in), (c_bk, bk_in),
                             (c_bv, bv_in), (c_bo, bo_in), (c_misc, misc_in)):
                e.dma_start(out=dst, in_=src.ap()[:, :]).then_inc(d_con.h, 16)
                d_con.bump(16)

        @blk.vector
        def _(e):
            e.wait_ge(d_con.h, d_con.n)
            e.reciprocal(out=c_rsq, in_=c_sqrt)
            e.tensor_copy(out=BIG[:, 33536:33600].bitcast(mybir.dt.bfloat16),
                          in_=c_ident)
    nc.const_aps.register(F32, EPS, c_eps[:, 0:1])

    RTC = 8
    RNCH = TPC // RTC
    NRCH = 3 * RNCH

    xnt_hi = BIG[:, 0:16384].bitcast(mybir.dt.bfloat16)
    xnt_lo = BIG[:, 16384:32768].bitcast(mybir.dt.bfloat16)
    xh3q = xnt_hi.rearrange("p (fc t) -> p fc t", fc=16)
    xl3q = xnt_lo.rearrange("p (fc t) -> p fc t", fc=16)
    ln_x = [small(0, 2048), small(2048, 2048)]
    ln_t = small(4096, 2048)
    ln_s = small(6144, 2048)
    ln_c = small(8192, 8)

    T0, V0, S0 = s_t.n, s_v.n, s_s.n
    LNV = lambda tc, i: V0 + tc * 10 + i
    LNS = lambda tc, i: S0 + tc * 6 + i
    LNT = lambda tc, g: T0 + tc * 4 + g + 1

    with nc.Block() as blk:
        @blk.sync
        def _(e):
            for tc in range(16):
                if tc >= 2:
                    e.wait_ge(s_t.h, LNT(tc - 2, 3))
                e.dma_start(out=ln_x[tc % 2],
                            in_=x_in.ap()[tc * 128:(tc + 1) * 128, :]
                            ).then_inc(d_x[tc % 2].h, 16)
                d_x[tc % 2].bump(16)

        @blk.vector
        def _(e):
            for tc in range(16):
                e.wait_ge(d_x[tc % 2].h, 16 * (tc // 2 + 1))
                xs = ln_x[tc % 2]
                e.reduce_sum(out=ln_c[:, 0:1], in_=xs, axis=AXX).then_inc(s_v.h, 1)
                e.wait_ge(s_v.h, s_v.bump(1))
                e.tensor_scalar_mul(ln_c[:, 1:2], ln_c[:, 0:1], 1.0 / 2048.0
                                    ).then_inc(s_v.h, 1)
                e.wait_ge(s_v.h, s_v.bump(1))
                e.tensor_scalar(out=ln_t, in0=xs, scalar1=ln_c[:, 1:2],
                                scalar2=None, op0=ALU.subtract).then_inc(s_v.h, 1)
                s_v.bump(1)
                e.wait_ge(s_s.h, LNS(tc, 1))
                e.reduce_sum(out=ln_c[:, 0:1], in_=ln_s, axis=AXX).then_inc(s_v.h, 1)
                s_v.bump(1)
                e.wait_ge(s_s.h, LNS(tc, 2))
                e.reciprocal(out=ln_c[:, 3:4], in_=ln_c[:, 2:3]).then_inc(s_v.h, 1)
                e.wait_ge(s_v.h, s_v.bump(1))
                e.tensor_scalar(out=xs, in0=ln_t, scalar1=ln_c[:, 3:4],
                                scalar2=None, op0=ALU.mult).then_inc(s_v.h, 1)
                s_v.bump(1)
                for g in range(4):
                    e.wait_ge(s_s.h, LNS(tc, 2 + g + 1))
                    dsth = xh3q[:, g * 4:(g + 1) * 4, tc * 128:(tc + 1) * 128]
                    dstl = xl3q[:, g * 4:(g + 1) * 4, tc * 128:(tc + 1) * 128]
                    e.tensor_tensor(out=dstl,
                                    in0=p6.rearrange("p (j t) -> p j t", j=4),
                                    in1=dsth, op=ALU.subtract).then_inc(s_v.h, 1)
                    s_v.bump(1)

        @blk.scalar
        def _(e):
            for tc in range(16):
                e.wait_ge(s_v.h, LNV(tc, 3))
                e.activation(out=ln_s, in_=ln_t, func=ACT.Square).then_inc(s_s.h, 1)
                s_s.bump(1)
                e.wait_ge(s_v.h, LNV(tc, 4))
                e.activation(out=ln_c[:, 2:3], in_=ln_c[:, 0:1], func=ACT.Sqrt,
                             scale=1.0 / 2048.0, bias=EPS).then_inc(s_s.h, 1)
                s_s.bump(1)
                for g in range(4):
                    e.wait_ge(s_t.h, LNT(tc, g))
                    dsth = xh3q[:, g * 4:(g + 1) * 4, tc * 128:(tc + 1) * 128]
                    e.activation(out=dsth,
                                 in_=p6.rearrange("p (j t) -> p j t", j=4),
                                 func=ACT.Copy).then_inc(s_s.h, 1)
                    s_s.bump(1)

        @blk.tensor
        def _(e):
            for tc in range(16):
                e.wait_ge(s_v.h, LNV(tc, 6))
                for g in range(4):
                    prev = LNV(tc, 6 + g) if g >= 1 else (
                        LNV(tc - 1, 10) if tc >= 1 else None)
                    if prev is not None:
                        e.wait_ge(s_v.h, prev)
                    for j in range(4):
                        fc = g * 4 + j
                        ins = e.transpose(p6[:, j * 128:(j + 1) * 128],
                                          ln_x[tc % 2][:, fc * 128:(fc + 1) * 128],
                                          c_ident)
                    ins.then_inc(s_t.h, 1)
                    s_t.bump(1)

    with nc.Block() as blk:
        @blk.gpsimd
        def _(e):
            e.dma_start(out=xno_sc.ap()[:, 0:16 * TOK]
                        .rearrange("p (fc t) -> p fc t", fc=16),
                        in_=xh3q[:, :, 0:TOK]).then_inc(d_sp.h, 16)
            d_sp.bump(16)
            e.dma_start(out=xno_sc.ap()[:, 16 * TOK:]
                        .rearrange("p (fc t) -> p fc t", fc=16),
                        in_=xl3q[:, :, 0:TOK]).then_inc(d_sp.h, 16)
            d_sp.bump(16)

    w_slab = small(0, 2048)
    w_pair = [SMALL[:, 2048:4096].bitcast(mybir.dt.bfloat16),
              SMALL[:, 4096:6144].bitcast(mybir.dt.bfloat16)]
    pr_st = [small(6144, 512), small(6656, 512)]
    r_f = [small(7168, 1024), small(8192, 1024)]
    r_acc = small(9216, 1024)
    r_bac = small(10240, 1024)
    r_dlt = small(11264, 1024)
    r_stage = [small(12288, 1024), small(13312, 1024)]
    t_q = small(14336, 1024)
    t_cps = big(32768, 640)
    t_cpp = big(33408, 128)

    s_rv = Cm("s_rv")
    s_rt = Cm("s_rt")
    d_ag = [Cm("d_ag0"), Cm("d_ag1")]
    RPREP, REVAL, RCAST, REVA, RT = {}, {}, {}, {}, {}
    _rv = 0
    _rt = 0
    for c in range(NRCH):
        w, t = c // RNCH, c % RNCH
        if t == 0:
            _rv += 1
            RPREP[w] = _rv
        _rv += 1
        REVAL[c] = _rv
        _rv += 2
        RCAST[c] = _rv
        REVA[c] = []
        for g in range(4):
            _rv += 1
            REVA[c].append(_rv)
        RT[c] = []
        for g in range(4):
            _rt += 1
            RT[c].append(_rt)

    def emit_recon_tables(e, wv):
        e.dma_start(out=t_cps, in_=cp_in.ap()[wv, :, :]).then_inc(d_rc.h, 16)
        d_rc.bump(16)
        e.dma_start(out=t_cpp, in_=cpp_in.ap()[wv, :, :]).then_inc(d_rc.h, 16)
        d_rc.bump(16)
        if wv == 0:
            e.dma_start(out=r_f[0], in_=fpl_in.ap()[0, :, 0:RTC * 128]
                        ).then_inc(d_f[0].h, 16)
            d_f[0].bump(16)

    def emit_recon_sync(e, cl):
        for c in cl:
            nx = c + 1
            if nx >= NRCH:
                continue
            w, t = nx // RNCH, nx % RNCH
            if nx >= 2:
                e.wait_ge(s_rv.h, REVAL[nx - 2])
            e.dma_start(out=r_f[nx % 2],
                        in_=fpl_in.ap()[w, :, t * RTC * 128:(t + 1) * RTC * 128]
                        ).then_inc(d_f[nx % 2].h, 16)
            d_f[nx % 2].bump(16)

    def emit_recon_prep(e, w):
        if True:
            if True:
                e.wait_ge(d_rc.h, 32 * (w + 1))
                cv = t_cps.rearrange("p (tt m) -> p tt m", m=5)
                sv = [cv[:, :, m] for m in range(5)]
                q = [t_q[:, k * TPC:(k + 1) * TPC] for k in range(8)]



                tmp = r_acc
                tcol = lambda i: tmp[:, i * TPC:(i + 1) * TPC]
                ops = []

                def plan(dst, a, b, cc, d, base):
                    u = [tcol(base + i) for i in range(4)]
                    ops.append(("c", dst[0], b))
                    ops.append(("s", u[0], cc, a, ALU.subtract))
                    ops.append(("m", dst[1], u[0], 0.5))
                    ops.append(("m", u[1], b, -2.5))
                    ops.append(("s", u[1], u[1], a, ALU.add))
                    ops.append(("m", u[2], cc, 2.0))
                    ops.append(("s", u[1], u[1], u[2], ALU.add))
                    ops.append(("m", u[2], d, -0.5))
                    ops.append(("s", dst[2], u[1], u[2], ALU.add))
                    ops.append(("m", u[3], b, 1.5))
                    ops.append(("m", u[0], a, -0.5))
                    ops.append(("s", u[3], u[3], u[0], ALU.add))
                    ops.append(("m", u[0], cc, -1.5))
                    ops.append(("s", u[3], u[3], u[0], ALU.add))
                    ops.append(("m", u[0], d, 0.5))
                    ops.append(("s", dst[3], u[3], u[0], ALU.add))

                plan([q[0], q[1], q[2], q[3]], sv[0], sv[1], sv[2], sv[3], 0)
                opsA = ops
                ops = []
                plan([q[4], q[5], q[6], q[7]], sv[1], sv[2], sv[3], sv[4], 4)
                opsB = ops
                seq = []
                for i in range(max(len(opsA), len(opsB))):
                    if i < len(opsA):
                        seq.append(opsA[i])
                    if i < len(opsB):
                        seq.append(opsB[i])
                for k in range(4):
                    seq.append(("s", q[4 + k], q[4 + k], q[k], ALU.subtract))
                last = None
                for op in seq:
                    if op[0] == "c":
                        last = e.tensor_copy(out=op[1], in_=op[2])
                    elif op[0] == "m":
                        last = e.tensor_scalar_mul(op[1], op[2], op[3])
                    else:
                        last = e.tensor_tensor(out=op[1], in0=op[2], in1=op[3],
                                               op=op[4])
                last.then_inc(s_rv.h, 1)
                s_rv.bump(1)
    def emit_recon_eval(e, cl):
        for c in cl:
            w, t = c // RNCH, c % RNCH
            sl = c % 2
            t0 = t * RTC
            e.wait_ge(d_f[sl].h, rf_fill_at[c])
            if c >= 1:
                e.wait_ge(s_rt.h, RT[c - 1][1])
            fb = r_f[sl].rearrange("p (tt u) -> p tt u", tt=RTC)
            acc = r_acc.rearrange("p (tt u) -> p tt u", tt=RTC)
            bac = r_bac.rearrange("p (tt u) -> p tt u", tt=RTC)
            dlt = r_dlt.rearrange("p (tt u) -> p tt u", tt=RTC)
            qv = lambda k: t_q[:, k * TPC + t0:k * TPC + t0 + RTC] \
                .unsqueeze(2).broadcast_to([128, RTC, 128])
            cpv = t_cpp[:, t0:t0 + RTC].unsqueeze(2).broadcast_to([128, RTC, 128])
            iov = c_iota.unsqueeze(1).broadcast_to([128, RTC, 128])
            e.tensor_tensor(out=dlt, in0=iov, in1=cpv, op=ALU.is_ge)
            e.tensor_tensor(out=acc, in0=fb, in1=qv(3), op=ALU.mult)
            e.tensor_tensor(out=bac, in0=fb, in1=qv(7), op=ALU.mult)
            e.tensor_tensor(out=acc, in0=acc, in1=qv(2), op=ALU.add)
            e.tensor_tensor(out=bac, in0=bac, in1=qv(6), op=ALU.add)
            e.tensor_tensor(out=acc, in0=acc, in1=fb, op=ALU.mult)
            e.tensor_tensor(out=bac, in0=bac, in1=fb, op=ALU.mult)
            e.tensor_tensor(out=acc, in0=acc, in1=qv(1), op=ALU.add)
            e.tensor_tensor(out=bac, in0=bac, in1=qv(5), op=ALU.add)
            e.tensor_tensor(out=acc, in0=acc, in1=fb, op=ALU.mult)
            e.tensor_tensor(out=bac, in0=bac, in1=fb, op=ALU.mult)
            e.tensor_tensor(out=acc, in0=acc, in1=qv(0), op=ALU.add)
            e.tensor_tensor(out=bac, in0=bac, in1=qv(4), op=ALU.add
                            ).then_inc(s_x.h, 1)
            e.wait_ge(s_x.h, s_x.bump(1))
            e.tensor_tensor(out=bac, in0=bac, in1=dlt, op=ALU.mult
                            ).then_inc(s_x.h, 1)
            e.wait_ge(s_x.h, s_x.bump(1))
            e.tensor_tensor(out=acc, in0=acc, in1=bac, op=ALU.add
                            ).then_inc(s_rv.h, 1)
            e.wait_ge(s_rv.h, s_rv.bump(1))
            rbhi = r_bac.bitcast(BF16)[:, 0:1024]
            rblo = r_dlt.bitcast(BF16)[:, 0:1024]
            e.tensor_copy(out=rbhi, in_=r_acc).then_inc(s_x.h, 1)
            e.wait_ge(s_x.h, s_x.bump(1))
            e.tensor_tensor(out=rblo, in0=r_acc, in1=rbhi, op=ALU.subtract
                            ).then_inc(s_rv.h, 1)
            s_rv.bump(1)
            e.engine_nop().then_inc(s_rv.h, 1)
            s_rv.bump(1)

    def emit_recon_stevac(e, cl):
        for c in cl:
            stg4 = r_stage[c % 2].bitcast(BF16).rearrange(
                "p (tt x f) -> p tt x f", tt=8, x=2, f=128)
            for g in range(4):
                e.wait_ge(s_rt.h, RT[c][g])
                if g == 0 and c >= 2:
                    e.wait_ge(d_ag[c % 2].h, ag_fill_at[c])
                xx, g2 = g // 2, g % 2
                e.tensor_copy(out=stg4[:, g2 * 4:(g2 + 1) * 4, xx, :],
                              in_=p4.bitcast(BF16)[:, 0:512]
                              .rearrange("p (j f) -> p j f", j=4)
                              ).then_inc(s_rv.h, 1)
                s_rv.bump(1)


    def emit_recon_tensor(e, cl):
        for c in cl:
            e.wait_ge(s_rv.h, RCAST[c])
            rbhi = r_bac.bitcast(BF16)[:, 0:1024]
            rblo = r_dlt.bitcast(BF16)[:, 0:1024]
            for g in range(4):
                srcb = (rbhi, rblo)[g // 2]
                if g >= 1:
                    e.wait_ge(s_rv.h, REVA[c][g - 1])
                elif c >= 1:
                    e.wait_ge(s_rv.h, REVA[c - 1][3])
                identb = BIG[:, 33536:33600].bitcast(BF16)
                p4b = p4.bitcast(BF16)
                for j in range(4):
                    ti = (g % 2) * 4 + j
                    ins = e.transpose(p4b[:, j * 128:(j + 1) * 128],
                                      srcb[:, ti * 128:(ti + 1) * 128], identb)
                ins.then_inc(s_rt.h, 1)
                s_rt.bump(1)

    def emit_recon_gpsimd(e, cl):
        for c in cl:
            w, t = c // RNCH, c % RNCH
            e.wait_ge(s_rv.h, REVA[c][3])
            PTILE = 2 * TILE_EL
            dst = (ag_in_gu.ap()[(w * TPC + t * RTC) * PTILE:
                                 (w * TPC + (t + 1) * RTC) * PTILE]
                   if w < 2 else
                   ag_in_d.ap()[t * RTC * PTILE:(t + 1) * RTC * PTILE])
            stg = r_stage[c % 2].bitcast(BF16)
            e.dma_start(out=dst.rearrange("(tx p f) -> p tx f",
                                          tx=2 * RTC, p=128, f=128),
                        in_=stg.rearrange("p (tx f) -> p tx f", tx=2 * RTC),
                        ).then_inc(d_ag[c % 2].h, 16)
            d_ag[c % 2].bump(16)
            if c == NRCH - 1:
                e.wait_ge(d_ag[0].h, d_ag[0].n)
                e.wait_ge(d_ag[1].h, d_ag[1].n)
                e.collective_compute(
                    "AllGather", ALU.bypass,
                    replica_groups=[list(range(NCORES))],
                    ins=[ag_in_gu.ap().opt()],
                    outs=[ag_out_gu.ap().opt()],
                ).then_inc(s_cc.h, 1)
                s_cc.bump(1)
            if c == NRCH - 1:
                e.wait_ge(d_ag[0].h, d_ag[0].n)
                e.wait_ge(d_ag[1].h, d_ag[1].n)
                e.collective_compute(
                    "AllGather", ALU.bypass,
                    replica_groups=[list(range(NCORES))],
                    ins=[ag_in_d.ap().opt()],
                    outs=[ag_out_d.ap().opt()],
                ).then_inc(s_cc.h, 1)
                s_cc.bump(1)

    rf_fill_at = {}
    _f = [0, 0]
    _f[0] = 16
    rf_fill_at[0] = 16
    for nx in range(1, NRCH):
        _f[nx % 2] += 16
        rf_fill_at[nx] = _f[nx % 2]
    ag_fill_at = {}
    _a = [0, 0]
    for c in range(NRCH):
        ag_fill_at[c] = _a[c % 2] if c >= 2 else 0
        _a[c % 2] += 16


    def proj(wten, bias_col, n_tt, dst_sc, div, rhs_fn, rw=None):
        TT0, VV0, SS0, SP0 = s_t.n, s_v.n, s_s.n, d_sp.n
        W0 = d_w[0].n
        PT = lambda idx: TT0 + idx + 1
        PVC = lambda dc: VV0 + dc * (n_tt + 1) + 1
        PV = lambda idx: VV0 + (idx // n_tt) * (n_tt + 1) + 1 + (idx % n_tt) + 1
        PSH = lambda dc: SS0 + dc + 1
        PS = lambda idx: SP0 + 16 * (idx + 1)
        rch = lambda dc: ([rw * RNCH + dc] if rw is not None else [])
        with nc.Block() as blk:
            @blk.sync
            def _(e):
                if rw is not None:
                    emit_recon_tables(e, rw)
                for dc in range(16):
                    if dc >= 1:
                        e.wait_ge(s_v.h, PVC(dc - 1))
                    e.dma_start(out=w_slab
                                .rearrange("p (k f) -> p k f", k=16),
                                in_=wten.ap()[dc].rearrange("k p f -> p k f")
                                ).then_inc(d_w[0].h, 16)
                    d_w[0].bump(16)
                    emit_recon_sync(e, rch(dc))

            @blk.scalar
            def _(e):
                for dc in range(16):
                    e.wait_ge(d_w[0].h, W0 + 16 * (dc + 1))
                    if dc >= 2:
                        e.wait_ge(s_t.h, PT((dc - 1) * n_tt - 1))
                    e.activation(out=w_pair[dc % 2][:, 0:2048], in_=w_slab,
                                 func=ACT.Copy).then_inc(s_s.h, 1)
                    s_s.bump(1)

            @blk.tensor
            def _(e):
                for dc in range(16):
                    e.wait_ge(s_v.h, PVC(dc))
                    for tt in range(n_tt):
                        idx = dc * n_tt + tt
                        ps = (p0, p1)[idx % 2]
                        if idx >= 2:
                            e.wait_ge(s_v.h, PV(idx - 2))
                        wp = w_pair[dc % 2]
                        for hc in range(16):
                            wh = wp[:, hc * 128:(hc + 1) * 128]
                            wl = wp[:, 2048 + hc * 128:2048 + (hc + 1) * 128]
                            xh, xl = rhs_fn(hc, tt)
                            e.matmul(ps[:, :], wh, xh,
                                     start=(hc == 0), stop=False)
                            e.matmul(ps[:, :], wh, xl, start=False, stop=False)
                            ins = e.matmul(ps[:, :], wl, xh,
                                           start=False, stop=(hc == 15))
                        ins.then_inc(s_t.h, 1)
                        s_t.bump(1)
                    emit_recon_tensor(e, rch(dc))

            @blk.vector
            def _(e):
                if rw is not None:
                    emit_recon_prep(e, rw)
                for dc in range(16):
                    e.wait_ge(s_s.h, PSH(dc))
                    e.tensor_tensor(out=w_pair[dc % 2][:, 2048:4096],
                                    in0=w_slab, in1=w_pair[dc % 2][:, 0:2048],
                                    op=ALU.subtract).then_inc(s_v.h, 1)
                    s_v.bump(1)
                    emit_recon_eval(e, rch(dc))
                    for tt in range(n_tt):
                        idx = dc * n_tt + tt
                        ps = (p0, p1)[idx % 2]
                        e.wait_ge(s_t.h, PT(idx))
                        if idx >= 2:
                            e.wait_ge(d_sp.h, PS(idx - 2))
                        if div:
                            e.tensor_scalar(out=pr_st[idx % 2], in0=ps[:, :],
                                            scalar1=bias_col[:, dc:dc + 1],
                                            scalar2=c_rsq[:, 0:1],
                                            op0=ALU.add, op1=ALU.mult
                                            ).then_inc(s_v.h, 1)
                        else:
                            e.tensor_scalar(out=pr_st[idx % 2], in0=ps[:, :],
                                            scalar1=bias_col[:, dc:dc + 1],
                                            scalar2=None, op0=ALU.add
                                            ).then_inc(s_v.h, 1)
                        s_v.bump(1)
                    emit_recon_stevac(e, rch(dc))

            @blk.gpsimd
            def _(e):
                for dc in range(16):
                    for tt in range(n_tt):
                        idx = dc * n_tt + tt
                        e.wait_ge(s_v.h, PV(idx))
                        e.dma_start(out=dst_sc.ap()[dc * 128:(dc + 1) * 128,
                                                    tt * 512:(tt + 1) * 512],
                                    in_=pr_st[idx % 2]).then_inc(d_sp.h, 16)
                        d_sp.bump(16)
                    emit_recon_gpsimd(e, rch(dc))

    xnt_rhs = lambda hc, tt: (xh3q[:, hc, tt * 512:(tt + 1) * 512],
                              xl3q[:, hc, tt * 512:(tt + 1) * 512])
    proj(wkb_in, c_bk, 4, kt_sc, False, xnt_rhs, rw=0)
    proj(wvb_in, c_bv, 4, vt_sc, False, xnt_rhs, rw=1)
    proj(wqb_in, c_bq, 2, qt_sc, True, xnt_rhs, rw=2)

    ctxt = big(0, 16384)
    mks = big(16384, 8192)
    x_exp = [big(24576, 512), big(25088, 512)]
    x_run = big(25600, 512)
    x_run2 = big(26112, 512)
    a_kt = [small(0, 2048), small(2048, 2048)]
    a_vt = [small(4096, 2048), small(6144, 2048)]
    a_qt = [small(8192, 1024), small(9216, 1024)]
    a_pl = [small(10240, 1024), small(11264, 1024)]
    a_vtr = small(12288, 2048)

    with nc.Block() as blk:
        @blk.sync
        def _(e):
            e.dma_start(out=mks.rearrange("p (m f) -> p m f", m=16),
                        in_=mask_in.ap().rearrange("m p f -> p m f")
                        ).then_inc(d_con.h, 16)
            d_con.bump(16)

    mask_slot = {p: i for i, p in enumerate(MASK_PAIRS)}
    ev_ctx_last = {}
    ev_norm = {}
    ev_vtrev = {}
    psc_hist = []

    AT, AV, AS = {}, {}, {}
    tn, vn, sn = s_t.n, s_v.n, s_s.n
    exp_slot_seq = []
    sc_bank_seq = []
    for h in range(H):
        for g in range(4):
            tn += 1
            AT[("vtr", h, g)] = tn
        for tile in range(2):
            kcl = (KCS0, KCS1)[tile]
            n = len(kcl)
            order = [("sc", 0), ("sc", 1)]
            for k in range(n - 2):
                order += [("ctx", k), ("sc", k + 2)]
            order += [("ctx", n - 2), ("ctx", n - 1), ("den",), ("bc",)]
            for op in order:
                tn += 1
                AT[(op[0], h, tile) + tuple(op[1:])] = tn
                if op[0] == "sc":
                    sc_bank_seq.append((h, tile, op[1]))
        for g in range(4):
            vn += 1
            AV[("vtrev", h, g)] = vn
        for tile in range(2):
            kcl = (KCS0, KCS1)[tile]
            n = len(kcl)
            order = [("arg", 0), ("arg", 1)]
            for k in range(n - 2):
                order += [("run", k), ("arg", k + 2)]
            order += [("run", n - 2), ("run", n - 1), ("recip",), ("bcev",),
                      ("norm",)]
            for op in order:
                vn += 1
                AV[(op[0], h, tile) + tuple(op[1:])] = vn
        for tile in range(2):
            n = len((KCS0, KCS1)[tile])
            for ki in range(n):
                sn += 1
                AS[("exp", h, tile, ki)] = sn
                exp_slot_seq.append((h, tile, ki))
    sc_prev2 = {key: (sc_bank_seq[i - 2] if i >= 2 else None)
                for i, key in enumerate(sc_bank_seq)}
    exp_prev2 = {key: (exp_slot_seq[i - 2] if i >= 2 else None)
                 for i, key in enumerate(exp_slot_seq)}
    bcev_order = [(h, t) for h in range(H) for t in range(2)]

    with nc.Block() as blk:
        @blk.tensor
        def _(e):
            for h in range(H):
                sl = h % 2
                e.wait_ge(d_h[sl].h, 64 * (h // 2 + 1))
                if h == 0:
                    e.wait_ge(d_con.h, d_con.n)
                for g in range(4):
                    prev = ("vtrev", h, g - 1) if g >= 1 else (
                        ("vtrev", h - 1, 3) if h >= 1 else None)
                    if prev is not None:
                        e.wait_ge(s_v.h, AV[prev])
                    for j in range(4):
                        kc = g * 4 + j
                        ins = e.transpose(p6[:, j * 128:(j + 1) * 128],
                                          a_vt[sl][:, kc * 128:(kc + 1) * 128],
                                          c_ident)
                    ins.then_inc(s_t.h, 1)
                    s_t.bump(1)
                for tile in range(2):
                    kcl = (KCS0, KCS1)[tile]
                    n = len(kcl)
                    pctx = (p2, p3)[tile]

                    def esc(ki, h=h, tile=tile, sl=sl, kcl=kcl):
                        key = (h, tile, ki)
                        prev = sc_prev2[key]
                        if prev is not None:
                            e.wait_ge(s_s.h, AS[("exp",) + prev])
                        kc = kcl[ki]
                        ps = (p0, p1)[ki % 2]
                        e.matmul(ps[:, :],
                                 a_kt[sl][:, kc * 128:(kc + 1) * 128],
                                 a_qt[sl][:, tile * 512:(tile + 1) * 512],
                                 start=True, stop=True).then_inc(s_t.h, 1)
                        s_t.bump(1)

                    def ectx(ki, h=h, tile=tile, kcl=kcl, n=n, pctx=pctx):
                        if ki == 0:
                            e.wait_ge(s_v.h, AV[("vtrev", h, 3)])
                            if h >= 1:
                                e.wait_ge(s_v.h, AV[("norm", h - 1, tile)])
                        e.wait_ge(s_s.h, AS[("exp", h, tile, ki)])
                        kc = kcl[ki]
                        e.matmul(pctx[:, :], a_vtr[:, kc * 128:(kc + 1) * 128],
                                 x_exp[ki % 2],
                                 start=(ki == 0), stop=(ki == n - 1)
                                 ).then_inc(s_t.h, 1)
                        s_t.bump(1)

                    esc(0)
                    esc(1)
                    for k in range(n - 2):
                        ectx(k)
                        esc(k + 2)
                    ectx(n - 2)
                    ectx(n - 1)
                    e.wait_ge(s_v.h, AV[("run", h, tile, n - 1)])
                    bi = bcev_order.index((h, tile))
                    if bi >= 1:
                        e.wait_ge(s_v.h, AV[("bcev",) + bcev_order[bi - 1]])
                    e.matmul(p7[0:1, :], c_ones, x_run, start=True, stop=False)
                    e.matmul(p7[0:1, :], c_ones, x_run2, start=False, stop=True
                             ).then_inc(s_t.h, 1)
                    s_t.bump(1)
                    e.wait_ge(s_v.h, AV[("recip", h, tile)])
                    e.matmul(p7[:, :], c_onesr[0:1, :], x_run[0:1, :],
                             start=True, stop=True).then_inc(s_t.h, 1)
                    s_t.bump(1)

        @blk.vector
        def _(e):
            for h in range(H):
                sl = h % 2
                for g in range(4):
                    e.wait_ge(s_t.h, AT[("vtr", h, g)])
                    if h >= 1 and g == 0:
                        e.wait_ge(s_t.h, AT[("ctx", h - 1, 1, len(KCS1) - 1)])
                    e.tensor_copy(out=a_vtr[:, g * 512:(g + 1) * 512], in_=p6[:, :]
                                  ).then_inc(s_v.h, 1)
                    s_v.bump(1)
                for tile in range(2):
                    kcl = (KCS0, KCS1)[tile]
                    n = len(kcl)
                    pctx = (p2, p3)[tile]

                    def earg(ki, h=h, tile=tile, sl=sl, kcl=kcl):
                        e.wait_ge(s_t.h, AT[("sc", h, tile, ki)])
                        kc = kcl[ki]
                        ps = (p0, p1)[ki % 2]
                        ins = e.tensor_tensor(
                            out=ps[:, :], in0=ps[:, :],
                            in1=a_pl[sl][:, tile * 512:(tile + 1) * 512],
                            op=ALU.add)
                        mi = MASK_SLOT.get((tile, kc))
                        if mi is not None:
                            ins.then_inc(s_x.h, 1)
                            e.wait_ge(s_x.h, s_x.bump(1))
                            ins = e.tensor_tensor(
                                out=ps[:, :], in0=ps[:, :],
                                in1=mks[:, mi * 512:(mi + 1) * 512], op=ALU.add)
                        ins.then_inc(s_v.h, 1)
                        s_v.bump(1)

                    def erun(ki, h=h, tile=tile, n=n):
                        e.wait_ge(s_s.h, AS[("exp", h, tile, ki)])
                        rbuf = (x_run, x_run2)[ki % 2]
                        if ki < 2:
                            bi = bcev_order.index((h, tile))
                            if bi >= 1:
                                e.wait_ge(s_t.h, AT[("den",) + bcev_order[bi - 1]])
                            e.tensor_copy(out=rbuf, in_=x_exp[ki % 2]
                                          ).then_inc(s_v.h, 1)
                        else:
                            e.tensor_tensor(out=rbuf, in0=rbuf,
                                            in1=x_exp[ki % 2], op=ALU.add
                                            ).then_inc(s_v.h, 1)
                        s_v.bump(1)

                    earg(0)
                    earg(1)
                    for k in range(n - 2):
                        erun(k)
                        earg(k + 2)
                    erun(n - 2)
                    erun(n - 1)
                    e.wait_ge(s_t.h, AT[("den", h, tile)])
                    e.reciprocal(out=x_run[0:1, :], in_=p7[0:1, :]
                                 ).then_inc(s_v.h, 1)
                    s_v.bump(1)
                    e.wait_ge(s_t.h, AT[("bc", h, tile)])
                    e.tensor_copy(out=x_run, in_=p7[:, :]).then_inc(s_x.h, 1)
                    e.wait_ge(s_x.h, s_x.bump(1))
                    e.engine_nop().then_inc(s_v.h, 1)
                    s_v.bump(1)
                    e.wait_ge(s_t.h, AT[("ctx", h, tile, n - 1)])
                    e.scalar_tensor_tensor(
                        out=ctxt[:, h * 1024 + tile * 512:h * 1024 + (tile + 1) * 512],
                        in0=pctx[:, :], scalar=1.0, in1=x_run,
                        op0=ALU.mult, op1=ALU.mult).then_inc(s_v.h, 1)
                    s_v.bump(1)

        @blk.scalar
        def _(e):
            for h in range(H):
                for tile in range(2):
                    n = len((KCS0, KCS1)[tile])
                    for ki in range(n):
                        key = (h, tile, ki)
                        e.wait_ge(s_v.h, AV[("arg", h, tile, ki)])
                        prev = exp_prev2[key]
                        if prev is not None:
                            ph, pt, pk = prev
                            e.wait_ge(s_t.h, AT[("ctx", ph, pt, pk)])
                            e.wait_ge(s_v.h, AV[("run", ph, pt, pk)])
                        ps = (p0, p1)[ki % 2]
                        e.activation(out=x_exp[ki % 2], in_=ps[:, :], func=ACT.Exp,
                                     bias=c_btab[:, h * 16 + (KCS0, KCS1)[tile][ki]:
                                                 h * 16 + (KCS0, KCS1)[tile][ki] + 1],
                                     scale=1.0).then_inc(s_s.h, 1)
                        s_s.bump(1)

        @blk.sync
        def _(e):
            for h in range(H):
                sl = h % 2
                if h >= 2:
                    e.wait_ge(s_t.h, AT[("ctx", h - 2, 1, len(KCS1) - 1)])
                if h == 0:
                    e.wait_ge(d_sp.h, d_sp.n)
                e.dma_start(out=a_kt[sl], in_=kt_sc.ap()[h * 128:(h + 1) * 128, :]
                            ).then_inc(d_h[sl].h, 16)
                d_h[sl].bump(16)
                e.dma_start(out=a_vt[sl], in_=vt_sc.ap()[h * 128:(h + 1) * 128, :]
                            ).then_inc(d_h[sl].h, 16)
                d_h[sl].bump(16)
                e.dma_start(out=a_qt[sl], in_=qt_sc.ap()[h * 128:(h + 1) * 128, :]
                            ).then_inc(d_h[sl].h, 16)
                d_h[sl].bump(16)
                e.dma_start(out=a_pl[sl].rearrange("p (t f) -> p t f", t=2),
                            in_=rpt_in.ap()[h].rearrange("t p f -> p t f")
                            ).then_inc(d_h[sl].h, 16)
                d_h[sl].bump(16)

    ctx_hi = BIG[:, 16384:24576].bitcast(mybir.dt.bfloat16)
    ctx_lo = BIG[:, 24576:32768].bitcast(mybir.dt.bfloat16)
    with nc.Block() as blk:
        @blk.scalar
        def _(e):
            e.activation(out=ctx_hi, in_=ctxt, func=ACT.Copy).then_inc(s_s.h, 1)
            s_s.bump(1)

        @blk.vector
        def _(e):
            e.wait_ge(s_s.h, s_s.n)
            e.tensor_tensor(out=ctx_lo, in0=ctxt, in1=ctx_hi, op=ALU.subtract)

    ctx_rhs = lambda hc, tt: (
        ctx_hi[:, hc * 1024 + tt * 512:hc * 1024 + (tt + 1) * 512],
        ctx_lo[:, hc * 1024 + tt * 512:hc * 1024 + (tt + 1) * 512])
    proj(wob_in, c_bo, 2, at_sc, False, ctx_rhs)

    intT_hi = BIG[:, 0:16384].bitcast(BF16)
    intT_lo = BIG[:, 16384:32768].bitcast(BF16)
    iT_hi3 = intT_hi.rearrange("p (ic t) -> p ic t", ic=64)
    iT_lo3 = intT_lo.rearrange("p (ic t) -> p ic t", ic=64)
    f_si = [big(32768, 512), big(33280, 512)]
    f_imt = [big(33792, 512), big(34304, 512)]
    xnh_hi = SMALL[:, 2048:6144].bitcast(BF16)
    xnh_lo = SMALL[:, 6144:10240].bitcast(BF16)
    xh3 = xnh_hi.rearrange("p (hc t) -> p hc t", hc=16)
    xl3 = xnh_lo.rearrange("p (hc t) -> p hc t", hc=16)
    f_sl = [SMALL[:, 10240:12288].bitcast(BF16),
            SMALL[:, 12288:14336].bitcast(BF16)]
    f_dsl = [SMALL[:, 0:2048].bitcast(BF16),
              SMALL[:, 2048:4096].bitcast(BF16)]
    f_at = [small(4096, 512), small(4608, 512)]
    f_sum = small(5120, 8192)
    f_os = [small(0, 2048), small(2048, 2048)]
    f_xt = small(13312, 2048)
    PTILE = 2 * TILE_EL

    def gate_slab_ap(w, ic, k, x):
        base = (w * TPC + ic * 2 + k) * PTILE + x * TILE_EL
        return ag_out_gu.ap()[:, base:base + TILE_EL] \
            .rearrange("r (p f) -> p r f", p=128, f=128)

    def down_slab_ap(hp, pc, r, x):
        return ag_out_d.ap()[2 * pc + r, :] \
            .rearrange("(icw hq x p f) -> icw hq x p f",
                       icw=8, hq=16, x=2, p=128, f=128)[:, hp, x] \
            .rearrange("icw p f -> p icw f")

    def slg(t, k, x):
        return t.rearrange("p (hc x f) -> p hc x f", hc=16, x=2)[:, k::2, x, :]

    def sld(t, r, x):
        return t.rearrange("p (r icw x f) -> p r icw x f",
                           r=2, icw=8, x=2, f=128)[:, r, :, x, :]

    for th in range(2):
        TT0, VV0, SS0 = s_t.n, s_v.n, s_s.n
        G0, U0 = d_g[0].n, d_g[1].n
        FTg = lambda ic: TT0 + 2 * ic + 1
        FTu = lambda ic: TT0 + 2 * ic + 2
        FV1 = lambda ic: VV0 + 2 * ic + 1
        FV2 = lambda ic: VV0 + 2 * ic + 2
        FS1 = lambda ic: SS0 + 2 * ic + 1
        FS2 = lambda ic: SS0 + 2 * ic + 2
        XH0 = d_xh.n
        with nc.Block() as blk:
            @blk.sync
            def _(e, th=th):
                e.dma_start(out=xh3,
                            in_=xno_sc.ap()[:, 0:16 * TOK]
                            .rearrange("p (hc t) -> p hc t", hc=16)
                            [:, :, th * 512:(th + 1) * 512]).then_inc(d_xh.h, 16)
                d_xh.bump(16)
                e.dma_start(out=xl3,
                            in_=xno_sc.ap()[:, 16 * TOK:]
                            .rearrange("p (hc t) -> p hc t", hc=16)
                            [:, :, th * 512:(th + 1) * 512]).then_inc(d_xh.h, 16)
                d_xh.bump(16)

            @blk.scalar
            def _(e, th=th):
                for ic in range(64):
                    e.wait_ge(s_t.h, FTg(ic))
                    if ic >= 2:
                        e.wait_ge(s_v.h, FV1(ic - 2))
                    pg = (p0, p1)[ic % 2]
                    e.activation(out=f_si[ic % 2], in_=pg[:, :], func=ACT.Silu
                                 ).then_inc(s_s.h, 1)
                    s_s.bump(1)
                    e.wait_ge(s_v.h, FV1(ic))
                    e.activation(out=iT_hi3[:, ic, :], in_=f_imt[ic % 2],
                                 func=ACT.Copy).then_inc(s_s.h, 1)
                    s_s.bump(1)

            @blk.vector
            def _(e, th=th):
                for ic in range(64):
                    e.wait_ge(s_s.h, FS1(ic))
                    e.wait_ge(s_t.h, FTu(ic))
                    if ic >= 2:
                        e.wait_ge(s_s.h, FS2(ic - 2))
                    pu = (p2, p3)[ic % 2]
                    e.tensor_tensor(out=f_imt[ic % 2], in0=f_si[ic % 2],
                                    in1=pu[:, :], op=ALU.mult).then_inc(s_v.h, 1)
                    s_v.bump(1)
                    e.wait_ge(s_s.h, FS2(ic))
                    e.tensor_tensor(out=iT_lo3[:, ic, :], in0=f_imt[ic % 2],
                                    in1=iT_hi3[:, ic, :], op=ALU.subtract
                                    ).then_inc(s_v.h, 1)
                    s_v.bump(1)

            @blk.gpsimd
            def _(e, th=th):
                if th == 0:
                    e.wait_ge(s_cc.h, 1)
                for ic in range(64):
                    if ic >= 1:
                        e.wait_ge(s_t.h, FTg(ic - 1))
                    for kkk in range(2):
                        for xx in range(2):
                            e.dma_start(out=slg(f_sl[0], kkk, xx),
                                        in_=gate_slab_ap(0, ic, kkk, xx)
                                        ).then_inc(d_g[0].h, 16)
                            d_g[0].bump(16)
                    if ic >= 1:
                        e.wait_ge(s_t.h, FTu(ic - 1))
                    for kkk in range(2):
                        for xx in range(2):
                            e.dma_start(out=slg(f_sl[1], kkk, xx),
                                        in_=gate_slab_ap(1, ic, kkk, xx)
                                        ).then_inc(d_g[1].h, 16)
                            d_g[1].bump(16)

            @blk.tensor
            def _(e, th=th):
                e.wait_ge(d_xh.h, XH0 + 32)
                for ic in range(64):
                    pg = (p0, p1)[ic % 2]
                    pu = (p2, p3)[ic % 2]
                    if ic >= 2:
                        e.wait_ge(s_s.h, FS1(ic - 2))
                    e.wait_ge(d_g[0].h, G0 + 64 * (ic + 1))
                    for hc in range(16):
                        wh = f_sl[0][:, hc * 256:hc * 256 + 128]
                        wl = f_sl[0][:, hc * 256 + 128:hc * 256 + 256]
                        e.matmul(pg[:, :], wh, xh3[:, hc, :],
                                 start=(hc == 0), stop=False)
                        e.matmul(pg[:, :], wh, xl3[:, hc, :],
                                 start=False, stop=False)
                        ins = e.matmul(pg[:, :], wl, xh3[:, hc, :],
                                       start=False, stop=(hc == 15))
                    ins.then_inc(s_t.h, 1)
                    s_t.bump(1)
                    if ic >= 2:
                        e.wait_ge(s_v.h, FV1(ic - 2))
                    e.wait_ge(d_g[1].h, U0 + 64 * (ic + 1))
                    for hc in range(16):
                        wh = f_sl[1][:, hc * 256:hc * 256 + 128]
                        wl = f_sl[1][:, hc * 256 + 128:hc * 256 + 256]
                        e.matmul(pu[:, :], wh, xh3[:, hc, :],
                                 start=(hc == 0), stop=False)
                        e.matmul(pu[:, :], wh, xl3[:, hc, :],
                                 start=False, stop=False)
                        ins = e.matmul(pu[:, :], wl, xh3[:, hc, :],
                                       start=False, stop=(hc == 15))
                    ins.then_inc(s_t.h, 1)
                    s_t.bump(1)

        TT1, VV1 = s_t.n, s_v.n
        A0 = (d_a[0].n, d_a[1].n)
        D0 = (d_d[0].n, d_d[1].n)
        DT = lambda hp, pc: TT1 + hp * 4 + pc + 1
        DV = lambda hp: VV1 + hp + 1
        with nc.Block() as blk:
            @blk.sync
            def _(e, th=th):
                for hp in range(16):
                    if hp >= 2:
                        e.wait_ge(s_v.h, DV(hp - 2))
                    e.dma_start(out=f_at[hp % 2],
                                in_=at_sc.ap()[hp * 128:(hp + 1) * 128,
                                               th * 512:(th + 1) * 512]
                                ).then_inc(d_a[hp % 2].h, 16)
                    d_a[hp % 2].bump(16)

            @blk.gpsimd
            def _(e, th=th):
                if th == 0:
                    e.wait_ge(s_cc.h, 2)
                for hp in range(16):
                    for pc in range(4):
                        gi = hp * 4 + pc
                        if gi >= 2:
                            e.wait_ge(s_t.h, DT((gi - 2) // 4, (gi - 2) % 4))
                        for rrr in range(2):
                            for xx in range(2):
                                e.dma_start(out=sld(f_dsl[gi % 2], rrr, xx),
                                            in_=down_slab_ap(hp, pc, rrr, xx)
                                            ).then_inc(d_d[gi % 2].h, 16)
                                d_d[gi % 2].bump(16)

            @blk.tensor
            def _(e, th=th):
                for hp in range(16):
                    pd = (p4, p5)[hp % 2]
                    if hp >= 2:
                        e.wait_ge(s_v.h, DV(hp - 2))
                    for pc in range(4):
                        gi = hp * 4 + pc
                        e.wait_ge(d_d[gi % 2].h, D0[gi % 2] + 64 * (gi // 2 + 1))
                        for kk in range(16):
                            ic = pc * 16 + kk
                            wh = f_dsl[gi % 2][:, kk * 256:kk * 256 + 128]
                            wl = f_dsl[gi % 2][:, kk * 256 + 128:kk * 256 + 256]
                            e.matmul(pd[:, :], wh, iT_hi3[:, ic, :],
                                     start=(pc == 0 and kk == 0), stop=False)
                            e.matmul(pd[:, :], wh, iT_lo3[:, ic, :],
                                     start=False, stop=False)
                            ins = e.matmul(pd[:, :], wl, iT_hi3[:, ic, :],
                                           start=False,
                                           stop=(pc == 3 and kk == 15))
                        ins.then_inc(s_t.h, 1)
                        s_t.bump(1)

            @blk.vector
            def _(e, th=th):
                for hp in range(16):
                    e.wait_ge(s_t.h, DT(hp, 3))
                    e.wait_ge(d_a[hp % 2].h, A0[hp % 2] + 16 * (hp // 2 + 1))
                    pd = (p4, p5)[hp % 2]
                    e.scalar_tensor_tensor(
                        out=f_sum[:, hp * 512:(hp + 1) * 512],
                        in0=pd[:, :], scalar=1.0, in1=f_at[hp % 2],
                        op0=ALU.mult, op1=ALU.add).then_inc(s_v.h, 1)
                    s_v.bump(1)

        TT2, VV2, XR0, DY0 = s_t.n, s_v.n, d_xr.n, d_y.n
        GT = lambda tc, g: TT2 + tc * 4 + g + 1
        GV = lambda tc, i: VV2 + tc * 5 + i + 1
        with nc.Block() as blk:
            @blk.sync
            def _(e, th=th):
                for tc in range(4):
                    if tc >= 1:
                        e.wait_ge(s_v.h, GV(tc - 1, 4))
                    e.dma_start(out=f_xt,
                                in_=x_in.ap()[(th * 4 + tc) * 128:
                                              (th * 4 + tc + 1) * 128, :]
                                ).then_inc(d_xr.h, 16)
                    d_xr.bump(16)

            @blk.tensor
            def _(e, th=th):
                for tc in range(4):
                    for g in range(4):
                        prev = (tc, g - 1) if g >= 1 else (
                            (tc - 1, 3) if tc >= 1 else None)
                        if prev is not None:
                            e.wait_ge(s_v.h, GV(prev[0], prev[1]))
                        for j in range(4):
                            hp = g * 4 + j
                            ins = e.transpose(p6[:, j * 128:(j + 1) * 128],
                                              f_sum[:, hp * 512 + tc * 128:
                                                    hp * 512 + (tc + 1) * 128],
                                              c_ident)
                        ins.then_inc(s_t.h, 1)
                        s_t.bump(1)

            @blk.vector
            def _(e, th=th):
                for tc in range(4):
                    for g in range(4):
                        e.wait_ge(s_t.h, GT(tc, g))
                        if g == 0:
                            if tc >= 2:
                                e.wait_ge(d_y.h, DY0 + 16 * (tc - 1))
                            elif th > 0:
                                e.wait_ge(d_y.h, DY0)
                        e.tensor_copy(out=f_os[tc % 2][:, g * 512:(g + 1) * 512],
                                      in_=p6[:, :]).then_inc(s_v.h, 1)
                        s_v.bump(1)
                    e.wait_ge(d_xr.h, XR0 + 16 * (tc + 1))
                    e.wait_ge(s_v.h, s_v.n)
                    e.tensor_tensor(out=f_os[tc % 2], in0=f_os[tc % 2],
                                    in1=f_xt, op=ALU.add).then_inc(s_v.h, 1)
                    s_v.bump(1)

            @blk.gpsimd
            def _(e, th=th):
                for tc in range(4):
                    e.wait_ge(s_v.h, GV(tc, 4))
                    e.dma_start(out=y_out.ap()[(th * 4 + tc) * 128:
                                               (th * 4 + tc + 1) * 128, :],
                                in_=f_os[tc % 2]).then_inc(d_y.h, 16)
                    d_y.bump(16)

    with nc.Block() as blk:
        @blk.gpsimd
        def _(e):
            e.wait_ge(d_y.h, d_y.n)
            if DEBUG:
                for dst, srcp in ((dbg_xno, xno_sc), (dbg_kt, kt_sc),
                                  (dbg_vt, vt_sc), (dbg_qt, qt_sc),
                                  (dbg_at, at_sc)):
                    e.dma_start(out=dst.ap()[:], in_=srcp.ap()[:]
                                ).then_inc(d_y.h, 16)
                    d_y.bump(16)
                e.dma_start(out=dbg_ag.ap()[0:2 * WOFF], in_=ag_in_gu.ap()[:]
                            ).then_inc(d_y.h, 16)
                d_y.bump(16)
                e.dma_start(out=dbg_ag.ap()[2 * WOFF:], in_=ag_in_d.ap()[:]
                            ).then_inc(d_y.h, 16)
                d_y.bump(16)
                e.dma_start(out=dbg_ago.ap()[:, 0:2 * WOFF], in_=ag_out_gu.ap()[:]
                            ).then_inc(d_y.h, 16)
                d_y.bump(16)
                e.dma_start(out=dbg_ago.ap()[:, 2 * WOFF:], in_=ag_out_d.ap()[:]
                            ).then_inc(d_y.h, 16)
                d_y.bump(16)
                e.wait_ge(d_y.h, d_y.n)

    nc.finalize()
    return nc



def _block4(w):
    hh, dd = w.shape
    return np.ascontiguousarray(
        w.reshape(hh // 128, 128, dd // 128, 128).transpose(2, 0, 1, 3))


def kernel(L0_attn_x, L0_attn_Wq, L0_attn_bq, L0_attn_Wk, L0_attn_bk,
           L0_attn_Wv, L0_attn_bv, L0_attn_Wo, L0_attn_bo,
           L0_ffn_gate_cp, L0_ffn_up_cp, L0_ffn_down_cp):
    st = _build_static()
    x = np.asarray(L0_attn_x, dtype=np.float32)
    wq = _block4(np.asarray(L0_attn_Wq, np.float32))
    wk = _block4(np.asarray(L0_attn_Wk, np.float32))
    wv = _block4(np.asarray(L0_attn_Wv, np.float32))
    wo = _block4(np.asarray(L0_attn_Wo, np.float32))
    bq = np.ascontiguousarray(np.asarray(L0_attn_bq, np.float32).reshape(16, 128).T)
    bk = np.ascontiguousarray(np.asarray(L0_attn_bk, np.float32).reshape(16, 128).T)
    bv = np.ascontiguousarray(np.asarray(L0_attn_bv, np.float32).reshape(16, 128).T)
    bo = np.ascontiguousarray(np.asarray(L0_attn_bo, np.float32).reshape(16, 128).T)
    cps = [np.asarray(L0_ffn_gate_cp, np.float32),
           np.asarray(L0_ffn_up_cp, np.float32),
           np.asarray(L0_ffn_down_cp, np.float32)]

    nc = build_nc()
    in_maps = []
    for c in range(NCORES):
        b = c // 2
        rows = np.concatenate([np.arange(bb * 128, bb * 128 + 128)
                               for bb in st["perms"][c]])
        xp = np.ascontiguousarray(x[b][rows])
        cgat = np.empty((3, 128, TPC * 5), np.float32)
        for w in range(3):
            cgat[w] = cps[w][st["cps_idx"][c][w]].reshape(128, TPC * 5)
        in_maps.append(dict(
            x=xp, wqb=wq, wkb=wk, wvb=wv, wob=wo,
            btab=st["btabs"][c], rpt=st["rpts"][c], maskt=st["masks"][c],
            bqt=bq, bkt=bk, bvt=bv, bot=bo,
            cpst=cgat, cppt=st["cpps"][c], fplt=st["fplans"][c],
            iota=st["iota"], ident=st["ident"], misc=st["misc"],
        ))

    res = run_bass_kernel_spmd(nc, in_maps, core_ids=list(range(NCORES)))
    out = np.empty((BATCH, S, D), np.float32)
    for c in range(NCORES):
        b = c // 2
        rows = np.concatenate([np.arange(bb * 128, bb * 128 + 128)
                               for bb in st["perms"][c]])[:TOK]
        out[b][rows] = res.results[c]["y"]
    return out


# version 18
# speedup vs baseline: 1.3435x; 1.0104x over previous
import numpy as np
import concourse.bass as bass
import concourse.mybir as mybir
from concourse.bass_utils import run_bass_kernel_spmd

F32 = mybir.dt.float32
ACT = mybir.ActivationFunctionType
ALU = mybir.AluOpType
AXX = mybir.AxisListType.X

D = 2048
H = 16
BATCH = 4
S = 2048
INTER = 8192
NCORES = 8
TOK = 1024
NCTRL = 130156
NS = 16777216
EPS = float(np.float32(1e-5))
FMIN = np.float32(np.finfo(np.float32).min)
SQRT_HD = float(np.float32(np.sqrt(np.float32(128.0))))

TPC = 128
TILE_EL = 128 * 128
WOFF = TPC * TILE_EL
SHARD = 3 * WOFF

KCS0 = [0, 1, 2, 3, 8, 9, 10, 11]
KCS1 = list(range(16))
MASK_PAIRS = [(0, kc) for kc in KCS0] + [(1, kc) for kc in (4, 5, 6, 7, 12, 13, 14, 15)]
MASK_SLOT = {p: i for i, p in enumerate(MASK_PAIRS)}

_static = {}
DEBUG = False



def _alibi_bias_f32():
    import jax
    import jax.numpy as jnp
    with jax.default_device(jax.devices("cpu")[0]):
        ratio = 2.0 ** (-8.0 / H)
        slopes = ratio ** jnp.arange(1, H + 1, dtype=jnp.float32)
        distances = jnp.arange(1 - S, 1, dtype=jnp.float32)
        bias = slopes[:, None] * distances[None, :]
        return np.asarray(bias), np.asarray(slopes)


def _recon_tiles_for(rank, w):
    tiles = []
    if w < 2:
        for t in range(TPC):
            oc, hcw = t // 2, t % 2
            tiles.append((oc * 128, (2 * rank + hcw) * 128, D))
    else:
        for t in range(TPC):
            icw, hpc = t // 16, t % 16
            tiles.append((hpc * 128, (8 * rank + icw) * 128, INTER))
    return tiles


def _build_static():
    if _static:
        return _static
    t64 = np.linspace(0.0, NCTRL - 1.0, NS, dtype=np.float64)
    i_arr = np.clip(np.floor(t64).astype(np.int64), 0, NCTRL - 2)
    f_arr = (t64 - i_arr).astype(np.float32)
    del t64
    bias_f32, slopes = _alibi_bias_f32()

    perms, btabs, rpts, masks = [], [], [], []
    for c in range(NCORES):
        par = c % 2
        blocks = [2 * k + par for k in range(8)] + [2 * k + 1 - par for k in range(8)]
        perms.append(blocks)
        gl = np.concatenate([np.arange(b * 128, b * 128 + 128) for b in blocks])
        kglob, qglob = gl, gl[:TOK]
        btab = np.empty((128, H * 16), np.float32)
        for h in range(H):
            for kc in range(16):
                btab[:, h * 16 + kc] = bias_f32[h, kglob[kc * 128:(kc + 1) * 128]]
        btabs.append(btab)
        rpt = np.empty((H, 2, 128, 512), np.float32)
        for h in range(H):
            row = (slopes[h] * (np.float32(S - 1) - qglob.astype(np.float32))
                   ).astype(np.float32)
            rpt[h, 0] = np.broadcast_to(row[:512], (128, 512))
            rpt[h, 1] = np.broadcast_to(row[512:], (128, 512))
        rpts.append(rpt)
        mk = np.zeros((len(MASK_PAIRS), 128, 512), np.float32)
        for mi, (tile, kc) in enumerate(MASK_PAIRS):
            kpos = kglob[kc * 128:(kc + 1) * 128]
            qpos = qglob[tile * 512:(tile + 1) * 512]
            mk[mi][kpos[:, None] > qpos[None, :]] = FMIN
        masks.append(mk)

    cps_idx, cpps, fplans = [], [], []
    u128 = np.arange(128)
    for c in range(NCORES):
        ci = np.empty((3, 128, TPC, 5), np.int64)
        cp = np.empty((3, 128, TPC), np.float32)
        fp = np.empty((3, 128, TPC, 128), np.float32)
        for w in range(3):
            for t, (row0, col0, IN) in enumerate(_recon_tiles_for(c, w)):
                j0 = (row0 + u128) * IN + col0
                ib = i_arr[j0]
                iv = i_arr[j0[:, None] + u128[None, :]]
                dlt = iv - ib[:, None]
                assert dlt.min() >= 0 and dlt.max() <= 1
                adv = dlt.argmax(axis=1).astype(np.float32)
                adv[dlt.max(axis=1) == 0] = 128.0
                cp[w, :, t] = adv
                for m in range(5):
                    ci[w, :, t, m] = np.clip(ib + m - 1, 0, NCTRL - 1)
                fp[w, :, t, :] = f_arr[j0[:, None] + u128[None, :]]
        cps_idx.append(ci)
        cpps.append(cp)
        fplans.append(fp.reshape(3, 128, TPC * 128))

    _static.update(dict(
        perms=perms, btabs=btabs, rpts=rpts, masks=masks,
        cps_idx=cps_idx, cpps=cpps, fplans=fplans,
        iota=np.ascontiguousarray(
            np.broadcast_to(np.arange(128, dtype=np.float32), (128, 128))),
        ident=np.eye(128, dtype=np.float32),
        misc=np.concatenate([
            np.ones((128, 129), np.float32),
            np.full((128, 1), np.float32(1e-5), np.float32),
            np.full((128, 1), np.float32(SQRT_HD), np.float32)], axis=1),
    ))
    return _static



class Ctr:
    def __init__(self, nc, name):
        self.h = nc.alloc_semaphore(name)
        self.n = 0

    def bump(self, k):
        self.n += k
        return self.n


def build_nc():
    nc = bass.Bass("TRN2", target_bir_lowering=False, debug=False,
                   detect_race_conditions=False)

    din = lambda n, sh: nc.dram_tensor(n, sh, F32, kind="ExternalInput")
    x_in = din("x", [S, D])
    wqb_in = din("wqb", [16, 16, 128, 128])
    wkb_in = din("wkb", [16, 16, 128, 128])
    wvb_in = din("wvb", [16, 16, 128, 128])
    wob_in = din("wob", [16, 16, 128, 128])
    btab_in = din("btab", [128, H * 16])
    rpt_in = din("rpt", [H, 2, 128, 512])
    mask_in = din("maskt", [len(MASK_PAIRS), 128, 512])
    bq_in = din("bqt", [128, 16])
    bk_in = din("bkt", [128, 16])
    bv_in = din("bvt", [128, 16])
    bo_in = din("bot", [128, 16])
    cp_in = din("cpst", [3, 128, TPC * 5])
    cpp_in = din("cppt", [3, 128, TPC])
    fpl_in = din("fplt", [3, 128, TPC * 128])
    iota_in = din("iota", [128, 128])
    ident_in = din("ident", [128, 128])
    misc_in = din("misc", [128, 131])

    y_out = nc.dram_tensor("y", [TOK, D], F32, kind="ExternalOutput")
    if DEBUG:
        dbg_xno = nc.dram_tensor("dbg_xno", [128, 16 * TOK], F32, kind="ExternalOutput")
        dbg_kt = nc.dram_tensor("dbg_kt", [D, S], F32, kind="ExternalOutput")
        dbg_vt = nc.dram_tensor("dbg_vt", [D, S], F32, kind="ExternalOutput")
        dbg_qt = nc.dram_tensor("dbg_qt", [D, TOK], F32, kind="ExternalOutput")
        dbg_at = nc.dram_tensor("dbg_at", [D, TOK], F32, kind="ExternalOutput")
        dbg_ag = nc.dram_tensor("dbg_ag", [SHARD], F32, kind="ExternalOutput")
        dbg_ago = nc.dram_tensor("dbg_ago", [NCORES, SHARD], F32, kind="ExternalOutput")

    BF16 = mybir.dt.bfloat16
    ag_in_gu = nc.dram_tensor("ag_in_gu", [2 * TPC * 2 * TILE_EL], BF16)
    ag_out_gu = nc.dram_tensor("ag_out_gu", [NCORES, 2 * TPC * 2 * TILE_EL], BF16,
                               addr_space="Shared")
    ag_in_d = nc.dram_tensor("ag_in_d", [TPC * 2 * TILE_EL], BF16)
    ag_out_d = nc.dram_tensor("ag_out_d", [NCORES, TPC * 2 * TILE_EL], BF16,
                              addr_space="Shared")
    kt_sc = nc.dram_tensor("kt_sc", [D, S], F32)
    vt_sc = nc.dram_tensor("vt_sc", [D, S], F32)
    qt_sc = nc.dram_tensor("qt_sc", [D, TOK], F32)
    xno_sc = nc.dram_tensor("xno_sc", [128, 2 * 16 * TOK], mybir.dt.bfloat16)
    at_sc = nc.dram_tensor("at_sc", [D, TOK], F32)

    BF16 = mybir.dt.bfloat16
    A_BIG = (nc._sbuf_addr_for_side("left") + 31) & ~31
    A_SMALL = A_BIG + 139264
    BIG = nc.alloc_sbuf_tensor_at("BIG", [128, 34816], F32, offset=A_BIG)
    SMALL = nc.alloc_sbuf_tensor_at("SMALL", [128, 15360], F32,
                                    offset=A_SMALL)
    CON = nc.alloc_sbuf_tensor_at("CON", [128, 768], F32,
                                  offset=A_SMALL + 61440)
    sb_at = lambda n, cols, off: nc.alloc_sbuf_tensor_at(
        n, [128, cols], BF16, offset=off)
    XNTH = [sb_at("XNTH0", 16384, A_BIG), sb_at("XNTH1", 16384, A_BIG + 32768)]
    XNTL = [sb_at("XNTL0", 16384, A_BIG + 65536),
            sb_at("XNTL1", 16384, A_BIG + 98304)]
    CTXH = sb_at("CTXH", 16384, A_BIG + 65536)
    CTXL = sb_at("CTXL", 16384, A_BIG + 98304)
    INTH = [sb_at("INTH0", 16384, A_BIG), sb_at("INTH1", 16384, A_BIG + 32768)]
    INTL = [sb_at("INTL0", 16384, A_BIG + 65536),
            sb_at("INTL1", 16384, A_BIG + 98304)]
    IDB = sb_at("IDBal", 128, A_BIG + 134144)
    WPAIR = sb_at("WPAIR", 8192, A_SMALL + 8192)
    RBH = sb_at("RBH", 1024, A_SMALL + 40960)
    RBL = sb_at("RBL", 1024, A_SMALL + 45056)
    STG = sb_at("STG", 4096, A_SMALL + 49152)
    XNHH = sb_at("XNHH", 8192, A_SMALL + 8192)
    XNHL = sb_at("XNHL", 8192, A_SMALL + 24576)
    FSL = sb_at("FSL", 8192, A_SMALL + 40960)
    FDSL = sb_at("FDSL", 8192, A_SMALL)
    big = lambda o, n: BIG[:, o:o + n]
    small = lambda o, n: SMALL[:, o:o + n]

    c_ident = CON[:, 0:128]
    c_iota = CON[:, 128:256]
    c_btab = CON[:, 256:512]
    c_bq = CON[:, 512:528]
    c_bk = CON[:, 528:544]
    c_bv = CON[:, 544:560]
    c_bo = CON[:, 560:576]
    c_misc = CON[:, 576:707]
    c_onesr = c_misc[:, 0:128]
    c_ones = c_misc[:, 128:129]
    c_eps = c_misc[:, 129:130]
    c_sqrt = c_misc[:, 130:131]
    c_rsq = CON[:, 707:708]

    p0 = nc.alloc_psum_tensor("p0", [128, 512], F32)
    p1 = nc.alloc_psum_tensor("p1", [128, 512], F32)
    p2 = nc.alloc_psum_tensor("p2", [128, 512], F32)
    p3 = nc.alloc_psum_tensor("p3", [128, 512], F32)
    pb = nc.alloc_psum_tensor("pb", [128, 1024], BF16)
    p5 = nc.alloc_psum_tensor("p5", [128, 512], F32)
    p6 = nc.alloc_psum_tensor("p6", [128, 512], F32)
    p7 = nc.alloc_psum_tensor("p7", [128, 512], F32)

    Cm = lambda n: Ctr(nc, n)
    d_con = Cm("d_con")
    d_f = [Cm("d_f0"), Cm("d_f1")]
    d_x = [Cm("d_x0"), Cm("d_x1")]
    d_w = [Cm("d_w0"), Cm("d_w1")]
    d_h = [Cm("d_h0"), Cm("d_h1")]
    d_g = [Cm("d_g0"), Cm("d_g1")]
    d_u = [Cm("d_u0"), Cm("d_u1")]
    d_d = [Cm("d_d0"), Cm("d_d1")]
    d_a = [Cm("d_a0"), Cm("d_a1")]
    d_xr = Cm("d_xr")
    d_xh = Cm("d_xh")
    d_sp = Cm("d_sp")
    d_y = Cm("d_y")
    s_cc = Cm("s_cc")
    d_rc = Cm("d_rc")
    s_x = Cm("s_x")
    s_t = Cm("s_t")
    s_v = Cm("s_v")
    s_s = Cm("s_s")

    with nc.Block() as blk:
        @blk.sync
        def _(e):
            for dst, src in ((c_ident, ident_in), (c_iota, iota_in),
                             (c_btab, btab_in), (c_bq, bq_in), (c_bk, bk_in),
                             (c_bv, bv_in), (c_bo, bo_in), (c_misc, misc_in)):
                e.dma_start(out=dst, in_=src.ap()[:, :]).then_inc(d_con.h, 16)
                d_con.bump(16)

        @blk.vector
        def _(e):
            e.wait_ge(d_con.h, d_con.n)
            e.reciprocal(out=c_rsq, in_=c_sqrt)
            e.tensor_copy(out=IDB[:, :], in_=c_ident)
    nc.const_aps.register(F32, EPS, c_eps[:, 0:1])

    RTC = 8
    RNCH = TPC // RTC
    NRCH = 3 * RNCH

    xh3qp = [t.rearrange("p (fc t) -> p fc t", fc=8) for t in XNTH]
    xl3qp = [t.rearrange("p (fc t) -> p fc t", fc=8) for t in XNTL]
    ln_x = [small(0, 2048), small(2048, 2048)]
    ln_t = small(4096, 2048)
    ln_s = small(6144, 2048)
    ln_c = small(8192, 8)

    T0, V0, S0 = s_t.n, s_v.n, s_s.n
    LNV = lambda tc, i: V0 + tc * 10 + i
    LNS = lambda tc, i: S0 + tc * 6 + i
    LNT = lambda tc, g: T0 + tc * 4 + g + 1

    with nc.Block() as blk:
        @blk.sync
        def _(e):
            for tc in range(16):
                if tc >= 2:
                    e.wait_ge(s_t.h, LNT(tc - 2, 3))
                e.dma_start(out=ln_x[tc % 2],
                            in_=x_in.ap()[tc * 128:(tc + 1) * 128, :]
                            ).then_inc(d_x[tc % 2].h, 16)
                d_x[tc % 2].bump(16)

        @blk.vector
        def _(e):
            for tc in range(16):
                e.wait_ge(d_x[tc % 2].h, 16 * (tc // 2 + 1))
                xs = ln_x[tc % 2]
                e.reduce_sum(out=ln_c[:, 0:1], in_=xs, axis=AXX).then_inc(s_v.h, 1)
                e.wait_ge(s_v.h, s_v.bump(1))
                e.tensor_scalar_mul(ln_c[:, 1:2], ln_c[:, 0:1], 1.0 / 2048.0
                                    ).then_inc(s_v.h, 1)
                e.wait_ge(s_v.h, s_v.bump(1))
                e.tensor_scalar(out=ln_t, in0=xs, scalar1=ln_c[:, 1:2],
                                scalar2=None, op0=ALU.subtract).then_inc(s_v.h, 1)
                s_v.bump(1)
                e.wait_ge(s_s.h, LNS(tc, 1))
                e.reduce_sum(out=ln_c[:, 0:1], in_=ln_s, axis=AXX).then_inc(s_v.h, 1)
                s_v.bump(1)
                e.wait_ge(s_s.h, LNS(tc, 2))
                e.reciprocal(out=ln_c[:, 3:4], in_=ln_c[:, 2:3]).then_inc(s_v.h, 1)
                e.wait_ge(s_v.h, s_v.bump(1))
                e.tensor_scalar(out=xs, in0=ln_t, scalar1=ln_c[:, 3:4],
                                scalar2=None, op0=ALU.mult).then_inc(s_v.h, 1)
                s_v.bump(1)
                for g in range(4):
                    e.wait_ge(s_s.h, LNS(tc, 2 + g + 1))
                    gs = (g % 2) * 4
                    dsth = xh3qp[g // 2][:, gs:gs + 4, tc * 128:(tc + 1) * 128]
                    dstl = xl3qp[g // 2][:, gs:gs + 4, tc * 128:(tc + 1) * 128]
                    e.tensor_tensor(out=dstl,
                                    in0=p6.rearrange("p (j t) -> p j t", j=4),
                                    in1=dsth, op=ALU.subtract).then_inc(s_v.h, 1)
                    s_v.bump(1)

        @blk.scalar
        def _(e):
            for tc in range(16):
                e.wait_ge(s_v.h, LNV(tc, 3))
                e.activation(out=ln_s, in_=ln_t, func=ACT.Square).then_inc(s_s.h, 1)
                s_s.bump(1)
                e.wait_ge(s_v.h, LNV(tc, 4))
                e.activation(out=ln_c[:, 2:3], in_=ln_c[:, 0:1], func=ACT.Sqrt,
                             scale=1.0 / 2048.0, bias=EPS).then_inc(s_s.h, 1)
                s_s.bump(1)
                for g in range(4):
                    e.wait_ge(s_t.h, LNT(tc, g))
                    gs = (g % 2) * 4
                    dsth = xh3qp[g // 2][:, gs:gs + 4, tc * 128:(tc + 1) * 128]
                    e.activation(out=dsth,
                                 in_=p6.rearrange("p (j t) -> p j t", j=4),
                                 func=ACT.Copy).then_inc(s_s.h, 1)
                    s_s.bump(1)

        @blk.tensor
        def _(e):
            for tc in range(16):
                e.wait_ge(s_v.h, LNV(tc, 6))
                for g in range(4):
                    prev = LNV(tc, 6 + g) if g >= 1 else (
                        LNV(tc - 1, 10) if tc >= 1 else None)
                    if prev is not None:
                        e.wait_ge(s_v.h, prev)
                    for j in range(4):
                        fc = g * 4 + j
                        ins = e.transpose(p6[:, j * 128:(j + 1) * 128],
                                          ln_x[tc % 2][:, fc * 128:(fc + 1) * 128],
                                          c_ident)
                    ins.then_inc(s_t.h, 1)
                    s_t.bump(1)

    with nc.Block() as blk:
        @blk.gpsimd
        def _(e):
            for pc in range(2):
                e.dma_start(out=xno_sc.ap()[:, pc * 8 * TOK:(pc + 1) * 8 * TOK]
                            .rearrange("p (fc t) -> p fc t", fc=8),
                            in_=xh3qp[pc][:, :, 0:TOK]).then_inc(d_sp.h, 16)
                d_sp.bump(16)
                e.dma_start(out=xno_sc.ap()[:, (16 + pc * 8) * TOK:
                                            (16 + (pc + 1) * 8) * TOK]
                            .rearrange("p (fc t) -> p fc t", fc=8),
                            in_=xl3qp[pc][:, :, 0:TOK]).then_inc(d_sp.h, 16)
                d_sp.bump(16)

    w_slab = small(0, 2048)
    w_pair = [WPAIR[:, 0:4096], WPAIR[:, 4096:8192]]
    pr_st = [small(6144, 512), small(6656, 512)]
    r_f = [small(7168, 1024), small(8192, 1024)]
    r_acc = small(9216, 1024)
    r_bac = small(10240, 1024)
    r_dlt = small(11264, 1024)
    t_q = small(14336, 1024)
    t_cps = big(32768, 640)
    t_cpp = big(33408, 128)

    s_rv = Cm("s_rv")
    s_rt = Cm("s_rt")
    d_ag = [Cm("d_ag0"), Cm("d_ag1")]
    RPREP, REVAL, RCAST, REVA, RT = {}, {}, {}, {}, {}
    _rv = 0
    _rt = 0
    for c in range(NRCH):
        w, t = c // RNCH, c % RNCH
        if t == 0:
            _rv += 1
            RPREP[w] = _rv
        _rv += 1
        REVAL[c] = _rv
        _rv += 2
        RCAST[c] = _rv
        REVA[c] = []
        for g in range(4):
            _rv += 1
            REVA[c].append(_rv)
        RT[c] = []
        for g in range(4):
            _rt += 1
            RT[c].append(_rt)

    def emit_recon_tables(e, wv):
        e.dma_start(out=t_cps, in_=cp_in.ap()[wv, :, :]).then_inc(d_rc.h, 16)
        d_rc.bump(16)
        e.dma_start(out=t_cpp, in_=cpp_in.ap()[wv, :, :]).then_inc(d_rc.h, 16)
        d_rc.bump(16)
        if wv == 0:
            e.dma_start(out=r_f[0], in_=fpl_in.ap()[0, :, 0:RTC * 128]
                        ).then_inc(d_f[0].h, 16)
            d_f[0].bump(16)

    def emit_recon_sync(e, cl):
        for c in cl:
            nx = c + 1
            if nx >= NRCH:
                continue
            w, t = nx // RNCH, nx % RNCH
            if nx >= 2:
                e.wait_ge(s_rv.h, REVAL[nx - 2])
            e.dma_start(out=r_f[nx % 2],
                        in_=fpl_in.ap()[w, :, t * RTC * 128:(t + 1) * RTC * 128]
                        ).then_inc(d_f[nx % 2].h, 16)
            d_f[nx % 2].bump(16)

    def emit_recon_prep(e, w):
        if True:
            if True:
                e.wait_ge(d_rc.h, 32 * (w + 1))
                cv = t_cps.rearrange("p (tt m) -> p tt m", m=5)
                sv = [cv[:, :, m] for m in range(5)]
                q = [t_q[:, k * TPC:(k + 1) * TPC] for k in range(8)]



                tmp = r_acc
                tcol = lambda i: tmp[:, i * TPC:(i + 1) * TPC]
                ops = []

                def plan(dst, a, b, cc, d, base):
                    u = [tcol(base + i) for i in range(4)]
                    ops.append(("c", dst[0], b))
                    ops.append(("s", u[0], cc, a, ALU.subtract))
                    ops.append(("m", dst[1], u[0], 0.5))
                    ops.append(("m", u[1], b, -2.5))
                    ops.append(("s", u[1], u[1], a, ALU.add))
                    ops.append(("m", u[2], cc, 2.0))
                    ops.append(("s", u[1], u[1], u[2], ALU.add))
                    ops.append(("m", u[2], d, -0.5))
                    ops.append(("s", dst[2], u[1], u[2], ALU.add))
                    ops.append(("m", u[3], b, 1.5))
                    ops.append(("m", u[0], a, -0.5))
                    ops.append(("s", u[3], u[3], u[0], ALU.add))
                    ops.append(("m", u[0], cc, -1.5))
                    ops.append(("s", u[3], u[3], u[0], ALU.add))
                    ops.append(("m", u[0], d, 0.5))
                    ops.append(("s", dst[3], u[3], u[0], ALU.add))

                plan([q[0], q[1], q[2], q[3]], sv[0], sv[1], sv[2], sv[3], 0)
                opsA = ops
                ops = []
                plan([q[4], q[5], q[6], q[7]], sv[1], sv[2], sv[3], sv[4], 4)
                opsB = ops
                seq = []
                for i in range(max(len(opsA), len(opsB))):
                    if i < len(opsA):
                        seq.append(opsA[i])
                    if i < len(opsB):
                        seq.append(opsB[i])
                for k in range(4):
                    seq.append(("s", q[4 + k], q[4 + k], q[k], ALU.subtract))
                last = None
                for op in seq:
                    if op[0] == "c":
                        last = e.tensor_copy(out=op[1], in_=op[2])
                    elif op[0] == "m":
                        last = e.tensor_scalar_mul(op[1], op[2], op[3])
                    else:
                        last = e.tensor_tensor(out=op[1], in0=op[2], in1=op[3],
                                               op=op[4])
                last.then_inc(s_rv.h, 1)
                s_rv.bump(1)
    def emit_recon_eval(e, cl):
        for c in cl:
            w, t = c // RNCH, c % RNCH
            sl = c % 2
            t0 = t * RTC
            e.wait_ge(d_f[sl].h, rf_fill_at[c])
            if c >= 1:
                e.wait_ge(s_rt.h, RT[c - 1][1])
            fb = r_f[sl].rearrange("p (tt u) -> p tt u", tt=RTC)
            acc = r_acc.rearrange("p (tt u) -> p tt u", tt=RTC)
            bac = r_bac.rearrange("p (tt u) -> p tt u", tt=RTC)
            dlt = r_dlt.rearrange("p (tt u) -> p tt u", tt=RTC)
            qv = lambda k: t_q[:, k * TPC + t0:k * TPC + t0 + RTC] \
                .unsqueeze(2).broadcast_to([128, RTC, 128])
            cpv = t_cpp[:, t0:t0 + RTC].unsqueeze(2).broadcast_to([128, RTC, 128])
            iov = c_iota.unsqueeze(1).broadcast_to([128, RTC, 128])
            e.tensor_tensor(out=dlt, in0=iov, in1=cpv, op=ALU.is_ge)
            e.tensor_tensor(out=acc, in0=fb, in1=qv(3), op=ALU.mult)
            e.tensor_tensor(out=bac, in0=fb, in1=qv(7), op=ALU.mult)
            e.tensor_tensor(out=acc, in0=acc, in1=qv(2), op=ALU.add)
            e.tensor_tensor(out=bac, in0=bac, in1=qv(6), op=ALU.add)
            e.tensor_tensor(out=acc, in0=acc, in1=fb, op=ALU.mult)
            e.tensor_tensor(out=bac, in0=bac, in1=fb, op=ALU.mult)
            e.tensor_tensor(out=acc, in0=acc, in1=qv(1), op=ALU.add)
            e.tensor_tensor(out=bac, in0=bac, in1=qv(5), op=ALU.add)
            e.tensor_tensor(out=acc, in0=acc, in1=fb, op=ALU.mult)
            e.tensor_tensor(out=bac, in0=bac, in1=fb, op=ALU.mult)
            e.tensor_tensor(out=acc, in0=acc, in1=qv(0), op=ALU.add)
            e.tensor_tensor(out=bac, in0=bac, in1=qv(4), op=ALU.add
                            ).then_inc(s_x.h, 1)
            e.wait_ge(s_x.h, s_x.bump(1))
            e.tensor_tensor(out=bac, in0=bac, in1=dlt, op=ALU.mult
                            ).then_inc(s_x.h, 1)
            e.wait_ge(s_x.h, s_x.bump(1))
            e.tensor_tensor(out=acc, in0=acc, in1=bac, op=ALU.add
                            ).then_inc(s_rv.h, 1)
            e.wait_ge(s_rv.h, s_rv.bump(1))
            rbhi = RBH[:, :]
            rblo = RBL[:, :]
            e.tensor_copy(out=rbhi, in_=r_acc).then_inc(s_x.h, 1)
            e.wait_ge(s_x.h, s_x.bump(1))
            e.tensor_tensor(out=rblo, in0=r_acc, in1=rbhi, op=ALU.subtract
                            ).then_inc(s_rv.h, 1)
            s_rv.bump(1)
            e.engine_nop().then_inc(s_rv.h, 1)
            s_rv.bump(1)

    def emit_recon_stevac(e, cl):
        for c in cl:
            stg4 = STG[:, (c % 2) * 2048:(c % 2) * 2048 + 2048] \
                .rearrange("p (tt x f) -> p tt x f", tt=8, x=2, f=128)
            for g in range(4):
                e.wait_ge(s_rt.h, RT[c][g])
                if g == 0 and c >= 2:
                    e.wait_ge(d_ag[c % 2].h, ag_fill_at[c])
                xx, g2 = g // 2, g % 2
                e.tensor_copy(out=stg4[:, g2 * 4:(g2 + 1) * 4, xx, :],
                              in_=pb[:, 0:512]
                              .rearrange("p (j f) -> p j f", j=4)
                              ).then_inc(s_rv.h, 1)
                s_rv.bump(1)


    def emit_recon_tensor(e, cl):
        for c in cl:
            e.wait_ge(s_rv.h, RCAST[c])
            rbhi = RBH[:, :]
            rblo = RBL[:, :]
            for g in range(4):
                srcb = (rbhi, rblo)[g // 2]
                if g >= 1:
                    e.wait_ge(s_rv.h, REVA[c][g - 1])
                elif c >= 1:
                    e.wait_ge(s_rv.h, REVA[c - 1][3])
                identb = IDB[:, :]
                p4b = pb
                for j in range(4):
                    ti = (g % 2) * 4 + j
                    ins = e.transpose(p4b[:, j * 128:(j + 1) * 128],
                                      srcb[:, ti * 128:(ti + 1) * 128], identb)
                ins.then_inc(s_rt.h, 1)
                s_rt.bump(1)

    def emit_recon_gpsimd(e, cl):
        for c in cl:
            w, t = c // RNCH, c % RNCH
            e.wait_ge(s_rv.h, REVA[c][3])
            PTILE = 2 * TILE_EL
            dst = (ag_in_gu.ap()[(w * TPC + t * RTC) * PTILE:
                                 (w * TPC + (t + 1) * RTC) * PTILE]
                   if w < 2 else
                   ag_in_d.ap()[t * RTC * PTILE:(t + 1) * RTC * PTILE])
            stg = STG[:, (c % 2) * 2048:(c % 2) * 2048 + 2048]
            e.dma_start(out=dst.rearrange("(tx p f) -> p tx f",
                                          tx=2 * RTC, p=128, f=128),
                        in_=stg.rearrange("p (tx f) -> p tx f", tx=2 * RTC),
                        ).then_inc(d_ag[c % 2].h, 16)
            d_ag[c % 2].bump(16)
            if c == NRCH - 1:
                e.wait_ge(d_ag[0].h, d_ag[0].n)
                e.wait_ge(d_ag[1].h, d_ag[1].n)
                e.collective_compute(
                    "AllGather", ALU.bypass,
                    replica_groups=[list(range(NCORES))],
                    ins=[ag_in_gu.ap().opt()],
                    outs=[ag_out_gu.ap().opt()],
                ).then_inc(s_cc.h, 1)
                s_cc.bump(1)
            if c == NRCH - 1:
                e.wait_ge(d_ag[0].h, d_ag[0].n)
                e.wait_ge(d_ag[1].h, d_ag[1].n)
                e.collective_compute(
                    "AllGather", ALU.bypass,
                    replica_groups=[list(range(NCORES))],
                    ins=[ag_in_d.ap().opt()],
                    outs=[ag_out_d.ap().opt()],
                ).then_inc(s_cc.h, 1)
                s_cc.bump(1)

    rf_fill_at = {}
    _f = [0, 0]
    _f[0] = 16
    rf_fill_at[0] = 16
    for nx in range(1, NRCH):
        _f[nx % 2] += 16
        rf_fill_at[nx] = _f[nx % 2]
    ag_fill_at = {}
    _a = [0, 0]
    for c in range(NRCH):
        ag_fill_at[c] = _a[c % 2] if c >= 2 else 0
        _a[c % 2] += 16


    def proj(wten, bias_col, n_tt, dst_sc, div, rhs_fn, rw=None):
        TT0, VV0, SS0, SP0 = s_t.n, s_v.n, s_s.n, d_sp.n
        W0 = d_w[0].n
        PT = lambda idx: TT0 + idx + 1
        PVC = lambda dc: VV0 + dc * (n_tt + 1) + 1
        PV = lambda idx: VV0 + (idx // n_tt) * (n_tt + 1) + 1 + (idx % n_tt) + 1
        PSH = lambda dc: SS0 + dc + 1
        PS = lambda idx: SP0 + 16 * (idx + 1)
        rch = lambda dc: ([rw * RNCH + dc] if rw is not None else [])
        with nc.Block() as blk:
            @blk.sync
            def _(e):
                if rw is not None:
                    emit_recon_tables(e, rw)
                for dc in range(16):
                    if dc >= 1:
                        e.wait_ge(s_v.h, PVC(dc - 1))
                    e.dma_start(out=w_slab
                                .rearrange("p (k f) -> p k f", k=16),
                                in_=wten.ap()[dc].rearrange("k p f -> p k f")
                                ).then_inc(d_w[0].h, 16)
                    d_w[0].bump(16)
                    emit_recon_sync(e, rch(dc))

            @blk.scalar
            def _(e):
                for dc in range(16):
                    e.wait_ge(d_w[0].h, W0 + 16 * (dc + 1))
                    if dc >= 2:
                        e.wait_ge(s_t.h, PT((dc - 1) * n_tt - 1))
                    e.activation(out=w_pair[dc % 2][:, 0:2048], in_=w_slab,
                                 func=ACT.Copy).then_inc(s_s.h, 1)
                    s_s.bump(1)

            @blk.tensor
            def _(e):
                for dc in range(16):
                    e.wait_ge(s_v.h, PVC(dc))
                    for tt in range(n_tt):
                        idx = dc * n_tt + tt
                        ps = (p0, p1)[idx % 2]
                        if idx >= 2:
                            e.wait_ge(s_v.h, PV(idx - 2))
                        wp = w_pair[dc % 2]
                        for hc in range(16):
                            wh = wp[:, hc * 128:(hc + 1) * 128]
                            wl = wp[:, 2048 + hc * 128:2048 + (hc + 1) * 128]
                            xh, xl = rhs_fn(hc, tt)
                            e.matmul(ps[:, :], wh, xh,
                                     start=(hc == 0), stop=False)
                            e.matmul(ps[:, :], wh, xl, start=False, stop=False)
                            ins = e.matmul(ps[:, :], wl, xh,
                                           start=False, stop=(hc == 15))
                        ins.then_inc(s_t.h, 1)
                        s_t.bump(1)
                    emit_recon_tensor(e, rch(dc))

            @blk.vector
            def _(e):
                if rw is not None:
                    emit_recon_prep(e, rw)
                for dc in range(16):
                    e.wait_ge(s_s.h, PSH(dc))
                    e.tensor_tensor(out=w_pair[dc % 2][:, 2048:4096],
                                    in0=w_slab, in1=w_pair[dc % 2][:, 0:2048],
                                    op=ALU.subtract).then_inc(s_v.h, 1)
                    s_v.bump(1)
                    emit_recon_eval(e, rch(dc))
                    for tt in range(n_tt):
                        idx = dc * n_tt + tt
                        ps = (p0, p1)[idx % 2]
                        e.wait_ge(s_t.h, PT(idx))
                        if idx >= 2:
                            e.wait_ge(d_sp.h, PS(idx - 2))
                        if div:
                            e.tensor_scalar(out=pr_st[idx % 2], in0=ps[:, :],
                                            scalar1=bias_col[:, dc:dc + 1],
                                            scalar2=c_rsq[:, 0:1],
                                            op0=ALU.add, op1=ALU.mult
                                            ).then_inc(s_v.h, 1)
                        else:
                            e.tensor_scalar(out=pr_st[idx % 2], in0=ps[:, :],
                                            scalar1=bias_col[:, dc:dc + 1],
                                            scalar2=None, op0=ALU.add
                                            ).then_inc(s_v.h, 1)
                        s_v.bump(1)
                    emit_recon_stevac(e, rch(dc))

            @blk.gpsimd
            def _(e):
                for dc in range(16):
                    for tt in range(n_tt):
                        idx = dc * n_tt + tt
                        e.wait_ge(s_v.h, PV(idx))
                        e.dma_start(out=dst_sc.ap()[dc * 128:(dc + 1) * 128,
                                                    tt * 512:(tt + 1) * 512],
                                    in_=pr_st[idx % 2]).then_inc(d_sp.h, 16)
                        d_sp.bump(16)
                    emit_recon_gpsimd(e, rch(dc))

    xnt_rhs = lambda hc, tt: (
        xh3qp[hc // 8][:, hc % 8, tt * 512:(tt + 1) * 512],
        xl3qp[hc // 8][:, hc % 8, tt * 512:(tt + 1) * 512])
    proj(wkb_in, c_bk, 4, kt_sc, False, xnt_rhs, rw=0)
    proj(wvb_in, c_bv, 4, vt_sc, False, xnt_rhs, rw=1)
    proj(wqb_in, c_bq, 2, qt_sc, True, xnt_rhs, rw=2)

    ctxt = big(0, 16384)
    mks = big(16384, 8192)
    x_exp = [big(24576, 512), big(25088, 512)]
    x_run = big(25600, 512)
    x_run2 = big(26112, 512)
    a_kt = [small(0, 2048), small(2048, 2048)]
    a_vt = [small(4096, 2048), small(6144, 2048)]
    a_qt = [small(8192, 1024), small(9216, 1024)]
    a_pl = [small(10240, 1024), small(11264, 1024)]
    a_vtr = small(12288, 2048)

    with nc.Block() as blk:
        @blk.sync
        def _(e):
            e.dma_start(out=mks.rearrange("p (m f) -> p m f", m=16),
                        in_=mask_in.ap().rearrange("m p f -> p m f")
                        ).then_inc(d_con.h, 16)
            d_con.bump(16)

    mask_slot = {p: i for i, p in enumerate(MASK_PAIRS)}
    ev_ctx_last = {}
    ev_norm = {}
    ev_vtrev = {}
    psc_hist = []

    AT, AV, AS = {}, {}, {}
    tn, vn, sn = s_t.n, s_v.n, s_s.n
    exp_slot_seq = []
    sc_bank_seq = []
    for h in range(H):
        for g in range(4):
            tn += 1
            AT[("vtr", h, g)] = tn
        for tile in range(2):
            kcl = (KCS0, KCS1)[tile]
            n = len(kcl)
            order = [("sc", 0), ("sc", 1)]
            for k in range(n - 2):
                order += [("ctx", k), ("sc", k + 2)]
            order += [("ctx", n - 2), ("ctx", n - 1), ("den",), ("bc",)]
            for op in order:
                tn += 1
                AT[(op[0], h, tile) + tuple(op[1:])] = tn
                if op[0] == "sc":
                    sc_bank_seq.append((h, tile, op[1]))
        for g in range(4):
            vn += 1
            AV[("vtrev", h, g)] = vn
        for tile in range(2):
            kcl = (KCS0, KCS1)[tile]
            n = len(kcl)
            order = [("arg", 0), ("arg", 1)]
            for k in range(n - 2):
                order += [("run", k), ("arg", k + 2)]
            order += [("run", n - 2), ("run", n - 1), ("recip",), ("bcev",),
                      ("norm",)]
            for op in order:
                vn += 1
                AV[(op[0], h, tile) + tuple(op[1:])] = vn
        for tile in range(2):
            n = len((KCS0, KCS1)[tile])
            for ki in range(n):
                sn += 1
                AS[("exp", h, tile, ki)] = sn
                exp_slot_seq.append((h, tile, ki))
    sc_prev2 = {key: (sc_bank_seq[i - 2] if i >= 2 else None)
                for i, key in enumerate(sc_bank_seq)}
    exp_prev2 = {key: (exp_slot_seq[i - 2] if i >= 2 else None)
                 for i, key in enumerate(exp_slot_seq)}
    bcev_order = [(h, t) for h in range(H) for t in range(2)]

    with nc.Block() as blk:
        @blk.tensor
        def _(e):
            for h in range(H):
                sl = h % 2
                e.wait_ge(d_h[sl].h, 64 * (h // 2 + 1))
                if h == 0:
                    e.wait_ge(d_con.h, d_con.n)
                for g in range(4):
                    prev = ("vtrev", h, g - 1) if g >= 1 else (
                        ("vtrev", h - 1, 3) if h >= 1 else None)
                    if prev is not None:
                        e.wait_ge(s_v.h, AV[prev])
                    for j in range(4):
                        kc = g * 4 + j
                        ins = e.transpose(p6[:, j * 128:(j + 1) * 128],
                                          a_vt[sl][:, kc * 128:(kc + 1) * 128],
                                          c_ident)
                    ins.then_inc(s_t.h, 1)
                    s_t.bump(1)
                for tile in range(2):
                    kcl = (KCS0, KCS1)[tile]
                    n = len(kcl)
                    pctx = (p2, p3)[tile]

                    def esc(ki, h=h, tile=tile, sl=sl, kcl=kcl):
                        key = (h, tile, ki)
                        prev = sc_prev2[key]
                        if prev is not None:
                            e.wait_ge(s_s.h, AS[("exp",) + prev])
                        kc = kcl[ki]
                        ps = (p0, p1)[ki % 2]
                        e.matmul(ps[:, :],
                                 a_kt[sl][:, kc * 128:(kc + 1) * 128],
                                 a_qt[sl][:, tile * 512:(tile + 1) * 512],
                                 start=True, stop=True).then_inc(s_t.h, 1)
                        s_t.bump(1)

                    def ectx(ki, h=h, tile=tile, kcl=kcl, n=n, pctx=pctx):
                        if ki == 0:
                            e.wait_ge(s_v.h, AV[("vtrev", h, 3)])
                            if h >= 1:
                                e.wait_ge(s_v.h, AV[("norm", h - 1, tile)])
                        e.wait_ge(s_s.h, AS[("exp", h, tile, ki)])
                        kc = kcl[ki]
                        e.matmul(pctx[:, :], a_vtr[:, kc * 128:(kc + 1) * 128],
                                 x_exp[ki % 2],
                                 start=(ki == 0), stop=(ki == n - 1)
                                 ).then_inc(s_t.h, 1)
                        s_t.bump(1)

                    esc(0)
                    esc(1)
                    for k in range(n - 2):
                        ectx(k)
                        esc(k + 2)
                    ectx(n - 2)
                    ectx(n - 1)
                    e.wait_ge(s_v.h, AV[("run", h, tile, n - 1)])
                    bi = bcev_order.index((h, tile))
                    if bi >= 1:
                        e.wait_ge(s_v.h, AV[("bcev",) + bcev_order[bi - 1]])
                    e.matmul(p7[0:1, :], c_ones, x_run, start=True, stop=False)
                    e.matmul(p7[0:1, :], c_ones, x_run2, start=False, stop=True
                             ).then_inc(s_t.h, 1)
                    s_t.bump(1)
                    e.wait_ge(s_v.h, AV[("recip", h, tile)])
                    e.matmul(p7[:, :], c_onesr[0:1, :], x_run[0:1, :],
                             start=True, stop=True).then_inc(s_t.h, 1)
                    s_t.bump(1)

        @blk.vector
        def _(e):
            for h in range(H):
                sl = h % 2
                for g in range(4):
                    e.wait_ge(s_t.h, AT[("vtr", h, g)])
                    if h >= 1 and g == 0:
                        e.wait_ge(s_t.h, AT[("ctx", h - 1, 1, len(KCS1) - 1)])
                    e.tensor_copy(out=a_vtr[:, g * 512:(g + 1) * 512], in_=p6[:, :]
                                  ).then_inc(s_v.h, 1)
                    s_v.bump(1)
                for tile in range(2):
                    kcl = (KCS0, KCS1)[tile]
                    n = len(kcl)
                    pctx = (p2, p3)[tile]

                    def earg(ki, h=h, tile=tile, sl=sl, kcl=kcl):
                        e.wait_ge(s_t.h, AT[("sc", h, tile, ki)])
                        kc = kcl[ki]
                        ps = (p0, p1)[ki % 2]
                        ins = e.tensor_tensor(
                            out=ps[:, :], in0=ps[:, :],
                            in1=a_pl[sl][:, tile * 512:(tile + 1) * 512],
                            op=ALU.add)
                        mi = MASK_SLOT.get((tile, kc))
                        if mi is not None:
                            ins.then_inc(s_x.h, 1)
                            e.wait_ge(s_x.h, s_x.bump(1))
                            ins = e.tensor_tensor(
                                out=ps[:, :], in0=ps[:, :],
                                in1=mks[:, mi * 512:(mi + 1) * 512], op=ALU.add)
                        ins.then_inc(s_v.h, 1)
                        s_v.bump(1)

                    def erun(ki, h=h, tile=tile, n=n):
                        e.wait_ge(s_s.h, AS[("exp", h, tile, ki)])
                        rbuf = (x_run, x_run2)[ki % 2]
                        if ki < 2:
                            bi = bcev_order.index((h, tile))
                            if bi >= 1:
                                e.wait_ge(s_t.h, AT[("den",) + bcev_order[bi - 1]])
                            e.tensor_copy(out=rbuf, in_=x_exp[ki % 2]
                                          ).then_inc(s_v.h, 1)
                        else:
                            e.tensor_tensor(out=rbuf, in0=rbuf,
                                            in1=x_exp[ki % 2], op=ALU.add
                                            ).then_inc(s_v.h, 1)
                        s_v.bump(1)

                    earg(0)
                    earg(1)
                    for k in range(n - 2):
                        erun(k)
                        earg(k + 2)
                    erun(n - 2)
                    erun(n - 1)
                    e.wait_ge(s_t.h, AT[("den", h, tile)])
                    e.reciprocal(out=x_run[0:1, :], in_=p7[0:1, :]
                                 ).then_inc(s_v.h, 1)
                    s_v.bump(1)
                    e.wait_ge(s_t.h, AT[("bc", h, tile)])
                    e.tensor_copy(out=x_run, in_=p7[:, :]).then_inc(s_x.h, 1)
                    e.wait_ge(s_x.h, s_x.bump(1))
                    e.engine_nop().then_inc(s_v.h, 1)
                    s_v.bump(1)
                    e.wait_ge(s_t.h, AT[("ctx", h, tile, n - 1)])
                    e.scalar_tensor_tensor(
                        out=ctxt[:, h * 1024 + tile * 512:h * 1024 + (tile + 1) * 512],
                        in0=pctx[:, :], scalar=1.0, in1=x_run,
                        op0=ALU.mult, op1=ALU.mult).then_inc(s_v.h, 1)
                    s_v.bump(1)

        @blk.scalar
        def _(e):
            for h in range(H):
                for tile in range(2):
                    n = len((KCS0, KCS1)[tile])
                    for ki in range(n):
                        key = (h, tile, ki)
                        e.wait_ge(s_v.h, AV[("arg", h, tile, ki)])
                        prev = exp_prev2[key]
                        if prev is not None:
                            ph, pt, pk = prev
                            e.wait_ge(s_t.h, AT[("ctx", ph, pt, pk)])
                            e.wait_ge(s_v.h, AV[("run", ph, pt, pk)])
                        ps = (p0, p1)[ki % 2]
                        e.activation(out=x_exp[ki % 2], in_=ps[:, :], func=ACT.Exp,
                                     bias=c_btab[:, h * 16 + (KCS0, KCS1)[tile][ki]:
                                                 h * 16 + (KCS0, KCS1)[tile][ki] + 1],
                                     scale=1.0).then_inc(s_s.h, 1)
                        s_s.bump(1)

        @blk.sync
        def _(e):
            for h in range(H):
                sl = h % 2
                if h >= 2:
                    e.wait_ge(s_t.h, AT[("ctx", h - 2, 1, len(KCS1) - 1)])
                if h == 0:
                    e.wait_ge(d_sp.h, d_sp.n)
                e.dma_start(out=a_kt[sl], in_=kt_sc.ap()[h * 128:(h + 1) * 128, :]
                            ).then_inc(d_h[sl].h, 16)
                d_h[sl].bump(16)
                e.dma_start(out=a_vt[sl], in_=vt_sc.ap()[h * 128:(h + 1) * 128, :]
                            ).then_inc(d_h[sl].h, 16)
                d_h[sl].bump(16)
                e.dma_start(out=a_qt[sl], in_=qt_sc.ap()[h * 128:(h + 1) * 128, :]
                            ).then_inc(d_h[sl].h, 16)
                d_h[sl].bump(16)
                e.dma_start(out=a_pl[sl].rearrange("p (t f) -> p t f", t=2),
                            in_=rpt_in.ap()[h].rearrange("t p f -> p t f")
                            ).then_inc(d_h[sl].h, 16)
                d_h[sl].bump(16)

    ctx_hi = CTXH[:, :]
    ctx_lo = CTXL[:, :]
    with nc.Block() as blk:
        @blk.scalar
        def _(e):
            e.activation(out=ctx_hi, in_=ctxt, func=ACT.Copy).then_inc(s_s.h, 1)
            s_s.bump(1)

        @blk.vector
        def _(e):
            e.wait_ge(s_s.h, s_s.n)
            e.tensor_tensor(out=ctx_lo, in0=ctxt, in1=ctx_hi, op=ALU.subtract)

    ctx_rhs = lambda hc, tt: (
        ctx_hi[:, hc * 1024 + tt * 512:hc * 1024 + (tt + 1) * 512],
        ctx_lo[:, hc * 1024 + tt * 512:hc * 1024 + (tt + 1) * 512])
    proj(wob_in, c_bo, 2, at_sc, False, ctx_rhs)

    iT_hi3p = [t.rearrange("p (ic t) -> p ic t", ic=32) for t in INTH]
    iT_lo3p = [t.rearrange("p (ic t) -> p ic t", ic=32) for t in INTL]
    ith = lambda ic: iT_hi3p[ic // 32][:, ic % 32, :]
    itl = lambda ic: iT_lo3p[ic // 32][:, ic % 32, :]
    f_si = [big(32768, 512), big(33280, 512)]
    f_imt = [big(33792, 512), big(34304, 512)]
    xnh_hi = XNHH[:, :]
    xnh_lo = XNHL[:, :]
    xh3 = xnh_hi.rearrange("p (hc t) -> p hc t", hc=16)
    xl3 = xnh_lo.rearrange("p (hc t) -> p hc t", hc=16)
    f_sl = [FSL[:, 0:4096],
            FSL[:, 4096:8192]]
    f_dsl = [FDSL[:, 0:4096],
             FDSL[:, 4096:8192]]
    f_at = [small(4096, 512), small(4608, 512)]
    f_sum = small(5120, 8192)
    f_os = [small(0, 2048), small(2048, 2048)]
    f_xt = small(13312, 2048)
    PTILE = 2 * TILE_EL

    def gate_slab_ap(w, ic, k, x):
        base = (w * TPC + ic * 2 + k) * PTILE + x * TILE_EL
        return ag_out_gu.ap()[:, base:base + TILE_EL] \
            .rearrange("r (p f) -> p r f", p=128, f=128)

    def down_slab_ap(hp, pc, r, x):
        return ag_out_d.ap()[2 * pc + r, :] \
            .rearrange("(icw hq x p f) -> icw hq x p f",
                       icw=8, hq=16, x=2, p=128, f=128)[:, hp, x] \
            .rearrange("icw p f -> p icw f")

    def slg(t, k, x):
        return t.rearrange("p (hc x f) -> p hc x f", hc=16, x=2)[:, k::2, x, :]

    def sld(t, r, x):
        return t.rearrange("p (r icw x f) -> p r icw x f",
                           r=2, icw=8, x=2, f=128)[:, r, :, x, :]

    for th in range(2):
        TT0, VV0, SS0 = s_t.n, s_v.n, s_s.n
        G0, U0 = d_g[0].n, d_g[1].n
        FTg = lambda ic: TT0 + 2 * ic + 1
        FTu = lambda ic: TT0 + 2 * ic + 2
        FV1 = lambda ic: VV0 + 2 * ic + 1
        FV2 = lambda ic: VV0 + 2 * ic + 2
        FS1 = lambda ic: SS0 + 2 * ic + 1
        FS2 = lambda ic: SS0 + 2 * ic + 2
        XH0 = d_xh.n
        with nc.Block() as blk:
            @blk.sync
            def _(e, th=th):
                e.dma_start(out=xh3,
                            in_=xno_sc.ap()[:, 0:16 * TOK]
                            .rearrange("p (hc t) -> p hc t", hc=16)
                            [:, :, th * 512:(th + 1) * 512]).then_inc(d_xh.h, 16)
                d_xh.bump(16)
                e.dma_start(out=xl3,
                            in_=xno_sc.ap()[:, 16 * TOK:]
                            .rearrange("p (hc t) -> p hc t", hc=16)
                            [:, :, th * 512:(th + 1) * 512]).then_inc(d_xh.h, 16)
                d_xh.bump(16)

            @blk.scalar
            def _(e, th=th):
                for ic in range(64):
                    e.wait_ge(s_t.h, FTg(ic))
                    if ic >= 2:
                        e.wait_ge(s_v.h, FV1(ic - 2))
                    pg = (p0, p1)[ic % 2]
                    e.activation(out=f_si[ic % 2], in_=pg[:, :], func=ACT.Silu
                                 ).then_inc(s_s.h, 1)
                    s_s.bump(1)
                    e.wait_ge(s_v.h, FV1(ic))
                    e.activation(out=ith(ic), in_=f_imt[ic % 2],
                                 func=ACT.Copy).then_inc(s_s.h, 1)
                    s_s.bump(1)

            @blk.vector
            def _(e, th=th):
                for ic in range(64):
                    e.wait_ge(s_s.h, FS1(ic))
                    e.wait_ge(s_t.h, FTu(ic))
                    if ic >= 2:
                        e.wait_ge(s_s.h, FS2(ic - 2))
                    pu = (p2, p3)[ic % 2]
                    e.tensor_tensor(out=f_imt[ic % 2], in0=f_si[ic % 2],
                                    in1=pu[:, :], op=ALU.mult).then_inc(s_v.h, 1)
                    s_v.bump(1)
                    e.wait_ge(s_s.h, FS2(ic))
                    e.tensor_tensor(out=itl(ic), in0=f_imt[ic % 2],
                                    in1=ith(ic), op=ALU.subtract
                                    ).then_inc(s_v.h, 1)
                    s_v.bump(1)

            @blk.gpsimd
            def _(e, th=th):
                if th == 0:
                    e.wait_ge(s_cc.h, 1)
                for ic in range(64):
                    if ic >= 1:
                        e.wait_ge(s_t.h, FTg(ic - 1))
                    for kkk in range(2):
                        for xx in range(2):
                            e.dma_start(out=slg(f_sl[0], kkk, xx),
                                        in_=gate_slab_ap(0, ic, kkk, xx)
                                        ).then_inc(d_g[0].h, 16)
                            d_g[0].bump(16)
                    if ic >= 1:
                        e.wait_ge(s_t.h, FTu(ic - 1))
                    for kkk in range(2):
                        for xx in range(2):
                            e.dma_start(out=slg(f_sl[1], kkk, xx),
                                        in_=gate_slab_ap(1, ic, kkk, xx)
                                        ).then_inc(d_g[1].h, 16)
                            d_g[1].bump(16)

            @blk.tensor
            def _(e, th=th):
                e.wait_ge(d_xh.h, XH0 + 32)
                for ic in range(64):
                    pg = (p0, p1)[ic % 2]
                    pu = (p2, p3)[ic % 2]
                    if ic >= 2:
                        e.wait_ge(s_s.h, FS1(ic - 2))
                    e.wait_ge(d_g[0].h, G0 + 64 * (ic + 1))
                    for hc in range(16):
                        wh = f_sl[0][:, hc * 256:hc * 256 + 128]
                        wl = f_sl[0][:, hc * 256 + 128:hc * 256 + 256]
                        e.matmul(pg[:, :], wh, xh3[:, hc, :],
                                 start=(hc == 0), stop=False)
                        e.matmul(pg[:, :], wh, xl3[:, hc, :],
                                 start=False, stop=False)
                        ins = e.matmul(pg[:, :], wl, xh3[:, hc, :],
                                       start=False, stop=(hc == 15))
                    ins.then_inc(s_t.h, 1)
                    s_t.bump(1)
                    if ic >= 2:
                        e.wait_ge(s_v.h, FV1(ic - 2))
                    e.wait_ge(d_g[1].h, U0 + 64 * (ic + 1))
                    for hc in range(16):
                        wh = f_sl[1][:, hc * 256:hc * 256 + 128]
                        wl = f_sl[1][:, hc * 256 + 128:hc * 256 + 256]
                        e.matmul(pu[:, :], wh, xh3[:, hc, :],
                                 start=(hc == 0), stop=False)
                        e.matmul(pu[:, :], wh, xl3[:, hc, :],
                                 start=False, stop=False)
                        ins = e.matmul(pu[:, :], wl, xh3[:, hc, :],
                                       start=False, stop=(hc == 15))
                    ins.then_inc(s_t.h, 1)
                    s_t.bump(1)

        TT1, VV1 = s_t.n, s_v.n
        A0 = (d_a[0].n, d_a[1].n)
        D0 = (d_d[0].n, d_d[1].n)
        DT = lambda hp, pc: TT1 + hp * 4 + pc + 1
        DV = lambda hp: VV1 + hp + 1
        with nc.Block() as blk:
            @blk.sync
            def _(e, th=th):
                for hp in range(16):
                    if hp >= 2:
                        e.wait_ge(s_v.h, DV(hp - 2))
                    e.dma_start(out=f_at[hp % 2],
                                in_=at_sc.ap()[hp * 128:(hp + 1) * 128,
                                               th * 512:(th + 1) * 512]
                                ).then_inc(d_a[hp % 2].h, 16)
                    d_a[hp % 2].bump(16)

            @blk.gpsimd
            def _(e, th=th):
                if th == 0:
                    e.wait_ge(s_cc.h, 2)
                for hp in range(16):
                    for pc in range(4):
                        gi = hp * 4 + pc
                        if gi >= 2:
                            e.wait_ge(s_t.h, DT((gi - 2) // 4, (gi - 2) % 4))
                        for rrr in range(2):
                            for xx in range(2):
                                e.dma_start(out=sld(f_dsl[gi % 2], rrr, xx),
                                            in_=down_slab_ap(hp, pc, rrr, xx)
                                            ).then_inc(d_d[gi % 2].h, 16)
                                d_d[gi % 2].bump(16)

            @blk.tensor
            def _(e, th=th):
                for hp in range(16):
                    pd = (p7, p5)[hp % 2]
                    if hp >= 2:
                        e.wait_ge(s_v.h, DV(hp - 2))
                    for pc in range(4):
                        gi = hp * 4 + pc
                        e.wait_ge(d_d[gi % 2].h, D0[gi % 2] + 64 * (gi // 2 + 1))
                        for kk in range(16):
                            ic = pc * 16 + kk
                            wh = f_dsl[gi % 2][:, kk * 256:kk * 256 + 128]
                            wl = f_dsl[gi % 2][:, kk * 256 + 128:kk * 256 + 256]
                            e.matmul(pd[:, :], wh, ith(ic),
                                     start=(pc == 0 and kk == 0), stop=False)
                            e.matmul(pd[:, :], wh, itl(ic),
                                     start=False, stop=False)
                            ins = e.matmul(pd[:, :], wl, ith(ic),
                                           start=False,
                                           stop=(pc == 3 and kk == 15))
                        ins.then_inc(s_t.h, 1)
                        s_t.bump(1)

            @blk.vector
            def _(e, th=th):
                for hp in range(16):
                    e.wait_ge(s_t.h, DT(hp, 3))
                    e.wait_ge(d_a[hp % 2].h, A0[hp % 2] + 16 * (hp // 2 + 1))
                    pd = (p7, p5)[hp % 2]
                    e.scalar_tensor_tensor(
                        out=f_sum[:, hp * 512:(hp + 1) * 512],
                        in0=pd[:, :], scalar=1.0, in1=f_at[hp % 2],
                        op0=ALU.mult, op1=ALU.add).then_inc(s_v.h, 1)
                    s_v.bump(1)

        TT2, VV2, XR0, DY0 = s_t.n, s_v.n, d_xr.n, d_y.n
        GT = lambda tc, g: TT2 + tc * 4 + g + 1
        GV = lambda tc, i: VV2 + tc * 5 + i + 1
        with nc.Block() as blk:
            @blk.sync
            def _(e, th=th):
                for tc in range(4):
                    if tc >= 1:
                        e.wait_ge(s_v.h, GV(tc - 1, 4))
                    e.dma_start(out=f_xt,
                                in_=x_in.ap()[(th * 4 + tc) * 128:
                                              (th * 4 + tc + 1) * 128, :]
                                ).then_inc(d_xr.h, 16)
                    d_xr.bump(16)

            @blk.tensor
            def _(e, th=th):
                for tc in range(4):
                    for g in range(4):
                        prev = (tc, g - 1) if g >= 1 else (
                            (tc - 1, 3) if tc >= 1 else None)
                        if prev is not None:
                            e.wait_ge(s_v.h, GV(prev[0], prev[1]))
                        for j in range(4):
                            hp = g * 4 + j
                            ins = e.transpose(p6[:, j * 128:(j + 1) * 128],
                                              f_sum[:, hp * 512 + tc * 128:
                                                    hp * 512 + (tc + 1) * 128],
                                              c_ident)
                        ins.then_inc(s_t.h, 1)
                        s_t.bump(1)

            @blk.vector
            def _(e, th=th):
                for tc in range(4):
                    for g in range(4):
                        e.wait_ge(s_t.h, GT(tc, g))
                        if g == 0:
                            if tc >= 2:
                                e.wait_ge(d_y.h, DY0 + 16 * (tc - 1))
                            elif th > 0:
                                e.wait_ge(d_y.h, DY0)
                        e.tensor_copy(out=f_os[tc % 2][:, g * 512:(g + 1) * 512],
                                      in_=p6[:, :]).then_inc(s_v.h, 1)
                        s_v.bump(1)
                    e.wait_ge(d_xr.h, XR0 + 16 * (tc + 1))
                    e.wait_ge(s_v.h, s_v.n)
                    e.tensor_tensor(out=f_os[tc % 2], in0=f_os[tc % 2],
                                    in1=f_xt, op=ALU.add).then_inc(s_v.h, 1)
                    s_v.bump(1)

            @blk.gpsimd
            def _(e, th=th):
                for tc in range(4):
                    e.wait_ge(s_v.h, GV(tc, 4))
                    e.dma_start(out=y_out.ap()[(th * 4 + tc) * 128:
                                               (th * 4 + tc + 1) * 128, :],
                                in_=f_os[tc % 2]).then_inc(d_y.h, 16)
                    d_y.bump(16)

    with nc.Block() as blk:
        @blk.gpsimd
        def _(e):
            e.wait_ge(d_y.h, d_y.n)
            if DEBUG:
                for dst, srcp in ((dbg_xno, xno_sc), (dbg_kt, kt_sc),
                                  (dbg_vt, vt_sc), (dbg_qt, qt_sc),
                                  (dbg_at, at_sc)):
                    e.dma_start(out=dst.ap()[:], in_=srcp.ap()[:]
                                ).then_inc(d_y.h, 16)
                    d_y.bump(16)
                e.dma_start(out=dbg_ag.ap()[0:2 * WOFF], in_=ag_in_gu.ap()[:]
                            ).then_inc(d_y.h, 16)
                d_y.bump(16)
                e.dma_start(out=dbg_ag.ap()[2 * WOFF:], in_=ag_in_d.ap()[:]
                            ).then_inc(d_y.h, 16)
                d_y.bump(16)
                e.dma_start(out=dbg_ago.ap()[:, 0:2 * WOFF], in_=ag_out_gu.ap()[:]
                            ).then_inc(d_y.h, 16)
                d_y.bump(16)
                e.dma_start(out=dbg_ago.ap()[:, 2 * WOFF:], in_=ag_out_d.ap()[:]
                            ).then_inc(d_y.h, 16)
                d_y.bump(16)
                e.wait_ge(d_y.h, d_y.n)

    nc.finalize()
    return nc



def _block4(w):
    hh, dd = w.shape
    return np.ascontiguousarray(
        w.reshape(hh // 128, 128, dd // 128, 128).transpose(2, 0, 1, 3))


def kernel(L0_attn_x, L0_attn_Wq, L0_attn_bq, L0_attn_Wk, L0_attn_bk,
           L0_attn_Wv, L0_attn_bv, L0_attn_Wo, L0_attn_bo,
           L0_ffn_gate_cp, L0_ffn_up_cp, L0_ffn_down_cp):
    st = _build_static()
    x = np.asarray(L0_attn_x, dtype=np.float32)
    wq = _block4(np.asarray(L0_attn_Wq, np.float32))
    wk = _block4(np.asarray(L0_attn_Wk, np.float32))
    wv = _block4(np.asarray(L0_attn_Wv, np.float32))
    wo = _block4(np.asarray(L0_attn_Wo, np.float32))
    bq = np.ascontiguousarray(np.asarray(L0_attn_bq, np.float32).reshape(16, 128).T)
    bk = np.ascontiguousarray(np.asarray(L0_attn_bk, np.float32).reshape(16, 128).T)
    bv = np.ascontiguousarray(np.asarray(L0_attn_bv, np.float32).reshape(16, 128).T)
    bo = np.ascontiguousarray(np.asarray(L0_attn_bo, np.float32).reshape(16, 128).T)
    cps = [np.asarray(L0_ffn_gate_cp, np.float32),
           np.asarray(L0_ffn_up_cp, np.float32),
           np.asarray(L0_ffn_down_cp, np.float32)]

    nc = build_nc()
    in_maps = []
    for c in range(NCORES):
        b = c // 2
        rows = np.concatenate([np.arange(bb * 128, bb * 128 + 128)
                               for bb in st["perms"][c]])
        xp = np.ascontiguousarray(x[b][rows])
        cgat = np.empty((3, 128, TPC * 5), np.float32)
        for w in range(3):
            cgat[w] = cps[w][st["cps_idx"][c][w]].reshape(128, TPC * 5)
        in_maps.append(dict(
            x=xp, wqb=wq, wkb=wk, wvb=wv, wob=wo,
            btab=st["btabs"][c], rpt=st["rpts"][c], maskt=st["masks"][c],
            bqt=bq, bkt=bk, bvt=bv, bot=bo,
            cpst=cgat, cppt=st["cpps"][c], fplt=st["fplans"][c],
            iota=st["iota"], ident=st["ident"], misc=st["misc"],
        ))

    res = run_bass_kernel_spmd(nc, in_maps, core_ids=list(range(NCORES)))
    out = np.empty((BATCH, S, D), np.float32)
    for c in range(NCORES):
        b = c // 2
        rows = np.concatenate([np.arange(bb * 128, bb * 128 + 128)
                               for bb in st["perms"][c]])[:TOK]
        out[b][rows] = res.results[c]["y"]
    return out
